# version 03
# baseline (speedup 1.0000x reference)
import numpy as np
from contextlib import ExitStack

import concourse.bass as bass
import concourse.mybir as mybir
from concourse.bass_utils import run_bass_kernel_spmd

F32 = mybir.dt.float32
BF16 = mybir.dt.bfloat16
I32 = mybir.dt.int32
AF = mybir.ActivationFunctionType
ALU = mybir.AluOpType
AX = mybir.AxisListType

D = 1024
T = 2048
NH = 16
DK = 64
DC = D // 128
TT = T // 128
FF0 = 4096
FF1 = 3072
N_SSM = 16
DT_RANK = 64
EPS_LN = 1e-5


class Sem:
    __slots__ = ("h", "n")
    def __init__(self, h):
        self.h = h
        self.n = 0
    def inc(self, inst, k=1):
        inst.then_inc(self.h, k)
        self.n += k
        return inst


class DmaChan:
    def __init__(self, nc, ctx, name, depth):
        self.sems = [Sem(ctx.enter_context(nc.semaphore(f"{name}{i}")))
                     for i in range(depth)]
        self.i = 0

    def issue(self, eng, out, in_):
        s = self.sems[self.i % len(self.sems)]
        self.i += 1
        eng.wait_ge(s.h, s.n)
        s.inc(eng.dma_start(out=out, in_=in_), 16)
        return (s.h, s.n)

    def drain(self, eng):
        for s in self.sems:
            eng.wait_ge(s.h, s.n)

    def snapshot(self):
        return (self.i, [s.n for s in self.sems])

    def plan(self, snap, j):
        i0, vals = snap
        k = len(self.sems)
        idx = (i0 + j) % k
        uses = j // k + 1
        return (self.sems[idx].h, vals[idx] + 16 * uses)


class Ctx:
    pass


def _wait_all(eng, c):
    for s in (c.sV, c.sA, c.sT, c.sG, c.sS):
        eng.wait_ge(s.h, s.n)


def emit_ph0_load(c):
    block = c.block
    t0, a0 = c.sT.n, c.sA.n
    tok = {}

    @block.sync
    def _(sync):
        for tt in range(TT):
            buf = c.stage[:, (tt % 2) * D:(tt % 2) * D + D]
            if tt >= 2:
                sync.wait_ge(c.sT.h, t0 + (tt - 1) * DC)
            tok[tt] = c.chL.issue(sync, buf, c.x_in[tt * 128:(tt + 1) * 128, :])

    @block.tensor
    def _(tensor):
        for tt in range(TT):
            buf = c.stage[:, (tt % 2) * D:(tt % 2) * D + D]
            tensor.wait_ge(*tok[tt])
            if tt == 0:
                tensor.wait_ge(*c.tok_ident)
            for dc in range(DC):
                pb = c.PB[dc]
                if tt >= 1:
                    tensor.wait_ge(c.sA.h, a0 + (tt - 1) * DC + dc + 1)
                c.sT.inc(tensor.transpose(pb[:, 0:128], buf[:, dc * 128:(dc + 1) * 128],
                                          c.ident_t[:]))

    @block.scalar
    def _(scalar):
        for tt in range(TT):
            for dc in range(DC):
                scalar.wait_ge(c.sT.h, t0 + tt * DC + dc + 1)
                c.sA.inc(scalar.activation(
                    c.res[:, dc * T + tt * 128: dc * T + tt * 128 + 128],
                    c.PB[dc][:, 0:128], AF.Copy, bias=0.0, scale=1.0))


def emit_ln(c, xh, xl, extra_full=None):
    block = c.block
    t0, a0, v0 = c.sT.n, c.sA.n, c.sV.n
    g0, s0 = c.sG.n, c.sS.n

    @block.scalar
    def _(scalar):
        _wait_all(scalar, c)
        for dc in range(DC):
            h = (dc % 2) * T
            if dc >= 2:
                scalar.wait_ge(c.sT.h, t0 + (dc - 1) * 16)
            r = c.res[:, dc * T:(dc + 1) * T]
            c.sA.inc(scalar.activation(c.sqbuf[:, h:h + T], r, AF.Square, bias=0.0, scale=1.0))
            c.sA.inc(scalar.activation(c.sbh[:, h:h + T], r, AF.Copy, bias=0.0, scale=1.0))
            scalar.wait_ge(c.sA.h, c.sA.n - 1)
            c.sA.inc(scalar.activation(c.sqh[:, h:h + T], c.sqbuf[:, h:h + T],
                                       AF.Copy, bias=0.0, scale=1.0))

    @block.vector
    def _(vector):
        for dc in range(DC):
            h = (dc % 2) * T
            vector.wait_ge(c.sA.h, a0 + dc * 3 + 3)
            r = c.res[:, dc * T:(dc + 1) * T]
            c.sV.inc(vector.tensor_sub(c.sbl[:, h:h + T], r, c.sbh[:, h:h + T]))
            c.sV.inc(vector.tensor_sub(c.sql[:, h:h + T], c.sqbuf[:, h:h + T],
                                       c.sqh[:, h:h + T]))

    @block.tensor
    def _(tensor):
        tensor.wait_ge(c.sV.h, v0)
        tensor.wait_ge(c.sA.h, a0)
        tensor.wait_ge(c.sG.h, g0)
        for dc in range(DC):
            h = (dc % 2) * T
            tensor.wait_ge(c.sV.h, v0 + dc * 2 + 2)
            st = (dc == 0)
            sp = (dc == DC - 1)
            for q in range(4):
                sl = slice(h + q * 512, h + (q + 1) * 512)
                c.sT.inc(tensor.matmul(c.PB[q][0:64, :], c.ones_bf[:, 0:64],
                                       c.sbh[:, sl], start=st, stop=False))
                c.sT.inc(tensor.matmul(c.PB[q][0:64, :], c.ones_bf[:, 0:64],
                                       c.sbl[:, sl], start=False, stop=sp))
                c.sT.inc(tensor.matmul(c.PB[4 + q][0:64, :], c.ones_bf[:, 0:64],
                                       c.sqh[:, sl], start=st, stop=False))
                c.sT.inc(tensor.matmul(c.PB[4 + q][0:64, :], c.ones_bf[:, 0:64],
                                       c.sql[:, sl], start=False, stop=sp))

    t1 = c.sT.n

    @block.vector
    def _(vector):
        vector.wait_ge(c.sT.h, t1)
        for q in range(4):
            sl = slice(q * 512, (q + 1) * 512)
            c.sV.inc(vector.tensor_scalar_mul(c.rows[0:1, sl], c.PB[q][0:1, :], -1.0 / D))
            c.sV.inc(vector.tensor_scalar_mul(c.rows[32:33, sl], c.PB[4 + q][0:1, :], 1.0 / D))
            vector.wait_ge(c.sV.h, c.sV.n)
            c.sV.inc(vector.scalar_tensor_tensor(
                c.PB[4 + q][32:33, :], c.rows[0:1, sl], -1.0, c.rows[0:1, sl],
                ALU.mult, ALU.mult))
            vector.wait_ge(c.sV.h, c.sV.n)
            c.sV.inc(vector.tensor_add(c.rows[64:65, sl], c.rows[32:33, sl],
                                       c.PB[4 + q][32:33, :]))
        vector.wait_ge(c.sV.h, c.sV.n)
        c.sV.inc(vector.tensor_scalar_add(c.rows[64:65, 0:T], c.rows[64:65, 0:T], EPS_LN))

    a1 = c.sA.n
    @block.scalar
    def _(scalar):
        scalar.wait_ge(c.sV.h, c.sV.n)
        c.sA.inc(scalar.activation(c.rows[96:97, 0:T], c.rows[64:65, 0:T], AF.Sqrt,
                                   bias=0.0, scale=1.0))

    v1 = c.sV.n
    @block.vector
    def _(vector):
        vector.wait_ge(c.sA.h, a1 + 1)
        c.sV.inc(vector.reciprocal(c.rows[32:33, 0:T], c.rows[96:97, 0:T]))

    t2, a2, v2 = c.sT.n, c.sA.n, c.sV.n

    @block.tensor
    def _(tensor):
        tensor.wait_ge(c.sV.h, v1 + 1)
        for q in range(4):
            sl = slice(q * 512, (q + 1) * 512)
            pm = c.PB[q % 2]
            pr = c.PB[2 + q % 2]
            if q >= 2:
                tensor.wait_ge(c.sV.h, v2 + (q - 1) * DC * 3)
            c.sT.inc(tensor.matmul(pm[:, :], c.ones_f32[0:1, :], c.rows[0:1, sl],
                                   start=True, stop=True))
            c.sT.inc(tensor.matmul(pr[:, :], c.ones_f32[32:33, :], c.rows[32:33, sl],
                                   start=True, stop=True))

    @block.vector
    def _(vector):
        for q in range(4):
            pm = c.PB[q % 2]
            pr = c.PB[2 + q % 2]
            vector.wait_ge(c.sT.h, t2 + 2 * (q + 1))
            for dc in range(DC):
                src = slice(dc * T + q * 512, dc * T + (q + 1) * 512)
                dst_full = c.xfull[:, 0:512] if extra_full is None else extra_full[:, src]
                c.sV.inc(vector.tensor_add(c.tmp1[:, 0:512], c.res[:, src], pm[:, :]))
                vector.wait_ge(c.sV.h, c.sV.n)
                c.sV.inc(vector.tensor_mul(dst_full, c.tmp1[:, 0:512], pr[:, :]))
                vector.wait_ge(c.sV.h, c.sV.n)
                vector.wait_ge(c.sA.h, a2 + q * DC + dc + 1)
                c.sV.inc(vector.tensor_sub(xl[:, src], dst_full, xh[:, src]))

    @block.scalar
    def _(scalar):
        for q in range(4):
            for dc in range(DC):
                src = slice(dc * T + q * 512, dc * T + (q + 1) * 512)
                dst_full = c.xfull[:, 0:512] if extra_full is None else extra_full[:, src]
                scalar.wait_ge(c.sV.h, v2 + (q * DC + dc) * 3 + 2)
                c.sA.inc(scalar.activation(xh[:, src], dst_full, AF.Copy,
                                           bias=0.0, scale=1.0))



def emit_l0_attn(c):
    for p in range(NH // 2):
        for wi, (wsrc, dsth, dstl, permuted) in enumerate([
                (c.w_q, c.WTQH, c.WTQL, True),
                (c.w_k, c.WTKH, c.WTKL, True),
                (c.w_v, c.WTVH, c.WTVL, False)]):
            _emit_wprep(c, p, wsrc, dsth, dstl, permuted)

        for which, (dsth, dstl, wth, wtl, bcol) in enumerate([
                (c.QH, c.QL, c.WTQH, c.WTQL, c.bias_qkv[:, 0 * 8 + p:0 * 8 + p + 1]),
                (c.KH, c.KL, c.WTKH, c.WTKL, c.bias_qkv[:, 1 * 8 + p:1 * 8 + p + 1])]):
            for tq in range(4):
                _emit_qk_proj_tile(c, p, which, tq, dsth, dstl, wth, wtl, bcol)

        _emit_bounds(c, p)

        for tq in range(4):
            _emit_v_proj_tile(c, p, tq)

        _emit_wprep(c, p, c.w_o, c.WTVH, c.WTVL, False, col_slice=True)

        c.attn_prev = {}
        for tq in range(4):
            _emit_attn_tile(c, p, tq)

        _emit_recip(c, p)

        for tq in range(4):
            _emit_attn_epilogue_tile(c, p, tq)


def _emit_wprep(c, p, wsrc, dsth, dstl, permuted, col_slice=False):
    block = c.block
    wraw = c.ARF[:, 0:1024]
    t0, a0, v0 = c.sT.n, c.sA.n, c.sV.n

    @block.sync
    def _(sync):
        sync.wait_ge(c.sT.h, t0)
        if col_slice:
            src = wsrc[:, p * 128:(p + 1) * 128].rearrange("(ec r) f -> ec r f", r=128)
            for ec in range(DC):
                c.wtok[ec] = c.chW.issue(sync, wraw[:, ec * 128:(ec + 1) * 128], src[ec])
        else:
            src = wsrc[p * 128:(p + 1) * 128, :]
            c.wtok[0] = c.chW.issue(sync, wraw, src)

    @block.tensor
    def _(tensor):
        tensor.wait_ge(c.sA.h, a0)
        tensor.wait_ge(c.sV.h, v0)
        for dc in range(DC):
            tensor.wait_ge(*c.wtok[dc if col_slice else 0])
            pb = c.PB[6 + dc % 2]
            if dc >= 2:
                tensor.wait_ge(c.sV.h, v0 + (dc - 1))
                tensor.wait_ge(c.sA.h, a0 + (dc - 1))
            c.sT.inc(tensor.transpose(pb[:, 0:128], wraw[:, dc * 128:(dc + 1) * 128],
                                      c.ident_t[:]))

    def psrc(pb):
        if not permuted:
            return pb[:, 0:128]
        return pb[:, 0:128].rearrange("p (h b a) -> p h b a", h=2, b=2, a=32).transpose([0, 1, 3, 2])

    def dvw(t, dc):
        sl = t[:, dc * 128:(dc + 1) * 128]
        if not permuted:
            return sl
        return sl.rearrange("p (h a b) -> p h a b", h=2, a=32, b=2)

    @block.scalar
    def _(scalar):
        for dc in range(DC):
            pb = c.PB[6 + dc % 2]
            scalar.wait_ge(c.sT.h, t0 + dc + 1)
            c.sA.inc(scalar.activation(dvw(dsth, dc), psrc(pb),
                                       AF.Copy, bias=0.0, scale=1.0))

    @block.vector
    def _(vector):
        for dc in range(DC):
            pb = c.PB[6 + dc % 2]
            vector.wait_ge(c.sA.h, a0 + dc + 1)
            c.sV.inc(vector.tensor_sub(dvw(dstl, dc), psrc(pb), dvw(dsth, dc)))


def _emit_qk_proj_tile(c, p, which, tq, dsth, dstl, wth, wtl, bcol):
    block = c.block
    sl = slice(tq * 512, (tq + 1) * 512)
    qb = c.ARF[:, 0:512]
    qs = c.ARF[:, 512:1024]
    m2 = c.ARF[:, 1024:1536]
    qf = c.ARF[:, 1536:2048]
    sqn = c.ARF[:, 2048:2560]
    t0, a0, v0 = c.sT.n, c.sA.n, c.sV.n

    @block.tensor
    def _(tensor):
        tensor.wait_ge(c.sA.h, a0)
        tensor.wait_ge(c.sV.h, v0)
        n = 0
        for wt, x in ((wth, XH_g(c)), (wth, XL_g(c)), (wtl, XH_g(c))):
            for dc in range(DC):
                n += 1
                c.sT.inc(tensor.matmul(c.PB[5][:, :],
                                       wt[:, dc * 128:(dc + 1) * 128],
                                       x[:, dc * T + tq * 512: dc * T + (tq + 1) * 512],
                                       start=(n == 1), stop=(n == 24)))

    t_mm = c.sT.n

    @block.scalar
    def _(scalar):
        scalar.wait_ge(c.sT.h, t_mm)
        c.sA.inc(scalar.activation(qb, c.PB[5][:, :], AF.Identity, bias=bcol, scale=1.0))

    @block.vector
    def _(vector):
        vector.wait_ge(c.sA.h, a0 + 1)
        c.sV.inc(vector.stream_shuffle(qs, qb, [(i ^ 1) for i in range(32)]))
        c.sV.inc(vector.tensor_mul(c.PB[6][:, :], qb, c.ropeA[:, sl]))
        vector.wait_ge(c.sV.h, c.sV.n)
        c.sV.inc(vector.tensor_mul(m2, qs, c.ropeB[:, sl]))
        vector.wait_ge(c.sV.h, c.sV.n)
        c.sV.inc(vector.tensor_add(qf, c.PB[6][:, :], m2))

    @block.scalar
    def _(scalar):
        scalar.wait_ge(c.sV.h, v0 + 4)
        c.sA.inc(scalar.activation(dsth[:, sl], qf, AF.Copy, bias=0.0, scale=1.0))
        c.sA.inc(scalar.activation(sqn, qf, AF.Square, bias=0.0, scale=1.0))

    @block.tensor
    def _(tensor):
        tensor.wait_ge(c.sA.h, a0 + 3)
        tensor.wait_ge(c.sV.h, v0)
        c.sT.inc(tensor.matmul(c.PB[6][0:64, :], c.ones_f32[0:64, 0:64], sqn[0:64, :],
                               start=True, stop=True))
        c.sT.inc(tensor.matmul(c.PB[7][0:64, :], c.ones_f32[64:128, 0:64], sqn[64:128, :],
                               start=True, stop=True))

    @block.vector
    def _(vector):
        vector.wait_ge(c.sA.h, a0 + 3)
        vector.wait_ge(c.sV.h, c.sV.n)
        c.sV.inc(vector.tensor_sub(dstl[:, sl], qf, dsth[:, sl]))
        vector.wait_ge(c.sT.h, t_mm + 2)
        cellA = c.nrm[0:1, which * 2:which * 2 + 1]
        cellB = c.nrm[0:1, which * 2 + 1:which * 2 + 2]
        scr = c.nrm[0:1, 8:9]
        c.sV.inc(vector.tensor_reduce(scr, c.PB[6][0:1, :], AX.X, ALU.max))
        vector.wait_ge(c.sV.h, c.sV.n)
        if tq == 0:
            c.sV.inc(vector.tensor_copy(cellA, scr))
        else:
            c.sV.inc(vector.tensor_max(cellA, cellA, scr))
        vector.wait_ge(c.sV.h, c.sV.n)
        c.sV.inc(vector.tensor_reduce(scr, c.PB[7][0:1, :], AX.X, ALU.max))
        vector.wait_ge(c.sV.h, c.sV.n)
        if tq == 0:
            c.sV.inc(vector.tensor_copy(cellB, scr))
        else:
            c.sV.inc(vector.tensor_max(cellB, cellB, scr))


def _emit_bounds(c, p):
    block = c.block
    v0, a0 = c.sV.n, c.sA.n

    @block.scalar
    def _(scalar):
        scalar.wait_ge(c.sV.h, v0)
        c.sA.inc(scalar.activation(c.nrm[0:1, 4:8], c.nrm[0:1, 0:4], AF.Sqrt,
                                   bias=0.0, scale=1.0))

    @block.vector
    def _(vector):
        vector.wait_ge(c.sA.h, a0 + 1)
        c.sV.inc(vector.tensor_mul(c.nrm[0:1, 9:10], c.nrm[0:1, 4:5], c.nrm[0:1, 6:7]))
        c.sV.inc(vector.tensor_mul(c.nrm[0:1, 10:11], c.nrm[0:1, 5:6], c.nrm[0:1, 7:8]))
        vector.wait_ge(c.sV.h, c.sV.n)
        c.sV.inc(vector.tensor_scalar_mul(c.nrm[0:1, 9:11], c.nrm[0:1, 9:11], -0.125))

    @block.sync
    def _(sync):
        sync.wait_ge(c.sV.h, c.sV.n)
        t1 = c.chC.issue(sync, c.scal_dram.ap()[0:1, 0:2], c.nrm[0:1, 9:11])
        sync.wait_ge(*t1)
        t2 = c.chC.issue(sync, c.biasA, c.scal_dram.ap()[0:1, 0:1].partition_broadcast(128))
        t3 = c.chC.issue(sync, c.biasB, c.scal_dram.ap()[0:1, 1:2].partition_broadcast(128))
        c.tok_bias = (t2, t3)


def _emit_v_proj_tile(c, p, tq):
    block = c.block
    qf = c.ARF[:, 1536:2048]
    bcol = c.bias_qkv[:, 2 * 8 + p:2 * 8 + p + 1]
    t0, a0, v0 = c.sT.n, c.sA.n, c.sV.n

    @block.tensor
    def _(tensor):
        tensor.wait_ge(c.sA.h, a0)
        n = 0
        for wt, x in ((c.WTVH, XH_g(c)), (c.WTVH, XL_g(c)), (c.WTVL, XH_g(c))):
            for dc in range(DC):
                n += 1
                c.sT.inc(tensor.matmul(c.PB[5][:, :],
                                       wt[:, dc * 128:(dc + 1) * 128],
                                       x[:, dc * T + tq * 512: dc * T + (tq + 1) * 512],
                                       start=(n == 1), stop=(n == 24)))

    t_mm = c.sT.n

    @block.scalar
    def _(scalar):
        scalar.wait_ge(c.sT.h, t_mm)
        c.sA.inc(scalar.activation(qf, c.PB[5][:, :], AF.Identity, bias=bcol, scale=1.0))

    @block.tensor
    def _(tensor):
        tensor.wait_ge(c.sA.h, a0 + 1)
        tensor.wait_ge(c.sV.h, v0)
        for sub in range(4):
            pb = c.PB[6 + sub % 2]
            if sub >= 2:
                tensor.wait_ge(c.sV.h, v0 + (sub - 1) * 2)
            c.sT.inc(tensor.transpose(pb[:, 0:128], qf[:, sub * 128:(sub + 1) * 128],
                                      c.ident_t[:]))

    @block.scalar
    def _(scalar):
        for sub in range(4):
            kc = tq * 4 + sub
            pb = c.PB[6 + sub % 2]
            scalar.wait_ge(c.sT.h, t_mm + sub + 1)
            c.sA.inc(scalar.activation(c.VSA[:, kc * 128: kc * 128 + 64], pb[:, 0:64],
                                       AF.Copy, bias=0.0, scale=1.0))
            c.sA.inc(scalar.activation(c.VSB[:, kc * 128: kc * 128 + 64], pb[:, 64:128],
                                       AF.Copy, bias=0.0, scale=1.0))

    @block.vector
    def _(vector):
        for sub in range(4):
            kc = tq * 4 + sub
            pb = c.PB[6 + sub % 2]
            vector.wait_ge(c.sA.h, a0 + 1 + 2 * (sub + 1))
            c.sV.inc(vector.tensor_sub(c.VSA[:, kc * 128 + 64: kc * 128 + 128],
                                       pb[:, 0:64], c.VSA[:, kc * 128: kc * 128 + 64]))
            c.sV.inc(vector.tensor_sub(c.VSB[:, kc * 128 + 64: kc * 128 + 128],
                                       pb[:, 64:128], c.VSB[:, kc * 128: kc * 128 + 64]))


def _emit_attn_tile(c, p, tq):
    block = c.block
    nk = 4 * (tq + 1)
    qsl = slice(tq * 512, (tq + 1) * 512)
    t0, a0, v0 = c.sT.n, c.sA.n, c.sV.n

    tS, tAV, aA, aB, vM, vPA, vPB = {}, {}, {}, {}, {}, {}, {}
    t = t0
    for kc in range(nk):
        t += 6
        tS[kc] = t
        if kc >= 1:
            t += 10
            tAV[kc - 1] = t
    t += 10
    tAV[nk - 1] = t
    a = a0
    for kc in range(nk):
        a += 2
        aA[kc] = a
        a += 2
        aB[kc] = a
    a_evac = a + 4
    v = v0
    for kc in range(nk):
        if 0 <= kc - 4 * tq < 4:
            v += 2
        vM[kc] = v
        v += 1
        vPA[kc] = v
        v += 1
        vPB[kc] = v

    prev_sub = c.attn_prev.get("sub", {})
    prev_av = c.attn_prev.get("av", {})
    prev_evac = c.attn_prev.get("evac", 0)

    @block.tensor
    def _(tensor):
        tensor.wait_ge(c.sA.h, prev_evac)
        if tq == 0:
            tensor.wait_ge(c.sV.h, v0)
        for kc in range(nk):
            psA = c.PB[kc % 2]
            psB = c.PB[2 + kc % 2]
            if kc >= 2:
                tensor.wait_ge(c.sV.h, vPB[kc - 2])
            elif (kc % 2) in prev_sub:
                tensor.wait_ge(c.sV.h, prev_sub[kc % 2])
            ksl = slice(kc * 128, (kc + 1) * 128)
            for ps, lo in ((psA, 0), (psB, 64)):
                h = slice(lo, lo + 64)
                c.sT.inc(tensor.matmul(ps[:, :], c.KH[h, ksl], c.QH[h, qsl],
                                       start=True, stop=False))
                c.sT.inc(tensor.matmul(ps[:, :], c.KH[h, ksl], c.QL[h, qsl],
                                       start=False, stop=False))
                c.sT.inc(tensor.matmul(ps[:, :], c.KL[h, ksl], c.QH[h, qsl],
                                       start=False, stop=True))
            if kc >= 1:
                _av_mms(c, tensor, tq, kc - 1, nk, vPB)
        _av_mms(c, tensor, tq, nk - 1, nk, vPB)

    @block.scalar
    def _(scalar):
        if tq == 0:
            scalar.wait_ge(*c.tok_bias[0])
            scalar.wait_ge(*c.tok_bias[1])
        for kc in range(nk):
            psA = c.PB[kc % 2]
            psB = c.PB[2 + kc % 2]
            ph = c.PHt[kc % 2]
            scalar.wait_ge(c.sT.h, tS[kc])
            scalar.wait_ge(c.sV.h, vM[kc])
            if kc >= 2:
                scalar.wait_ge(c.sT.h, tAV[kc - 2])
            elif (kc % 2) in prev_av:
                scalar.wait_ge(c.sT.h, prev_av[kc % 2])
            c.sA.inc(scalar.activation(ph[:, 0:512], psA[:, :], AF.Exp,
                                       bias=c.biasA[:, 0:1], scale=0.125))
            scalar.wait_ge(c.sA.h, c.sA.n)
            c.sA.inc(scalar.activation(psA[:, :], psA[:, :], AF.Exp,
                                       bias=c.biasA[:, 0:1], scale=0.125))
            c.sA.inc(scalar.activation(ph[:, 512:1024], psB[:, :], AF.Exp,
                                       bias=c.biasB[:, 0:1], scale=0.125))
            scalar.wait_ge(c.sA.h, c.sA.n)
            c.sA.inc(scalar.activation(psB[:, :], psB[:, :], AF.Exp,
                                       bias=c.biasB[:, 0:1], scale=0.125))
        scalar.wait_ge(c.sT.h, tAV[nk - 1])
        c.sA.inc(scalar.activation(c.AVS[0:64, tq * 512:(tq + 1) * 512],
                                   c.PB[4][0:64, :], AF.Copy, bias=0.0, scale=1.0))
        c.sA.inc(scalar.activation(c.AVS[64:128, tq * 512:(tq + 1) * 512],
                                   c.PB[5][64:128, :], AF.Copy, bias=0.0, scale=1.0))
        c.sA.inc(scalar.activation(c.rows[64:65, qsl], c.PB[6][0:1, :],
                                   AF.Copy, bias=0.0, scale=1.0))
        c.sA.inc(scalar.activation(c.rows[96:97, qsl], c.PB[7][0:1, :],
                                   AF.Copy, bias=0.0, scale=1.0))

    @block.vector
    def _(vector):
        for kc in range(nk):
            psA = c.PB[kc % 2]
            psB = c.PB[2 + kc % 2]
            ph = c.PHt[kc % 2]
            pl = c.PLt[kc % 2]
            mi = kc - 4 * tq
            vector.wait_ge(c.sT.h, tS[kc])
            if 0 <= mi < 4:
                msl = slice(384 - mi * 128, 384 - mi * 128 + 512)
                c.sV.inc(vector.tensor_add(psA[:, :], psA[:, :], c.MASKS[:, msl]))
                c.sV.inc(vector.tensor_add(psB[:, :], psB[:, :], c.MASKS[:, msl]))
            if kc >= 2:
                vector.wait_ge(c.sT.h, tAV[kc - 2])
            elif (kc % 2) in prev_av:
                vector.wait_ge(c.sT.h, prev_av[kc % 2])
            vector.wait_ge(c.sA.h, aA[kc])
            c.sV.inc(vector.tensor_sub(pl[:, 0:512], psA[:, :], ph[:, 0:512]))
            vector.wait_ge(c.sA.h, aB[kc])
            c.sV.inc(vector.tensor_sub(pl[:, 512:1024], psB[:, :], ph[:, 512:1024]))

    c.attn_prev = {
        "sub": {(nk - 1) % 2: vPB[nk - 1], (nk - 2) % 2: vPB[nk - 2]},
        "av": {(nk - 1) % 2: tAV[nk - 1], (nk - 2) % 2: tAV[nk - 2]},
        "evac": a_evac,
    }


def _av_mms(c, tensor, tq, kc, nk, vPB):
    ph = c.PHt[kc % 2]
    pl = c.PLt[kc % 2]
    tensor.wait_ge(c.sV.h, vPB[kc])
    first = (kc == 0)
    last = (kc == nk - 1)
    for lo, avb in ((0, c.PB[4]), (64, c.PB[5])):
        phs = ph[:, lo * 8: lo * 8 + 512]
        pls = pl[:, lo * 8: lo * 8 + 512]
        vh = c.VSA if lo == 0 else c.VSB
        hi_t = vh[:, kc * 128: kc * 128 + 64]
        lo_t = vh[:, kc * 128 + 64: kc * 128 + 128]
        n = 0
        for (st_, mv) in ((hi_t, phs), (lo_t, phs), (hi_t, pls)):
            n += 1
            c.sT.inc(tensor.matmul(avb[lo:lo + 64, :], st_, mv,
                                   start=(first and n == 1), stop=(last and n == 3),
                                   tile_position=(0, lo)))
        dn = c.PB[6] if lo == 0 else c.PB[7]
        c.sT.inc(tensor.matmul(dn[0:64, :], c.ones_bf[:, 0:64], phs,
                               start=first, stop=False))
        c.sT.inc(tensor.matmul(dn[0:64, :], c.ones_bf[:, 0:64], pls,
                               start=False, stop=last))


def _emit_recip(c, p):
    block = c.block
    a0 = c.sA.n
    recD = c.ARF[0:8, 2048:2560]

    @block.sync
    def _(sync):
        sync.wait_ge(c.sA.h, a0)
        t1 = c.chC.issue(sync, recD[0:4, :], c.rows[64:65, :])
        t2 = c.chC.issue(sync, recD[4:8, :], c.rows[96:97, :])
        c.tok_recd = (t1, t2)

    v0 = c.sV.n
    @block.vector
    def _(vector):
        vector.wait_ge(*c.tok_recd[0])
        vector.wait_ge(*c.tok_recd[1])
        c.sV.inc(vector.reciprocal(recD[0:8, :], recD[0:8, :]))

    @block.sync
    def _(sync):
        sync.wait_ge(c.sV.h, c.sV.n)
        t1 = c.chC.issue(sync, c.rows[0:1, :], recD[0:4, :])
        t2 = c.chC.issue(sync, c.rows[32:33, :], recD[4:8, :])
        c.tok_recr = (t1, t2)


def _emit_attn_epilogue_tile(c, p, tq):
    block = c.block
    qsl = slice(tq * 512, (tq + 1) * 512)
    osb = c.ARF[:, 2560:3072]
    t0, a0, v0 = c.sT.n, c.sA.n, c.sV.n

    @block.tensor
    def _(tensor):
        tensor.wait_ge(c.sV.h, v0)
        if tq == 0:
            tensor.wait_ge(*c.tok_recr[0])
            tensor.wait_ge(*c.tok_recr[1])
        c.sT.inc(tensor.matmul(c.PB[0][0:64, :], c.ones_f32[0:1, 0:64],
                               c.rows[0:1, qsl], start=True, stop=True,
                               tile_position=(0, 0)))
        c.sT.inc(tensor.matmul(c.PB[1][64:128, :], c.ones_f32[32:33, 0:64],
                               c.rows[32:33, qsl], start=True, stop=True,
                               tile_position=(32, 64)))

    t_bc = c.sT.n

    @block.vector
    def _(vector):
        vector.wait_ge(c.sT.h, t_bc)
        c.sV.inc(vector.tensor_mul(osb[0:64, :], c.AVS[0:64, qsl], c.PB[0][0:64, :]))
        c.sV.inc(vector.tensor_mul(osb[64:128, :], c.AVS[64:128, qsl], c.PB[1][64:128, :]))

    @block.scalar
    def _(scalar):
        scalar.wait_ge(c.sV.h, v0 + 2)
        c.sA.inc(scalar.activation(c.OH[:, 0:512], osb, AF.Copy, bias=0.0, scale=1.0))

    @block.vector
    def _(vector):
        vector.wait_ge(c.sA.h, a0 + 1)
        vector.wait_ge(c.sV.h, c.sV.n)
        c.sV.inc(vector.tensor_sub(c.OL[:, 0:512], osb, c.OH[:, 0:512]))

    v_spl = c.sV.n

    @block.tensor
    def _(tensor):
        tensor.wait_ge(c.sV.h, v_spl)
        for ec in range(DC):
            pb = c.PB[6 + ec % 2]
            if ec >= 2:
                tensor.wait_ge(c.sV.h, v_spl + (ec - 1))
            n = 0
            for wt, mv in ((c.WTVH, c.OH[:, 0:512]), (c.WTVH, c.OL[:, 0:512]),
                           (c.WTVL, c.OH[:, 0:512])):
                n += 1
                c.sT.inc(tensor.matmul(pb[:, :], wt[:, ec * 128:(ec + 1) * 128], mv,
                                       start=(n == 1), stop=(n == 3)))

    t_wo = c.sT.n

    @block.vector
    def _(vector):
        for ec in range(DC):
            pb = c.PB[6 + ec % 2]
            vector.wait_ge(c.sT.h, t_wo - 3 * (DC - 1 - ec))
            dst = c.res[:, ec * T + tq * 512: ec * T + (tq + 1) * 512]
            if p == 0:
                c.sV.inc(vector.scalar_tensor_tensor(
                    dst, pb[:, :], c.bias_qkv[:, 3 * 8 + ec:3 * 8 + ec + 1], dst,
                    ALU.add, ALU.add))
            else:
                c.sV.inc(vector.tensor_add(dst, dst, pb[:, :]))


def XH_g(c):
    return c.XH


def XL_g(c):
    return c.XL



def emit_ffn0_gamma(c, wsrc, nrows, ncols, cell):
    block = c.block
    wstage = c.ARF[:, 0:2048]
    nch = nrows // 128
    ncg = ncols // 1024
    a0, v0 = c.sA.n, c.sV.n
    tok = {}

    @block.sync
    def _(sync):
        _wait_all(sync, c)
        k = 0
        for ch in range(nch):
            for cg in range(ncg):
                if k >= 2:
                    sync.wait_ge(c.sA.h, a0 + (k - 1))
                tok[k] = c.chW.issue(sync, wstage[:, (k % 2) * 1024:(k % 2) * 1024 + 1024],
                                     wsrc[ch * 128:(ch + 1) * 128, cg * 1024:(cg + 1) * 1024])
                k += 1

    nk = nch * ncg

    @block.scalar
    def _(scalar):
        for k in range(nk):
            scalar.wait_ge(*tok[k])
            if k >= 1:
                scalar.wait_ge(c.sV.h, v0 + k)
            h = (k % 2) * 1024
            c.sA.inc(scalar.activation(wstage[:, h:h + 1024], wstage[:, h:h + 1024],
                                       AF.Abs, bias=0.0, scale=1.0,
                                       accum_out=c.acol[:, 0:1]))

    @block.vector
    def _(vector):
        for k in range(nk):
            vector.wait_ge(c.sA.h, a0 + k + 1)
            if k == 0:
                c.sV.inc(vector.tensor_copy(c.gcol[:, 0:1], c.acol[:, 0:1]))
            else:
                c.sV.inc(vector.tensor_add(c.gcol[:, 0:1], c.gcol[:, 0:1], c.acol[:, 0:1]))
            vector.wait_ge(c.sV.h, c.sV.n)

    @block.gpsimd
    def _(gpsimd):
        gpsimd.wait_ge(c.sV.h, c.sV.n)
        c.sG.inc(gpsimd.tensor_reduce(c.nrm[0:1, cell:cell + 1], c.gcol[:, 0:1],
                                      AX.C, ALU.add))

    @block.vector
    def _(vector):
        vector.wait_ge(c.sG.h, c.sG.n)
        c.sV.inc(vector.tensor_scalar(c.nrm[0:1, cell:cell + 1],
                                      c.nrm[0:1, cell:cell + 1],
                                      1.0 / (nrows * float(ncols)), None, ALU.mult))
        vector.wait_ge(c.sV.h, c.sV.n)
        c.sV.inc(vector.tensor_scalar(c.nrm[0:1, cell:cell + 1],
                                      c.nrm[0:1, cell:cell + 1],
                                      1e-05, 0.5, ALU.add, ALU.mult))


def emit_ffn0_wprep(c, wsrc, nrows, ncols, cell, thr_cells, dst_fn):
    block = c.block
    wstage = c.ARF[:, 0:2048]
    tpos, tneg = thr_cells
    nch = nrows // 128
    ncg = ncols // 1024
    t0, a0, v0 = c.sT.n, c.sA.n, c.sV.n
    tok = {}
    snapQ = c.chQ.snapshot()

    @block.sync
    def _(sync):
        _wait_all(sync, c)
        t1 = c.chC.issue(sync, c.scal_dram.ap()[0:1, 0:1], c.nrm[0:1, cell:cell + 1])
        sync.wait_ge(*t1)
        t2 = c.chC.issue(sync, c.ARF[:, tpos:tpos + 1],
                         c.scal_dram.ap()[0:1, 0:1].partition_broadcast(128))
        c.tok_thr = t2

    @block.vector
    def _(vector):
        vector.wait_ge(*c.tok_thr)
        c.sV.inc(vector.tensor_scalar_mul(c.ARF[:, tneg:tneg + 1],
                                          c.ARF[:, tpos:tpos + 1], -1.0))

    v_thr = c.sV.n

    @block.sync
    def _(sync):
        k = 0
        for ch in range(nch):
            for cg in range(ncg):
                if k >= 2:
                    sync.wait_ge(c.sT.h, t0 + (k - 1) * DC)
                tok[k] = c.chW.issue(sync, wstage[:, (k % 2) * 1024:(k % 2) * 1024 + 1024],
                                     wsrc[ch * 128:(ch + 1) * 128, cg * 1024:(cg + 1) * 1024])
                k += 1

    @block.tensor
    def _(tensor):
        tensor.wait_ge(c.sV.h, v_thr)
        k = 0
        for ch in range(nch):
            for cg in range(ncg):
                tensor.wait_ge(*tok[k])
                h = (k % 2) * 1024
                for dc in range(DC):
                    pb = c.PB[4 + (dc % 2)]
                    n = k * DC + dc
                    if n >= 2:
                        tensor.wait_ge(c.sV.h, v_thr + (n - 1) * 2)
                    c.sT.inc(tensor.transpose(pb[:, 0:128],
                                              wstage[:, h + dc * 128:h + (dc + 1) * 128],
                                              c.ident_t[:]))
                k += 1

    @block.vector
    def _(vector):
        for sm in c.chQ.sems:
            vector.wait_ge(sm.h, sm.n)
        n = 0
        for ch in range(nch):
            for cg in range(ncg):
                for dc in range(DC):
                    pb = c.PB[4 + (dc % 2)]
                    vector.wait_ge(c.sT.h, t0 + n + 1)
                    stg = (c.OH if (n % 2 == 0) else c.OL)[:, (n // 2 % 2) * 128:(n // 2 % 2) * 128 + 128]
                    if n >= 4:
                        vector.wait_ge(*c.chQ.plan(snapQ, n - 4))
                    vector.wait_ge(c.sV.h, c.sV.n)
                    c.sV.inc(vector.tensor_scalar(c.qtmp[:, 0:128], pb[:, 0:128],
                                                  c.ARF[:, tneg:tneg + 1], None, ALU.is_le))
                    vector.wait_ge(c.sV.h, c.sV.n)
                    c.sV.inc(vector.scalar_tensor_tensor(
                        stg, pb[:, 0:128], c.ARF[:, tpos:tpos + 1],
                        c.qtmp[:, 0:128], ALU.is_ge, ALU.subtract))
                    n += 1

    @block.scalar
    def _(scalar):
        n = 0
        for ch in range(nch):
            for cg in range(ncg):
                for dc in range(DC):
                    scalar.wait_ge(c.sV.h, v_thr + (n + 1) * 2)
                    stg = (c.OH if (n % 2 == 0) else c.OL)[:, (n // 2 % 2) * 128:(n // 2 % 2) * 128 + 128]
                    tkn = c.chQ.issue(scalar, dst_fn(ch, cg, dc), stg)
                    n += 1


def emit_ln_stats(c):
    block = c.block
    t0, a0, v0 = c.sT.n, c.sA.n, c.sV.n
    g0, s_0 = c.sG.n, c.sS.n

    @block.scalar
    def _(scalar):
        _wait_all(scalar, c)
        for dc in range(DC):
            h = (dc % 2) * T
            if dc >= 2:
                scalar.wait_ge(c.sT.h, t0 + (dc - 1) * 16)
            r = c.res[:, dc * T:(dc + 1) * T]
            c.sA.inc(scalar.activation(c.sqbuf[:, h:h + T], r, AF.Square, bias=0.0, scale=1.0))
            c.sA.inc(scalar.activation(c.sbh[:, h:h + T], r, AF.Copy, bias=0.0, scale=1.0))
            scalar.wait_ge(c.sA.h, c.sA.n - 1)
            c.sA.inc(scalar.activation(c.sqh[:, h:h + T], c.sqbuf[:, h:h + T],
                                       AF.Copy, bias=0.0, scale=1.0))

    @block.vector
    def _(vector):
        for dc in range(DC):
            h = (dc % 2) * T
            vector.wait_ge(c.sA.h, a0 + dc * 3 + 3)
            r = c.res[:, dc * T:(dc + 1) * T]
            c.sV.inc(vector.tensor_sub(c.sbl[:, h:h + T], r, c.sbh[:, h:h + T]))
            c.sV.inc(vector.tensor_sub(c.sql[:, h:h + T], c.sqbuf[:, h:h + T],
                                       c.sqh[:, h:h + T]))

    @block.tensor
    def _(tensor):
        tensor.wait_ge(c.sV.h, v0)
        tensor.wait_ge(c.sA.h, a0)
        tensor.wait_ge(c.sG.h, g0)
        for dc in range(DC):
            h = (dc % 2) * T
            tensor.wait_ge(c.sV.h, v0 + dc * 2 + 2)
            st = (dc == 0)
            sp = (dc == DC - 1)
            for q in range(4):
                sl = slice(h + q * 512, h + (q + 1) * 512)
                c.sT.inc(tensor.matmul(c.PB[q][0:64, :], c.ones_bf[:, 0:64],
                                       c.sbh[:, sl], start=st, stop=False))
                c.sT.inc(tensor.matmul(c.PB[q][0:64, :], c.ones_bf[:, 0:64],
                                       c.sbl[:, sl], start=False, stop=sp))
                c.sT.inc(tensor.matmul(c.PB[4 + q][0:64, :], c.ones_bf[:, 0:64],
                                       c.sqh[:, sl], start=st, stop=False))
                c.sT.inc(tensor.matmul(c.PB[4 + q][0:64, :], c.ones_bf[:, 0:64],
                                       c.sql[:, sl], start=False, stop=sp))

    t1 = c.sT.n

    @block.vector
    def _(vector):
        vector.wait_ge(c.sT.h, t1)
        for q in range(4):
            sl = slice(q * 512, (q + 1) * 512)
            c.sV.inc(vector.tensor_scalar_mul(c.rows[0:1, sl], c.PB[q][0:1, :], -1.0 / D))
            c.sV.inc(vector.tensor_scalar_mul(c.rows[32:33, sl], c.PB[4 + q][0:1, :], 1.0 / D))
            vector.wait_ge(c.sV.h, c.sV.n)
            c.sV.inc(vector.scalar_tensor_tensor(
                c.PB[4 + q][32:33, :], c.rows[0:1, sl], -1.0, c.rows[0:1, sl],
                ALU.mult, ALU.mult))
            vector.wait_ge(c.sV.h, c.sV.n)
            c.sV.inc(vector.tensor_add(c.rows[64:65, sl], c.rows[32:33, sl],
                                       c.PB[4 + q][32:33, :]))
        vector.wait_ge(c.sV.h, c.sV.n)
        c.sV.inc(vector.tensor_scalar_add(c.rows[64:65, 0:T], c.rows[64:65, 0:T], EPS_LN))

    a1 = c.sA.n
    @block.scalar
    def _(scalar):
        scalar.wait_ge(c.sV.h, c.sV.n)
        c.sA.inc(scalar.activation(c.rows[96:97, 0:T], c.rows[64:65, 0:T], AF.Sqrt,
                                   bias=0.0, scale=1.0))

    @block.vector
    def _(vector):
        vector.wait_ge(c.sA.h, a1 + 1)
        c.sV.inc(vector.reciprocal(c.rows[32:33, 0:T], c.rows[96:97, 0:T]))


def emit_ffn0_main(c):
    block = c.block
    X2H = c.ARB[:, 0:4096]
    X2L = c.ARB[:, 4096:8192]
    HH = c.ARB[:, 16384:32768]
    HL = c.ARB[:, 32768:49152]
    WQS = [c.ARB[:, 49152:51200], c.ARB[:, 51200:53248]]
    hfull = c.ARF[:, 2560:3072]

    @block.sync
    def _(sync):
        for sm in c.chQ.sems:
            sync.wait_ge(sm.h, sm.n)

    for tq in range(4):
        qsl = slice(tq * 512, (tq + 1) * 512)
        t0, a0, v0 = c.sT.n, c.sA.n, c.sV.n
        s_0 = c.sS.n
        wtok = {}

        @block.tensor
        def _(tensor):
            tensor.wait_ge(c.sV.h, v0)
            tensor.wait_ge(c.sA.h, a0)
            c.sT.inc(tensor.matmul(c.PB[0][:, :], c.ones_f32[0:1, :], c.rows[0:1, qsl],
                                   start=True, stop=True))
            c.sT.inc(tensor.matmul(c.PB[1][:, :], c.ones_f32[32:33, :], c.rows[32:33, qsl],
                                   start=True, stop=True))

        t_bc = c.sT.n

        @block.vector
        def _(vector):
            vector.wait_ge(c.sT.h, t_bc)
            for dc in range(DC):
                src = slice(dc * T + tq * 512, dc * T + (tq + 1) * 512)
                d2 = slice(dc * 512, (dc + 1) * 512)
                c.sV.inc(vector.tensor_add(c.tmp1[:, 0:512], c.res[:, src], c.PB[0][:, :]))
                vector.wait_ge(c.sV.h, c.sV.n)
                c.sV.inc(vector.tensor_mul(c.xfull[:, 0:512], c.tmp1[:, 0:512], c.PB[1][:, :]))
                vector.wait_ge(c.sV.h, c.sV.n)
                vector.wait_ge(c.sA.h, a0 + dc + 1)
                c.sV.inc(vector.tensor_sub(X2L[:, d2], c.xfull[:, 0:512], X2H[:, d2]))

        @block.scalar
        def _(scalar):
            for dc in range(DC):
                d2 = slice(dc * 512, (dc + 1) * 512)
                scalar.wait_ge(c.sV.h, v0 + dc * 3 + 2)
                c.sA.inc(scalar.activation(X2H[:, d2], c.xfull[:, 0:512], AF.Copy,
                                           bias=0.0, scale=1.0))

        v_norm = c.sV.n
        a_norm = c.sA.n

        @block.sync
        def _(sync):
            for fc in range(32):
                if fc >= 2:
                    sync.wait_ge(c.sT.h, t_bc + (fc - 1) * 16)
                if fc < 2:
                    sync.wait_ge(c.sT.h, t_bc)
                wtok[fc] = c.chW.issue(
                    sync, WQS[fc % 2][:, 0:1024].rearrange("p (dc f) -> p dc f", f=128),
                    c.wq1t_dram.ap()[fc].rearrange("(dc p) f -> p dc f", p=128))

        @block.tensor
        def _(tensor):
            tensor.wait_ge(c.sV.h, v_norm)
            for fc in range(32):
                tensor.wait_ge(*wtok[fc])
                pb = c.PB[6 + fc % 2]
                if fc >= 2:
                    tensor.wait_ge(c.sV.h, v_norm + (fc - 1))
                n = 0
                for dc in range(DC):
                    wt = WQS[fc % 2][:, dc * 128:(dc + 1) * 128]
                    for x in (X2H, X2L):
                        n += 1
                        c.sT.inc(tensor.matmul(pb[:, :], wt,
                                               x[:, dc * 512:(dc + 1) * 512],
                                               start=(n == 1), stop=(n == 16)))

        t_m1 = c.sT.n

        @block.scalar
        def _(scalar):
            for fc in range(32):
                pb = c.PB[6 + fc % 2]
                hsl = slice(fc * 512, (fc + 1) * 512)
                scalar.wait_ge(c.sT.h, t_bc + (fc + 1) * 16)
                if fc >= 1:
                    scalar.wait_ge(c.sV.h, v_norm + fc)
                bcol = c.bias_ffn[:, fc:fc + 1]
                c.sA.inc(scalar.activation(HH[:, hsl], pb[:, :], AF.Relu,
                                           bias=bcol, scale=1.0))
                c.sA.inc(scalar.activation(hfull, pb[:, :], AF.Relu,
                                           bias=bcol, scale=1.0))

        @block.vector
        def _(vector):
            for fc in range(32):
                hsl = slice(fc * 512, (fc + 1) * 512)
                vector.wait_ge(c.sA.h, a_norm + (fc + 1) * 2)
                c.sV.inc(vector.tensor_sub(HL[:, hsl], hfull, HH[:, hsl]))

        v_h = c.sV.n
        a_h = c.sA.n

        wtok2 = {}

        @block.sync
        def _(sync):
            for fc in range(32):
                sync.wait_ge(c.sT.h, t_m1 + (fc - 1) * 16 if fc >= 2 else t_m1)
                wtok2[fc] = c.chW.issue(sync, WQS[fc % 2][:, 0:1024],
                                        c.wq2t_dram.ap()[fc])

        @block.tensor
        def _(tensor):
            tensor.wait_ge(c.sV.h, v_h)
            tensor.wait_ge(c.sA.h, a_h)
            for fc in range(32):
                tensor.wait_ge(*wtok2[fc])
                hsl = slice(fc * 512, (fc + 1) * 512)
                n = 0
                for dco in range(DC):
                    wt = WQS[fc % 2][:, dco * 128:(dco + 1) * 128]
                    for hx in (HH, HL):
                        n += 1
                        c.sT.inc(tensor.matmul(c.PB[dco][:, :], wt, hx[:, hsl],
                                               start=(fc == 0 and n % 2 == 1),
                                               stop=(fc == 31 and n % 2 == 0)))

        t_m2 = c.sT.n

        @block.vector
        def _(vector):
            vector.wait_ge(c.sT.h, t_m2)
            for dco in range(DC):
                dst = c.res[:, dco * T + tq * 512: dco * T + (tq + 1) * 512]
                c.sV.inc(vector.scalar_tensor_tensor(
                    dst, c.PB[dco][:, :], c.bias_ffn[:, 32 + dco:32 + dco + 1], dst,
                    ALU.add, ALU.add))


def emit_ffn0(c):
    emit_ffn0_gamma(c, c.w1, FF0, 1024, 12)
    emit_ffn0_gamma(c, c.w2, D, 4096, 13)
    emit_ffn0_wprep(c, c.w1, FF0, 1024, 12, (5122, 5123),
                    lambda ch, cg, dc: c.wq1t_dram.ap()[ch, dc * 128:(dc + 1) * 128, :])
    emit_ffn0_wprep(c, c.w2, D, 4096, 13, (5124, 5125),
                    lambda ch, cg, dc: c.wq2t_dram.ap()[cg * 8 + dc, :, ch * 128:(ch + 1) * 128])
    emit_ln_stats(c)
    emit_ffn0_main(c)



def emit_ssm(c):
    block = c.block
    F = lambda a, b: c.ARB[:, 2 * a:2 * b].bitcast(F32)
    X3C = F(0, 4112)
    DTC = F(4112, 8208)
    UC = F(8208, 12304)
    ALPHA = F(12304, 14352)
    BETA = F(14352, 16400)
    H = F(16400, 18448)
    YB = F(18448, 20496)
    YR = F(20496, 20624)
    ACO = c.ropeA[:, 0:128]
    DCO = c.ropeA[:, 128:136]
    BDTP = c.ropeA[:, 136:144]
    CONVK = c.ropeA[:, 144:168]
    BDT = c.ropeA[:, 168:169]
    BBC = c.ropeA[:, 169:170]
    WBCDT = c.ropeA[:, 256:1024]
    WDTP = c.ropeB[:, 0:1024]
    DTL = c.ARF[0:64, 0:512]
    BCR = c.ARF[64:96, 512:1024]
    STATE = c.ARF[:, 1024:1152]

    @block.sync
    def _(sync):
        _wait_all(sync, c)
        c.tok_ssm = []
        ctx_nc = c.nc.allow_non_contiguous_dma(reason="tiny ssm weight vectors")
        ctx_nc.__enter__()
        c.tok_ssm.append(c.chW.issue(
            sync, c.ropeA[:, 0:128].rearrange("p (dc n) -> p dc n", n=16),
            c.a_log.rearrange("(dc p) n -> p dc n", p=128)))
        c.tok_ssm.append(c.chW.issue(
            sync, DCO, c.dcoeff.rearrange("(dc p) -> p dc", p=128)))
        c.tok_ssm.append(c.chW.issue(
            sync, BDTP, c.b_dtp.rearrange("(dc p) -> p dc", p=128)))
        c.tok_ssm.append(c.chW.issue(
            sync, CONVK.rearrange("p (k dc) -> p k dc", k=3),
            c.conv_k.rearrange("k o (dc p) -> p k (o dc)", p=128)))
        c.tok_ssm.append(c.chW.issue(sync, BDT[0:64, :], c.b_dt.rearrange("(a o) -> a o", o=1)))
        c.tok_ssm.append(c.chW.issue(sync, c.bbc_col,
                                     c.b_bc.rearrange("(a o) -> a o", o=1)))
        c.tok_ssm.append(c.chW.issue(
            sync, WBCDT.rearrange("p (dc f) -> p dc f", f=96)[:, :, 0:64],
            c.w_dt.rearrange("(dc p) f -> p dc f", p=128)))
        c.tok_ssm.append(c.chW.issue(
            sync, WBCDT.rearrange("p (dc f) -> p dc f", f=96)[:, :, 64:96],
            c.w_bc.rearrange("(dc p) f -> p dc f", p=128)))
        c.tok_ssm.append(c.chW.issue(sync, WDTP[0:64, :], c.w_dtp))
        ctx_nc.__exit__(None, None, None)

    @block.scalar
    def _(scalar):
        for t in c.tok_ssm:
            scalar.wait_ge(*t)
        c.sA.inc(scalar.activation(ACO, ACO, AF.Exp, bias=0.0, scale=1.0))

    @block.vector
    def _(vector):
        vector.wait_ge(c.sA.h, c.sA.n)
        c.sV.inc(vector.tensor_scalar_mul(ACO, ACO, -1.0))


    emit_ln_stats(c)

    NMB = F(20624, 22672)
    RSB = F(22672, 24720)
    a_st, v_st = c.sA.n, c.sV.n

    @block.tensor
    def _(tensor):
        tensor.wait_ge(c.sA.h, a_st)
        tensor.wait_ge(c.sV.h, v_st)
        for q in range(4):
            sl = slice(q * 512, (q + 1) * 512)
            c.sT.inc(tensor.matmul(c.PB[q][:, :], c.ones_f32[0:1, :],
                                   c.rows[0:1, sl], start=True, stop=True))
            c.sT.inc(tensor.matmul(c.PB[4 + q][:, :], c.ones_f32[32:33, :],
                                   c.rows[32:33, sl], start=True, stop=True))

    t_stb = c.sT.n

    @block.scalar
    def _(scalar):
        scalar.wait_ge(c.sT.h, t_stb)
        for q in range(4):
            sl = slice(q * 512, (q + 1) * 512)
            c.sA.inc(scalar.activation(NMB[:, sl], c.PB[q][:, :], AF.Copy,
                                       bias=0.0, scale=1.0))
            c.sA.inc(scalar.activation(RSB[:, sl], c.PB[4 + q][:, :], AF.Copy,
                                       bias=0.0, scale=1.0))

    c.ssm_nmb, c.ssm_rsb, c.ssm_astb = NMB, RSB, c.sA.n

    @block.vector
    def _(vector):
        vector.wait_ge(c.sA.h, c.ssm_astb)
        c.sV.inc(vector.memset(STATE, 0.0))

    for tc in range(4):
        _emit_ssm_chunk(c, tc, X3C, DTC, UC, ALPHA, BETA, H, YB, YR,
                        ACO, DCO, BDTP, CONVK, BDT, WBCDT, WDTP, DTL, BCR, STATE)


def _emit_ssm_chunk(c, tc, X3C, DTC, UC, ALPHA, BETA, H, YB, YR,
                    ACO, DCO, BDTP, CONVK, BDT, WBCDT, WDTP, DTL, BCR, STATE):
    block = c.block
    t0, a0, v0 = c.sT.n, c.sA.n, c.sV.n

    nsl = slice(tc * 512, (tc + 1) * 512)

    @block.vector
    def _(vector):
        vector.wait_ge(c.sA.h, max(a0, c.ssm_astb))
        for dc in range(DC):
            base = dc * 514
            if tc > 0:
                c.sV.inc(vector.tensor_copy(X3C[:, base:base + 1],
                                            X3C[:, base + 512:base + 513]))
                vector.wait_ge(c.sV.h, c.sV.n)
            src = slice(dc * T + tc * 512, dc * T + (tc + 1) * 512)
            c.sV.inc(vector.tensor_add(c.tmp1[:, 0:512], c.res[:, src],
                                       c.ssm_nmb[:, nsl]))
            vector.wait_ge(c.sV.h, c.sV.n)
            c.sV.inc(vector.tensor_mul(X3C[:, base + 1:base + 513],
                                       c.tmp1[:, 0:512], c.ssm_rsb[:, nsl]))
            vector.wait_ge(c.sV.h, c.sV.n)

    @block.vector
    def _(vector):
        for dc in range(DC):
            b = dc * 514
            ub = dc * 512
            k0 = CONVK[:, dc:dc + 1]
            k1 = CONVK[:, 8 + dc:8 + dc + 1]
            k2 = CONVK[:, 16 + dc:16 + dc + 1]
            c.sV.inc(vector.tensor_scalar_mul(c.tmp1[:, 0:512], X3C[:, b:b + 512], k0))
            vector.wait_ge(c.sV.h, c.sV.n)
            c.sV.inc(vector.scalar_tensor_tensor(
                c.tmp1[:, 0:512], X3C[:, b + 1:b + 513], k1, c.tmp1[:, 0:512],
                ALU.mult, ALU.add))
            vector.wait_ge(c.sV.h, c.sV.n)
            c.sV.inc(vector.scalar_tensor_tensor(
                UC[:, ub:ub + 512], X3C[:, b + 2:b + 514], k2, c.tmp1[:, 0:512],
                ALU.mult, ALU.add))
            vector.wait_ge(c.sV.h, c.sV.n)
            if tc == 0:
                c.sV.inc(vector.tensor_scalar_mul(c.tmp1[:, 0:1], X3C[:, b + 1:b + 2], k1))
                vector.wait_ge(c.sV.h, c.sV.n)
                c.sV.inc(vector.scalar_tensor_tensor(
                    UC[:, ub:ub + 1], X3C[:, b + 2:b + 3], k2, c.tmp1[:, 0:1],
                    ALU.mult, ALU.add))
                vector.wait_ge(c.sV.h, c.sV.n)
            if tc == 3:
                c.sV.inc(vector.tensor_scalar_mul(c.tmp1[:, 0:1], X3C[:, b + 511:b + 512], k0))
                vector.wait_ge(c.sV.h, c.sV.n)
                c.sV.inc(vector.scalar_tensor_tensor(
                    UC[:, ub + 511:ub + 512], X3C[:, b + 512:b + 513], k1, c.tmp1[:, 0:1],
                    ALU.mult, ALU.add))
                vector.wait_ge(c.sV.h, c.sV.n)

    v_conv = c.sV.n

    @block.scalar
    def _(scalar):
        scalar.wait_ge(c.sV.h, v_conv)
        c.sA.inc(scalar.activation(UC[:, 0:4096], UC[:, 0:4096], AF.Silu,
                                   bias=0.0, scale=1.0))

    a_silu = c.sA.n

    @block.tensor
    def _(tensor):
        tensor.wait_ge(c.sA.h, a_silu)
        for dc in range(DC):
            c.sT.inc(tensor.matmul(c.PB[6][0:96, :],
                                   WBCDT[:, dc * 96:(dc + 1) * 96],
                                   UC[:, dc * 512:(dc + 1) * 512],
                                   start=(dc == 0), stop=(dc == DC - 1)))

    t_p1 = c.sT.n

    @block.scalar
    def _(scalar):
        scalar.wait_ge(c.sT.h, t_p1)
        c.sA.inc(scalar.activation(DTL, c.PB[6][0:64, :], AF.Identity,
                                   bias=BDT[0:64, :], scale=1.0))
        c.sA.inc(scalar.activation(BCR, c.PB[6][64:96, :], AF.Identity,
                                   bias=c.bbc_col, scale=1.0))

    a_p1 = c.sA.n

    @block.tensor
    def _(tensor):
        tensor.wait_ge(c.sA.h, a_p1)
        for dc in range(DC):
            pb = c.PB[7] if dc % 2 else c.PB[5]
            if dc >= 2:
                tensor.wait_ge(c.sA.h, a_p1 + (dc - 1) * 4)
            c.sT.inc(tensor.matmul(pb[:, :], WDTP[0:64, dc * 128:(dc + 1) * 128],
                                   DTL, start=True, stop=True))

    t_p2 = c.sT.n

    sp = c.xfull[:, 0:512]
    v_sp0 = c.sV.n

    @block.scalar
    def _(scalar):
        for dc in range(DC):
            pb = c.PB[7] if dc % 2 else c.PB[5]
            scalar.wait_ge(c.sT.h, t_p2 - (DC - 1 - dc))
            if dc >= 1:
                scalar.wait_ge(c.sV.h, v_sp0 + dc)
            bcol = BDTP[:, dc:dc + 1]
            c.sA.inc(scalar.activation(sp, pb[:, :], AF.Abs, bias=bcol, scale=1.0))
            scalar.wait_ge(c.sA.h, c.sA.n)
            c.sA.inc(scalar.activation(sp, sp, AF.Exp, bias=0.0, scale=-1.0))
            scalar.wait_ge(c.sA.h, c.sA.n)
            c.sA.inc(scalar.activation(sp, sp, AF.Ln, bias=1.0, scale=1.0))
            c.sA.inc(scalar.activation(DTC[:, dc * 512:(dc + 1) * 512], pb[:, :],
                                       AF.Relu, bias=bcol, scale=1.0))

    a_dt0 = c.sA.n

    @block.vector
    def _(vector):
        for dc in range(DC):
            vector.wait_ge(c.sA.h, a_dt0 - (DC - 1 - dc) * 4)
            dsl_ = slice(dc * 512, (dc + 1) * 512)
            c.sV.inc(vector.tensor_add(DTC[:, dsl_], DTC[:, dsl_], sp))

    a_dt = c.sA.n

    @block.vector
    def _(vector):
        vector.wait_ge(c.sA.h, a_dt)
        vector.wait_ge(c.sV.h, c.sV.n)
        for dc in range(DC):
            c.sV.inc(vector.tensor_mul(UC[:, dc * 512:(dc + 1) * 512],
                                       X3C[:, dc * 514 + 1:dc * 514 + 513],
                                       DTC[:, dc * 512:(dc + 1) * 512]))

    v_xdt = c.sV.n

    @block.sync
    def _(sync):
        sync.wait_ge(c.sA.h, a_p1)
        c.tok_bcd = c.chC.issue(sync, c.bc_dram.ap(), BCR)

    for sub in range(4):
        _emit_ssm_sub(c, tc, sub, X3C, DTC, UC, ALPHA, BETA, H, YB, YR,
                      ACO, DCO, STATE, v_xdt)


def _emit_ssm_sub(c, tc, sub, X3C, DTC, UC, ALPHA, BETA, H, YB, YR,
                  ACO, DCO, STATE, v_xdt):
    block = c.block
    t0, a0, v0 = c.sT.n, c.sA.n, c.sV.n

    @block.sync
    def _(sync):
        sync.wait_ge(*c.tok_bcd)
        sync.wait_ge(c.sT.h, t0)
        c.tok_b = c.chC.issue(sync, c.rows[0:1, 0:2048],
                              c.bc_dram.ap()[0:16, sub * 128:(sub + 1) * 128])
        c.tok_cc2 = c.chC.issue(sync, c.rows[32:33, 0:2048],
                                c.bc_dram.ap()[16:32, sub * 128:(sub + 1) * 128])

    @block.tensor
    def _(tensor):
        tensor.wait_ge(*c.tok_b)
        tensor.wait_ge(*c.tok_cc2)
        tensor.wait_ge(c.sV.h, v0)
        for j in range(4):
            c.sT.inc(tensor.matmul(c.PB[j][:, :], c.ones_f32[0:1, :],
                                   c.rows[0:1, j * 512:(j + 1) * 512],
                                   start=True, stop=True))
        for j in range(4):
            c.sT.inc(tensor.matmul(c.PB[4 + j][:, :], c.ones_f32[32:33, :],
                                   c.rows[32:33, j * 512:(j + 1) * 512],
                                   start=True, stop=True))

    t_bc2 = c.sT.n

    for dc in range(DC):
        _emit_ssm_dc(c, tc, sub, dc, X3C, DTC, UC, ALPHA, BETA, H, YB, YR,
                     ACO, DCO, STATE, v_xdt, t_bc2)


def _emit_ssm_dc(c, tc, sub, dc, X3C, DTC, UC, ALPHA, BETA, H, YB, YR,
                 ACO, DCO, STATE, v_xdt, t_bc2):
    block = c.block
    dsl = slice(dc * 512 + sub * 128, dc * 512 + sub * 128 + 128)
    a0, v0 = c.sA.n, c.sV.n

    @block.scalar
    def _(scalar):
        scalar.wait_ge(c.sV.h, v0)
        for n in range(N_SSM):
            c.sA.inc(scalar.activation(ALPHA[:, n * 128:(n + 1) * 128],
                                       _dtc_slice(c, dsl), AF.Exp,
                                       scale=ACO[:, dc * 16 + n:dc * 16 + n + 1],
                                       bias=0.0))

    a_alpha = c.sA.n

    @block.vector
    def _(vector):
        vector.wait_ge(c.sT.h, t_bc2)
        vector.wait_ge(c.sA.h, a_alpha)
        for j in range(4):
            c.sV.inc(vector.tensor_tensor(
                BETA[:, j * 512:(j + 1) * 512].rearrange("p (n l) -> p n l", l=128),
                _uc_slice(c, dsl).unsqueeze(1).broadcast_to([128, 4, 128]),
                c.PB[j][:, :].rearrange("p (n l) -> p n l", l=128),
                ALU.mult))
        vector.wait_ge(c.sV.h, c.sV.n)
        for n in range(N_SSM):
            c.sV.inc(vector.tensor_tensor_scan(
                H[:, n * 128:(n + 1) * 128],
                ALPHA[:, n * 128:(n + 1) * 128],
                BETA[:, n * 128:(n + 1) * 128],
                STATE[:, dc * 16 + n:dc * 16 + n + 1],
                ALU.mult, ALU.add))
        vector.wait_ge(c.sV.h, c.sV.n)
        c.sV.inc(vector.tensor_copy(
            STATE[:, dc * 16:(dc + 1) * 16],
            H.rearrange("p (n l) -> p n l", l=128)[:, :, 127]))
        vector.wait_ge(c.sV.h, c.sV.n)
        for j in range(4):
            c.sV.inc(vector.tensor_tensor(
                YB[:, j * 512:(j + 1) * 512],
                H[:, j * 512:(j + 1) * 512],
                c.PB[4 + j][:, :], ALU.mult))
        vector.wait_ge(c.sV.h, c.sV.n)
        c.sV.inc(vector.tensor_reduce(YR, YB.rearrange("p (n l) -> p l n", l=128),
                                      AX.X, ALU.add))
        vector.wait_ge(c.sV.h, c.sV.n)
        c.sV.inc(vector.scalar_tensor_tensor(
            YR, _x3_slice(c, dc, sub), DCO[:, dc:dc + 1], YR, ALU.mult, ALU.add))
        vector.wait_ge(c.sV.h, c.sV.n)
        dst = c.res[:, dc * T + tc * 512 + sub * 128: dc * T + tc * 512 + sub * 128 + 128]
        c.sV.inc(vector.tensor_add(dst, dst, YR))


def _dtc_slice(c, dsl):
    return c.ARB[:, 2 * 4112:2 * 8208].bitcast(F32)[:, dsl]


def _uc_slice(c, dsl):
    return c.ARB[:, 2 * 8208:2 * 12304].bitcast(F32)[:, dsl]


def _x3_slice(c, dc, sub):
    X3C = c.ARB[:, 0:2 * 4112].bitcast(F32)
    b = dc * 514 + 1 + sub * 128
    return X3C[:, b:b + 128]



QMAXA = 32767.0


def _emit_ar(c, k, srcs, dsts, v_pre):
    block = c.block
    n = len(srcs)
    stg = c.ARF[0:1, 1432:1432 + n]

    @block.gpsimd
    def _(gpsimd):
        gpsimd.wait_ge(c.sV.h, v_pre)
        for i, s in enumerate(srcs):
            c.sG.inc(gpsimd.tensor_reduce(c.ARF[0:1, 1432 + i:1433 + i], s,
                                          AX.C, ALU.max))

    g_red = c.sG.n

    @block.sync
    def _(sync):
        sync.wait_ge(c.sG.h, g_red)
        c.tok_ar = c.chC.issue(sync, c.cc_in[k].ap()[:, 0:n], stg)

    @block.gpsimd
    def _(gpsimd):
        gpsimd.wait_ge(*c.tok_ar)
        c.sG.inc(gpsimd.collective_compute(
            "AllReduce", ALU.max, replica_groups=[c.ar_group],
            ins=[c.cc_in[k].ap()[:, 0:n]], outs=[c.cc_out[k].ap()[:, 0:n]]))

    g_cc = c.sG.n
    toks = []

    @block.sync
    def _(sync):
        sync.wait_ge(c.sG.h, g_cc)
        for i in range(n):
            raw = c.ARF[:, 1435 + i:1436 + i]
            toks.append(c.chC.issue(
                sync, raw, c.cc_out[k].ap()[:, i:i + 1].partition_broadcast(128)))

    @block.vector
    def _(vector):
        for t in toks:
            vector.wait_ge(*t)
        for i, (s_col, r_col, qden) in enumerate(dsts):
            raw = c.ARF[:, 1435 + i:1436 + i]
            c.sV.inc(vector.tensor_scalar(s_col, raw, 1.0 / qden, 1e-12,
                                          ALU.mult, ALU.max))
            vector.wait_ge(c.sV.h, c.sV.n)
            c.sV.inc(vector.reciprocal(r_col, s_col))
            vector.wait_ge(c.sV.h, c.sV.n)


def emit_l1ffn(c):
    block = c.block
    Bv = lambda a, b: c.ARB[:, a:b].bitcast(F32)
    X4F = Bv(0, 32768)
    X4I = c.ARB[:, 0:32768].bitcast(I32)
    NMB2 = Bv(36864, 40960)
    RSB2 = Bv(40960, 45056)
    S = lambda i: c.ARF[:, 1024 + i:1025 + i]
    mcol = S(0)
    sx, rx = S(1), S(2)
    sg, rg = S(3), S(4)
    su, ru = S(5), S(6)
    s3, r3 = S(7), S(8)
    s4, r4 = S(9), S(10)
    MXC = c.ARF[:, 1040:1048]
    MXWG = c.ARF[:, 2048:2144]
    MXWU = c.ARF[:, 2144:2240]
    MXW3 = c.ARF[:, 2240:2336]
    MXW4 = c.ARF[:, 2336:2432]
    mg, mu = c.ARF[:, 2432:2433], c.ARF[:, 2433:2434]

    emit_ln_stats(c)
    a_st, v_st = c.sA.n, c.sV.n

    @block.tensor
    def _(tensor):
        tensor.wait_ge(c.sA.h, a_st)
        tensor.wait_ge(c.sV.h, v_st)
        for q in range(4):
            sl = slice(q * 512, (q + 1) * 512)
            c.sT.inc(tensor.matmul(c.PB[q][:, :], c.ones_f32[0:1, :],
                                   c.rows[0:1, sl], start=True, stop=True))
            c.sT.inc(tensor.matmul(c.PB[4 + q][:, :], c.ones_f32[32:33, :],
                                   c.rows[32:33, sl], start=True, stop=True))

    t_stb = c.sT.n

    @block.scalar
    def _(scalar):
        scalar.wait_ge(c.sT.h, t_stb)
        for q in range(4):
            sl = slice(q * 512, (q + 1) * 512)
            c.sA.inc(scalar.activation(NMB2[:, sl], c.PB[q][:, :], AF.Copy,
                                       bias=0.0, scale=1.0))
            c.sA.inc(scalar.activation(RSB2[:, sl], c.PB[4 + q][:, :], AF.Copy,
                                       bias=0.0, scale=1.0))

    a_stb = c.sA.n
    tmpn = Bv(32768, 36864)

    @block.vector
    def _(vector):
        vector.wait_ge(c.sA.h, a_stb)
        for dc in range(DC):
            xsl = slice(dc * T, (dc + 1) * T)
            c.sV.inc(vector.tensor_add(tmpn, c.res[:, xsl], NMB2))
            vector.wait_ge(c.sV.h, c.sV.n)
            c.sV.inc(vector.tensor_mul(X4F[:, xsl], tmpn, RSB2))
            vector.wait_ge(c.sV.h, c.sV.n)
            c.sV.inc(vector.tensor_reduce(MXC[:, dc:dc + 1], X4F[:, xsl],
                                          AX.X, ALU.max, apply_absolute_value=True))
        vector.wait_ge(c.sV.h, c.sV.n)
        c.sV.inc(vector.tensor_reduce(mcol, MXC, AX.X, ALU.max))

    _emit_ar(c, 0, [mcol], [(sx, rx, QMAXA)], c.sV.n)
    v_ar1 = c.sV.n

    @block.scalar
    def _(scalar):
        scalar.wait_ge(c.sV.h, v_ar1)
        for dc in range(DC):
            xsl = slice(dc * T, (dc + 1) * T)
            c.sA.inc(scalar.activation(X4I[:, xsl], X4F[:, xsl], AF.Copy,
                                       bias=0.0, scale=rx))
            scalar.wait_ge(c.sA.h, c.sA.n)
            c.sA.inc(scalar.activation(X4F[:, xsl], X4I[:, xsl], AF.Copy,
                                       bias=0.0, scale=sx))
            scalar.wait_ge(c.sA.h, c.sA.n)

    c.l1_a_xq = c.sA.n

    _emit_l1_wmm(c, c.w_gate, c.gate_dram, MXWG)
    _emit_l1_wmm(c, c.w_up, c.up_dram, MXWU)

    @block.vector
    def _(vector):
        vector.wait_ge(c.sV.h, c.sV.n)
        c.sV.inc(vector.tensor_reduce(mg, MXWG, AX.X, ALU.max))
        c.sV.inc(vector.tensor_reduce(mu, MXWU, AX.X, ALU.max))

    _emit_ar(c, 1, [mg, mu], [(sg, rg, QMAXA), (su, ru, QMAXA)], c.sV.n)

    _emit_l1_downprep(c)
    _emit_l1_sig(c, MXW3)

    @block.vector
    def _(vector):
        vector.wait_ge(c.sV.h, c.sV.n)
        c.sV.inc(vector.tensor_reduce(mg, MXW3, AX.X, ALU.max))

    _emit_ar(c, 2, [mg], [(s3, r3, QMAXA)], c.sV.n)

    _emit_l1_prod(c, MXW4)

    @block.vector
    def _(vector):
        vector.wait_ge(c.sV.h, c.sV.n)
        c.sV.inc(vector.tensor_reduce(mg, MXW4, AX.X, ALU.max))

    _emit_ar(c, 3, [mg], [(s4, r4, QMAXA)], c.sV.n)

    _emit_l1_down(c)


def _emit_l1_wmm(c, wsrc, dst_dram, mxw):
    block = c.block
    Bv = lambda a, b: c.ARB[:, a:b].bitcast(F32)
    WST = [Bv(49152, 51200), Bv(51200, 53248)]
    WSC = Bv(53248, 55296)
    WSCI = c.ARB[:, 53248:55296].bitcast(I32)
    WQ2 = Bv(32768, 34816)
    STA = [Bv(45056, 47104), Bv(47104, 49152)]
    EVS = [Bv(36864 + 1024 * i, 37888 + 1024 * i) for i in range(4)]
    MMB = [c.PB[0], c.PB[1], c.PB[6], c.PB[7]]
    BMX = c.ARF[:, 1048:1080]
    SBL = c.ARF[:, 1080:1112]
    RBL = c.ARF[:, 1112:1144]

    v_entry, t_entry = c.sV.n, c.sT.n
    spt_carry = getattr(c, "l1_spt_carry", None)
    nmm_carry = getattr(c, "l1_nmm_carry", None)

    ntr = {}
    nst = {}
    nmm = {}
    nvq = {}
    nev = {}
    nmx = {}
    spt = {}
    ldt = {}

    for fc in range(24):
        @block.sync
        def _(sync, fc=fc):
            if fc >= 2:
                sync.wait_ge(c.sV.h, nvq[fc - 2])
            elif v_entry:
                sync.wait_ge(c.sV.h, v_entry)
            ldt[fc] = c.chW.issue(sync, WST[fc % 2],
                                  wsrc[fc * 128:(fc + 1) * 128, :])

        @block.vector
        def _(vector, fc=fc):
            vector.wait_ge(*ldt[fc])
            if fc >= 1:
                vector.wait_ge(c.sT.h, ntr[fc - 1])
            else:
                vector.wait_ge(c.sT.h, t_entry)
            c.sV.inc(vector.tensor_reduce(
                BMX, WST[fc % 2].rearrange("p (b k) -> p b k", k=32),
                AX.X, ALU.max, apply_absolute_value=True))
            vector.wait_ge(c.sV.h, c.sV.n)
            c.sV.inc(vector.tensor_scalar(SBL, BMX, 1.0 / 7.0, 1e-12,
                                          ALU.mult, ALU.max))
            vector.wait_ge(c.sV.h, c.sV.n)
            c.sV.inc(vector.reciprocal(RBL, SBL))
            vector.wait_ge(c.sV.h, c.sV.n)
            c.sV.inc(vector.tensor_tensor(
                WSC.rearrange("p (b k) -> p b k", k=32),
                WST[fc % 2].rearrange("p (b k) -> p b k", k=32),
                RBL.unsqueeze(2).broadcast_to([128, 32, 32]), ALU.mult))

        v_wsc = c.sV.n

        @block.scalar
        def _(scalar, fc=fc, v_wsc=v_wsc):
            scalar.wait_ge(c.sV.h, v_wsc)
            c.sA.inc(scalar.activation(WSCI, WSC, AF.Copy, bias=0.0, scale=1.0))
            scalar.wait_ge(c.sA.h, c.sA.n)
            c.sA.inc(scalar.activation(WSC, WSCI, AF.Copy, bias=0.0, scale=1.0))

        a_wq = c.sA.n

        @block.vector
        def _(vector, fc=fc, a_wq=a_wq):
            vector.wait_ge(c.sA.h, a_wq)
            c.sV.inc(vector.tensor_tensor(
                WQ2.rearrange("p (b k) -> p b k", k=32),
                WSC.rearrange("p (b k) -> p b k", k=32),
                SBL.unsqueeze(2).broadcast_to([128, 32, 32]), ALU.mult))

        nvq[fc] = c.sV.n
        a_tr_base = c.sA.n

        @block.tensor
        def _(tensor, fc=fc, a_tr_base=a_tr_base):
            tensor.wait_ge(c.sV.h, nvq[fc])
            for dc in range(DC):
                pb = c.PB[2 + dc % 4]
                if fc >= 1 and dc < 4:
                    tensor.wait_ge(c.sA.h, nst[fc - 1] - 8 + dc + 4 + 1)
                if dc >= 4:
                    tensor.wait_ge(c.sA.h, a_tr_base + (dc - 4) + 1)
                c.sT.inc(tensor.transpose(pb[:, 0:128],
                                          WQ2[:, dc * 128:(dc + 1) * 128],
                                          c.ident_t[:]))

        ntr[fc] = c.sT.n
        t_tr0 = ntr[fc] - DC

        @block.scalar
        def _(scalar, fc=fc, t_tr0=t_tr0):
            for dc in range(DC):
                pb = c.PB[2 + dc % 4]
                scalar.wait_ge(c.sT.h, t_tr0 + dc + 1)
                if dc == 0:
                    if fc >= 2:
                        scalar.wait_ge(c.sT.h, nmm[(fc - 2, 3)])
                    elif nmm_carry:
                        scalar.wait_ge(c.sT.h, nmm_carry[fc])
                c.sA.inc(scalar.activation(STA[fc % 2][:, dc * 128:(dc + 1) * 128],
                                           pb[0:128, 0:128], AF.Copy,
                                           bias=0.0, scale=1.0))

        nst[fc] = c.sA.n

        for tq in range(4):
            g = fc * 4 + tq
            bank = MMB[g % 4]

            @block.tensor
            def _(tensor, fc=fc, tq=tq, g=g, bank=bank):
                tensor.wait_ge(c.sA.h, nst[fc])
                if fc == 0 and tq == 0:
                    tensor.wait_ge(c.sA.h, c.l1_a_xq)
                if g >= 4:
                    tensor.wait_ge(c.sA.h, nev[g - 4])
                    tensor.wait_ge(c.sV.h, nmx[g - 4])
                for dc in range(DC):
                    c.sT.inc(tensor.matmul(
                        bank[:, :], STA[fc % 2][:, dc * 128:(dc + 1) * 128],
                        Bv(0, 32768)[:, dc * T + tq * 512: dc * T + (tq + 1) * 512],
                        start=(dc == 0), stop=(dc == DC - 1)))

            nmm[(fc, tq)] = c.sT.n

            @block.scalar
            def _(scalar, fc=fc, tq=tq, g=g, bank=bank):
                scalar.wait_ge(c.sT.h, nmm[(fc, tq)])
                if g >= 4:
                    scalar.wait_ge(*spt[g - 4])
                elif spt_carry:
                    scalar.wait_ge(*spt_carry[g])
                c.sA.inc(scalar.activation(EVS[g % 4], bank[:, :], AF.Copy,
                                           bias=0.0, scale=1.0))

            nev[g] = c.sA.n

            @block.vector
            def _(vector, fc=fc, tq=tq, g=g, bank=bank):
                vector.wait_ge(c.sT.h, nmm[(fc, tq)])
                c.sV.inc(vector.tensor_reduce(mxw[:, g:g + 1], bank[:, :],
                                              AX.X, ALU.max,
                                              apply_absolute_value=True))

            nmx[g] = c.sV.n

            @block.sync
            def _(sync, g=g):
                sync.wait_ge(c.sA.h, nev[g])
                spt[g] = c.chO.issue(sync, dst_dram.ap()[g], EVS[g % 4])

    c.l1_spt_carry = (spt[92], spt[93], spt[94], spt[95])
    c.l1_nmm_carry = (nmm[(22, 3)], nmm[(23, 3)])


def _emit_l1_downprep(c):
    block = c.block
    Bv = lambda a, b: c.ARB[:, a:b].bitcast(F32)
    WSTD = [Bv(0, 6144), Bv(6144, 12288)]
    WSCD = Bv(12288, 18432)
    WSCDI = c.ARB[:, 12288:18432].bitcast(I32)
    WQ2D = Bv(18432, 24576)
    DSTT = [Bv(24576 + 256 * i, 24832 + 256 * i) for i in range(4)]
    BMX9 = c.ARF[:, 1144:1240]
    SBL9 = c.ARF[:, 1240:1336]
    RBL9 = c.ARF[:, 1336:1432]

    ldt = {}
    nvq = {}
    ntr = {}
    nev = {}
    wbt = {}
    aev = {}
    t0 = c.sT.n

    for ch in range(8):
        @block.sync
        def _(sync, ch=ch):
            if ch == 0:
                sync.wait_ge(c.sT.h, t0)
            if ch >= 2:
                sync.wait_ge(c.sV.h, nvq[ch - 2])
            ldt[ch] = c.chW.issue(sync, WSTD[ch % 2],
                                  c.w_down[ch * 128:(ch + 1) * 128, :])

        @block.vector
        def _(vector, ch=ch):
            vector.wait_ge(*ldt[ch])
            if ch >= 1:
                vector.wait_ge(c.sT.h, ntr[ch - 1])
            c.sV.inc(vector.tensor_reduce(
                BMX9, WSTD[ch % 2].rearrange("p (b k) -> p b k", k=32),
                AX.X, ALU.max, apply_absolute_value=True))
            vector.wait_ge(c.sV.h, c.sV.n)
            c.sV.inc(vector.tensor_scalar(SBL9, BMX9, 1.0 / 7.0, 1e-12,
                                          ALU.mult, ALU.max))
            vector.wait_ge(c.sV.h, c.sV.n)
            c.sV.inc(vector.reciprocal(RBL9, SBL9))
            vector.wait_ge(c.sV.h, c.sV.n)
            if ch >= 1:
                vector.wait_ge(c.sA.h, nev[ch - 1])
            c.sV.inc(vector.tensor_tensor(
                WSCD.rearrange("p (b k) -> p b k", k=32),
                WSTD[ch % 2].rearrange("p (b k) -> p b k", k=32),
                RBL9.unsqueeze(2).broadcast_to([128, 96, 32]), ALU.mult))

        v_wsc = c.sV.n

        @block.scalar
        def _(scalar, ch=ch, v_wsc=v_wsc):
            scalar.wait_ge(c.sV.h, v_wsc)
            c.sA.inc(scalar.activation(WSCDI, WSCD, AF.Copy, bias=0.0, scale=1.0))
            scalar.wait_ge(c.sA.h, c.sA.n)
            c.sA.inc(scalar.activation(WSCD, WSCDI, AF.Copy, bias=0.0, scale=1.0))

        a_wq = c.sA.n

        @block.vector
        def _(vector, ch=ch, a_wq=a_wq):
            vector.wait_ge(c.sA.h, a_wq)
            c.sV.inc(vector.tensor_tensor(
                WQ2D.rearrange("p (b k) -> p b k", k=32),
                WSCD.rearrange("p (b k) -> p b k", k=32),
                SBL9.unsqueeze(2).broadcast_to([128, 96, 32]), ALU.mult))

        nvq[ch] = c.sV.n

        for fc in range(24):
            i = ch * 24 + fc

            @block.tensor
            def _(tensor, ch=ch, fc=fc, i=i):
                if fc == 0:
                    tensor.wait_ge(c.sV.h, nvq[ch])
                if i >= 4:
                    tensor.wait_ge(c.sA.h, aev[i - 4])
                c.sT.inc(tensor.transpose(c.PB[2 + i % 4][:, 0:128],
                                          WQ2D[:, fc * 128:(fc + 1) * 128],
                                          c.ident_t[:]))

            t_tr = c.sT.n

            @block.scalar
            def _(scalar, ch=ch, fc=fc, i=i, t_tr=t_tr):
                scalar.wait_ge(c.sT.h, t_tr)
                if i >= 4:
                    scalar.wait_ge(*wbt[i - 4])
                c.sA.inc(scalar.activation(DSTT[i % 4],
                                           c.PB[2 + i % 4][0:128, 0:128],
                                           AF.Copy, bias=0.0, scale=1.0))
                aev[i] = c.sA.n
                scalar.wait_ge(c.sA.h, aev[i])
                wbt[i] = c.chQ.issue(
                    scalar, c.wdt_dram.ap()[fc, :, ch * 128:(ch + 1) * 128],
                    DSTT[i % 4])

        ntr[ch] = c.sT.n
        nev[ch] = c.sA.n


def _emit_l1_sig(c, MXW3):
    block = c.block
    Bv = lambda a, b: c.ARB[:, a:b].bitcast(F32)
    GIN = [Bv(40960 + 1024 * i, 41984 + 1024 * i) for i in range(4)]
    GQF = [Bv(45056 + 1024 * i, 46080 + 1024 * i) for i in range(4)]
    TF = [Bv(49152 + 1024 * i, 50176 + 1024 * i) for i in range(4)]
    GOUT = [Bv(28672 + 1024 * i, 29696 + 1024 * i) for i in range(4)]
    GQI = c.ARB[:, 25600:26624].bitcast(I32)
    SIG = Bv(26624, 27648)
    TI = c.ARB[:, 27648:28672].bitcast(I32)
    sg, rg = c.ARF[:, 1027:1028], c.ARF[:, 1028:1029]

    v_ar2 = c.sV.n
    ldt = {}
    na = {}
    nv = {}
    spt = {}

    @block.sync
    def _(sync):
        c.chO.drain(sync)

    for g in range(96):
        @block.sync
        def _(sync, g=g):
            if g >= 4:
                sync.wait_ge(c.sA.h, na[g - 4][0])
            ldt[g] = c.chW.issue(sync, GIN[g % 4], c.gate_dram.ap()[g])

        @block.scalar
        def _(scalar, g=g, a_prev=c.sA.n):
            scalar.wait_ge(*ldt[g])
            scalar.wait_ge(c.sA.h, a_prev)
            if g == 0:
                scalar.wait_ge(c.sV.h, v_ar2)
            a1 = c.sA.n + 1
            c.sA.inc(scalar.activation(GQI, GIN[g % 4], AF.Copy,
                                       bias=0.0, scale=rg))
            scalar.wait_ge(c.sA.h, c.sA.n)
            if g >= 4:
                scalar.wait_ge(c.sV.h, nv[g - 4][1])
            c.sA.inc(scalar.activation(GQF[g % 4], GQI, AF.Copy,
                                       bias=0.0, scale=sg))
            scalar.wait_ge(c.sA.h, c.sA.n)
            c.sA.inc(scalar.activation(SIG, GQF[g % 4], AF.Sigmoid,
                                       bias=0.0, scale=1.0))
            scalar.wait_ge(c.sA.h, c.sA.n)
            c.sA.inc(scalar.activation(TI, SIG, AF.Copy, bias=0.0, scale=65536.0))
            scalar.wait_ge(c.sA.h, c.sA.n)
            if g >= 4:
                scalar.wait_ge(c.sV.h, nv[g - 4][1])
            c.sA.inc(scalar.activation(TF[g % 4], TI, AF.Copy, bias=0.0, scale=1.0))
            na[g] = (a1, c.sA.n)

        @block.vector
        def _(vector, g=g):
            vector.wait_ge(c.sA.h, na[g][1])
            c.sV.inc(vector.tensor_scalar(TF[g % 4], TF[g % 4], 65535.0,
                                          1.0 / 65536.0, ALU.min, ALU.mult))
            vector.wait_ge(c.sV.h, c.sV.n)
            if g >= 4:
                vector.wait_ge(*spt[g - 4])
            v2 = c.sV.n + 1
            c.sV.inc(vector.tensor_mul(GOUT[g % 4], GQF[g % 4], TF[g % 4]))
            vector.wait_ge(c.sV.h, c.sV.n)
            c.sV.inc(vector.tensor_reduce(MXW3[:, g:g + 1], GOUT[g % 4],
                                          AX.X, ALU.max, apply_absolute_value=True))
            nv[g] = (v2, c.sV.n)

        @block.sync
        def _(sync, g=g):
            sync.wait_ge(c.sV.h, nv[g][0])
            spt[g] = c.chO.issue(sync, c.gate_dram.ap()[g], GOUT[g % 4])


def _emit_l1_prod(c, MXW4):
    block = c.block
    Bv = lambda a, b: c.ARB[:, a:b].bitcast(F32)
    GSI = [Bv(40960 + 1024 * i, 41984 + 1024 * i) for i in range(4)]
    UIN = [Bv(45056 + 1024 * i, 46080 + 1024 * i) for i in range(4)]
    G2F = [Bv(49152 + 1024 * i, 50176 + 1024 * i) for i in range(4)]
    UF = [Bv(28672 + 1024 * i, 29696 + 1024 * i) for i in range(4)]
    POUT = [Bv(0 + 1024 * i, 1024 + 1024 * i) for i in range(4)]
    G2I = c.ARB[:, 25600:26624].bitcast(I32)
    UI = c.ARB[:, 26624:27648].bitcast(I32)
    su, ru = c.ARF[:, 1029:1030], c.ARF[:, 1030:1031]
    s3, r3 = c.ARF[:, 1031:1032], c.ARF[:, 1032:1033]

    v_ar3 = c.sV.n
    ldt = {}
    na = {}
    nv = {}
    spt = {}

    @block.sync
    def _(sync):
        c.chO.drain(sync)

    for g in range(96):
        @block.sync
        def _(sync, g=g):
            if g >= 4:
                sync.wait_ge(c.sA.h, na[g - 4])
            ldt[g] = (c.chW.issue(sync, GSI[g % 4], c.gate_dram.ap()[g]),
                      c.chL.issue(sync, UIN[g % 4], c.up_dram.ap()[g]))

        @block.scalar
        def _(scalar, g=g, a_prev=c.sA.n):
            scalar.wait_ge(*ldt[g][0])
            scalar.wait_ge(*ldt[g][1])
            scalar.wait_ge(c.sA.h, a_prev)
            if g == 0:
                scalar.wait_ge(c.sV.h, v_ar3)
            c.sA.inc(scalar.activation(G2I, GSI[g % 4], AF.Copy,
                                       bias=0.0, scale=r3))
            scalar.wait_ge(c.sA.h, c.sA.n)
            if g >= 4:
                scalar.wait_ge(c.sV.h, nv[g - 4])
            c.sA.inc(scalar.activation(G2F[g % 4], G2I, AF.Copy,
                                       bias=0.0, scale=s3))
            scalar.wait_ge(c.sA.h, c.sA.n)
            c.sA.inc(scalar.activation(UI, UIN[g % 4], AF.Copy,
                                       bias=0.0, scale=ru))
            scalar.wait_ge(c.sA.h, c.sA.n)
            c.sA.inc(scalar.activation(UF[g % 4], UI, AF.Copy,
                                       bias=0.0, scale=su))
            na[g] = c.sA.n

        @block.vector
        def _(vector, g=g):
            vector.wait_ge(c.sA.h, na[g])
            if g >= 4:
                vector.wait_ge(*spt[g - 4])
            v1 = c.sV.n + 1
            c.sV.inc(vector.tensor_mul(POUT[g % 4], G2F[g % 4], UF[g % 4]))
            vector.wait_ge(c.sV.h, c.sV.n)
            c.sV.inc(vector.tensor_reduce(MXW4[:, g:g + 1], POUT[g % 4],
                                          AX.X, ALU.max, apply_absolute_value=True))
            nv[g] = v1

        @block.sync
        def _(sync, g=g):
            sync.wait_ge(c.sV.h, nv[g])
            spt[g] = c.chO.issue(sync, c.up_dram.ap()[g], POUT[g % 4])


def _emit_l1_down(c):
    block = c.block
    Bv = lambda a, b: c.ARB[:, a:b].bitcast(F32)
    OF = Bv(0, 24576)
    PIN = [Bv(24576 + 1024 * i, 25600 + 1024 * i) for i in range(4)]
    OI = c.ARB[:, 28672:29696].bitcast(I32)
    STD = [Bv(45056, 47104), Bv(47104, 49152)]
    s4, r4 = c.ARF[:, 1033:1034], c.ARF[:, 1034:1035]

    v_ar4 = c.sV.n

    @block.sync
    def _(sync):
        c.chO.drain(sync)
        for sm in c.chQ.sems:
            sync.wait_ge(sm.h, sm.n)

    t_dn_prev = c.sT.n
    a_prev_end = c.sA.n

    for tq in range(4):
        ldt = {}
        stt = {}
        na = {}
        nmm = {}
        v_res_prev = c.sV.n

        for fc in range(24):
            @block.sync
            def _(sync, fc=fc, tq=tq, t_dn_prev=t_dn_prev, a_prev_end=a_prev_end):
                if fc >= 4:
                    sync.wait_ge(c.sA.h, na[fc - 4])
                elif tq > 0:
                    sync.wait_ge(c.sA.h, a_prev_end)
                ldt[fc] = c.chW.issue(sync, PIN[fc % 4],
                                      c.up_dram.ap()[fc * 4 + tq])
                if fc >= 2:
                    sync.wait_ge(c.sT.h, nmm[fc - 2])
                elif tq > 0:
                    sync.wait_ge(c.sT.h, t_dn_prev)
                stt[fc] = c.chL.issue(sync, STD[fc % 2], c.wdt_dram.ap()[fc])

            @block.scalar
            def _(scalar, fc=fc, tq=tq, t_dn_prev=t_dn_prev, a_prev=c.sA.n):
                scalar.wait_ge(*ldt[fc])
                scalar.wait_ge(c.sA.h, a_prev)
                if fc == 0 and tq == 0:
                    scalar.wait_ge(c.sV.h, v_ar4)
                if fc == 0 and tq > 0:
                    scalar.wait_ge(c.sT.h, t_dn_prev)
                c.sA.inc(scalar.activation(OI, PIN[fc % 4], AF.Copy,
                                           bias=0.0, scale=r4))
                scalar.wait_ge(c.sA.h, c.sA.n)
                c.sA.inc(scalar.activation(OF[:, fc * 512:(fc + 1) * 512], OI,
                                           AF.Copy, bias=0.0, scale=s4))
                na[fc] = c.sA.n

            @block.tensor
            def _(tensor, fc=fc, tq=tq, v_res_prev=v_res_prev):
                tensor.wait_ge(*stt[fc])
                tensor.wait_ge(c.sA.h, na[fc])
                if fc == 0:
                    tensor.wait_ge(c.sV.h, v_res_prev)
                for dc in range(DC):
                    c.sT.inc(tensor.matmul(
                        c.PB[dc][:, :], STD[fc % 2][:, dc * 128:(dc + 1) * 128],
                        OF[:, fc * 512:(fc + 1) * 512],
                        start=(fc == 0), stop=(fc == 23)))
                nmm[fc] = c.sT.n

        t_dn = c.sT.n

        @block.vector
        def _(vector, tq=tq, t_dn=t_dn):
            vector.wait_ge(c.sT.h, t_dn)
            for dc in range(DC):
                dst = c.res[:, dc * T + tq * 512: dc * T + (tq + 1) * 512]
                c.sV.inc(vector.tensor_add(dst, dst, c.PB[dc][:, :]))

        t_dn_prev = t_dn
        a_prev_end = c.sA.n


def emit_ph6_store(c):
    block = c.block
    t0, a0 = c.sT.n, c.sA.n
    tok = {}

    @block.tensor
    def _(tensor):
        _wait_all(tensor, c)
        for tt in range(TT):
            for dc in range(DC):
                pb = c.PB[dc]
                if tt >= 1:
                    tensor.wait_ge(c.sA.h, a0 + (tt - 1) * DC + dc + 1)
                c.sT.inc(tensor.transpose(
                    pb[:, 0:128],
                    c.res[:, dc * T + tt * 128: dc * T + tt * 128 + 128],
                    c.ident_t[:]))

    @block.sync
    def _(sync):
        for tt in range(TT):
            buf = c.stage[:, (tt % 2) * D:(tt % 2) * D + D]
            sync.wait_ge(c.sA.h, a0 + (tt + 1) * DC)
            tok[tt] = c.chO.issue(sync, c.out[tt * 128:(tt + 1) * 128, :], buf)
        c.chO.drain(sync)

    @block.scalar
    def _(scalar):
        for tt in range(TT):
            buf = c.stage[:, (tt % 2) * D:(tt % 2) * D + D]
            if tt >= 2:
                scalar.wait_ge(*tok[tt - 2])
            for dc in range(DC):
                scalar.wait_ge(c.sT.h, t0 + tt * DC + dc + 1)
                c.sA.inc(scalar.activation(
                    buf[:, dc * 128:(dc + 1) * 128],
                    c.PB[dc][:, 0:128], AF.Copy, bias=0.0, scale=1.0))


def build(stop_after=None, ar_group=None):
    nc = bass.Bass("TRN2", target_bir_lowering=False, dynamic_dma_scratch_size=2048)
    c = Ctx()
    c.nc = nc

    c.x_in = nc.dram_tensor("x_in", [T, D], F32, kind="ExternalInput").ap()
    c.out = nc.dram_tensor("out", [T, D], F32, kind="ExternalOutput").ap()
    c.ident = nc.dram_tensor("ident", [128, 128], F32, kind="ExternalInput").ap()
    c.ones_in = nc.dram_tensor("ones_in", [128, 128], F32, kind="ExternalInput").ap()
    c.ropeA_in = nc.dram_tensor("ropeA_in", [128, T], F32, kind="ExternalInput").ap()
    c.ropeB_in = nc.dram_tensor("ropeB_in", [128, T], F32, kind="ExternalInput").ap()
    c.masks_in = nc.dram_tensor("masks_in", [128, 896], BF16, kind="ExternalInput").ap()
    c.bias_in = nc.dram_tensor("bias_in", [128, 32], F32, kind="ExternalInput").ap()
    c.w_q = nc.dram_tensor("w_q", [D, D], F32, kind="ExternalInput").ap()
    c.w_k = nc.dram_tensor("w_k", [D, D], F32, kind="ExternalInput").ap()
    c.w_v = nc.dram_tensor("w_v", [D, D], F32, kind="ExternalInput").ap()
    c.w_o = nc.dram_tensor("w_o", [D, D], F32, kind="ExternalInput").ap()
    c.w1 = nc.dram_tensor("w1", [FF0, D], F32, kind="ExternalInput").ap()
    c.w2 = nc.dram_tensor("w2", [D, FF0], F32, kind="ExternalInput").ap()
    c.bias_ffn_in = nc.dram_tensor("bias_ffn_in", [128, 40], F32, kind="ExternalInput").ap()
    c.scal_dram = nc.dram_tensor("scal_dram", [1, 8], F32)
    c.wq1t_dram = nc.dram_tensor("wq1t_dram", [32, 1024, 128], BF16)
    c.wq2t_dram = nc.dram_tensor("wq2t_dram", [32, 128, 1024], BF16)
    c.bc_dram = nc.dram_tensor("bc_dram", [32, 512], F32)
    c.conv_k = nc.dram_tensor("conv_k", [3, 1, D], F32, kind="ExternalInput").ap()
    c.a_log = nc.dram_tensor("a_log", [D, N_SSM], F32, kind="ExternalInput").ap()
    c.w_bc = nc.dram_tensor("w_bc", [D, 2 * N_SSM], F32, kind="ExternalInput").ap()
    c.b_bc = nc.dram_tensor("b_bc", [2 * N_SSM], F32, kind="ExternalInput").ap()
    c.dcoeff = nc.dram_tensor("dcoeff", [D], F32, kind="ExternalInput").ap()
    c.w_dt = nc.dram_tensor("w_dt", [D, DT_RANK], F32, kind="ExternalInput").ap()
    c.b_dt = nc.dram_tensor("b_dt", [DT_RANK], F32, kind="ExternalInput").ap()
    c.w_dtp = nc.dram_tensor("w_dtp", [DT_RANK, D], F32, kind="ExternalInput").ap()
    c.b_dtp = nc.dram_tensor("b_dtp", [D], F32, kind="ExternalInput").ap()
    c.w_gate = nc.dram_tensor("w_gate", [FF1, D], F32, kind="ExternalInput").ap()
    c.w_up = nc.dram_tensor("w_up", [FF1, D], F32, kind="ExternalInput").ap()
    c.w_down = nc.dram_tensor("w_down", [D, FF1], F32, kind="ExternalInput").ap()
    c.gate_dram = nc.dram_tensor("gate_dram", [96, 128, 512], F32)
    c.up_dram = nc.dram_tensor("up_dram", [96, 128, 512], F32)
    c.wdt_dram = nc.dram_tensor("wdt_dram", [24, 128, 1024], F32)
    c.cc_in = [nc.dram_tensor(f"cc_in{i}", [1, 2], F32) for i in range(4)]
    c.cc_out = [nc.dram_tensor(f"cc_out{i}", [1, 2], F32) for i in range(4)]
    c.ar_group = list(range(8)) if ar_group is None else ar_group

    ctx = ExitStack()
    with ctx:
        e = ctx.enter_context
        c.res = e(nc.sbuf_tensor("res", [128, DC * T], F32))
        c.rows = e(nc.sbuf_tensor("rows", [128, T], F32))
        c.ropeA = e(nc.sbuf_tensor("ropeA", [128, T], F32))
        c.ropeB = e(nc.sbuf_tensor("ropeB", [128, T], F32))
        c.ident_t = e(nc.sbuf_tensor("ident_t", [128, 128], F32))
        c.ones_f32 = e(nc.sbuf_tensor("ones_f32", [128, 128], F32))
        c.ones_bf = e(nc.sbuf_tensor("ones_bf", [128, 64], BF16))
        c.bias_qkv = e(nc.sbuf_tensor("bias_qkv", [128, 32], F32))
        c.bias_ffn = e(nc.sbuf_tensor("bias_ffn", [128, 40], F32))
        c.nrm = e(nc.sbuf_tensor("nrm", [1, 16], F32))
        c.ARF = e(nc.sbuf_tensor("ARF", [128, 5136], F32))
        c.ARB = e(nc.sbuf_tensor("ARB", [128, 57216], BF16))

        c.stage = c.ARF[:, 0:2048]
        c.sqbuf = c.ARF[:, 0:4096]
        c.tmp1 = c.ARF[:, 4096:4608]
        c.xfull = c.ARF[:, 4608:5120]
        c.AVS = c.ARF[:, 3072:5120]
        c.biasA = c.ARF[:, 5120:5121]
        c.biasB = c.ARF[:, 5121:5122]
        c.qtmp = c.ARF[:, 2048:2176]
        c.bbc_col = c.ropeA[64:96, 169:170]
        c.acol = c.ARF[:, 5126:5127]
        c.gcol = c.ARF[:, 5127:5128]

        def B(a, b):
            return c.ARB[:, a:b]
        c.XH = B(0, 16384)
        c.XL = B(16384, 32768)
        c.QH = B(32768, 34816)
        c.QL = B(34816, 36864)
        c.KH = B(36864, 38912)
        c.KL = B(38912, 40960)
        c.VSA = B(40960, 43008)
        c.VSB = B(43008, 45056)
        c.WTQH = B(45056, 46080)
        c.WTQL = B(46080, 47104)
        c.WTKH = B(47104, 48128)
        c.WTKL = B(48128, 49152)
        c.WTVH = B(49152, 50176)
        c.WTVL = B(50176, 51200)
        c.PHt = [B(51200, 52224), B(52224, 53248)]
        c.PLt = [B(53248, 54272), B(54272, 55296)]
        c.OH = B(55296, 55808)
        c.OL = B(55808, 56320)
        c.MASKS = B(56320, 57216)
        c.sbh = B(32768, 36864)
        c.sbl = B(36864, 40960)
        c.sqh = B(40960, 45056)
        c.sql = B(45056, 49152)

        c.PB = [e(nc.psum_tensor(f"pb{i}", [128, 512], F32)) for i in range(8)]

        c.sS = Sem(e(nc.semaphore("sS")))
        c.sG = Sem(e(nc.semaphore("sG")))
        c.sT = Sem(e(nc.semaphore("sT")))
        c.sV = Sem(e(nc.semaphore("sV")))
        c.sA = Sem(e(nc.semaphore("sA")))

        c.block = e(nc.Block())

        c.chL = DmaChan(nc, ctx, "chL", 4)
        c.chO = DmaChan(nc, ctx, "chO", 4)
        c.chW = DmaChan(nc, ctx, "chW", 4)
        c.chC = DmaChan(nc, ctx, "chC", 2)
        c.chQ = DmaChan(nc, ctx, "chQ", 4)
        c.wtok = {}

        @c.block.sync
        def _(sync):
            c.tok_ident = c.chL.issue(sync, c.ident_t[:], c.ident)
            c.tok_ones = c.chL.issue(sync, c.ones_f32[:], c.ones_in)
            c.tok_ropeA = c.chL.issue(sync, c.ropeA[:], c.ropeA_in)
            c.tok_ropeB = c.chL.issue(sync, c.ropeB[:], c.ropeB_in)
            c.tok_masks = c.chL.issue(sync, c.MASKS, c.masks_in)
            c.tok_bias_in = c.chL.issue(sync, c.bias_qkv[:], c.bias_in)
            c.tok_bias_ffn = c.chL.issue(sync, c.bias_ffn[:], c.bias_ffn_in)

        @c.block.vector
        def _(vector):
            vector.wait_ge(*c.tok_ones)
            c.sV.inc(vector.tensor_copy(c.ones_bf[:, 0:64], c.ones_f32[:, 0:64]))

        emit_ph0_load(c)

        @c.block.tensor
        def _(tensor):
            tensor.wait_ge(*c.tok_ones)
            tensor.wait_ge(*c.tok_ropeA)
            tensor.wait_ge(*c.tok_ropeB)
            tensor.wait_ge(*c.tok_masks)
            tensor.wait_ge(*c.tok_bias_in)
            tensor.wait_ge(c.sV.h, c.sV.n)

        if stop_after != "ph0":
            emit_ln(c, c.XH, c.XL)
            if stop_after != "ln1":
                emit_l0_attn(c)
                if stop_after != "attn":
                    emit_ffn0(c)
                    if stop_after != "ffn0":
                        emit_ssm(c)
                        if stop_after != "ssm":
                            emit_l1ffn(c)

        emit_ph6_store(c)

    return nc


def _host_consts():
    ident = np.eye(128, dtype=np.float32)
    ones = np.ones((128, 128), dtype=np.float32)
    half = DK // 2
    freq = (1.0 / 10000.0 ** (2.0 * np.arange(half, dtype=np.float64) / DK))
    ang = np.arange(T, dtype=np.float64)[:, None] * freq[None, :]
    cos = np.cos(ang).astype(np.float32)
    sin = np.sin(ang).astype(np.float32)
    ropeA = np.zeros((128, T), dtype=np.float32)
    ropeB = np.zeros((128, T), dtype=np.float32)
    for j in range(64):
        i = j // 2
        ropeA[j] = cos[:, i]
        ropeB[j] = (np.float32(-1.0) if j % 2 == 0 else np.float32(1.0)) * sin[:, i]
    ropeA[64:128] = ropeA[0:64]
    ropeB[64:128] = ropeB[0:64]
    import ml_dtypes
    k_rel = np.arange(128)[:, None]
    u = np.arange(896)[None, :]
    masks = np.where(k_rel > u - 384, -1e30, 0.0).astype(ml_dtypes.bfloat16)
    return ident, ones, ropeA, ropeB, masks


_PERM = None

def _perm_idx():
    global _PERM
    if _PERM is None:
        j = np.arange(128)
        h = j // 64
        jj = j % 64
        _PERM = h * 64 + (jj % 2) * 32 + (jj // 2)
    return _PERM


def _bias_tile(inputs):
    bias = np.zeros((128, 32), dtype=np.float32)
    perm = _perm_idx()
    bq = np.asarray(inputs["L0_attn_WQ_b"], np.float32)
    bk = np.asarray(inputs["L0_attn_WK_b"], np.float32)
    bv = np.asarray(inputs["L0_attn_WV_b"], np.float32)
    bo = np.asarray(inputs["L0_attn_Wo_b"], np.float32)
    for p in range(8):
        bias[:, p] = bq[p * 128 + perm]
        bias[:, 8 + p] = bk[p * 128 + perm]
        bias[:, 16 + p] = bv[p * 128:(p + 1) * 128]
        bias[:, 24 + p] = bo[p * 128:(p + 1) * 128]
    return bias


_NC_CACHE = None


def _get_nc():
    global _NC_CACHE
    if _NC_CACHE is None:
        _NC_CACHE = build()
    return _NC_CACHE


def kernel(**inputs):
    nc = _get_nc()
    x = np.ascontiguousarray(np.asarray(inputs["L0_attn_x"], dtype=np.float32))
    ident, ones, ropeA, ropeB, masks = _host_consts()
    bias = _bias_tile(inputs)
    wq = np.ascontiguousarray(np.asarray(inputs["L0_attn_WQ_w"], np.float32))
    wk = np.ascontiguousarray(np.asarray(inputs["L0_attn_WK_w"], np.float32))
    wv = np.ascontiguousarray(np.asarray(inputs["L0_attn_WV_w"], np.float32))
    wo = np.ascontiguousarray(np.asarray(inputs["L0_attn_Wo_w"], np.float32))
    w1 = np.ascontiguousarray(np.asarray(inputs["L0_ffn_W1"], np.float32))
    w2 = np.ascontiguousarray(np.asarray(inputs["L0_ffn_W2"], np.float32))
    bias_ffn = np.zeros((128, 40), dtype=np.float32)
    b1 = np.asarray(inputs["L0_ffn_b1"], np.float32)
    b2 = np.asarray(inputs["L0_ffn_b2"], np.float32)
    for fc in range(32):
        bias_ffn[:, fc] = b1[fc * 128:(fc + 1) * 128]
    for dco in range(8):
        bias_ffn[:, 32 + dco] = b2[dco * 128:(dco + 1) * 128]
    in_maps = []
    for b in range(8):
        m = {"x_in": np.ascontiguousarray(x[b]), "ident": ident, "ones_in": ones,
             "ropeA_in": ropeA, "ropeB_in": ropeB, "masks_in": masks,
             "bias_in": bias, "w_q": wq, "w_k": wk, "w_v": wv, "w_o": wo,
             "w1": w1, "w2": w2, "bias_ffn_in": bias_ffn,
             "conv_k": np.ascontiguousarray(np.asarray(inputs["L1_attn_conv_kernel"], np.float32)),
             "a_log": np.ascontiguousarray(np.asarray(inputs["L1_attn_A_log"], np.float32)),
             "w_bc": np.ascontiguousarray(np.asarray(inputs["L1_attn_W_bc"], np.float32)),
             "b_bc": np.ascontiguousarray(np.asarray(inputs["L1_attn_b_bc"], np.float32)),
             "dcoeff": np.ascontiguousarray(np.asarray(inputs["L1_attn_Dcoeff"], np.float32)),
             "w_dt": np.ascontiguousarray(np.asarray(inputs["L1_attn_W_dt"], np.float32)),
             "b_dt": np.ascontiguousarray(np.asarray(inputs["L1_attn_b_dt"], np.float32)),
             "w_dtp": np.ascontiguousarray(np.asarray(inputs["L1_attn_W_dtp"], np.float32)),
             "b_dtp": np.ascontiguousarray(np.asarray(inputs["L1_attn_b_dtp"], np.float32)),
             "w_gate": np.ascontiguousarray(np.asarray(inputs["L1_ffn_w_gate"], np.float32)),
             "w_up": np.ascontiguousarray(np.asarray(inputs["L1_ffn_w_up"], np.float32)),
             "w_down": np.ascontiguousarray(np.asarray(inputs["L1_ffn_w_down"], np.float32))}
        in_maps.append(m)
    res = run_bass_kernel_spmd(nc, in_maps, core_ids=list(range(8)))
    out = np.stack([np.asarray(res.results[b]["out"], dtype=np.float32) for b in range(8)], axis=0)
    return out


# version 04
# speedup vs baseline: 1.0560x; 1.0560x over previous
import numpy as np
from contextlib import ExitStack

import concourse.bass as bass
import concourse.mybir as mybir
from concourse.bass_utils import run_bass_kernel_spmd

F32 = mybir.dt.float32
BF16 = mybir.dt.bfloat16
I32 = mybir.dt.int32
AF = mybir.ActivationFunctionType
ALU = mybir.AluOpType
AX = mybir.AxisListType

D = 1024
T = 2048
NH = 16
DK = 64
DC = D // 128
TT = T // 128
FF0 = 4096
FF1 = 3072
N_SSM = 16
DT_RANK = 64
EPS_LN = 1e-5


class Sem:
    __slots__ = ("h", "n")
    def __init__(self, h):
        self.h = h
        self.n = 0
    def inc(self, inst, k=1):
        inst.then_inc(self.h, k)
        self.n += k
        return inst


class DmaChan:
    def __init__(self, nc, ctx, name, depth):
        self.sems = [Sem(ctx.enter_context(nc.semaphore(f"{name}{i}")))
                     for i in range(depth)]
        self.i = 0

    def issue(self, eng, out, in_):
        s = self.sems[self.i % len(self.sems)]
        self.i += 1
        eng.wait_ge(s.h, s.n)
        s.inc(eng.dma_start(out=out, in_=in_), 16)
        return (s.h, s.n)

    def drain(self, eng):
        for s in self.sems:
            eng.wait_ge(s.h, s.n)

    def snapshot(self):
        return (self.i, [s.n for s in self.sems])

    def plan(self, snap, j):
        i0, vals = snap
        k = len(self.sems)
        idx = (i0 + j) % k
        uses = j // k + 1
        return (self.sems[idx].h, vals[idx] + 16 * uses)


class Ctx:
    pass


def _wait_all(eng, c):
    for s in (c.sV, c.sA, c.sT, c.sG, c.sS):
        eng.wait_ge(s.h, s.n)


def emit_ph0_load(c):
    block = c.block
    t0, a0 = c.sT.n, c.sA.n
    tok = {}

    @block.sync
    def _(sync):
        for tt in range(TT):
            buf = c.stage[:, (tt % 2) * D:(tt % 2) * D + D]
            if tt >= 2:
                sync.wait_ge(c.sT.h, t0 + (tt - 1) * DC)
            tok[tt] = c.chL.issue(sync, buf, c.x_in[tt * 128:(tt + 1) * 128, :])

    @block.tensor
    def _(tensor):
        for tt in range(TT):
            buf = c.stage[:, (tt % 2) * D:(tt % 2) * D + D]
            tensor.wait_ge(*tok[tt])
            if tt == 0:
                tensor.wait_ge(*c.tok_ident)
            for dc in range(DC):
                pb = c.PB[dc]
                if tt >= 1:
                    tensor.wait_ge(c.sA.h, a0 + (tt - 1) * DC + dc + 1)
                c.sT.inc(tensor.transpose(pb[:, 0:128], buf[:, dc * 128:(dc + 1) * 128],
                                          c.ident_t[:]))

    @block.scalar
    def _(scalar):
        for tt in range(TT):
            for dc in range(DC):
                scalar.wait_ge(c.sT.h, t0 + tt * DC + dc + 1)
                c.sA.inc(scalar.activation(
                    c.res[:, dc * T + tt * 128: dc * T + tt * 128 + 128],
                    c.PB[dc][:, 0:128], AF.Copy, bias=0.0, scale=1.0))


def emit_ln(c, xh, xl, extra_full=None):
    block = c.block
    t0, a0, v0 = c.sT.n, c.sA.n, c.sV.n
    g0, s0 = c.sG.n, c.sS.n

    @block.scalar
    def _(scalar):
        _wait_all(scalar, c)
        for dc in range(DC):
            h = (dc % 2) * T
            if dc >= 2:
                scalar.wait_ge(c.sT.h, t0 + (dc - 1) * 16)
            r = c.res[:, dc * T:(dc + 1) * T]
            c.sA.inc(scalar.activation(c.sqbuf[:, h:h + T], r, AF.Square, bias=0.0, scale=1.0))
            c.sA.inc(scalar.activation(c.sbh[:, h:h + T], r, AF.Copy, bias=0.0, scale=1.0))
            scalar.wait_ge(c.sA.h, c.sA.n - 1)
            c.sA.inc(scalar.activation(c.sqh[:, h:h + T], c.sqbuf[:, h:h + T],
                                       AF.Copy, bias=0.0, scale=1.0))

    @block.vector
    def _(vector):
        for dc in range(DC):
            h = (dc % 2) * T
            vector.wait_ge(c.sA.h, a0 + dc * 3 + 3)
            r = c.res[:, dc * T:(dc + 1) * T]
            c.sV.inc(vector.tensor_sub(c.sbl[:, h:h + T], r, c.sbh[:, h:h + T]))
            c.sV.inc(vector.tensor_sub(c.sql[:, h:h + T], c.sqbuf[:, h:h + T],
                                       c.sqh[:, h:h + T]))

    @block.tensor
    def _(tensor):
        tensor.wait_ge(c.sV.h, v0)
        tensor.wait_ge(c.sA.h, a0)
        tensor.wait_ge(c.sG.h, g0)
        for dc in range(DC):
            h = (dc % 2) * T
            tensor.wait_ge(c.sV.h, v0 + dc * 2 + 2)
            st = (dc == 0)
            sp = (dc == DC - 1)
            for q in range(4):
                sl = slice(h + q * 512, h + (q + 1) * 512)
                c.sT.inc(tensor.matmul(c.PB[q][0:64, :], c.ones_bf[:, 0:64],
                                       c.sbh[:, sl], start=st, stop=False))
                c.sT.inc(tensor.matmul(c.PB[q][0:64, :], c.ones_bf[:, 0:64],
                                       c.sbl[:, sl], start=False, stop=sp))
                c.sT.inc(tensor.matmul(c.PB[4 + q][0:64, :], c.ones_bf[:, 0:64],
                                       c.sqh[:, sl], start=st, stop=False))
                c.sT.inc(tensor.matmul(c.PB[4 + q][0:64, :], c.ones_bf[:, 0:64],
                                       c.sql[:, sl], start=False, stop=sp))

    t1 = c.sT.n

    @block.vector
    def _(vector):
        vector.wait_ge(c.sT.h, t1)
        for q in range(4):
            sl = slice(q * 512, (q + 1) * 512)
            c.sV.inc(vector.tensor_scalar_mul(c.rows[0:1, sl], c.PB[q][0:1, :], -1.0 / D))
            c.sV.inc(vector.tensor_scalar_mul(c.rows[32:33, sl], c.PB[4 + q][0:1, :], 1.0 / D))
            vector.wait_ge(c.sV.h, c.sV.n)
            c.sV.inc(vector.scalar_tensor_tensor(
                c.PB[4 + q][32:33, :], c.rows[0:1, sl], -1.0, c.rows[0:1, sl],
                ALU.mult, ALU.mult))
            vector.wait_ge(c.sV.h, c.sV.n)
            c.sV.inc(vector.tensor_add(c.rows[64:65, sl], c.rows[32:33, sl],
                                       c.PB[4 + q][32:33, :]))
        vector.wait_ge(c.sV.h, c.sV.n)
        c.sV.inc(vector.tensor_scalar_add(c.rows[64:65, 0:T], c.rows[64:65, 0:T], EPS_LN))

    a1 = c.sA.n
    @block.scalar
    def _(scalar):
        scalar.wait_ge(c.sV.h, c.sV.n)
        c.sA.inc(scalar.activation(c.rows[96:97, 0:T], c.rows[64:65, 0:T], AF.Sqrt,
                                   bias=0.0, scale=1.0))

    v1 = c.sV.n
    @block.vector
    def _(vector):
        vector.wait_ge(c.sA.h, a1 + 1)
        c.sV.inc(vector.reciprocal(c.rows[32:33, 0:T], c.rows[96:97, 0:T]))

    t2, a2, v2 = c.sT.n, c.sA.n, c.sV.n

    @block.tensor
    def _(tensor):
        tensor.wait_ge(c.sV.h, v1 + 1)
        for q in range(4):
            sl = slice(q * 512, (q + 1) * 512)
            pm = c.PB[q % 2]
            pr = c.PB[2 + q % 2]
            if q >= 2:
                tensor.wait_ge(c.sV.h, v2 + (q - 1) * DC * 3)
            c.sT.inc(tensor.matmul(pm[:, :], c.ones_f32[0:1, :], c.rows[0:1, sl],
                                   start=True, stop=True))
            c.sT.inc(tensor.matmul(pr[:, :], c.ones_f32[32:33, :], c.rows[32:33, sl],
                                   start=True, stop=True))

    @block.vector
    def _(vector):
        for q in range(4):
            pm = c.PB[q % 2]
            pr = c.PB[2 + q % 2]
            vector.wait_ge(c.sT.h, t2 + 2 * (q + 1))
            for dc in range(DC):
                src = slice(dc * T + q * 512, dc * T + (q + 1) * 512)
                dst_full = c.xfull[:, 0:512] if extra_full is None else extra_full[:, src]
                c.sV.inc(vector.tensor_add(c.tmp1[:, 0:512], c.res[:, src], pm[:, :]))
                vector.wait_ge(c.sV.h, c.sV.n)
                c.sV.inc(vector.tensor_mul(dst_full, c.tmp1[:, 0:512], pr[:, :]))
                vector.wait_ge(c.sV.h, c.sV.n)
                vector.wait_ge(c.sA.h, a2 + q * DC + dc + 1)
                c.sV.inc(vector.tensor_sub(xl[:, src], dst_full, xh[:, src]))

    @block.scalar
    def _(scalar):
        for q in range(4):
            for dc in range(DC):
                src = slice(dc * T + q * 512, dc * T + (q + 1) * 512)
                dst_full = c.xfull[:, 0:512] if extra_full is None else extra_full[:, src]
                scalar.wait_ge(c.sV.h, v2 + (q * DC + dc) * 3 + 2)
                c.sA.inc(scalar.activation(xh[:, src], dst_full, AF.Copy,
                                           bias=0.0, scale=1.0))



def emit_l0_attn(c):
    for p in range(NH // 2):
        for wi, (wsrc, dsth, dstl, permuted) in enumerate([
                (c.w_q, c.WTQH, c.WTQL, True),
                (c.w_k, c.WTKH, c.WTKL, True),
                (c.w_v, c.WTVH, c.WTVL, False)]):
            _emit_wprep(c, p, wsrc, dsth, dstl, permuted)

        for which, (dsth, dstl, wth, wtl, bcol) in enumerate([
                (c.QH, c.QL, c.WTQH, c.WTQL, c.bias_qkv[:, 0 * 8 + p:0 * 8 + p + 1]),
                (c.KH, c.KL, c.WTKH, c.WTKL, c.bias_qkv[:, 1 * 8 + p:1 * 8 + p + 1])]):
            for tq in range(4):
                _emit_qk_proj_tile(c, p, which, tq, dsth, dstl, wth, wtl, bcol)

        _emit_bounds(c, p)

        for tq in range(4):
            _emit_v_proj_tile(c, p, tq)

        _emit_wprep(c, p, c.w_o, c.WTVH, c.WTVL, False, col_slice=True)

        c.attn_prev = {}
        for tq in range(4):
            _emit_attn_tile(c, p, tq)

        _emit_recip(c, p)

        for tq in range(4):
            _emit_attn_epilogue_tile(c, p, tq)


def _emit_wprep(c, p, wsrc, dsth, dstl, permuted, col_slice=False):
    block = c.block
    wraw = c.ARF[:, 0:1024]
    t0, a0, v0 = c.sT.n, c.sA.n, c.sV.n

    @block.sync
    def _(sync):
        sync.wait_ge(c.sT.h, t0)
        if col_slice:
            src = wsrc[:, p * 128:(p + 1) * 128].rearrange("(ec r) f -> ec r f", r=128)
            for ec in range(DC):
                c.wtok[ec] = c.chW.issue(sync, wraw[:, ec * 128:(ec + 1) * 128], src[ec])
        else:
            src = wsrc[p * 128:(p + 1) * 128, :]
            c.wtok[0] = c.chW.issue(sync, wraw, src)

    @block.tensor
    def _(tensor):
        tensor.wait_ge(c.sA.h, a0)
        tensor.wait_ge(c.sV.h, v0)
        for dc in range(DC):
            tensor.wait_ge(*c.wtok[dc if col_slice else 0])
            pb = c.PB[6 + dc % 2]
            if dc >= 2:
                tensor.wait_ge(c.sV.h, v0 + (dc - 1))
                tensor.wait_ge(c.sA.h, a0 + (dc - 1))
            c.sT.inc(tensor.transpose(pb[:, 0:128], wraw[:, dc * 128:(dc + 1) * 128],
                                      c.ident_t[:]))

    def psrc(pb):
        if not permuted:
            return pb[:, 0:128]
        return pb[:, 0:128].rearrange("p (h b a) -> p h b a", h=2, b=2, a=32).transpose([0, 1, 3, 2])

    def dvw(t, dc):
        sl = t[:, dc * 128:(dc + 1) * 128]
        if not permuted:
            return sl
        return sl.rearrange("p (h a b) -> p h a b", h=2, a=32, b=2)

    @block.scalar
    def _(scalar):
        for dc in range(DC):
            pb = c.PB[6 + dc % 2]
            scalar.wait_ge(c.sT.h, t0 + dc + 1)
            c.sA.inc(scalar.activation(dvw(dsth, dc), psrc(pb),
                                       AF.Copy, bias=0.0, scale=1.0))

    @block.vector
    def _(vector):
        for dc in range(DC):
            pb = c.PB[6 + dc % 2]
            vector.wait_ge(c.sA.h, a0 + dc + 1)
            c.sV.inc(vector.tensor_sub(dvw(dstl, dc), psrc(pb), dvw(dsth, dc)))


def _emit_qk_proj_tile(c, p, which, tq, dsth, dstl, wth, wtl, bcol):
    block = c.block
    sl = slice(tq * 512, (tq + 1) * 512)
    qb = c.ARF[:, 0:512]
    qs = c.ARF[:, 512:1024]
    m2 = c.ARF[:, 1024:1536]
    qf = c.ARF[:, 1536:2048]
    sqn = c.ARF[:, 2048:2560]
    t0, a0, v0 = c.sT.n, c.sA.n, c.sV.n

    @block.tensor
    def _(tensor):
        tensor.wait_ge(c.sA.h, a0)
        tensor.wait_ge(c.sV.h, v0)
        n = 0
        for wt, x in ((wth, XH_g(c)), (wth, XL_g(c)), (wtl, XH_g(c))):
            for dc in range(DC):
                n += 1
                c.sT.inc(tensor.matmul(c.PB[5][:, :],
                                       wt[:, dc * 128:(dc + 1) * 128],
                                       x[:, dc * T + tq * 512: dc * T + (tq + 1) * 512],
                                       start=(n == 1), stop=(n == 24)))

    t_mm = c.sT.n

    @block.scalar
    def _(scalar):
        scalar.wait_ge(c.sT.h, t_mm)
        c.sA.inc(scalar.activation(qb, c.PB[5][:, :], AF.Identity, bias=bcol, scale=1.0))

    @block.vector
    def _(vector):
        vector.wait_ge(c.sA.h, a0 + 1)
        c.sV.inc(vector.stream_shuffle(qs, qb, [(i ^ 1) for i in range(32)]))
        c.sV.inc(vector.tensor_mul(c.PB[6][:, :], qb, c.ropeA[:, sl]))
        vector.wait_ge(c.sV.h, c.sV.n)
        c.sV.inc(vector.tensor_mul(m2, qs, c.ropeB[:, sl]))
        vector.wait_ge(c.sV.h, c.sV.n)
        c.sV.inc(vector.tensor_add(qf, c.PB[6][:, :], m2))

    @block.scalar
    def _(scalar):
        scalar.wait_ge(c.sV.h, v0 + 4)
        c.sA.inc(scalar.activation(dsth[:, sl], qf, AF.Copy, bias=0.0, scale=1.0))
        c.sA.inc(scalar.activation(sqn, qf, AF.Square, bias=0.0, scale=1.0))

    @block.tensor
    def _(tensor):
        tensor.wait_ge(c.sA.h, a0 + 3)
        tensor.wait_ge(c.sV.h, v0)
        c.sT.inc(tensor.matmul(c.PB[6][0:64, :], c.ones_f32[0:64, 0:64], sqn[0:64, :],
                               start=True, stop=True))
        c.sT.inc(tensor.matmul(c.PB[7][0:64, :], c.ones_f32[64:128, 0:64], sqn[64:128, :],
                               start=True, stop=True))

    @block.vector
    def _(vector):
        vector.wait_ge(c.sA.h, a0 + 3)
        vector.wait_ge(c.sV.h, c.sV.n)
        c.sV.inc(vector.tensor_sub(dstl[:, sl], qf, dsth[:, sl]))
        vector.wait_ge(c.sT.h, t_mm + 2)
        cellA = c.nrm[0:1, which * 2:which * 2 + 1]
        cellB = c.nrm[0:1, which * 2 + 1:which * 2 + 2]
        scr = c.nrm[0:1, 8:9]
        c.sV.inc(vector.tensor_reduce(scr, c.PB[6][0:1, :], AX.X, ALU.max))
        vector.wait_ge(c.sV.h, c.sV.n)
        if tq == 0:
            c.sV.inc(vector.tensor_copy(cellA, scr))
        else:
            c.sV.inc(vector.tensor_max(cellA, cellA, scr))
        vector.wait_ge(c.sV.h, c.sV.n)
        c.sV.inc(vector.tensor_reduce(scr, c.PB[7][0:1, :], AX.X, ALU.max))
        vector.wait_ge(c.sV.h, c.sV.n)
        if tq == 0:
            c.sV.inc(vector.tensor_copy(cellB, scr))
        else:
            c.sV.inc(vector.tensor_max(cellB, cellB, scr))


def _emit_bounds(c, p):
    block = c.block
    v0, a0 = c.sV.n, c.sA.n

    @block.scalar
    def _(scalar):
        scalar.wait_ge(c.sV.h, v0)
        c.sA.inc(scalar.activation(c.nrm[0:1, 4:8], c.nrm[0:1, 0:4], AF.Sqrt,
                                   bias=0.0, scale=1.0))

    @block.vector
    def _(vector):
        vector.wait_ge(c.sA.h, a0 + 1)
        c.sV.inc(vector.tensor_mul(c.nrm[0:1, 9:10], c.nrm[0:1, 4:5], c.nrm[0:1, 6:7]))
        c.sV.inc(vector.tensor_mul(c.nrm[0:1, 10:11], c.nrm[0:1, 5:6], c.nrm[0:1, 7:8]))
        vector.wait_ge(c.sV.h, c.sV.n)
        c.sV.inc(vector.tensor_scalar_mul(c.nrm[0:1, 9:11], c.nrm[0:1, 9:11], -0.125))

    @block.sync
    def _(sync):
        sync.wait_ge(c.sV.h, c.sV.n)
        t1 = c.chC.issue(sync, c.scal_dram.ap()[0:1, 0:2], c.nrm[0:1, 9:11])
        sync.wait_ge(*t1)
        t2 = c.chC.issue(sync, c.biasA, c.scal_dram.ap()[0:1, 0:1].partition_broadcast(128))
        t3 = c.chC.issue(sync, c.biasB, c.scal_dram.ap()[0:1, 1:2].partition_broadcast(128))
        c.tok_bias = (t2, t3)


def _emit_v_proj_tile(c, p, tq):
    block = c.block
    qf = c.ARF[:, 1536:2048]
    bcol = c.bias_qkv[:, 2 * 8 + p:2 * 8 + p + 1]
    t0, a0, v0 = c.sT.n, c.sA.n, c.sV.n

    @block.tensor
    def _(tensor):
        tensor.wait_ge(c.sA.h, a0)
        n = 0
        for wt, x in ((c.WTVH, XH_g(c)), (c.WTVH, XL_g(c)), (c.WTVL, XH_g(c))):
            for dc in range(DC):
                n += 1
                c.sT.inc(tensor.matmul(c.PB[5][:, :],
                                       wt[:, dc * 128:(dc + 1) * 128],
                                       x[:, dc * T + tq * 512: dc * T + (tq + 1) * 512],
                                       start=(n == 1), stop=(n == 24)))

    t_mm = c.sT.n

    @block.scalar
    def _(scalar):
        scalar.wait_ge(c.sT.h, t_mm)
        c.sA.inc(scalar.activation(qf, c.PB[5][:, :], AF.Identity, bias=bcol, scale=1.0))

    @block.tensor
    def _(tensor):
        tensor.wait_ge(c.sA.h, a0 + 1)
        tensor.wait_ge(c.sV.h, v0)
        for sub in range(4):
            pb = c.PB[6 + sub % 2]
            if sub >= 2:
                tensor.wait_ge(c.sV.h, v0 + (sub - 1) * 2)
            c.sT.inc(tensor.transpose(pb[:, 0:128], qf[:, sub * 128:(sub + 1) * 128],
                                      c.ident_t[:]))

    @block.scalar
    def _(scalar):
        for sub in range(4):
            kc = tq * 4 + sub
            pb = c.PB[6 + sub % 2]
            scalar.wait_ge(c.sT.h, t_mm + sub + 1)
            c.sA.inc(scalar.activation(c.VSA[:, kc * 128: kc * 128 + 64], pb[:, 0:64],
                                       AF.Copy, bias=0.0, scale=1.0))
            c.sA.inc(scalar.activation(c.VSB[:, kc * 128: kc * 128 + 64], pb[:, 64:128],
                                       AF.Copy, bias=0.0, scale=1.0))

    @block.vector
    def _(vector):
        for sub in range(4):
            kc = tq * 4 + sub
            pb = c.PB[6 + sub % 2]
            vector.wait_ge(c.sA.h, a0 + 1 + 2 * (sub + 1))
            c.sV.inc(vector.tensor_sub(c.VSA[:, kc * 128 + 64: kc * 128 + 128],
                                       pb[:, 0:64], c.VSA[:, kc * 128: kc * 128 + 64]))
            c.sV.inc(vector.tensor_sub(c.VSB[:, kc * 128 + 64: kc * 128 + 128],
                                       pb[:, 64:128], c.VSB[:, kc * 128: kc * 128 + 64]))


def _emit_attn_tile(c, p, tq):
    block = c.block
    nk = 4 * (tq + 1)
    qsl = slice(tq * 512, (tq + 1) * 512)
    t0, a0, v0 = c.sT.n, c.sA.n, c.sV.n

    tS, tAV, aA, aB, vM, vPA, vPB = {}, {}, {}, {}, {}, {}, {}
    t = t0
    for kc in range(nk):
        t += 6
        tS[kc] = t
        if kc >= 1:
            t += 10
            tAV[kc - 1] = t
    t += 10
    tAV[nk - 1] = t
    a = a0
    for kc in range(nk):
        a += 2
        aA[kc] = a
        a += 2
        aB[kc] = a
    a_evac = a + 4
    v = v0
    for kc in range(nk):
        if 0 <= kc - 4 * tq < 4:
            v += 2
        vM[kc] = v
        v += 1
        vPA[kc] = v
        v += 1
        vPB[kc] = v

    prev_sub = c.attn_prev.get("sub", {})
    prev_av = c.attn_prev.get("av", {})
    prev_evac = c.attn_prev.get("evac", 0)

    @block.tensor
    def _(tensor):
        tensor.wait_ge(c.sA.h, prev_evac)
        if tq == 0:
            tensor.wait_ge(c.sV.h, v0)
        for kc in range(nk):
            psA = c.PB[kc % 2]
            psB = c.PB[2 + kc % 2]
            if kc >= 2:
                tensor.wait_ge(c.sV.h, vPB[kc - 2])
            elif (kc % 2) in prev_sub:
                tensor.wait_ge(c.sV.h, prev_sub[kc % 2])
            ksl = slice(kc * 128, (kc + 1) * 128)
            for ps, lo in ((psA, 0), (psB, 64)):
                h = slice(lo, lo + 64)
                c.sT.inc(tensor.matmul(ps[:, :], c.KH[h, ksl], c.QH[h, qsl],
                                       start=True, stop=False))
                c.sT.inc(tensor.matmul(ps[:, :], c.KH[h, ksl], c.QL[h, qsl],
                                       start=False, stop=False))
                c.sT.inc(tensor.matmul(ps[:, :], c.KL[h, ksl], c.QH[h, qsl],
                                       start=False, stop=True))
            if kc >= 1:
                _av_mms(c, tensor, tq, kc - 1, nk, vPB)
        _av_mms(c, tensor, tq, nk - 1, nk, vPB)

    @block.scalar
    def _(scalar):
        if tq == 0:
            scalar.wait_ge(*c.tok_bias[0])
            scalar.wait_ge(*c.tok_bias[1])
        for kc in range(nk):
            psA = c.PB[kc % 2]
            psB = c.PB[2 + kc % 2]
            ph = c.PHt[kc % 2]
            scalar.wait_ge(c.sT.h, tS[kc])
            scalar.wait_ge(c.sV.h, vM[kc])
            if kc >= 2:
                scalar.wait_ge(c.sT.h, tAV[kc - 2])
            elif (kc % 2) in prev_av:
                scalar.wait_ge(c.sT.h, prev_av[kc % 2])
            c.sA.inc(scalar.activation(ph[:, 0:512], psA[:, :], AF.Exp,
                                       bias=c.biasA[:, 0:1], scale=0.125))
            scalar.wait_ge(c.sA.h, c.sA.n)
            c.sA.inc(scalar.activation(psA[:, :], psA[:, :], AF.Exp,
                                       bias=c.biasA[:, 0:1], scale=0.125))
            c.sA.inc(scalar.activation(ph[:, 512:1024], psB[:, :], AF.Exp,
                                       bias=c.biasB[:, 0:1], scale=0.125))
            scalar.wait_ge(c.sA.h, c.sA.n)
            c.sA.inc(scalar.activation(psB[:, :], psB[:, :], AF.Exp,
                                       bias=c.biasB[:, 0:1], scale=0.125))
        scalar.wait_ge(c.sT.h, tAV[nk - 1])
        c.sA.inc(scalar.activation(c.AVS[0:64, tq * 512:(tq + 1) * 512],
                                   c.PB[4][0:64, :], AF.Copy, bias=0.0, scale=1.0))
        c.sA.inc(scalar.activation(c.AVS[64:128, tq * 512:(tq + 1) * 512],
                                   c.PB[5][64:128, :], AF.Copy, bias=0.0, scale=1.0))
        c.sA.inc(scalar.activation(c.rows[64:65, qsl], c.PB[6][0:1, :],
                                   AF.Copy, bias=0.0, scale=1.0))
        c.sA.inc(scalar.activation(c.rows[96:97, qsl], c.PB[7][0:1, :],
                                   AF.Copy, bias=0.0, scale=1.0))

    @block.vector
    def _(vector):
        for kc in range(nk):
            psA = c.PB[kc % 2]
            psB = c.PB[2 + kc % 2]
            ph = c.PHt[kc % 2]
            pl = c.PLt[kc % 2]
            mi = kc - 4 * tq
            vector.wait_ge(c.sT.h, tS[kc])
            if 0 <= mi < 4:
                msl = slice(384 - mi * 128, 384 - mi * 128 + 512)
                c.sV.inc(vector.tensor_add(psA[:, :], psA[:, :], c.MASKS[:, msl]))
                c.sV.inc(vector.tensor_add(psB[:, :], psB[:, :], c.MASKS[:, msl]))
            if kc >= 2:
                vector.wait_ge(c.sT.h, tAV[kc - 2])
            elif (kc % 2) in prev_av:
                vector.wait_ge(c.sT.h, prev_av[kc % 2])
            vector.wait_ge(c.sA.h, aA[kc])
            c.sV.inc(vector.tensor_sub(pl[:, 0:512], psA[:, :], ph[:, 0:512]))
            vector.wait_ge(c.sA.h, aB[kc])
            c.sV.inc(vector.tensor_sub(pl[:, 512:1024], psB[:, :], ph[:, 512:1024]))

    c.attn_prev = {
        "sub": {(nk - 1) % 2: vPB[nk - 1], (nk - 2) % 2: vPB[nk - 2]},
        "av": {(nk - 1) % 2: tAV[nk - 1], (nk - 2) % 2: tAV[nk - 2]},
        "evac": a_evac,
    }


def _av_mms(c, tensor, tq, kc, nk, vPB):
    ph = c.PHt[kc % 2]
    pl = c.PLt[kc % 2]
    tensor.wait_ge(c.sV.h, vPB[kc])
    first = (kc == 0)
    last = (kc == nk - 1)
    for lo, avb in ((0, c.PB[4]), (64, c.PB[5])):
        phs = ph[:, lo * 8: lo * 8 + 512]
        pls = pl[:, lo * 8: lo * 8 + 512]
        vh = c.VSA if lo == 0 else c.VSB
        hi_t = vh[:, kc * 128: kc * 128 + 64]
        lo_t = vh[:, kc * 128 + 64: kc * 128 + 128]
        n = 0
        for (st_, mv) in ((hi_t, phs), (lo_t, phs), (hi_t, pls)):
            n += 1
            c.sT.inc(tensor.matmul(avb[lo:lo + 64, :], st_, mv,
                                   start=(first and n == 1), stop=(last and n == 3),
                                   tile_position=(0, lo)))
        dn = c.PB[6] if lo == 0 else c.PB[7]
        c.sT.inc(tensor.matmul(dn[0:64, :], c.ones_bf[:, 0:64], phs,
                               start=first, stop=False))
        c.sT.inc(tensor.matmul(dn[0:64, :], c.ones_bf[:, 0:64], pls,
                               start=False, stop=last))


def _emit_recip(c, p):
    block = c.block
    a0 = c.sA.n
    recD = c.ARF[0:8, 2048:2560]

    @block.sync
    def _(sync):
        sync.wait_ge(c.sA.h, a0)
        t1 = c.chC.issue(sync, recD[0:4, :], c.rows[64:65, :])
        t2 = c.chC.issue(sync, recD[4:8, :], c.rows[96:97, :])
        c.tok_recd = (t1, t2)

    v0 = c.sV.n
    @block.vector
    def _(vector):
        vector.wait_ge(*c.tok_recd[0])
        vector.wait_ge(*c.tok_recd[1])
        c.sV.inc(vector.reciprocal(recD[0:8, :], recD[0:8, :]))

    @block.sync
    def _(sync):
        sync.wait_ge(c.sV.h, c.sV.n)
        t1 = c.chC.issue(sync, c.rows[0:1, :], recD[0:4, :])
        t2 = c.chC.issue(sync, c.rows[32:33, :], recD[4:8, :])
        c.tok_recr = (t1, t2)


def _emit_attn_epilogue_tile(c, p, tq):
    block = c.block
    qsl = slice(tq * 512, (tq + 1) * 512)
    osb = c.ARF[:, 2560:3072]
    t0, a0, v0 = c.sT.n, c.sA.n, c.sV.n

    @block.tensor
    def _(tensor):
        tensor.wait_ge(c.sV.h, v0)
        if tq == 0:
            tensor.wait_ge(*c.tok_recr[0])
            tensor.wait_ge(*c.tok_recr[1])
        c.sT.inc(tensor.matmul(c.PB[0][0:64, :], c.ones_f32[0:1, 0:64],
                               c.rows[0:1, qsl], start=True, stop=True,
                               tile_position=(0, 0)))
        c.sT.inc(tensor.matmul(c.PB[1][64:128, :], c.ones_f32[32:33, 0:64],
                               c.rows[32:33, qsl], start=True, stop=True,
                               tile_position=(32, 64)))

    t_bc = c.sT.n

    @block.vector
    def _(vector):
        vector.wait_ge(c.sT.h, t_bc)
        c.sV.inc(vector.tensor_mul(osb[0:64, :], c.AVS[0:64, qsl], c.PB[0][0:64, :]))
        c.sV.inc(vector.tensor_mul(osb[64:128, :], c.AVS[64:128, qsl], c.PB[1][64:128, :]))

    @block.scalar
    def _(scalar):
        scalar.wait_ge(c.sV.h, v0 + 2)
        c.sA.inc(scalar.activation(c.OH[:, 0:512], osb, AF.Copy, bias=0.0, scale=1.0))

    @block.vector
    def _(vector):
        vector.wait_ge(c.sA.h, a0 + 1)
        vector.wait_ge(c.sV.h, c.sV.n)
        c.sV.inc(vector.tensor_sub(c.OL[:, 0:512], osb, c.OH[:, 0:512]))

    v_spl = c.sV.n

    @block.tensor
    def _(tensor):
        tensor.wait_ge(c.sV.h, v_spl)
        for ec in range(DC):
            pb = c.PB[6 + ec % 2]
            if ec >= 2:
                tensor.wait_ge(c.sV.h, v_spl + (ec - 1))
            n = 0
            for wt, mv in ((c.WTVH, c.OH[:, 0:512]), (c.WTVH, c.OL[:, 0:512]),
                           (c.WTVL, c.OH[:, 0:512])):
                n += 1
                c.sT.inc(tensor.matmul(pb[:, :], wt[:, ec * 128:(ec + 1) * 128], mv,
                                       start=(n == 1), stop=(n == 3)))

    t_wo = c.sT.n

    @block.vector
    def _(vector):
        for ec in range(DC):
            pb = c.PB[6 + ec % 2]
            vector.wait_ge(c.sT.h, t_wo - 3 * (DC - 1 - ec))
            dst = c.res[:, ec * T + tq * 512: ec * T + (tq + 1) * 512]
            if p == 0:
                c.sV.inc(vector.scalar_tensor_tensor(
                    dst, pb[:, :], c.bias_qkv[:, 3 * 8 + ec:3 * 8 + ec + 1], dst,
                    ALU.add, ALU.add))
            else:
                c.sV.inc(vector.tensor_add(dst, dst, pb[:, :]))


def XH_g(c):
    return c.XH


def XL_g(c):
    return c.XL



def emit_ffn0_gamma(c, wsrc, nrows, ncols, cell):
    block = c.block
    wstage = c.ARF[:, 0:2048]
    nch = nrows // 128
    ncg = ncols // 1024
    a0, v0 = c.sA.n, c.sV.n
    tok = {}

    @block.sync
    def _(sync):
        _wait_all(sync, c)
        k = 0
        for ch in range(nch):
            for cg in range(ncg):
                if k >= 2:
                    sync.wait_ge(c.sA.h, a0 + (k - 1))
                tok[k] = c.chW.issue(sync, wstage[:, (k % 2) * 1024:(k % 2) * 1024 + 1024],
                                     wsrc[ch * 128:(ch + 1) * 128, cg * 1024:(cg + 1) * 1024])
                k += 1

    nk = nch * ncg

    @block.scalar
    def _(scalar):
        for k in range(nk):
            scalar.wait_ge(*tok[k])
            if k >= 1:
                scalar.wait_ge(c.sV.h, v0 + k)
            h = (k % 2) * 1024
            c.sA.inc(scalar.activation(wstage[:, h:h + 1024], wstage[:, h:h + 1024],
                                       AF.Abs, bias=0.0, scale=1.0,
                                       accum_out=c.acol[:, 0:1]))

    @block.vector
    def _(vector):
        for k in range(nk):
            vector.wait_ge(c.sA.h, a0 + k + 1)
            if k == 0:
                c.sV.inc(vector.tensor_copy(c.gcol[:, 0:1], c.acol[:, 0:1]))
            else:
                c.sV.inc(vector.tensor_add(c.gcol[:, 0:1], c.gcol[:, 0:1], c.acol[:, 0:1]))
            vector.wait_ge(c.sV.h, c.sV.n)

    @block.gpsimd
    def _(gpsimd):
        gpsimd.wait_ge(c.sV.h, c.sV.n)
        c.sG.inc(gpsimd.tensor_reduce(c.nrm[0:1, cell:cell + 1], c.gcol[:, 0:1],
                                      AX.C, ALU.add))

    @block.vector
    def _(vector):
        vector.wait_ge(c.sG.h, c.sG.n)
        c.sV.inc(vector.tensor_scalar(c.nrm[0:1, cell:cell + 1],
                                      c.nrm[0:1, cell:cell + 1],
                                      1.0 / (nrows * float(ncols)), None, ALU.mult))
        vector.wait_ge(c.sV.h, c.sV.n)
        c.sV.inc(vector.tensor_scalar(c.nrm[0:1, cell:cell + 1],
                                      c.nrm[0:1, cell:cell + 1],
                                      1e-05, 0.5, ALU.add, ALU.mult))


def emit_ffn0_wprep(c, wsrc, nrows, ncols, cell, thr_cells, dst_fn):
    block = c.block
    wstage = c.ARF[:, 0:2048]
    tpos, tneg = thr_cells
    nch = nrows // 128
    ncg = ncols // 1024
    t0, a0, v0 = c.sT.n, c.sA.n, c.sV.n
    tok = {}
    snapQ = c.chQ.snapshot()

    @block.sync
    def _(sync):
        _wait_all(sync, c)
        t1 = c.chC.issue(sync, c.scal_dram.ap()[0:1, 0:1], c.nrm[0:1, cell:cell + 1])
        sync.wait_ge(*t1)
        t2 = c.chC.issue(sync, c.ARF[:, tpos:tpos + 1],
                         c.scal_dram.ap()[0:1, 0:1].partition_broadcast(128))
        c.tok_thr = t2

    @block.vector
    def _(vector):
        vector.wait_ge(*c.tok_thr)
        c.sV.inc(vector.tensor_scalar_mul(c.ARF[:, tneg:tneg + 1],
                                          c.ARF[:, tpos:tpos + 1], -1.0))

    v_thr = c.sV.n

    @block.sync
    def _(sync):
        k = 0
        for ch in range(nch):
            for cg in range(ncg):
                if k >= 2:
                    sync.wait_ge(c.sT.h, t0 + (k - 1) * DC)
                tok[k] = c.chW.issue(sync, wstage[:, (k % 2) * 1024:(k % 2) * 1024 + 1024],
                                     wsrc[ch * 128:(ch + 1) * 128, cg * 1024:(cg + 1) * 1024])
                k += 1

    @block.tensor
    def _(tensor):
        tensor.wait_ge(c.sV.h, v_thr)
        k = 0
        for ch in range(nch):
            for cg in range(ncg):
                tensor.wait_ge(*tok[k])
                h = (k % 2) * 1024
                for dc in range(DC):
                    pb = c.PB[4 + (dc % 2)]
                    n = k * DC + dc
                    if n >= 2:
                        tensor.wait_ge(c.sV.h, v_thr + (n - 1) * 2)
                    c.sT.inc(tensor.transpose(pb[:, 0:128],
                                              wstage[:, h + dc * 128:h + (dc + 1) * 128],
                                              c.ident_t[:]))
                k += 1

    @block.vector
    def _(vector):
        for sm in c.chQ.sems:
            vector.wait_ge(sm.h, sm.n)
        n = 0
        for ch in range(nch):
            for cg in range(ncg):
                for dc in range(DC):
                    pb = c.PB[4 + (dc % 2)]
                    vector.wait_ge(c.sT.h, t0 + n + 1)
                    stg = (c.OH if (n % 2 == 0) else c.OL)[:, (n // 2 % 2) * 128:(n // 2 % 2) * 128 + 128]
                    if n >= 4:
                        vector.wait_ge(*c.chQ.plan(snapQ, n - 4))
                    vector.wait_ge(c.sV.h, c.sV.n)
                    c.sV.inc(vector.tensor_scalar(c.qtmp[:, 0:128], pb[:, 0:128],
                                                  c.ARF[:, tneg:tneg + 1], None, ALU.is_le))
                    vector.wait_ge(c.sV.h, c.sV.n)
                    c.sV.inc(vector.scalar_tensor_tensor(
                        stg, pb[:, 0:128], c.ARF[:, tpos:tpos + 1],
                        c.qtmp[:, 0:128], ALU.is_ge, ALU.subtract))
                    n += 1

    @block.scalar
    def _(scalar):
        n = 0
        for ch in range(nch):
            for cg in range(ncg):
                for dc in range(DC):
                    scalar.wait_ge(c.sV.h, v_thr + (n + 1) * 2)
                    stg = (c.OH if (n % 2 == 0) else c.OL)[:, (n // 2 % 2) * 128:(n // 2 % 2) * 128 + 128]
                    tkn = c.chQ.issue(scalar, dst_fn(ch, cg, dc), stg)
                    n += 1


def emit_ln_stats(c):
    block = c.block
    t0, a0, v0 = c.sT.n, c.sA.n, c.sV.n
    g0, s_0 = c.sG.n, c.sS.n

    @block.scalar
    def _(scalar):
        _wait_all(scalar, c)
        for dc in range(DC):
            h = (dc % 2) * T
            if dc >= 2:
                scalar.wait_ge(c.sT.h, t0 + (dc - 1) * 16)
            r = c.res[:, dc * T:(dc + 1) * T]
            c.sA.inc(scalar.activation(c.sqbuf[:, h:h + T], r, AF.Square, bias=0.0, scale=1.0))
            c.sA.inc(scalar.activation(c.sbh[:, h:h + T], r, AF.Copy, bias=0.0, scale=1.0))
            scalar.wait_ge(c.sA.h, c.sA.n - 1)
            c.sA.inc(scalar.activation(c.sqh[:, h:h + T], c.sqbuf[:, h:h + T],
                                       AF.Copy, bias=0.0, scale=1.0))

    @block.vector
    def _(vector):
        for dc in range(DC):
            h = (dc % 2) * T
            vector.wait_ge(c.sA.h, a0 + dc * 3 + 3)
            r = c.res[:, dc * T:(dc + 1) * T]
            c.sV.inc(vector.tensor_sub(c.sbl[:, h:h + T], r, c.sbh[:, h:h + T]))
            c.sV.inc(vector.tensor_sub(c.sql[:, h:h + T], c.sqbuf[:, h:h + T],
                                       c.sqh[:, h:h + T]))

    @block.tensor
    def _(tensor):
        tensor.wait_ge(c.sV.h, v0)
        tensor.wait_ge(c.sA.h, a0)
        tensor.wait_ge(c.sG.h, g0)
        for dc in range(DC):
            h = (dc % 2) * T
            tensor.wait_ge(c.sV.h, v0 + dc * 2 + 2)
            st = (dc == 0)
            sp = (dc == DC - 1)
            for q in range(4):
                sl = slice(h + q * 512, h + (q + 1) * 512)
                c.sT.inc(tensor.matmul(c.PB[q][0:64, :], c.ones_bf[:, 0:64],
                                       c.sbh[:, sl], start=st, stop=False))
                c.sT.inc(tensor.matmul(c.PB[q][0:64, :], c.ones_bf[:, 0:64],
                                       c.sbl[:, sl], start=False, stop=sp))
                c.sT.inc(tensor.matmul(c.PB[4 + q][0:64, :], c.ones_bf[:, 0:64],
                                       c.sqh[:, sl], start=st, stop=False))
                c.sT.inc(tensor.matmul(c.PB[4 + q][0:64, :], c.ones_bf[:, 0:64],
                                       c.sql[:, sl], start=False, stop=sp))

    t1 = c.sT.n

    @block.vector
    def _(vector):
        vector.wait_ge(c.sT.h, t1)
        for q in range(4):
            sl = slice(q * 512, (q + 1) * 512)
            c.sV.inc(vector.tensor_scalar_mul(c.rows[0:1, sl], c.PB[q][0:1, :], -1.0 / D))
            c.sV.inc(vector.tensor_scalar_mul(c.rows[32:33, sl], c.PB[4 + q][0:1, :], 1.0 / D))
            vector.wait_ge(c.sV.h, c.sV.n)
            c.sV.inc(vector.scalar_tensor_tensor(
                c.PB[4 + q][32:33, :], c.rows[0:1, sl], -1.0, c.rows[0:1, sl],
                ALU.mult, ALU.mult))
            vector.wait_ge(c.sV.h, c.sV.n)
            c.sV.inc(vector.tensor_add(c.rows[64:65, sl], c.rows[32:33, sl],
                                       c.PB[4 + q][32:33, :]))
        vector.wait_ge(c.sV.h, c.sV.n)
        c.sV.inc(vector.tensor_scalar_add(c.rows[64:65, 0:T], c.rows[64:65, 0:T], EPS_LN))

    a1 = c.sA.n
    @block.scalar
    def _(scalar):
        scalar.wait_ge(c.sV.h, c.sV.n)
        c.sA.inc(scalar.activation(c.rows[96:97, 0:T], c.rows[64:65, 0:T], AF.Sqrt,
                                   bias=0.0, scale=1.0))

    @block.vector
    def _(vector):
        vector.wait_ge(c.sA.h, a1 + 1)
        c.sV.inc(vector.reciprocal(c.rows[32:33, 0:T], c.rows[96:97, 0:T]))


def emit_ffn0_main(c):
    block = c.block
    X2H = c.ARB[:, 0:4096]
    X2L = c.ARB[:, 4096:8192]
    HH = c.ARB[:, 16384:32768]
    HL = c.ARB[:, 32768:49152]
    WQS = [c.ARB[:, 49152:51200], c.ARB[:, 51200:53248]]
    hfull = c.ARF[:, 2560:3072]

    @block.sync
    def _(sync):
        for sm in c.chQ.sems:
            sync.wait_ge(sm.h, sm.n)

    for tq in range(4):
        qsl = slice(tq * 512, (tq + 1) * 512)
        t0, a0, v0 = c.sT.n, c.sA.n, c.sV.n
        s_0 = c.sS.n
        wtok = {}

        @block.tensor
        def _(tensor):
            tensor.wait_ge(c.sV.h, v0)
            tensor.wait_ge(c.sA.h, a0)
            c.sT.inc(tensor.matmul(c.PB[0][:, :], c.ones_f32[0:1, :], c.rows[0:1, qsl],
                                   start=True, stop=True))
            c.sT.inc(tensor.matmul(c.PB[1][:, :], c.ones_f32[32:33, :], c.rows[32:33, qsl],
                                   start=True, stop=True))

        t_bc = c.sT.n

        @block.vector
        def _(vector):
            vector.wait_ge(c.sT.h, t_bc)
            for dc in range(DC):
                src = slice(dc * T + tq * 512, dc * T + (tq + 1) * 512)
                d2 = slice(dc * 512, (dc + 1) * 512)
                c.sV.inc(vector.tensor_add(c.tmp1[:, 0:512], c.res[:, src], c.PB[0][:, :]))
                vector.wait_ge(c.sV.h, c.sV.n)
                c.sV.inc(vector.tensor_mul(c.xfull[:, 0:512], c.tmp1[:, 0:512], c.PB[1][:, :]))
                vector.wait_ge(c.sV.h, c.sV.n)
                vector.wait_ge(c.sA.h, a0 + dc + 1)
                c.sV.inc(vector.tensor_sub(X2L[:, d2], c.xfull[:, 0:512], X2H[:, d2]))

        @block.scalar
        def _(scalar):
            for dc in range(DC):
                d2 = slice(dc * 512, (dc + 1) * 512)
                scalar.wait_ge(c.sV.h, v0 + dc * 3 + 2)
                c.sA.inc(scalar.activation(X2H[:, d2], c.xfull[:, 0:512], AF.Copy,
                                           bias=0.0, scale=1.0))

        v_norm = c.sV.n
        a_norm = c.sA.n

        @block.sync
        def _(sync):
            for fc in range(32):
                if fc >= 2:
                    sync.wait_ge(c.sT.h, t_bc + (fc - 1) * 16)
                if fc < 2:
                    sync.wait_ge(c.sT.h, t_bc)
                wtok[fc] = c.chW.issue(
                    sync, WQS[fc % 2][:, 0:1024].rearrange("p (dc f) -> p dc f", f=128),
                    c.wq1t_dram.ap()[fc].rearrange("(dc p) f -> p dc f", p=128))

        @block.tensor
        def _(tensor):
            tensor.wait_ge(c.sV.h, v_norm)
            for fc in range(32):
                tensor.wait_ge(*wtok[fc])
                pb = c.PB[6 + fc % 2]
                if fc >= 2:
                    tensor.wait_ge(c.sV.h, v_norm + (fc - 1))
                n = 0
                for dc in range(DC):
                    wt = WQS[fc % 2][:, dc * 128:(dc + 1) * 128]
                    for x in (X2H, X2L):
                        n += 1
                        c.sT.inc(tensor.matmul(pb[:, :], wt,
                                               x[:, dc * 512:(dc + 1) * 512],
                                               start=(n == 1), stop=(n == 16)))

        t_m1 = c.sT.n

        @block.scalar
        def _(scalar):
            for fc in range(32):
                pb = c.PB[6 + fc % 2]
                hsl = slice(fc * 512, (fc + 1) * 512)
                scalar.wait_ge(c.sT.h, t_bc + (fc + 1) * 16)
                if fc >= 1:
                    scalar.wait_ge(c.sV.h, v_norm + fc)
                bcol = c.bias_ffn[:, fc:fc + 1]
                c.sA.inc(scalar.activation(HH[:, hsl], pb[:, :], AF.Relu,
                                           bias=bcol, scale=1.0))
                c.sA.inc(scalar.activation(hfull, pb[:, :], AF.Relu,
                                           bias=bcol, scale=1.0))

        @block.vector
        def _(vector):
            for fc in range(32):
                hsl = slice(fc * 512, (fc + 1) * 512)
                vector.wait_ge(c.sA.h, a_norm + (fc + 1) * 2)
                c.sV.inc(vector.tensor_sub(HL[:, hsl], hfull, HH[:, hsl]))

        v_h = c.sV.n
        a_h = c.sA.n

        wtok2 = {}

        @block.sync
        def _(sync):
            for fc in range(32):
                sync.wait_ge(c.sT.h, t_m1 + (fc - 1) * 16 if fc >= 2 else t_m1)
                wtok2[fc] = c.chW.issue(sync, WQS[fc % 2][:, 0:1024],
                                        c.wq2t_dram.ap()[fc])

        @block.tensor
        def _(tensor):
            tensor.wait_ge(c.sV.h, v_h)
            tensor.wait_ge(c.sA.h, a_h)
            for fc in range(32):
                tensor.wait_ge(*wtok2[fc])
                hsl = slice(fc * 512, (fc + 1) * 512)
                n = 0
                for dco in range(DC):
                    wt = WQS[fc % 2][:, dco * 128:(dco + 1) * 128]
                    for hx in (HH, HL):
                        n += 1
                        c.sT.inc(tensor.matmul(c.PB[dco][:, :], wt, hx[:, hsl],
                                               start=(fc == 0 and n % 2 == 1),
                                               stop=(fc == 31 and n % 2 == 0)))

        t_m2 = c.sT.n

        @block.vector
        def _(vector):
            vector.wait_ge(c.sT.h, t_m2)
            for dco in range(DC):
                dst = c.res[:, dco * T + tq * 512: dco * T + (tq + 1) * 512]
                c.sV.inc(vector.scalar_tensor_tensor(
                    dst, c.PB[dco][:, :], c.bias_ffn[:, 32 + dco:32 + dco + 1], dst,
                    ALU.add, ALU.add))


def emit_ffn0(c):
    emit_ffn0_gamma(c, c.w1, FF0, 1024, 12)
    emit_ffn0_gamma(c, c.w2, D, 4096, 13)
    emit_ffn0_wprep(c, c.w1, FF0, 1024, 12, (5122, 5123),
                    lambda ch, cg, dc: c.wq1t_dram.ap()[ch, dc * 128:(dc + 1) * 128, :])
    emit_ffn0_wprep(c, c.w2, D, 4096, 13, (5124, 5125),
                    lambda ch, cg, dc: c.wq2t_dram.ap()[cg * 8 + dc, :, ch * 128:(ch + 1) * 128])
    emit_ln_stats(c)
    emit_ffn0_main(c)



def emit_ssm(c):
    block = c.block
    F = lambda a, b: c.ARB[:, 2 * a:2 * b].bitcast(F32)
    X3C = F(0, 4112)
    DTC = F(4112, 8208)
    UC = F(8208, 12304)
    ALPHA = F(12304, 14352)
    BETA = F(14352, 16400)
    H = F(16400, 18448)
    YB = F(18448, 20496)
    YR = F(20496, 20624)
    ACO = c.ropeA[:, 0:128]
    DCO = c.ropeA[:, 128:136]
    BDTP = c.ropeA[:, 136:144]
    CONVK = c.ropeA[:, 144:168]
    BDT = c.ropeA[:, 168:169]
    BBC = c.ropeA[:, 169:170]
    WBCDT = c.ropeA[:, 256:1024]
    WDTP = c.ropeB[:, 0:1024]
    DTL = c.ARF[0:64, 0:512]
    BCR = c.ARF[64:96, 512:1024]
    STATE = c.ARF[:, 1024:1152]

    @block.sync
    def _(sync):
        _wait_all(sync, c)
        c.tok_ssm = []
        ctx_nc = c.nc.allow_non_contiguous_dma(reason="tiny ssm weight vectors")
        ctx_nc.__enter__()
        c.tok_ssm.append(c.chW.issue(
            sync, c.ropeA[:, 0:128].rearrange("p (dc n) -> p dc n", n=16),
            c.a_log.rearrange("(dc p) n -> p dc n", p=128)))
        c.tok_ssm.append(c.chW.issue(
            sync, DCO, c.dcoeff.rearrange("(dc p) -> p dc", p=128)))
        c.tok_ssm.append(c.chW.issue(
            sync, BDTP, c.b_dtp.rearrange("(dc p) -> p dc", p=128)))
        c.tok_ssm.append(c.chW.issue(
            sync, CONVK.rearrange("p (k dc) -> p k dc", k=3),
            c.conv_k.rearrange("k o (dc p) -> p k (o dc)", p=128)))
        c.tok_ssm.append(c.chW.issue(sync, BDT[0:64, :], c.b_dt.rearrange("(a o) -> a o", o=1)))
        c.tok_ssm.append(c.chW.issue(sync, c.bbc_col,
                                     c.b_bc.rearrange("(a o) -> a o", o=1)))
        c.tok_ssm.append(c.chW.issue(
            sync, WBCDT.rearrange("p (dc f) -> p dc f", f=96)[:, :, 0:64],
            c.w_dt.rearrange("(dc p) f -> p dc f", p=128)))
        c.tok_ssm.append(c.chW.issue(
            sync, WBCDT.rearrange("p (dc f) -> p dc f", f=96)[:, :, 64:96],
            c.w_bc.rearrange("(dc p) f -> p dc f", p=128)))
        c.tok_ssm.append(c.chW.issue(sync, WDTP[0:64, :], c.w_dtp))
        ctx_nc.__exit__(None, None, None)

    @block.scalar
    def _(scalar):
        for t in c.tok_ssm:
            scalar.wait_ge(*t)
        c.sA.inc(scalar.activation(ACO, ACO, AF.Exp, bias=0.0, scale=1.0))

    @block.vector
    def _(vector):
        vector.wait_ge(c.sA.h, c.sA.n)
        c.sV.inc(vector.tensor_scalar_mul(ACO, ACO, -1.0))


    emit_ln_stats(c)

    NMB = F(20624, 22672)
    RSB = F(22672, 24720)
    a_st, v_st = c.sA.n, c.sV.n

    @block.tensor
    def _(tensor):
        tensor.wait_ge(c.sA.h, a_st)
        tensor.wait_ge(c.sV.h, v_st)
        for q in range(4):
            sl = slice(q * 512, (q + 1) * 512)
            c.sT.inc(tensor.matmul(c.PB[q][:, :], c.ones_f32[0:1, :],
                                   c.rows[0:1, sl], start=True, stop=True))
            c.sT.inc(tensor.matmul(c.PB[4 + q][:, :], c.ones_f32[32:33, :],
                                   c.rows[32:33, sl], start=True, stop=True))

    t_stb = c.sT.n

    @block.scalar
    def _(scalar):
        scalar.wait_ge(c.sT.h, t_stb)
        for q in range(4):
            sl = slice(q * 512, (q + 1) * 512)
            c.sA.inc(scalar.activation(NMB[:, sl], c.PB[q][:, :], AF.Copy,
                                       bias=0.0, scale=1.0))
            c.sA.inc(scalar.activation(RSB[:, sl], c.PB[4 + q][:, :], AF.Copy,
                                       bias=0.0, scale=1.0))

    c.ssm_nmb, c.ssm_rsb, c.ssm_astb = NMB, RSB, c.sA.n

    @block.vector
    def _(vector):
        vector.wait_ge(c.sA.h, c.ssm_astb)
        c.sV.inc(vector.memset(STATE, 0.0))

    for tc in range(4):
        _emit_ssm_chunk(c, tc, X3C, DTC, UC, ALPHA, BETA, H, YB, YR,
                        ACO, DCO, BDTP, CONVK, BDT, WBCDT, WDTP, DTL, BCR, STATE)


def _emit_ssm_chunk(c, tc, X3C, DTC, UC, ALPHA, BETA, H, YB, YR,
                    ACO, DCO, BDTP, CONVK, BDT, WBCDT, WDTP, DTL, BCR, STATE):
    block = c.block
    t0, a0, v0 = c.sT.n, c.sA.n, c.sV.n

    nsl = slice(tc * 512, (tc + 1) * 512)

    @block.vector
    def _(vector):
        vector.wait_ge(c.sA.h, max(a0, c.ssm_astb))
        for dc in range(DC):
            base = dc * 514
            if tc > 0:
                c.sV.inc(vector.tensor_copy(X3C[:, base:base + 1],
                                            X3C[:, base + 512:base + 513]))
                vector.wait_ge(c.sV.h, c.sV.n)
            src = slice(dc * T + tc * 512, dc * T + (tc + 1) * 512)
            c.sV.inc(vector.tensor_add(c.tmp1[:, 0:512], c.res[:, src],
                                       c.ssm_nmb[:, nsl]))
            vector.wait_ge(c.sV.h, c.sV.n)
            c.sV.inc(vector.tensor_mul(X3C[:, base + 1:base + 513],
                                       c.tmp1[:, 0:512], c.ssm_rsb[:, nsl]))
            vector.wait_ge(c.sV.h, c.sV.n)

    @block.vector
    def _(vector):
        for dc in range(DC):
            b = dc * 514
            ub = dc * 512
            k0 = CONVK[:, dc:dc + 1]
            k1 = CONVK[:, 8 + dc:8 + dc + 1]
            k2 = CONVK[:, 16 + dc:16 + dc + 1]
            c.sV.inc(vector.tensor_scalar_mul(c.tmp1[:, 0:512], X3C[:, b:b + 512], k0))
            vector.wait_ge(c.sV.h, c.sV.n)
            c.sV.inc(vector.scalar_tensor_tensor(
                c.tmp1[:, 0:512], X3C[:, b + 1:b + 513], k1, c.tmp1[:, 0:512],
                ALU.mult, ALU.add))
            vector.wait_ge(c.sV.h, c.sV.n)
            c.sV.inc(vector.scalar_tensor_tensor(
                UC[:, ub:ub + 512], X3C[:, b + 2:b + 514], k2, c.tmp1[:, 0:512],
                ALU.mult, ALU.add))
            vector.wait_ge(c.sV.h, c.sV.n)
            if tc == 0:
                c.sV.inc(vector.tensor_scalar_mul(c.tmp1[:, 0:1], X3C[:, b + 1:b + 2], k1))
                vector.wait_ge(c.sV.h, c.sV.n)
                c.sV.inc(vector.scalar_tensor_tensor(
                    UC[:, ub:ub + 1], X3C[:, b + 2:b + 3], k2, c.tmp1[:, 0:1],
                    ALU.mult, ALU.add))
                vector.wait_ge(c.sV.h, c.sV.n)
            if tc == 3:
                c.sV.inc(vector.tensor_scalar_mul(c.tmp1[:, 0:1], X3C[:, b + 511:b + 512], k0))
                vector.wait_ge(c.sV.h, c.sV.n)
                c.sV.inc(vector.scalar_tensor_tensor(
                    UC[:, ub + 511:ub + 512], X3C[:, b + 512:b + 513], k1, c.tmp1[:, 0:1],
                    ALU.mult, ALU.add))
                vector.wait_ge(c.sV.h, c.sV.n)

    v_conv = c.sV.n

    @block.scalar
    def _(scalar):
        scalar.wait_ge(c.sV.h, v_conv)
        c.sA.inc(scalar.activation(UC[:, 0:4096], UC[:, 0:4096], AF.Silu,
                                   bias=0.0, scale=1.0))

    a_silu = c.sA.n

    @block.tensor
    def _(tensor):
        tensor.wait_ge(c.sA.h, a_silu)
        for dc in range(DC):
            c.sT.inc(tensor.matmul(c.PB[6][0:96, :],
                                   WBCDT[:, dc * 96:(dc + 1) * 96],
                                   UC[:, dc * 512:(dc + 1) * 512],
                                   start=(dc == 0), stop=(dc == DC - 1)))

    t_p1 = c.sT.n

    @block.scalar
    def _(scalar):
        scalar.wait_ge(c.sT.h, t_p1)
        c.sA.inc(scalar.activation(DTL, c.PB[6][0:64, :], AF.Identity,
                                   bias=BDT[0:64, :], scale=1.0))
        c.sA.inc(scalar.activation(BCR, c.PB[6][64:96, :], AF.Identity,
                                   bias=c.bbc_col, scale=1.0))

    a_p1 = c.sA.n

    @block.tensor
    def _(tensor):
        tensor.wait_ge(c.sA.h, a_p1)
        for dc in range(DC):
            pb = c.PB[7] if dc % 2 else c.PB[5]
            if dc >= 2:
                tensor.wait_ge(c.sA.h, a_p1 + (dc - 1) * 4)
            c.sT.inc(tensor.matmul(pb[:, :], WDTP[0:64, dc * 128:(dc + 1) * 128],
                                   DTL, start=True, stop=True))

    t_p2 = c.sT.n

    sp = c.xfull[:, 0:512]
    v_sp0 = c.sV.n

    @block.scalar
    def _(scalar):
        for dc in range(DC):
            pb = c.PB[7] if dc % 2 else c.PB[5]
            scalar.wait_ge(c.sT.h, t_p2 - (DC - 1 - dc))
            if dc >= 1:
                scalar.wait_ge(c.sV.h, v_sp0 + dc)
            bcol = BDTP[:, dc:dc + 1]
            c.sA.inc(scalar.activation(sp, pb[:, :], AF.Abs, bias=bcol, scale=1.0))
            scalar.wait_ge(c.sA.h, c.sA.n)
            c.sA.inc(scalar.activation(sp, sp, AF.Exp, bias=0.0, scale=-1.0))
            scalar.wait_ge(c.sA.h, c.sA.n)
            c.sA.inc(scalar.activation(sp, sp, AF.Ln, bias=1.0, scale=1.0))
            c.sA.inc(scalar.activation(DTC[:, dc * 512:(dc + 1) * 512], pb[:, :],
                                       AF.Relu, bias=bcol, scale=1.0))

    a_dt0 = c.sA.n

    @block.vector
    def _(vector):
        for dc in range(DC):
            vector.wait_ge(c.sA.h, a_dt0 - (DC - 1 - dc) * 4)
            dsl_ = slice(dc * 512, (dc + 1) * 512)
            c.sV.inc(vector.tensor_add(DTC[:, dsl_], DTC[:, dsl_], sp))

    a_dt = c.sA.n

    @block.vector
    def _(vector):
        vector.wait_ge(c.sA.h, a_dt)
        vector.wait_ge(c.sV.h, c.sV.n)
        for dc in range(DC):
            c.sV.inc(vector.tensor_mul(UC[:, dc * 512:(dc + 1) * 512],
                                       X3C[:, dc * 514 + 1:dc * 514 + 513],
                                       DTC[:, dc * 512:(dc + 1) * 512]))

    v_xdt = c.sV.n

    @block.sync
    def _(sync):
        sync.wait_ge(c.sA.h, a_p1)
        c.tok_bcd = c.chC.issue(sync, c.bc_dram.ap(), BCR)

    for sub in range(4):
        _emit_ssm_sub(c, tc, sub, X3C, DTC, UC, ALPHA, BETA, H, YB, YR,
                      ACO, DCO, STATE, v_xdt)


def _emit_ssm_sub(c, tc, sub, X3C, DTC, UC, ALPHA, BETA, H, YB, YR,
                  ACO, DCO, STATE, v_xdt):
    block = c.block
    t0, a0, v0 = c.sT.n, c.sA.n, c.sV.n

    @block.sync
    def _(sync):
        sync.wait_ge(*c.tok_bcd)
        sync.wait_ge(c.sT.h, t0)
        c.tok_b = c.chC.issue(sync, c.rows[0:1, 0:2048],
                              c.bc_dram.ap()[0:16, sub * 128:(sub + 1) * 128])
        c.tok_cc2 = c.chC.issue(sync, c.rows[32:33, 0:2048],
                                c.bc_dram.ap()[16:32, sub * 128:(sub + 1) * 128])

    @block.tensor
    def _(tensor):
        tensor.wait_ge(*c.tok_b)
        tensor.wait_ge(*c.tok_cc2)
        tensor.wait_ge(c.sV.h, v0)
        for j in range(4):
            c.sT.inc(tensor.matmul(c.PB[j][:, :], c.ones_f32[0:1, :],
                                   c.rows[0:1, j * 512:(j + 1) * 512],
                                   start=True, stop=True))
        for j in range(4):
            c.sT.inc(tensor.matmul(c.PB[4 + j][:, :], c.ones_f32[32:33, :],
                                   c.rows[32:33, j * 512:(j + 1) * 512],
                                   start=True, stop=True))

    t_bc2 = c.sT.n

    for dc in range(DC):
        _emit_ssm_dc(c, tc, sub, dc, X3C, DTC, UC, ALPHA, BETA, H, YB, YR,
                     ACO, DCO, STATE, v_xdt, t_bc2)


def _emit_ssm_dc(c, tc, sub, dc, X3C, DTC, UC, ALPHA, BETA, H, YB, YR,
                 ACO, DCO, STATE, v_xdt, t_bc2):
    block = c.block
    dsl = slice(dc * 512 + sub * 128, dc * 512 + sub * 128 + 128)
    a0, v0 = c.sA.n, c.sV.n

    @block.scalar
    def _(scalar):
        scalar.wait_ge(c.sV.h, v0)
        for n in range(N_SSM):
            c.sA.inc(scalar.activation(ALPHA[:, n * 128:(n + 1) * 128],
                                       _dtc_slice(c, dsl), AF.Exp,
                                       scale=ACO[:, dc * 16 + n:dc * 16 + n + 1],
                                       bias=0.0))

    a_alpha = c.sA.n

    @block.vector
    def _(vector):
        vector.wait_ge(c.sT.h, t_bc2)
        vector.wait_ge(c.sA.h, a_alpha)
        for j in range(4):
            c.sV.inc(vector.tensor_tensor(
                BETA[:, j * 512:(j + 1) * 512].rearrange("p (n l) -> p n l", l=128),
                _uc_slice(c, dsl).unsqueeze(1).broadcast_to([128, 4, 128]),
                c.PB[j][:, :].rearrange("p (n l) -> p n l", l=128),
                ALU.mult))
        vector.wait_ge(c.sV.h, c.sV.n)
        for n in range(N_SSM):
            c.sV.inc(vector.tensor_tensor_scan(
                H[:, n * 128:(n + 1) * 128],
                ALPHA[:, n * 128:(n + 1) * 128],
                BETA[:, n * 128:(n + 1) * 128],
                STATE[:, dc * 16 + n:dc * 16 + n + 1],
                ALU.mult, ALU.add))
        vector.wait_ge(c.sV.h, c.sV.n)
        c.sV.inc(vector.tensor_copy(
            STATE[:, dc * 16:(dc + 1) * 16],
            H.rearrange("p (n l) -> p n l", l=128)[:, :, 127]))
        vector.wait_ge(c.sV.h, c.sV.n)
        for j in range(4):
            c.sV.inc(vector.tensor_tensor(
                YB[:, j * 512:(j + 1) * 512],
                H[:, j * 512:(j + 1) * 512],
                c.PB[4 + j][:, :], ALU.mult))
        vector.wait_ge(c.sV.h, c.sV.n)
        c.sV.inc(vector.tensor_reduce(YR, YB.rearrange("p (n l) -> p l n", l=128),
                                      AX.X, ALU.add))
        vector.wait_ge(c.sV.h, c.sV.n)
        c.sV.inc(vector.scalar_tensor_tensor(
            YR, _x3_slice(c, dc, sub), DCO[:, dc:dc + 1], YR, ALU.mult, ALU.add))
        vector.wait_ge(c.sV.h, c.sV.n)
        dst = c.res[:, dc * T + tc * 512 + sub * 128: dc * T + tc * 512 + sub * 128 + 128]
        c.sV.inc(vector.tensor_add(dst, dst, YR))


def _dtc_slice(c, dsl):
    return c.ARB[:, 2 * 4112:2 * 8208].bitcast(F32)[:, dsl]


def _uc_slice(c, dsl):
    return c.ARB[:, 2 * 8208:2 * 12304].bitcast(F32)[:, dsl]


def _x3_slice(c, dc, sub):
    X3C = c.ARB[:, 0:2 * 4112].bitcast(F32)
    b = dc * 514 + 1 + sub * 128
    return X3C[:, b:b + 128]



QMAXA = 32767.0


def _emit_ar(c, k, srcs, dsts, v_pre):
    block = c.block
    n = len(srcs)
    stg = c.ARF[0:1, 1432:1432 + n]

    @block.gpsimd
    def _(gpsimd):
        gpsimd.wait_ge(c.sV.h, v_pre)
        for i, s in enumerate(srcs):
            c.sG.inc(gpsimd.tensor_reduce(c.ARF[0:1, 1432 + i:1433 + i], s,
                                          AX.C, ALU.max))

    g_red = c.sG.n

    @block.sync
    def _(sync):
        sync.wait_ge(c.sG.h, g_red)
        c.tok_ar = c.chC.issue(sync, c.cc_in[k].ap()[:, 0:n], stg)

    @block.gpsimd
    def _(gpsimd):
        gpsimd.wait_ge(*c.tok_ar)
        c.sG.inc(gpsimd.collective_compute(
            "AllReduce", ALU.max, replica_groups=[c.ar_group],
            ins=[c.cc_in[k].ap()[:, 0:n]], outs=[c.cc_out[k].ap()[:, 0:n]]))

    g_cc = c.sG.n
    toks = []

    @block.sync
    def _(sync):
        sync.wait_ge(c.sG.h, g_cc)
        for i in range(n):
            raw = c.ARF[:, 1435 + i:1436 + i]
            toks.append(c.chC.issue(
                sync, raw, c.cc_out[k].ap()[:, i:i + 1].partition_broadcast(128)))

    @block.vector
    def _(vector):
        for t in toks:
            vector.wait_ge(*t)
        for i, (s_col, r_col, qden) in enumerate(dsts):
            raw = c.ARF[:, 1435 + i:1436 + i]
            c.sV.inc(vector.tensor_scalar(s_col, raw, 1.0 / qden, 1e-12,
                                          ALU.mult, ALU.max))
            vector.wait_ge(c.sV.h, c.sV.n)
            c.sV.inc(vector.reciprocal(r_col, s_col))
            vector.wait_ge(c.sV.h, c.sV.n)


def emit_l1ffn(c):
    block = c.block
    Bv = lambda a, b: c.ARB[:, a:b].bitcast(F32)
    X4F = Bv(0, 32768)
    X4I = c.ARB[:, 0:32768].bitcast(I32)
    NMB2 = Bv(36864, 40960)
    RSB2 = Bv(40960, 45056)
    S = lambda i: c.ARF[:, 1024 + i:1025 + i]
    mcol = S(0)
    sx, rx = S(1), S(2)
    sg, rg = S(3), S(4)
    su, ru = S(5), S(6)
    s3, r3 = S(7), S(8)
    s4, r4 = S(9), S(10)
    MXC = c.ARF[:, 1040:1048]
    MXWG = c.ARF[:, 2048:2144]
    MXWU = c.ARF[:, 2144:2240]
    MXW3 = c.ARF[:, 2240:2336]
    MXW4 = c.ARF[:, 2336:2432]
    mg, mu = c.ARF[:, 2432:2433], c.ARF[:, 2433:2434]

    emit_ln_stats(c)
    a_st, v_st = c.sA.n, c.sV.n

    @block.tensor
    def _(tensor):
        tensor.wait_ge(c.sA.h, a_st)
        tensor.wait_ge(c.sV.h, v_st)
        for q in range(4):
            sl = slice(q * 512, (q + 1) * 512)
            c.sT.inc(tensor.matmul(c.PB[q][:, :], c.ones_f32[0:1, :],
                                   c.rows[0:1, sl], start=True, stop=True))
            c.sT.inc(tensor.matmul(c.PB[4 + q][:, :], c.ones_f32[32:33, :],
                                   c.rows[32:33, sl], start=True, stop=True))

    t_stb = c.sT.n

    @block.scalar
    def _(scalar):
        scalar.wait_ge(c.sT.h, t_stb)
        for q in range(4):
            sl = slice(q * 512, (q + 1) * 512)
            c.sA.inc(scalar.activation(NMB2[:, sl], c.PB[q][:, :], AF.Copy,
                                       bias=0.0, scale=1.0))
            c.sA.inc(scalar.activation(RSB2[:, sl], c.PB[4 + q][:, :], AF.Copy,
                                       bias=0.0, scale=1.0))

    a_stb = c.sA.n
    tmpn = Bv(32768, 36864)

    @block.vector
    def _(vector):
        vector.wait_ge(c.sA.h, a_stb)
        for dc in range(DC):
            xsl = slice(dc * T, (dc + 1) * T)
            c.sV.inc(vector.tensor_add(tmpn, c.res[:, xsl], NMB2))
            vector.wait_ge(c.sV.h, c.sV.n)
            c.sV.inc(vector.tensor_mul(X4F[:, xsl], tmpn, RSB2))
            vector.wait_ge(c.sV.h, c.sV.n)
            c.sV.inc(vector.tensor_reduce(MXC[:, dc:dc + 1], X4F[:, xsl],
                                          AX.X, ALU.max, apply_absolute_value=True))
        vector.wait_ge(c.sV.h, c.sV.n)
        c.sV.inc(vector.tensor_reduce(mcol, MXC, AX.X, ALU.max))

    _emit_ar(c, 0, [mcol], [(sx, rx, QMAXA)], c.sV.n)
    v_ar1 = c.sV.n

    @block.scalar
    def _(scalar):
        scalar.wait_ge(c.sV.h, v_ar1)
        for dc in range(DC):
            xsl = slice(dc * T, (dc + 1) * T)
            c.sA.inc(scalar.activation(X4I[:, xsl], X4F[:, xsl], AF.Copy,
                                       bias=0.0, scale=rx))
            scalar.wait_ge(c.sA.h, c.sA.n)
            c.sA.inc(scalar.activation(X4F[:, xsl], X4I[:, xsl], AF.Copy,
                                       bias=0.0, scale=sx))
            scalar.wait_ge(c.sA.h, c.sA.n)

    c.l1_a_xq = c.sA.n

    _emit_l1_wmm(c, c.w_gate, c.gate_dram, MXWG)
    _emit_l1_wmm(c, c.w_up, c.up_dram, MXWU)

    @block.vector
    def _(vector):
        vector.wait_ge(c.sV.h, c.sV.n)
        c.sV.inc(vector.tensor_reduce(mg, MXWG, AX.X, ALU.max))
        c.sV.inc(vector.tensor_reduce(mu, MXWU, AX.X, ALU.max))

    _emit_ar(c, 1, [mg, mu], [(sg, rg, QMAXA), (su, ru, QMAXA)], c.sV.n)

    _emit_l1_downprep(c)
    _emit_l1_sig(c, MXW3)

    @block.vector
    def _(vector):
        vector.wait_ge(c.sV.h, c.sV.n)
        c.sV.inc(vector.tensor_reduce(mg, MXW3[:, 0:16], AX.X, ALU.max))

    _emit_ar(c, 2, [mg], [(s3, r3, QMAXA)], c.sV.n)

    _emit_l1_prod(c, MXW4)

    @block.vector
    def _(vector):
        vector.wait_ge(c.sV.h, c.sV.n)
        c.sV.inc(vector.tensor_reduce(mg, MXW4[:, 0:16], AX.X, ALU.max))

    _emit_ar(c, 3, [mg], [(s4, r4, QMAXA)], c.sV.n)

    _emit_l1_down(c)


def _emit_l1_wmm(c, wsrc, dst_dram, mxw):
    block = c.block
    Bv = lambda a, b: c.ARB[:, a:b].bitcast(F32)
    WST = [Bv(49152, 51200), Bv(51200, 53248)]
    WSC = Bv(53248, 55296)
    WSCI = c.ARB[:, 53248:55296].bitcast(I32)
    WQ2 = Bv(32768, 34816)
    STA = [Bv(45056, 47104), Bv(47104, 49152)]
    EVS = [Bv(36864 + 1024 * i, 37888 + 1024 * i) for i in range(4)]
    MMB = [c.PB[0], c.PB[1], c.PB[6], c.PB[7]]
    BMX = c.ARF[:, 1048:1080]
    SBL = c.ARF[:, 1080:1112]
    RBL = c.ARF[:, 1112:1144]

    v_entry, t_entry = c.sV.n, c.sT.n
    spt_carry = getattr(c, "l1_spt_carry", None)
    nmm_carry = getattr(c, "l1_nmm_carry", None)

    ntr = {}
    nst = {}
    nmm = {}
    nvq = {}
    nev = {}
    nmx = {}
    spt = {}
    ldt = {}

    for fc in range(24):
        @block.sync
        def _(sync, fc=fc):
            if fc >= 2:
                sync.wait_ge(c.sV.h, nvq[fc - 2])
            elif v_entry:
                sync.wait_ge(c.sV.h, v_entry)
            ldt[fc] = c.chW.issue(sync, WST[fc % 2],
                                  wsrc[fc * 128:(fc + 1) * 128, :])

        @block.vector
        def _(vector, fc=fc):
            vector.wait_ge(*ldt[fc])
            if fc >= 1:
                vector.wait_ge(c.sT.h, ntr[fc - 1])
            else:
                vector.wait_ge(c.sT.h, t_entry)
            c.sV.inc(vector.tensor_reduce(
                BMX, WST[fc % 2].rearrange("p (b k) -> p b k", k=32),
                AX.X, ALU.max, apply_absolute_value=True))
            vector.wait_ge(c.sV.h, c.sV.n)
            c.sV.inc(vector.tensor_scalar(SBL, BMX, 1.0 / 7.0, 1e-12,
                                          ALU.mult, ALU.max))
            vector.wait_ge(c.sV.h, c.sV.n)
            c.sV.inc(vector.reciprocal(RBL, SBL))
            vector.wait_ge(c.sV.h, c.sV.n)
            c.sV.inc(vector.tensor_tensor(
                WSC.rearrange("p (b k) -> p b k", k=32),
                WST[fc % 2].rearrange("p (b k) -> p b k", k=32),
                RBL.unsqueeze(2).broadcast_to([128, 32, 32]), ALU.mult))

        v_wsc = c.sV.n

        @block.scalar
        def _(scalar, fc=fc, v_wsc=v_wsc):
            scalar.wait_ge(c.sV.h, v_wsc)
            c.sA.inc(scalar.activation(WSCI, WSC, AF.Copy, bias=0.0, scale=1.0))
            scalar.wait_ge(c.sA.h, c.sA.n)
            c.sA.inc(scalar.activation(WSC, WSCI, AF.Copy, bias=0.0, scale=1.0))

        a_wq = c.sA.n

        @block.vector
        def _(vector, fc=fc, a_wq=a_wq):
            vector.wait_ge(c.sA.h, a_wq)
            c.sV.inc(vector.tensor_tensor(
                WQ2.rearrange("p (b k) -> p b k", k=32),
                WSC.rearrange("p (b k) -> p b k", k=32),
                SBL.unsqueeze(2).broadcast_to([128, 32, 32]), ALU.mult))

        nvq[fc] = c.sV.n
        a_tr_base = c.sA.n

        @block.tensor
        def _(tensor, fc=fc, a_tr_base=a_tr_base):
            tensor.wait_ge(c.sV.h, nvq[fc])
            for dc in range(DC):
                pb = c.PB[2 + dc % 4]
                if fc >= 1 and dc < 4:
                    tensor.wait_ge(c.sA.h, nst[fc - 1] - 8 + dc + 4 + 1)
                if dc >= 4:
                    tensor.wait_ge(c.sA.h, a_tr_base + (dc - 4) + 1)
                c.sT.inc(tensor.transpose(pb[:, 0:128],
                                          WQ2[:, dc * 128:(dc + 1) * 128],
                                          c.ident_t[:]))

        ntr[fc] = c.sT.n
        t_tr0 = ntr[fc] - DC

        @block.scalar
        def _(scalar, fc=fc, t_tr0=t_tr0):
            for dc in range(DC):
                pb = c.PB[2 + dc % 4]
                scalar.wait_ge(c.sT.h, t_tr0 + dc + 1)
                if dc == 0:
                    if fc >= 2:
                        scalar.wait_ge(c.sT.h, nmm[(fc - 2, 3)])
                    elif nmm_carry:
                        scalar.wait_ge(c.sT.h, nmm_carry[fc])
                c.sA.inc(scalar.activation(STA[fc % 2][:, dc * 128:(dc + 1) * 128],
                                           pb[0:128, 0:128], AF.Copy,
                                           bias=0.0, scale=1.0))

        nst[fc] = c.sA.n

        for tq in range(4):
            g = fc * 4 + tq
            bank = MMB[g % 4]

            @block.tensor
            def _(tensor, fc=fc, tq=tq, g=g, bank=bank):
                tensor.wait_ge(c.sA.h, nst[fc])
                if fc == 0 and tq == 0:
                    tensor.wait_ge(c.sA.h, c.l1_a_xq)
                if g >= 4:
                    tensor.wait_ge(c.sA.h, nev[g - 4])
                    tensor.wait_ge(c.sV.h, nmx[g - 4])
                for dc in range(DC):
                    c.sT.inc(tensor.matmul(
                        bank[:, :], STA[fc % 2][:, dc * 128:(dc + 1) * 128],
                        Bv(0, 32768)[:, dc * T + tq * 512: dc * T + (tq + 1) * 512],
                        start=(dc == 0), stop=(dc == DC - 1)))

            nmm[(fc, tq)] = c.sT.n

            @block.scalar
            def _(scalar, fc=fc, tq=tq, g=g, bank=bank):
                scalar.wait_ge(c.sT.h, nmm[(fc, tq)])
                if g >= 4:
                    scalar.wait_ge(*spt[g - 4])
                elif spt_carry:
                    scalar.wait_ge(*spt_carry[g])
                c.sA.inc(scalar.activation(EVS[g % 4], bank[:, :], AF.Copy,
                                           bias=0.0, scale=1.0))

            nev[g] = c.sA.n

            @block.vector
            def _(vector, fc=fc, tq=tq, g=g, bank=bank):
                vector.wait_ge(c.sT.h, nmm[(fc, tq)])
                c.sV.inc(vector.tensor_reduce(mxw[:, g:g + 1], bank[:, :],
                                              AX.X, ALU.max,
                                              apply_absolute_value=True))

            nmx[g] = c.sV.n

            @block.sync
            def _(sync, g=g):
                sync.wait_ge(c.sA.h, nev[g])
                spt[g] = c.chO.issue(sync, dst_dram.ap()[g], EVS[g % 4])

    c.l1_spt_carry = (spt[92], spt[93], spt[94], spt[95])
    c.l1_nmm_carry = (nmm[(22, 3)], nmm[(23, 3)])


def _emit_l1_downprep(c):
    block = c.block
    Bv = lambda a, b: c.ARB[:, a:b].bitcast(F32)
    WSTD = [Bv(0, 6144), Bv(6144, 12288)]
    WSCD = Bv(12288, 18432)
    WSCDI = c.ARB[:, 12288:18432].bitcast(I32)
    WQ2D = Bv(18432, 24576)
    DSTT = [Bv(24576 + 256 * i, 24832 + 256 * i) for i in range(4)]
    BMX9 = c.ARF[:, 1144:1240]
    SBL9 = c.ARF[:, 1240:1336]
    RBL9 = c.ARF[:, 1336:1432]

    ldt = {}
    nvq = {}
    ntr = {}
    nev = {}
    wbt = {}
    aev = {}
    t0 = c.sT.n

    for ch in range(8):
        @block.sync
        def _(sync, ch=ch):
            if ch == 0:
                sync.wait_ge(c.sT.h, t0)
            if ch >= 2:
                sync.wait_ge(c.sV.h, nvq[ch - 2])
            ldt[ch] = c.chW.issue(sync, WSTD[ch % 2],
                                  c.w_down[ch * 128:(ch + 1) * 128, :])

        @block.vector
        def _(vector, ch=ch):
            vector.wait_ge(*ldt[ch])
            if ch >= 1:
                vector.wait_ge(c.sT.h, ntr[ch - 1])
            c.sV.inc(vector.tensor_reduce(
                BMX9, WSTD[ch % 2].rearrange("p (b k) -> p b k", k=32),
                AX.X, ALU.max, apply_absolute_value=True))
            vector.wait_ge(c.sV.h, c.sV.n)
            c.sV.inc(vector.tensor_scalar(SBL9, BMX9, 1.0 / 7.0, 1e-12,
                                          ALU.mult, ALU.max))
            vector.wait_ge(c.sV.h, c.sV.n)
            c.sV.inc(vector.reciprocal(RBL9, SBL9))
            vector.wait_ge(c.sV.h, c.sV.n)
            if ch >= 1:
                vector.wait_ge(c.sA.h, nev[ch - 1])
            c.sV.inc(vector.tensor_tensor(
                WSCD.rearrange("p (b k) -> p b k", k=32),
                WSTD[ch % 2].rearrange("p (b k) -> p b k", k=32),
                RBL9.unsqueeze(2).broadcast_to([128, 96, 32]), ALU.mult))

        v_wsc = c.sV.n

        @block.scalar
        def _(scalar, ch=ch, v_wsc=v_wsc):
            scalar.wait_ge(c.sV.h, v_wsc)
            c.sA.inc(scalar.activation(WSCDI, WSCD, AF.Copy, bias=0.0, scale=1.0))
            scalar.wait_ge(c.sA.h, c.sA.n)
            c.sA.inc(scalar.activation(WSCD, WSCDI, AF.Copy, bias=0.0, scale=1.0))

        a_wq = c.sA.n

        @block.vector
        def _(vector, ch=ch, a_wq=a_wq):
            vector.wait_ge(c.sA.h, a_wq)
            c.sV.inc(vector.tensor_tensor(
                WQ2D.rearrange("p (b k) -> p b k", k=32),
                WSCD.rearrange("p (b k) -> p b k", k=32),
                SBL9.unsqueeze(2).broadcast_to([128, 96, 32]), ALU.mult))

        nvq[ch] = c.sV.n

        for fc in range(24):
            i = ch * 24 + fc

            @block.tensor
            def _(tensor, ch=ch, fc=fc, i=i):
                if fc == 0:
                    tensor.wait_ge(c.sV.h, nvq[ch])
                if i >= 4:
                    tensor.wait_ge(c.sA.h, aev[i - 4])
                c.sT.inc(tensor.transpose(c.PB[2 + i % 4][:, 0:128],
                                          WQ2D[:, fc * 128:(fc + 1) * 128],
                                          c.ident_t[:]))

            t_tr = c.sT.n

            @block.scalar
            def _(scalar, ch=ch, fc=fc, i=i, t_tr=t_tr):
                scalar.wait_ge(c.sT.h, t_tr)
                if i >= 4:
                    scalar.wait_ge(*wbt[i - 4])
                c.sA.inc(scalar.activation(DSTT[i % 4],
                                           c.PB[2 + i % 4][0:128, 0:128],
                                           AF.Copy, bias=0.0, scale=1.0))
                aev[i] = c.sA.n
                scalar.wait_ge(c.sA.h, aev[i])
                wbt[i] = c.chQ.issue(
                    scalar, c.wdt_dram.ap()[fc, :, ch * 128:(ch + 1) * 128],
                    DSTT[i % 4])

        ntr[ch] = c.sT.n
        nev[ch] = c.sA.n


def _emit_l1_sig(c, MXW3):
    block = c.block
    Bv = lambda a, b: c.ARB[:, a:b].bitcast(F32)
    W = 6
    NBATCH = 96 // W
    GIN = [Bv(25600, 31744), Bv(31744, 37888)]
    GINI = [c.ARB[:, 25600:31744].bitcast(I32),
            c.ARB[:, 31744:37888].bitcast(I32)]
    GQF = [Bv(37888, 44032), Bv(44032, 50176)]
    sg, rg = c.ARF[:, 1027:1028], c.ARF[:, 1028:1029]

    v_ar2 = c.sV.n
    ldt = {}
    na = {}
    nv = {}
    spt = {}

    @block.sync
    def _(sync):
        c.chO.drain(sync)

    def tiles(b):
        return c.gate_dram.ap()[b * W:(b + 1) * W].transpose([1, 0, 2])

    for b in range(NBATCH):
        @block.sync
        def _(sync, b=b):
            if b >= 2:
                sync.wait_ge(*spt[b - 2])
            ldt[b] = c.chW.issue(sync, GIN[b % 2], tiles(b))

        @block.scalar
        def _(scalar, b=b, a_prev=c.sA.n):
            scalar.wait_ge(*ldt[b])
            scalar.wait_ge(c.sA.h, a_prev)
            if b == 0:
                scalar.wait_ge(c.sV.h, v_ar2)
            if b >= 2:
                scalar.wait_ge(c.sV.h, nv[b - 2])
            a1 = c.sA.n + 1
            c.sA.inc(scalar.activation(GINI[b % 2], GIN[b % 2], AF.Copy,
                                       bias=0.0, scale=rg))
            scalar.wait_ge(c.sA.h, c.sA.n)
            c.sA.inc(scalar.activation(GQF[b % 2], GINI[b % 2], AF.Copy,
                                       bias=0.0, scale=sg))
            scalar.wait_ge(c.sA.h, c.sA.n)
            c.sA.inc(scalar.activation(GIN[b % 2], GQF[b % 2], AF.Sigmoid,
                                       bias=0.0, scale=1.0))
            scalar.wait_ge(c.sA.h, c.sA.n)
            c.sA.inc(scalar.activation(GINI[b % 2], GIN[b % 2], AF.Copy,
                                       bias=0.0, scale=65536.0))
            scalar.wait_ge(c.sA.h, c.sA.n)
            c.sA.inc(scalar.activation(GIN[b % 2], GINI[b % 2], AF.Copy,
                                       bias=0.0, scale=1.0))
            na[b] = (a1, c.sA.n)

        @block.vector
        def _(vector, b=b):
            vector.wait_ge(c.sA.h, na[b][1])
            c.sV.inc(vector.tensor_scalar(GIN[b % 2], GIN[b % 2], 65535.0,
                                          1.0 / 65536.0, ALU.min, ALU.mult))
            vector.wait_ge(c.sV.h, c.sV.n)
            c.sV.inc(vector.tensor_mul(GIN[b % 2], GQF[b % 2], GIN[b % 2]))
            vector.wait_ge(c.sV.h, c.sV.n)
            c.sV.inc(vector.tensor_reduce(MXW3[:, b:b + 1], GIN[b % 2],
                                          AX.X, ALU.max, apply_absolute_value=True))
            nv[b] = c.sV.n

        @block.sync
        def _(sync, b=b):
            sync.wait_ge(c.sV.h, nv[b])
            spt[b] = c.chO.issue(sync, tiles(b), GIN[b % 2])


def _emit_l1_prod(c, MXW4):
    block = c.block
    Bv = lambda a, b: c.ARB[:, a:b].bitcast(F32)
    W = 6
    NBATCH = 96 // W
    GSI = [Bv(25600, 31744), Bv(31744, 37888)]
    GSII = [c.ARB[:, 25600:31744].bitcast(I32),
            c.ARB[:, 31744:37888].bitcast(I32)]
    UIN = [Bv(37888, 44032), Bv(44032, 50176)]
    UINI = [c.ARB[:, 37888:44032].bitcast(I32),
            c.ARB[:, 44032:50176].bitcast(I32)]
    su, ru = c.ARF[:, 1029:1030], c.ARF[:, 1030:1031]
    s3, r3 = c.ARF[:, 1031:1032], c.ARF[:, 1032:1033]

    v_ar3 = c.sV.n
    ldt = {}
    na = {}
    nv = {}
    spt = {}

    @block.sync
    def _(sync):
        c.chO.drain(sync)

    def gtiles(b):
        return c.gate_dram.ap()[b * W:(b + 1) * W].transpose([1, 0, 2])

    def utiles(b):
        return c.up_dram.ap()[b * W:(b + 1) * W].transpose([1, 0, 2])

    for b in range(NBATCH):
        @block.sync
        def _(sync, b=b):
            if b >= 2:
                sync.wait_ge(*spt[b - 2])
                sync.wait_ge(c.sV.h, nv[b - 2])
            ldt[b] = (c.chW.issue(sync, GSI[b % 2], gtiles(b)),
                      c.chL.issue(sync, UIN[b % 2], utiles(b)))

        @block.scalar
        def _(scalar, b=b, a_prev=c.sA.n):
            scalar.wait_ge(*ldt[b][0])
            scalar.wait_ge(*ldt[b][1])
            scalar.wait_ge(c.sA.h, a_prev)
            if b == 0:
                scalar.wait_ge(c.sV.h, v_ar3)
            c.sA.inc(scalar.activation(GSII[b % 2], GSI[b % 2], AF.Copy,
                                       bias=0.0, scale=r3))
            scalar.wait_ge(c.sA.h, c.sA.n)
            c.sA.inc(scalar.activation(GSI[b % 2], GSII[b % 2], AF.Copy,
                                       bias=0.0, scale=s3))
            scalar.wait_ge(c.sA.h, c.sA.n)
            c.sA.inc(scalar.activation(UINI[b % 2], UIN[b % 2], AF.Copy,
                                       bias=0.0, scale=ru))
            scalar.wait_ge(c.sA.h, c.sA.n)
            c.sA.inc(scalar.activation(UIN[b % 2], UINI[b % 2], AF.Copy,
                                       bias=0.0, scale=su))
            na[b] = c.sA.n

        @block.vector
        def _(vector, b=b):
            vector.wait_ge(c.sA.h, na[b])
            c.sV.inc(vector.tensor_mul(GSI[b % 2], GSI[b % 2], UIN[b % 2]))
            vector.wait_ge(c.sV.h, c.sV.n)
            c.sV.inc(vector.tensor_reduce(MXW4[:, b:b + 1], GSI[b % 2],
                                          AX.X, ALU.max, apply_absolute_value=True))
            nv[b] = c.sV.n

        @block.sync
        def _(sync, b=b):
            sync.wait_ge(c.sV.h, nv[b])
            spt[b] = c.chO.issue(sync, utiles(b), GSI[b % 2])


def _emit_l1_down(c):
    block = c.block
    Bv = lambda a, b: c.ARB[:, a:b].bitcast(F32)
    OF = Bv(0, 24576)
    PIN = [Bv(24576 + 1024 * i, 25600 + 1024 * i) for i in range(4)]
    OI = c.ARB[:, 28672:29696].bitcast(I32)
    STD = [Bv(45056, 47104), Bv(47104, 49152)]
    s4, r4 = c.ARF[:, 1033:1034], c.ARF[:, 1034:1035]

    v_ar4 = c.sV.n

    @block.sync
    def _(sync):
        c.chO.drain(sync)
        for sm in c.chQ.sems:
            sync.wait_ge(sm.h, sm.n)

    t_dn_prev = c.sT.n
    a_prev_end = c.sA.n

    for tq in range(4):
        ldt = {}
        stt = {}
        na = {}
        nmm = {}
        v_res_prev = c.sV.n

        for fc in range(24):
            @block.sync
            def _(sync, fc=fc, tq=tq, t_dn_prev=t_dn_prev, a_prev_end=a_prev_end):
                if fc >= 4:
                    sync.wait_ge(c.sA.h, na[fc - 4])
                elif tq > 0:
                    sync.wait_ge(c.sA.h, a_prev_end)
                ldt[fc] = c.chW.issue(sync, PIN[fc % 4],
                                      c.up_dram.ap()[fc * 4 + tq])
                if fc >= 2:
                    sync.wait_ge(c.sT.h, nmm[fc - 2])
                elif tq > 0:
                    sync.wait_ge(c.sT.h, t_dn_prev)
                stt[fc] = c.chL.issue(sync, STD[fc % 2], c.wdt_dram.ap()[fc])

            @block.scalar
            def _(scalar, fc=fc, tq=tq, t_dn_prev=t_dn_prev, a_prev=c.sA.n):
                scalar.wait_ge(*ldt[fc])
                scalar.wait_ge(c.sA.h, a_prev)
                if fc == 0 and tq == 0:
                    scalar.wait_ge(c.sV.h, v_ar4)
                if fc == 0 and tq > 0:
                    scalar.wait_ge(c.sT.h, t_dn_prev)
                c.sA.inc(scalar.activation(OI, PIN[fc % 4], AF.Copy,
                                           bias=0.0, scale=r4))
                scalar.wait_ge(c.sA.h, c.sA.n)
                c.sA.inc(scalar.activation(OF[:, fc * 512:(fc + 1) * 512], OI,
                                           AF.Copy, bias=0.0, scale=s4))
                na[fc] = c.sA.n

            @block.tensor
            def _(tensor, fc=fc, tq=tq, v_res_prev=v_res_prev):
                tensor.wait_ge(*stt[fc])
                tensor.wait_ge(c.sA.h, na[fc])
                if fc == 0:
                    tensor.wait_ge(c.sV.h, v_res_prev)
                for dc in range(DC):
                    c.sT.inc(tensor.matmul(
                        c.PB[dc][:, :], STD[fc % 2][:, dc * 128:(dc + 1) * 128],
                        OF[:, fc * 512:(fc + 1) * 512],
                        start=(fc == 0), stop=(fc == 23)))
                nmm[fc] = c.sT.n

        t_dn = c.sT.n

        @block.vector
        def _(vector, tq=tq, t_dn=t_dn):
            vector.wait_ge(c.sT.h, t_dn)
            for dc in range(DC):
                dst = c.res[:, dc * T + tq * 512: dc * T + (tq + 1) * 512]
                c.sV.inc(vector.tensor_add(dst, dst, c.PB[dc][:, :]))

        t_dn_prev = t_dn
        a_prev_end = c.sA.n


def emit_ph6_store(c):
    block = c.block
    t0, a0 = c.sT.n, c.sA.n
    tok = {}

    @block.tensor
    def _(tensor):
        _wait_all(tensor, c)
        for tt in range(TT):
            for dc in range(DC):
                pb = c.PB[dc]
                if tt >= 1:
                    tensor.wait_ge(c.sA.h, a0 + (tt - 1) * DC + dc + 1)
                c.sT.inc(tensor.transpose(
                    pb[:, 0:128],
                    c.res[:, dc * T + tt * 128: dc * T + tt * 128 + 128],
                    c.ident_t[:]))

    @block.sync
    def _(sync):
        for tt in range(TT):
            buf = c.stage[:, (tt % 2) * D:(tt % 2) * D + D]
            sync.wait_ge(c.sA.h, a0 + (tt + 1) * DC)
            tok[tt] = c.chO.issue(sync, c.out[tt * 128:(tt + 1) * 128, :], buf)
        c.chO.drain(sync)

    @block.scalar
    def _(scalar):
        for tt in range(TT):
            buf = c.stage[:, (tt % 2) * D:(tt % 2) * D + D]
            if tt >= 2:
                scalar.wait_ge(*tok[tt - 2])
            for dc in range(DC):
                scalar.wait_ge(c.sT.h, t0 + tt * DC + dc + 1)
                c.sA.inc(scalar.activation(
                    buf[:, dc * 128:(dc + 1) * 128],
                    c.PB[dc][:, 0:128], AF.Copy, bias=0.0, scale=1.0))


def build(stop_after=None, ar_group=None):
    nc = bass.Bass("TRN2", target_bir_lowering=False, dynamic_dma_scratch_size=2048)
    c = Ctx()
    c.nc = nc

    c.x_in = nc.dram_tensor("x_in", [T, D], F32, kind="ExternalInput").ap()
    c.out = nc.dram_tensor("out", [T, D], F32, kind="ExternalOutput").ap()
    c.ident = nc.dram_tensor("ident", [128, 128], F32, kind="ExternalInput").ap()
    c.ones_in = nc.dram_tensor("ones_in", [128, 128], F32, kind="ExternalInput").ap()
    c.ropeA_in = nc.dram_tensor("ropeA_in", [128, T], F32, kind="ExternalInput").ap()
    c.ropeB_in = nc.dram_tensor("ropeB_in", [128, T], F32, kind="ExternalInput").ap()
    c.masks_in = nc.dram_tensor("masks_in", [128, 896], BF16, kind="ExternalInput").ap()
    c.bias_in = nc.dram_tensor("bias_in", [128, 32], F32, kind="ExternalInput").ap()
    c.w_q = nc.dram_tensor("w_q", [D, D], F32, kind="ExternalInput").ap()
    c.w_k = nc.dram_tensor("w_k", [D, D], F32, kind="ExternalInput").ap()
    c.w_v = nc.dram_tensor("w_v", [D, D], F32, kind="ExternalInput").ap()
    c.w_o = nc.dram_tensor("w_o", [D, D], F32, kind="ExternalInput").ap()
    c.w1 = nc.dram_tensor("w1", [FF0, D], F32, kind="ExternalInput").ap()
    c.w2 = nc.dram_tensor("w2", [D, FF0], F32, kind="ExternalInput").ap()
    c.bias_ffn_in = nc.dram_tensor("bias_ffn_in", [128, 40], F32, kind="ExternalInput").ap()
    c.scal_dram = nc.dram_tensor("scal_dram", [1, 8], F32)
    c.wq1t_dram = nc.dram_tensor("wq1t_dram", [32, 1024, 128], BF16)
    c.wq2t_dram = nc.dram_tensor("wq2t_dram", [32, 128, 1024], BF16)
    c.bc_dram = nc.dram_tensor("bc_dram", [32, 512], F32)
    c.conv_k = nc.dram_tensor("conv_k", [3, 1, D], F32, kind="ExternalInput").ap()
    c.a_log = nc.dram_tensor("a_log", [D, N_SSM], F32, kind="ExternalInput").ap()
    c.w_bc = nc.dram_tensor("w_bc", [D, 2 * N_SSM], F32, kind="ExternalInput").ap()
    c.b_bc = nc.dram_tensor("b_bc", [2 * N_SSM], F32, kind="ExternalInput").ap()
    c.dcoeff = nc.dram_tensor("dcoeff", [D], F32, kind="ExternalInput").ap()
    c.w_dt = nc.dram_tensor("w_dt", [D, DT_RANK], F32, kind="ExternalInput").ap()
    c.b_dt = nc.dram_tensor("b_dt", [DT_RANK], F32, kind="ExternalInput").ap()
    c.w_dtp = nc.dram_tensor("w_dtp", [DT_RANK, D], F32, kind="ExternalInput").ap()
    c.b_dtp = nc.dram_tensor("b_dtp", [D], F32, kind="ExternalInput").ap()
    c.w_gate = nc.dram_tensor("w_gate", [FF1, D], F32, kind="ExternalInput").ap()
    c.w_up = nc.dram_tensor("w_up", [FF1, D], F32, kind="ExternalInput").ap()
    c.w_down = nc.dram_tensor("w_down", [D, FF1], F32, kind="ExternalInput").ap()
    c.gate_dram = nc.dram_tensor("gate_dram", [96, 128, 512], F32)
    c.up_dram = nc.dram_tensor("up_dram", [96, 128, 512], F32)
    c.wdt_dram = nc.dram_tensor("wdt_dram", [24, 128, 1024], F32)
    c.cc_in = [nc.dram_tensor(f"cc_in{i}", [1, 2], F32) for i in range(4)]
    c.cc_out = [nc.dram_tensor(f"cc_out{i}", [1, 2], F32) for i in range(4)]
    c.ar_group = list(range(8)) if ar_group is None else ar_group

    ctx = ExitStack()
    with ctx:
        e = ctx.enter_context
        c.res = e(nc.sbuf_tensor("res", [128, DC * T], F32))
        c.rows = e(nc.sbuf_tensor("rows", [128, T], F32))
        c.ropeA = e(nc.sbuf_tensor("ropeA", [128, T], F32))
        c.ropeB = e(nc.sbuf_tensor("ropeB", [128, T], F32))
        c.ident_t = e(nc.sbuf_tensor("ident_t", [128, 128], F32))
        c.ones_f32 = e(nc.sbuf_tensor("ones_f32", [128, 128], F32))
        c.ones_bf = e(nc.sbuf_tensor("ones_bf", [128, 64], BF16))
        c.bias_qkv = e(nc.sbuf_tensor("bias_qkv", [128, 32], F32))
        c.bias_ffn = e(nc.sbuf_tensor("bias_ffn", [128, 40], F32))
        c.nrm = e(nc.sbuf_tensor("nrm", [1, 16], F32))
        c.ARF = e(nc.sbuf_tensor("ARF", [128, 5136], F32))
        c.ARB = e(nc.sbuf_tensor("ARB", [128, 57216], BF16))

        c.stage = c.ARF[:, 0:2048]
        c.sqbuf = c.ARF[:, 0:4096]
        c.tmp1 = c.ARF[:, 4096:4608]
        c.xfull = c.ARF[:, 4608:5120]
        c.AVS = c.ARF[:, 3072:5120]
        c.biasA = c.ARF[:, 5120:5121]
        c.biasB = c.ARF[:, 5121:5122]
        c.qtmp = c.ARF[:, 2048:2176]
        c.bbc_col = c.ropeA[64:96, 169:170]
        c.acol = c.ARF[:, 5126:5127]
        c.gcol = c.ARF[:, 5127:5128]

        def B(a, b):
            return c.ARB[:, a:b]
        c.XH = B(0, 16384)
        c.XL = B(16384, 32768)
        c.QH = B(32768, 34816)
        c.QL = B(34816, 36864)
        c.KH = B(36864, 38912)
        c.KL = B(38912, 40960)
        c.VSA = B(40960, 43008)
        c.VSB = B(43008, 45056)
        c.WTQH = B(45056, 46080)
        c.WTQL = B(46080, 47104)
        c.WTKH = B(47104, 48128)
        c.WTKL = B(48128, 49152)
        c.WTVH = B(49152, 50176)
        c.WTVL = B(50176, 51200)
        c.PHt = [B(51200, 52224), B(52224, 53248)]
        c.PLt = [B(53248, 54272), B(54272, 55296)]
        c.OH = B(55296, 55808)
        c.OL = B(55808, 56320)
        c.MASKS = B(56320, 57216)
        c.sbh = B(32768, 36864)
        c.sbl = B(36864, 40960)
        c.sqh = B(40960, 45056)
        c.sql = B(45056, 49152)

        c.PB = [e(nc.psum_tensor(f"pb{i}", [128, 512], F32)) for i in range(8)]

        c.sS = Sem(e(nc.semaphore("sS")))
        c.sG = Sem(e(nc.semaphore("sG")))
        c.sT = Sem(e(nc.semaphore("sT")))
        c.sV = Sem(e(nc.semaphore("sV")))
        c.sA = Sem(e(nc.semaphore("sA")))

        c.block = e(nc.Block())

        c.chL = DmaChan(nc, ctx, "chL", 4)
        c.chO = DmaChan(nc, ctx, "chO", 4)
        c.chW = DmaChan(nc, ctx, "chW", 4)
        c.chC = DmaChan(nc, ctx, "chC", 2)
        c.chQ = DmaChan(nc, ctx, "chQ", 4)
        c.wtok = {}

        @c.block.sync
        def _(sync):
            c.tok_ident = c.chL.issue(sync, c.ident_t[:], c.ident)
            c.tok_ones = c.chL.issue(sync, c.ones_f32[:], c.ones_in)
            c.tok_ropeA = c.chL.issue(sync, c.ropeA[:], c.ropeA_in)
            c.tok_ropeB = c.chL.issue(sync, c.ropeB[:], c.ropeB_in)
            c.tok_masks = c.chL.issue(sync, c.MASKS, c.masks_in)
            c.tok_bias_in = c.chL.issue(sync, c.bias_qkv[:], c.bias_in)
            c.tok_bias_ffn = c.chL.issue(sync, c.bias_ffn[:], c.bias_ffn_in)

        @c.block.vector
        def _(vector):
            vector.wait_ge(*c.tok_ones)
            c.sV.inc(vector.tensor_copy(c.ones_bf[:, 0:64], c.ones_f32[:, 0:64]))

        emit_ph0_load(c)

        @c.block.tensor
        def _(tensor):
            tensor.wait_ge(*c.tok_ones)
            tensor.wait_ge(*c.tok_ropeA)
            tensor.wait_ge(*c.tok_ropeB)
            tensor.wait_ge(*c.tok_masks)
            tensor.wait_ge(*c.tok_bias_in)
            tensor.wait_ge(c.sV.h, c.sV.n)

        if stop_after != "ph0":
            emit_ln(c, c.XH, c.XL)
            if stop_after != "ln1":
                emit_l0_attn(c)
                if stop_after != "attn":
                    emit_ffn0(c)
                    if stop_after != "ffn0":
                        emit_ssm(c)
                        if stop_after != "ssm":
                            emit_l1ffn(c)

        emit_ph6_store(c)

    return nc


def _host_consts():
    ident = np.eye(128, dtype=np.float32)
    ones = np.ones((128, 128), dtype=np.float32)
    half = DK // 2
    freq = (1.0 / 10000.0 ** (2.0 * np.arange(half, dtype=np.float64) / DK))
    ang = np.arange(T, dtype=np.float64)[:, None] * freq[None, :]
    cos = np.cos(ang).astype(np.float32)
    sin = np.sin(ang).astype(np.float32)
    ropeA = np.zeros((128, T), dtype=np.float32)
    ropeB = np.zeros((128, T), dtype=np.float32)
    for j in range(64):
        i = j // 2
        ropeA[j] = cos[:, i]
        ropeB[j] = (np.float32(-1.0) if j % 2 == 0 else np.float32(1.0)) * sin[:, i]
    ropeA[64:128] = ropeA[0:64]
    ropeB[64:128] = ropeB[0:64]
    import ml_dtypes
    k_rel = np.arange(128)[:, None]
    u = np.arange(896)[None, :]
    masks = np.where(k_rel > u - 384, -1e30, 0.0).astype(ml_dtypes.bfloat16)
    return ident, ones, ropeA, ropeB, masks


_PERM = None

def _perm_idx():
    global _PERM
    if _PERM is None:
        j = np.arange(128)
        h = j // 64
        jj = j % 64
        _PERM = h * 64 + (jj % 2) * 32 + (jj // 2)
    return _PERM


def _bias_tile(inputs):
    bias = np.zeros((128, 32), dtype=np.float32)
    perm = _perm_idx()
    bq = np.asarray(inputs["L0_attn_WQ_b"], np.float32)
    bk = np.asarray(inputs["L0_attn_WK_b"], np.float32)
    bv = np.asarray(inputs["L0_attn_WV_b"], np.float32)
    bo = np.asarray(inputs["L0_attn_Wo_b"], np.float32)
    for p in range(8):
        bias[:, p] = bq[p * 128 + perm]
        bias[:, 8 + p] = bk[p * 128 + perm]
        bias[:, 16 + p] = bv[p * 128:(p + 1) * 128]
        bias[:, 24 + p] = bo[p * 128:(p + 1) * 128]
    return bias


_NC_CACHE = None


def _get_nc():
    global _NC_CACHE
    if _NC_CACHE is None:
        _NC_CACHE = build()
    return _NC_CACHE


def kernel(**inputs):
    nc = _get_nc()
    x = np.ascontiguousarray(np.asarray(inputs["L0_attn_x"], dtype=np.float32))
    ident, ones, ropeA, ropeB, masks = _host_consts()
    bias = _bias_tile(inputs)
    wq = np.ascontiguousarray(np.asarray(inputs["L0_attn_WQ_w"], np.float32))
    wk = np.ascontiguousarray(np.asarray(inputs["L0_attn_WK_w"], np.float32))
    wv = np.ascontiguousarray(np.asarray(inputs["L0_attn_WV_w"], np.float32))
    wo = np.ascontiguousarray(np.asarray(inputs["L0_attn_Wo_w"], np.float32))
    w1 = np.ascontiguousarray(np.asarray(inputs["L0_ffn_W1"], np.float32))
    w2 = np.ascontiguousarray(np.asarray(inputs["L0_ffn_W2"], np.float32))
    bias_ffn = np.zeros((128, 40), dtype=np.float32)
    b1 = np.asarray(inputs["L0_ffn_b1"], np.float32)
    b2 = np.asarray(inputs["L0_ffn_b2"], np.float32)
    for fc in range(32):
        bias_ffn[:, fc] = b1[fc * 128:(fc + 1) * 128]
    for dco in range(8):
        bias_ffn[:, 32 + dco] = b2[dco * 128:(dco + 1) * 128]
    in_maps = []
    for b in range(8):
        m = {"x_in": np.ascontiguousarray(x[b]), "ident": ident, "ones_in": ones,
             "ropeA_in": ropeA, "ropeB_in": ropeB, "masks_in": masks,
             "bias_in": bias, "w_q": wq, "w_k": wk, "w_v": wv, "w_o": wo,
             "w1": w1, "w2": w2, "bias_ffn_in": bias_ffn,
             "conv_k": np.ascontiguousarray(np.asarray(inputs["L1_attn_conv_kernel"], np.float32)),
             "a_log": np.ascontiguousarray(np.asarray(inputs["L1_attn_A_log"], np.float32)),
             "w_bc": np.ascontiguousarray(np.asarray(inputs["L1_attn_W_bc"], np.float32)),
             "b_bc": np.ascontiguousarray(np.asarray(inputs["L1_attn_b_bc"], np.float32)),
             "dcoeff": np.ascontiguousarray(np.asarray(inputs["L1_attn_Dcoeff"], np.float32)),
             "w_dt": np.ascontiguousarray(np.asarray(inputs["L1_attn_W_dt"], np.float32)),
             "b_dt": np.ascontiguousarray(np.asarray(inputs["L1_attn_b_dt"], np.float32)),
             "w_dtp": np.ascontiguousarray(np.asarray(inputs["L1_attn_W_dtp"], np.float32)),
             "b_dtp": np.ascontiguousarray(np.asarray(inputs["L1_attn_b_dtp"], np.float32)),
             "w_gate": np.ascontiguousarray(np.asarray(inputs["L1_ffn_w_gate"], np.float32)),
             "w_up": np.ascontiguousarray(np.asarray(inputs["L1_ffn_w_up"], np.float32)),
             "w_down": np.ascontiguousarray(np.asarray(inputs["L1_ffn_w_down"], np.float32))}
        in_maps.append(m)
    res = run_bass_kernel_spmd(nc, in_maps, core_ids=list(range(8)))
    out = np.stack([np.asarray(res.results[b]["out"], dtype=np.float32) for b in range(8)], axis=0)
    return out


# version 05
# speedup vs baseline: 1.0978x; 1.0396x over previous
import numpy as np
from contextlib import ExitStack

import concourse.bass as bass
import concourse.mybir as mybir
from concourse.bass_utils import run_bass_kernel_spmd

F32 = mybir.dt.float32
BF16 = mybir.dt.bfloat16
I32 = mybir.dt.int32
AF = mybir.ActivationFunctionType
ALU = mybir.AluOpType
AX = mybir.AxisListType

D = 1024
T = 2048
NH = 16
DK = 64
DC = D // 128
TT = T // 128
FF0 = 4096
FF1 = 3072
N_SSM = 16
DT_RANK = 64
EPS_LN = 1e-5


class Sem:
    __slots__ = ("h", "n")
    def __init__(self, h):
        self.h = h
        self.n = 0
    def inc(self, inst, k=1):
        inst.then_inc(self.h, k)
        self.n += k
        return inst


class DmaChan:
    def __init__(self, nc, ctx, name, depth):
        self.sems = [Sem(ctx.enter_context(nc.semaphore(f"{name}{i}")))
                     for i in range(depth)]
        self.i = 0

    def issue(self, eng, out, in_):
        s = self.sems[self.i % len(self.sems)]
        self.i += 1
        eng.wait_ge(s.h, s.n)
        s.inc(eng.dma_start(out=out, in_=in_), 16)
        return (s.h, s.n)

    def drain(self, eng):
        for s in self.sems:
            eng.wait_ge(s.h, s.n)

    def snapshot(self):
        return (self.i, [s.n for s in self.sems])

    def plan(self, snap, j):
        i0, vals = snap
        k = len(self.sems)
        idx = (i0 + j) % k
        uses = j // k + 1
        return (self.sems[idx].h, vals[idx] + 16 * uses)


class Ctx:
    pass


def _wait_all(eng, c):
    for s in (c.sV, c.sA, c.sT, c.sG, c.sS):
        eng.wait_ge(s.h, s.n)


def emit_ph0_load(c):
    block = c.block
    t0, a0 = c.sT.n, c.sA.n
    tok = {}

    @block.sync
    def _(sync):
        for tt in range(TT):
            buf = c.stage[:, (tt % 2) * D:(tt % 2) * D + D]
            if tt >= 2:
                sync.wait_ge(c.sT.h, t0 + (tt - 1) * DC)
            tok[tt] = c.chL.issue(sync, buf, c.x_in[tt * 128:(tt + 1) * 128, :])

    @block.tensor
    def _(tensor):
        for tt in range(TT):
            buf = c.stage[:, (tt % 2) * D:(tt % 2) * D + D]
            tensor.wait_ge(*tok[tt])
            if tt == 0:
                tensor.wait_ge(*c.tok_ident)
            for dc in range(DC):
                pb = c.PB[dc]
                if tt >= 1:
                    tensor.wait_ge(c.sA.h, a0 + (tt - 1) * DC + dc + 1)
                c.sT.inc(tensor.transpose(pb[:, 0:128], buf[:, dc * 128:(dc + 1) * 128],
                                          c.ident_t[:]))

    @block.scalar
    def _(scalar):
        for tt in range(TT):
            for dc in range(DC):
                scalar.wait_ge(c.sT.h, t0 + tt * DC + dc + 1)
                c.sA.inc(scalar.activation(
                    c.res[:, dc * T + tt * 128: dc * T + tt * 128 + 128],
                    c.PB[dc][:, 0:128], AF.Copy, bias=0.0, scale=1.0))


def emit_ln(c, xh, xl, extra_full=None):
    block = c.block
    t0, a0, v0 = c.sT.n, c.sA.n, c.sV.n
    g0, s0 = c.sG.n, c.sS.n

    @block.scalar
    def _(scalar):
        _wait_all(scalar, c)
        for dc in range(DC):
            h = (dc % 2) * T
            if dc >= 2:
                scalar.wait_ge(c.sT.h, t0 + (dc - 1) * 16)
            r = c.res[:, dc * T:(dc + 1) * T]
            c.sA.inc(scalar.activation(c.sqbuf[:, h:h + T], r, AF.Square, bias=0.0, scale=1.0))
            c.sA.inc(scalar.activation(c.sbh[:, h:h + T], r, AF.Copy, bias=0.0, scale=1.0))
            scalar.wait_ge(c.sA.h, c.sA.n - 1)
            c.sA.inc(scalar.activation(c.sqh[:, h:h + T], c.sqbuf[:, h:h + T],
                                       AF.Copy, bias=0.0, scale=1.0))

    @block.vector
    def _(vector):
        for dc in range(DC):
            h = (dc % 2) * T
            vector.wait_ge(c.sA.h, a0 + dc * 3 + 3)
            r = c.res[:, dc * T:(dc + 1) * T]
            c.sV.inc(vector.tensor_sub(c.sbl[:, h:h + T], r, c.sbh[:, h:h + T]))
            c.sV.inc(vector.tensor_sub(c.sql[:, h:h + T], c.sqbuf[:, h:h + T],
                                       c.sqh[:, h:h + T]))

    @block.tensor
    def _(tensor):
        tensor.wait_ge(c.sV.h, v0)
        tensor.wait_ge(c.sA.h, a0)
        tensor.wait_ge(c.sG.h, g0)
        for dc in range(DC):
            h = (dc % 2) * T
            tensor.wait_ge(c.sV.h, v0 + dc * 2 + 2)
            st = (dc == 0)
            sp = (dc == DC - 1)
            for q in range(4):
                sl = slice(h + q * 512, h + (q + 1) * 512)
                c.sT.inc(tensor.matmul(c.PB[q][0:64, :], c.ones_bf[:, 0:64],
                                       c.sbh[:, sl], start=st, stop=False))
                c.sT.inc(tensor.matmul(c.PB[q][0:64, :], c.ones_bf[:, 0:64],
                                       c.sbl[:, sl], start=False, stop=sp))
                c.sT.inc(tensor.matmul(c.PB[4 + q][0:64, :], c.ones_bf[:, 0:64],
                                       c.sqh[:, sl], start=st, stop=False))
                c.sT.inc(tensor.matmul(c.PB[4 + q][0:64, :], c.ones_bf[:, 0:64],
                                       c.sql[:, sl], start=False, stop=sp))

    t1 = c.sT.n

    @block.vector
    def _(vector):
        vector.wait_ge(c.sT.h, t1)
        for q in range(4):
            sl = slice(q * 512, (q + 1) * 512)
            c.sV.inc(vector.tensor_scalar_mul(c.rows[0:1, sl], c.PB[q][0:1, :], -1.0 / D))
            c.sV.inc(vector.tensor_scalar_mul(c.rows[32:33, sl], c.PB[4 + q][0:1, :], 1.0 / D))
            vector.wait_ge(c.sV.h, c.sV.n)
            c.sV.inc(vector.scalar_tensor_tensor(
                c.PB[4 + q][32:33, :], c.rows[0:1, sl], -1.0, c.rows[0:1, sl],
                ALU.mult, ALU.mult))
            vector.wait_ge(c.sV.h, c.sV.n)
            c.sV.inc(vector.tensor_add(c.rows[64:65, sl], c.rows[32:33, sl],
                                       c.PB[4 + q][32:33, :]))
        vector.wait_ge(c.sV.h, c.sV.n)
        c.sV.inc(vector.tensor_scalar_add(c.rows[64:65, 0:T], c.rows[64:65, 0:T], EPS_LN))

    a1 = c.sA.n
    @block.scalar
    def _(scalar):
        scalar.wait_ge(c.sV.h, c.sV.n)
        c.sA.inc(scalar.activation(c.rows[96:97, 0:T], c.rows[64:65, 0:T], AF.Sqrt,
                                   bias=0.0, scale=1.0))

    v1 = c.sV.n
    @block.vector
    def _(vector):
        vector.wait_ge(c.sA.h, a1 + 1)
        c.sV.inc(vector.reciprocal(c.rows[32:33, 0:T], c.rows[96:97, 0:T]))

    t2, a2, v2 = c.sT.n, c.sA.n, c.sV.n

    @block.tensor
    def _(tensor):
        tensor.wait_ge(c.sV.h, v1 + 1)
        for q in range(4):
            sl = slice(q * 512, (q + 1) * 512)
            pm = c.PB[q % 2]
            pr = c.PB[2 + q % 2]
            if q >= 2:
                tensor.wait_ge(c.sV.h, v2 + (q - 1) * DC * 3)
            c.sT.inc(tensor.matmul(pm[:, :], c.ones_f32[0:1, :], c.rows[0:1, sl],
                                   start=True, stop=True))
            c.sT.inc(tensor.matmul(pr[:, :], c.ones_f32[32:33, :], c.rows[32:33, sl],
                                   start=True, stop=True))

    @block.vector
    def _(vector):
        for q in range(4):
            pm = c.PB[q % 2]
            pr = c.PB[2 + q % 2]
            vector.wait_ge(c.sT.h, t2 + 2 * (q + 1))
            for dc in range(DC):
                src = slice(dc * T + q * 512, dc * T + (q + 1) * 512)
                dst_full = c.xfull[:, 0:512] if extra_full is None else extra_full[:, src]
                c.sV.inc(vector.tensor_add(c.tmp1[:, 0:512], c.res[:, src], pm[:, :]))
                vector.wait_ge(c.sV.h, c.sV.n)
                c.sV.inc(vector.tensor_mul(dst_full, c.tmp1[:, 0:512], pr[:, :]))
                vector.wait_ge(c.sV.h, c.sV.n)
                vector.wait_ge(c.sA.h, a2 + q * DC + dc + 1)
                c.sV.inc(vector.tensor_sub(xl[:, src], dst_full, xh[:, src]))

    @block.scalar
    def _(scalar):
        for q in range(4):
            for dc in range(DC):
                src = slice(dc * T + q * 512, dc * T + (q + 1) * 512)
                dst_full = c.xfull[:, 0:512] if extra_full is None else extra_full[:, src]
                scalar.wait_ge(c.sV.h, v2 + (q * DC + dc) * 3 + 2)
                c.sA.inc(scalar.activation(xh[:, src], dst_full, AF.Copy,
                                           bias=0.0, scale=1.0))



def emit_l0_attn(c):
    for p in range(NH // 2):
        for wi, (wsrc, dsth, dstl, permuted) in enumerate([
                (c.w_q, c.WTQH, c.WTQL, True),
                (c.w_k, c.WTKH, c.WTKL, True),
                (c.w_v, c.WTVH, c.WTVL, False)]):
            _emit_wprep(c, p, wsrc, dsth, dstl, permuted)

        for which, (dsth, dstl, wth, wtl, bcol) in enumerate([
                (c.QH, c.QL, c.WTQH, c.WTQL, c.bias_qkv[:, 0 * 8 + p:0 * 8 + p + 1]),
                (c.KH, c.KL, c.WTKH, c.WTKL, c.bias_qkv[:, 1 * 8 + p:1 * 8 + p + 1])]):
            for tq in range(4):
                _emit_qk_proj_tile(c, p, which, tq, dsth, dstl, wth, wtl, bcol)

        _emit_bounds(c, p)

        for tq in range(4):
            _emit_v_proj_tile(c, p, tq)

        _emit_wprep(c, p, c.w_o, c.WTVH, c.WTVL, False, col_slice=True)

        c.attn_prev = {}
        for tq in range(4):
            _emit_attn_tile(c, p, tq)

        _emit_recip(c, p)

        for tq in range(4):
            _emit_attn_epilogue_tile(c, p, tq)


def _emit_wprep(c, p, wsrc, dsth, dstl, permuted, col_slice=False):
    block = c.block
    wraw = c.ARF[:, 0:1024]
    t0, a0, v0 = c.sT.n, c.sA.n, c.sV.n

    @block.sync
    def _(sync):
        sync.wait_ge(c.sT.h, t0)
        if col_slice:
            src = wsrc[:, p * 128:(p + 1) * 128].rearrange("(ec r) f -> ec r f", r=128)
            for ec in range(DC):
                c.wtok[ec] = c.chW.issue(sync, wraw[:, ec * 128:(ec + 1) * 128], src[ec])
        else:
            src = wsrc[p * 128:(p + 1) * 128, :]
            c.wtok[0] = c.chW.issue(sync, wraw, src)

    @block.tensor
    def _(tensor):
        tensor.wait_ge(c.sA.h, a0)
        tensor.wait_ge(c.sV.h, v0)
        for dc in range(DC):
            tensor.wait_ge(*c.wtok[dc if col_slice else 0])
            pb = c.PB[6 + dc % 2]
            if dc >= 2:
                tensor.wait_ge(c.sV.h, v0 + (dc - 1))
                tensor.wait_ge(c.sA.h, a0 + (dc - 1))
            c.sT.inc(tensor.transpose(pb[:, 0:128], wraw[:, dc * 128:(dc + 1) * 128],
                                      c.ident_t[:]))

    def psrc(pb):
        if not permuted:
            return pb[:, 0:128]
        return pb[:, 0:128].rearrange("p (h b a) -> p h b a", h=2, b=2, a=32).transpose([0, 1, 3, 2])

    def dvw(t, dc):
        sl = t[:, dc * 128:(dc + 1) * 128]
        if not permuted:
            return sl
        return sl.rearrange("p (h a b) -> p h a b", h=2, a=32, b=2)

    @block.scalar
    def _(scalar):
        for dc in range(DC):
            pb = c.PB[6 + dc % 2]
            scalar.wait_ge(c.sT.h, t0 + dc + 1)
            c.sA.inc(scalar.activation(dvw(dsth, dc), psrc(pb),
                                       AF.Copy, bias=0.0, scale=1.0))

    @block.vector
    def _(vector):
        for dc in range(DC):
            pb = c.PB[6 + dc % 2]
            vector.wait_ge(c.sA.h, a0 + dc + 1)
            c.sV.inc(vector.tensor_sub(dvw(dstl, dc), psrc(pb), dvw(dsth, dc)))


def _emit_qk_proj_tile(c, p, which, tq, dsth, dstl, wth, wtl, bcol):
    block = c.block
    sl = slice(tq * 512, (tq + 1) * 512)
    qb = c.ARF[:, 0:512]
    qs = c.ARF[:, 512:1024]
    m2 = c.ARF[:, 1024:1536]
    qf = c.ARF[:, 1536:2048]
    sqn = c.ARF[:, 2048:2560]
    t0, a0, v0 = c.sT.n, c.sA.n, c.sV.n

    @block.tensor
    def _(tensor):
        tensor.wait_ge(c.sA.h, a0)
        tensor.wait_ge(c.sV.h, v0)
        n = 0
        for wt, x in ((wth, XH_g(c)), (wth, XL_g(c)), (wtl, XH_g(c))):
            for dc in range(DC):
                n += 1
                c.sT.inc(tensor.matmul(c.PB[5][:, :],
                                       wt[:, dc * 128:(dc + 1) * 128],
                                       x[:, dc * T + tq * 512: dc * T + (tq + 1) * 512],
                                       start=(n == 1), stop=(n == 24)))

    t_mm = c.sT.n

    @block.scalar
    def _(scalar):
        scalar.wait_ge(c.sT.h, t_mm)
        c.sA.inc(scalar.activation(qb, c.PB[5][:, :], AF.Identity, bias=bcol, scale=1.0))

    @block.vector
    def _(vector):
        vector.wait_ge(c.sA.h, a0 + 1)
        c.sV.inc(vector.stream_shuffle(qs, qb, [(i ^ 1) for i in range(32)]))
        c.sV.inc(vector.tensor_mul(c.PB[6][:, :], qb, c.ropeA[:, sl]))
        vector.wait_ge(c.sV.h, c.sV.n)
        c.sV.inc(vector.tensor_mul(m2, qs, c.ropeB[:, sl]))
        vector.wait_ge(c.sV.h, c.sV.n)
        c.sV.inc(vector.tensor_add(qf, c.PB[6][:, :], m2))

    @block.scalar
    def _(scalar):
        scalar.wait_ge(c.sV.h, v0 + 4)
        c.sA.inc(scalar.activation(dsth[:, sl], qf, AF.Copy, bias=0.0, scale=1.0))
        c.sA.inc(scalar.activation(sqn, qf, AF.Square, bias=0.0, scale=1.0))

    @block.tensor
    def _(tensor):
        tensor.wait_ge(c.sA.h, a0 + 3)
        tensor.wait_ge(c.sV.h, v0)
        c.sT.inc(tensor.matmul(c.PB[6][0:64, :], c.ones_f32[0:64, 0:64], sqn[0:64, :],
                               start=True, stop=True))
        c.sT.inc(tensor.matmul(c.PB[7][0:64, :], c.ones_f32[64:128, 0:64], sqn[64:128, :],
                               start=True, stop=True))

    @block.vector
    def _(vector):
        vector.wait_ge(c.sA.h, a0 + 3)
        vector.wait_ge(c.sV.h, c.sV.n)
        c.sV.inc(vector.tensor_sub(dstl[:, sl], qf, dsth[:, sl]))
        vector.wait_ge(c.sT.h, t_mm + 2)
        cellA = c.nrm[0:1, which * 2:which * 2 + 1]
        cellB = c.nrm[0:1, which * 2 + 1:which * 2 + 2]
        scr = c.nrm[0:1, 8:9]
        c.sV.inc(vector.tensor_reduce(scr, c.PB[6][0:1, :], AX.X, ALU.max))
        vector.wait_ge(c.sV.h, c.sV.n)
        if tq == 0:
            c.sV.inc(vector.tensor_copy(cellA, scr))
        else:
            c.sV.inc(vector.tensor_max(cellA, cellA, scr))
        vector.wait_ge(c.sV.h, c.sV.n)
        c.sV.inc(vector.tensor_reduce(scr, c.PB[7][0:1, :], AX.X, ALU.max))
        vector.wait_ge(c.sV.h, c.sV.n)
        if tq == 0:
            c.sV.inc(vector.tensor_copy(cellB, scr))
        else:
            c.sV.inc(vector.tensor_max(cellB, cellB, scr))


def _emit_bounds(c, p):
    block = c.block
    v0, a0 = c.sV.n, c.sA.n

    @block.scalar
    def _(scalar):
        scalar.wait_ge(c.sV.h, v0)
        c.sA.inc(scalar.activation(c.nrm[0:1, 4:8], c.nrm[0:1, 0:4], AF.Sqrt,
                                   bias=0.0, scale=1.0))

    @block.vector
    def _(vector):
        vector.wait_ge(c.sA.h, a0 + 1)
        c.sV.inc(vector.tensor_mul(c.nrm[0:1, 9:10], c.nrm[0:1, 4:5], c.nrm[0:1, 6:7]))
        c.sV.inc(vector.tensor_mul(c.nrm[0:1, 10:11], c.nrm[0:1, 5:6], c.nrm[0:1, 7:8]))
        vector.wait_ge(c.sV.h, c.sV.n)
        c.sV.inc(vector.tensor_scalar_mul(c.nrm[0:1, 9:11], c.nrm[0:1, 9:11], -0.125))

    @block.sync
    def _(sync):
        sync.wait_ge(c.sV.h, c.sV.n)
        t1 = c.chC.issue(sync, c.scal_dram.ap()[0:1, 0:2], c.nrm[0:1, 9:11])
        sync.wait_ge(*t1)
        t2 = c.chC.issue(sync, c.biasA, c.scal_dram.ap()[0:1, 0:1].partition_broadcast(128))
        t3 = c.chC.issue(sync, c.biasB, c.scal_dram.ap()[0:1, 1:2].partition_broadcast(128))
        c.tok_bias = (t2, t3)


def _emit_v_proj_tile(c, p, tq):
    block = c.block
    qf = c.ARF[:, 1536:2048]
    bcol = c.bias_qkv[:, 2 * 8 + p:2 * 8 + p + 1]
    t0, a0, v0 = c.sT.n, c.sA.n, c.sV.n

    @block.tensor
    def _(tensor):
        tensor.wait_ge(c.sA.h, a0)
        n = 0
        for wt, x in ((c.WTVH, XH_g(c)), (c.WTVH, XL_g(c)), (c.WTVL, XH_g(c))):
            for dc in range(DC):
                n += 1
                c.sT.inc(tensor.matmul(c.PB[5][:, :],
                                       wt[:, dc * 128:(dc + 1) * 128],
                                       x[:, dc * T + tq * 512: dc * T + (tq + 1) * 512],
                                       start=(n == 1), stop=(n == 24)))

    t_mm = c.sT.n

    @block.scalar
    def _(scalar):
        scalar.wait_ge(c.sT.h, t_mm)
        c.sA.inc(scalar.activation(qf, c.PB[5][:, :], AF.Identity, bias=bcol, scale=1.0))

    @block.tensor
    def _(tensor):
        tensor.wait_ge(c.sA.h, a0 + 1)
        tensor.wait_ge(c.sV.h, v0)
        for sub in range(4):
            pb = c.PB[6 + sub % 2]
            if sub >= 2:
                tensor.wait_ge(c.sV.h, v0 + (sub - 1) * 2)
            c.sT.inc(tensor.transpose(pb[:, 0:128], qf[:, sub * 128:(sub + 1) * 128],
                                      c.ident_t[:]))

    @block.scalar
    def _(scalar):
        for sub in range(4):
            kc = tq * 4 + sub
            pb = c.PB[6 + sub % 2]
            scalar.wait_ge(c.sT.h, t_mm + sub + 1)
            c.sA.inc(scalar.activation(c.VSA[:, kc * 128: kc * 128 + 64], pb[:, 0:64],
                                       AF.Copy, bias=0.0, scale=1.0))
            c.sA.inc(scalar.activation(c.VSB[:, kc * 128: kc * 128 + 64], pb[:, 64:128],
                                       AF.Copy, bias=0.0, scale=1.0))

    @block.vector
    def _(vector):
        for sub in range(4):
            kc = tq * 4 + sub
            pb = c.PB[6 + sub % 2]
            vector.wait_ge(c.sA.h, a0 + 1 + 2 * (sub + 1))
            c.sV.inc(vector.tensor_sub(c.VSA[:, kc * 128 + 64: kc * 128 + 128],
                                       pb[:, 0:64], c.VSA[:, kc * 128: kc * 128 + 64]))
            c.sV.inc(vector.tensor_sub(c.VSB[:, kc * 128 + 64: kc * 128 + 128],
                                       pb[:, 64:128], c.VSB[:, kc * 128: kc * 128 + 64]))


def _emit_attn_tile(c, p, tq):
    block = c.block
    nk = 4 * (tq + 1)
    qsl = slice(tq * 512, (tq + 1) * 512)
    t0, a0, v0 = c.sT.n, c.sA.n, c.sV.n

    tS, tAV, aA, aB, vM, vPA, vPB = {}, {}, {}, {}, {}, {}, {}
    t = t0
    for kc in range(nk):
        t += 6
        tS[kc] = t
        if kc >= 1:
            t += 10
            tAV[kc - 1] = t
    t += 10
    tAV[nk - 1] = t
    a = a0
    for kc in range(nk):
        a += 2
        aA[kc] = a
        a += 2
        aB[kc] = a
    a_evac = a + 4
    v = v0
    for kc in range(nk):
        if 0 <= kc - 4 * tq < 4:
            v += 2
        vM[kc] = v
        v += 1
        vPA[kc] = v
        v += 1
        vPB[kc] = v

    prev_sub = c.attn_prev.get("sub", {})
    prev_av = c.attn_prev.get("av", {})
    prev_evac = c.attn_prev.get("evac", 0)

    @block.tensor
    def _(tensor):
        tensor.wait_ge(c.sA.h, prev_evac)
        if tq == 0:
            tensor.wait_ge(c.sV.h, v0)
        for kc in range(nk):
            psA = c.PB[kc % 2]
            psB = c.PB[2 + kc % 2]
            if kc >= 2:
                tensor.wait_ge(c.sV.h, vPB[kc - 2])
            elif (kc % 2) in prev_sub:
                tensor.wait_ge(c.sV.h, prev_sub[kc % 2])
            ksl = slice(kc * 128, (kc + 1) * 128)
            for ps, lo in ((psA, 0), (psB, 64)):
                h = slice(lo, lo + 64)
                c.sT.inc(tensor.matmul(ps[:, :], c.KH[h, ksl], c.QH[h, qsl],
                                       start=True, stop=False))
                c.sT.inc(tensor.matmul(ps[:, :], c.KH[h, ksl], c.QL[h, qsl],
                                       start=False, stop=False))
                c.sT.inc(tensor.matmul(ps[:, :], c.KL[h, ksl], c.QH[h, qsl],
                                       start=False, stop=True))
            if kc >= 1:
                _av_mms(c, tensor, tq, kc - 1, nk, vPB)
        _av_mms(c, tensor, tq, nk - 1, nk, vPB)

    @block.scalar
    def _(scalar):
        if tq == 0:
            scalar.wait_ge(*c.tok_bias[0])
            scalar.wait_ge(*c.tok_bias[1])
        for kc in range(nk):
            psA = c.PB[kc % 2]
            psB = c.PB[2 + kc % 2]
            ph = c.PHt[kc % 2]
            scalar.wait_ge(c.sT.h, tS[kc])
            scalar.wait_ge(c.sV.h, vM[kc])
            if kc >= 2:
                scalar.wait_ge(c.sT.h, tAV[kc - 2])
            elif (kc % 2) in prev_av:
                scalar.wait_ge(c.sT.h, prev_av[kc % 2])
            c.sA.inc(scalar.activation(ph[:, 0:512], psA[:, :], AF.Exp,
                                       bias=c.biasA[:, 0:1], scale=0.125))
            scalar.wait_ge(c.sA.h, c.sA.n)
            c.sA.inc(scalar.activation(psA[:, :], psA[:, :], AF.Exp,
                                       bias=c.biasA[:, 0:1], scale=0.125))
            c.sA.inc(scalar.activation(ph[:, 512:1024], psB[:, :], AF.Exp,
                                       bias=c.biasB[:, 0:1], scale=0.125))
            scalar.wait_ge(c.sA.h, c.sA.n)
            c.sA.inc(scalar.activation(psB[:, :], psB[:, :], AF.Exp,
                                       bias=c.biasB[:, 0:1], scale=0.125))
        scalar.wait_ge(c.sT.h, tAV[nk - 1])
        c.sA.inc(scalar.activation(c.AVS[0:64, tq * 512:(tq + 1) * 512],
                                   c.PB[4][0:64, :], AF.Copy, bias=0.0, scale=1.0))
        c.sA.inc(scalar.activation(c.AVS[64:128, tq * 512:(tq + 1) * 512],
                                   c.PB[5][64:128, :], AF.Copy, bias=0.0, scale=1.0))
        c.sA.inc(scalar.activation(c.rows[64:65, qsl], c.PB[6][0:1, :],
                                   AF.Copy, bias=0.0, scale=1.0))
        c.sA.inc(scalar.activation(c.rows[96:97, qsl], c.PB[7][0:1, :],
                                   AF.Copy, bias=0.0, scale=1.0))

    @block.vector
    def _(vector):
        for kc in range(nk):
            psA = c.PB[kc % 2]
            psB = c.PB[2 + kc % 2]
            ph = c.PHt[kc % 2]
            pl = c.PLt[kc % 2]
            mi = kc - 4 * tq
            vector.wait_ge(c.sT.h, tS[kc])
            if 0 <= mi < 4:
                msl = slice(384 - mi * 128, 384 - mi * 128 + 512)
                c.sV.inc(vector.tensor_add(psA[:, :], psA[:, :], c.MASKS[:, msl]))
                c.sV.inc(vector.tensor_add(psB[:, :], psB[:, :], c.MASKS[:, msl]))
            if kc >= 2:
                vector.wait_ge(c.sT.h, tAV[kc - 2])
            elif (kc % 2) in prev_av:
                vector.wait_ge(c.sT.h, prev_av[kc % 2])
            vector.wait_ge(c.sA.h, aA[kc])
            c.sV.inc(vector.tensor_sub(pl[:, 0:512], psA[:, :], ph[:, 0:512]))
            vector.wait_ge(c.sA.h, aB[kc])
            c.sV.inc(vector.tensor_sub(pl[:, 512:1024], psB[:, :], ph[:, 512:1024]))

    c.attn_prev = {
        "sub": {(nk - 1) % 2: vPB[nk - 1], (nk - 2) % 2: vPB[nk - 2]},
        "av": {(nk - 1) % 2: tAV[nk - 1], (nk - 2) % 2: tAV[nk - 2]},
        "evac": a_evac,
    }


def _av_mms(c, tensor, tq, kc, nk, vPB):
    ph = c.PHt[kc % 2]
    pl = c.PLt[kc % 2]
    tensor.wait_ge(c.sV.h, vPB[kc])
    first = (kc == 0)
    last = (kc == nk - 1)
    for lo, avb in ((0, c.PB[4]), (64, c.PB[5])):
        phs = ph[:, lo * 8: lo * 8 + 512]
        pls = pl[:, lo * 8: lo * 8 + 512]
        vh = c.VSA if lo == 0 else c.VSB
        hi_t = vh[:, kc * 128: kc * 128 + 64]
        lo_t = vh[:, kc * 128 + 64: kc * 128 + 128]
        n = 0
        for (st_, mv) in ((hi_t, phs), (lo_t, phs), (hi_t, pls)):
            n += 1
            c.sT.inc(tensor.matmul(avb[lo:lo + 64, :], st_, mv,
                                   start=(first and n == 1), stop=(last and n == 3),
                                   tile_position=(0, lo)))
        dn = c.PB[6] if lo == 0 else c.PB[7]
        c.sT.inc(tensor.matmul(dn[0:64, :], c.ones_bf[:, 0:64], phs,
                               start=first, stop=False))
        c.sT.inc(tensor.matmul(dn[0:64, :], c.ones_bf[:, 0:64], pls,
                               start=False, stop=last))


def _emit_recip(c, p):
    block = c.block
    a0 = c.sA.n
    recD = c.ARF[0:8, 2048:2560]

    @block.sync
    def _(sync):
        sync.wait_ge(c.sA.h, a0)
        t1 = c.chC.issue(sync, recD[0:4, :], c.rows[64:65, :])
        t2 = c.chC.issue(sync, recD[4:8, :], c.rows[96:97, :])
        c.tok_recd = (t1, t2)

    v0 = c.sV.n
    @block.vector
    def _(vector):
        vector.wait_ge(*c.tok_recd[0])
        vector.wait_ge(*c.tok_recd[1])
        c.sV.inc(vector.reciprocal(recD[0:8, :], recD[0:8, :]))

    @block.sync
    def _(sync):
        sync.wait_ge(c.sV.h, c.sV.n)
        t1 = c.chC.issue(sync, c.rows[0:1, :], recD[0:4, :])
        t2 = c.chC.issue(sync, c.rows[32:33, :], recD[4:8, :])
        c.tok_recr = (t1, t2)


def _emit_attn_epilogue_tile(c, p, tq):
    block = c.block
    qsl = slice(tq * 512, (tq + 1) * 512)
    osb = c.ARF[:, 2560:3072]
    t0, a0, v0 = c.sT.n, c.sA.n, c.sV.n

    @block.tensor
    def _(tensor):
        tensor.wait_ge(c.sV.h, v0)
        if tq == 0:
            tensor.wait_ge(*c.tok_recr[0])
            tensor.wait_ge(*c.tok_recr[1])
        c.sT.inc(tensor.matmul(c.PB[0][0:64, :], c.ones_f32[0:1, 0:64],
                               c.rows[0:1, qsl], start=True, stop=True,
                               tile_position=(0, 0)))
        c.sT.inc(tensor.matmul(c.PB[1][64:128, :], c.ones_f32[32:33, 0:64],
                               c.rows[32:33, qsl], start=True, stop=True,
                               tile_position=(32, 64)))

    t_bc = c.sT.n

    @block.vector
    def _(vector):
        vector.wait_ge(c.sT.h, t_bc)
        c.sV.inc(vector.tensor_mul(osb[0:64, :], c.AVS[0:64, qsl], c.PB[0][0:64, :]))
        c.sV.inc(vector.tensor_mul(osb[64:128, :], c.AVS[64:128, qsl], c.PB[1][64:128, :]))

    @block.scalar
    def _(scalar):
        scalar.wait_ge(c.sV.h, v0 + 2)
        c.sA.inc(scalar.activation(c.OH[:, 0:512], osb, AF.Copy, bias=0.0, scale=1.0))

    @block.vector
    def _(vector):
        vector.wait_ge(c.sA.h, a0 + 1)
        vector.wait_ge(c.sV.h, c.sV.n)
        c.sV.inc(vector.tensor_sub(c.OL[:, 0:512], osb, c.OH[:, 0:512]))

    v_spl = c.sV.n

    @block.tensor
    def _(tensor):
        tensor.wait_ge(c.sV.h, v_spl)
        for ec in range(DC):
            pb = c.PB[6 + ec % 2]
            if ec >= 2:
                tensor.wait_ge(c.sV.h, v_spl + (ec - 1))
            n = 0
            for wt, mv in ((c.WTVH, c.OH[:, 0:512]), (c.WTVH, c.OL[:, 0:512]),
                           (c.WTVL, c.OH[:, 0:512])):
                n += 1
                c.sT.inc(tensor.matmul(pb[:, :], wt[:, ec * 128:(ec + 1) * 128], mv,
                                       start=(n == 1), stop=(n == 3)))

    t_wo = c.sT.n

    @block.vector
    def _(vector):
        for ec in range(DC):
            pb = c.PB[6 + ec % 2]
            vector.wait_ge(c.sT.h, t_wo - 3 * (DC - 1 - ec))
            dst = c.res[:, ec * T + tq * 512: ec * T + (tq + 1) * 512]
            if p == 0:
                c.sV.inc(vector.scalar_tensor_tensor(
                    dst, pb[:, :], c.bias_qkv[:, 3 * 8 + ec:3 * 8 + ec + 1], dst,
                    ALU.add, ALU.add))
            else:
                c.sV.inc(vector.tensor_add(dst, dst, pb[:, :]))


def XH_g(c):
    return c.XH


def XL_g(c):
    return c.XL



def emit_ffn0_gamma(c, wsrc, nrows, ncols, cell):
    block = c.block
    wstage = c.ARF[:, 0:2048]
    nch = nrows // 128
    ncg = ncols // 1024
    a0, v0 = c.sA.n, c.sV.n
    tok = {}

    @block.sync
    def _(sync):
        _wait_all(sync, c)
        k = 0
        for ch in range(nch):
            for cg in range(ncg):
                if k >= 2:
                    sync.wait_ge(c.sA.h, a0 + (k - 1))
                tok[k] = c.chW.issue(sync, wstage[:, (k % 2) * 1024:(k % 2) * 1024 + 1024],
                                     wsrc[ch * 128:(ch + 1) * 128, cg * 1024:(cg + 1) * 1024])
                k += 1

    nk = nch * ncg

    @block.scalar
    def _(scalar):
        for k in range(nk):
            scalar.wait_ge(*tok[k])
            if k >= 1:
                scalar.wait_ge(c.sV.h, v0 + k)
            h = (k % 2) * 1024
            c.sA.inc(scalar.activation(wstage[:, h:h + 1024], wstage[:, h:h + 1024],
                                       AF.Abs, bias=0.0, scale=1.0,
                                       accum_out=c.acol[:, 0:1]))

    @block.vector
    def _(vector):
        for k in range(nk):
            vector.wait_ge(c.sA.h, a0 + k + 1)
            if k == 0:
                c.sV.inc(vector.tensor_copy(c.gcol[:, 0:1], c.acol[:, 0:1]))
            else:
                c.sV.inc(vector.tensor_add(c.gcol[:, 0:1], c.gcol[:, 0:1], c.acol[:, 0:1]))
            vector.wait_ge(c.sV.h, c.sV.n)

    @block.gpsimd
    def _(gpsimd):
        gpsimd.wait_ge(c.sV.h, c.sV.n)
        c.sG.inc(gpsimd.tensor_reduce(c.nrm[0:1, cell:cell + 1], c.gcol[:, 0:1],
                                      AX.C, ALU.add))

    @block.vector
    def _(vector):
        vector.wait_ge(c.sG.h, c.sG.n)
        c.sV.inc(vector.tensor_scalar(c.nrm[0:1, cell:cell + 1],
                                      c.nrm[0:1, cell:cell + 1],
                                      1.0 / (nrows * float(ncols)), None, ALU.mult))
        vector.wait_ge(c.sV.h, c.sV.n)
        c.sV.inc(vector.tensor_scalar(c.nrm[0:1, cell:cell + 1],
                                      c.nrm[0:1, cell:cell + 1],
                                      1e-05, 0.5, ALU.add, ALU.mult))


def emit_ffn0_wprep(c, wsrc, nrows, ncols, cell, thr_cells, dst_fn):
    block = c.block
    wstage = c.ARF[:, 0:2048]
    tpos, tneg = thr_cells
    nch = nrows // 128
    ncg = ncols // 1024
    t0, a0, v0 = c.sT.n, c.sA.n, c.sV.n
    tok = {}
    snapQ = c.chQ.snapshot()

    @block.sync
    def _(sync):
        _wait_all(sync, c)
        t1 = c.chC.issue(sync, c.scal_dram.ap()[0:1, 0:1], c.nrm[0:1, cell:cell + 1])
        sync.wait_ge(*t1)
        t2 = c.chC.issue(sync, c.ARF[:, tpos:tpos + 1],
                         c.scal_dram.ap()[0:1, 0:1].partition_broadcast(128))
        c.tok_thr = t2

    @block.vector
    def _(vector):
        vector.wait_ge(*c.tok_thr)
        c.sV.inc(vector.tensor_scalar_mul(c.ARF[:, tneg:tneg + 1],
                                          c.ARF[:, tpos:tpos + 1], -1.0))

    v_thr = c.sV.n

    @block.sync
    def _(sync):
        k = 0
        for ch in range(nch):
            for cg in range(ncg):
                if k >= 2:
                    sync.wait_ge(c.sT.h, t0 + (k - 1) * DC)
                tok[k] = c.chW.issue(sync, wstage[:, (k % 2) * 1024:(k % 2) * 1024 + 1024],
                                     wsrc[ch * 128:(ch + 1) * 128, cg * 1024:(cg + 1) * 1024])
                k += 1

    @block.tensor
    def _(tensor):
        tensor.wait_ge(c.sV.h, v_thr)
        k = 0
        for ch in range(nch):
            for cg in range(ncg):
                tensor.wait_ge(*tok[k])
                h = (k % 2) * 1024
                for dc in range(DC):
                    pb = c.PB[4 + (dc % 2)]
                    n = k * DC + dc
                    if n >= 2:
                        tensor.wait_ge(c.sV.h, v_thr + (n - 1) * 2)
                    c.sT.inc(tensor.transpose(pb[:, 0:128],
                                              wstage[:, h + dc * 128:h + (dc + 1) * 128],
                                              c.ident_t[:]))
                k += 1

    @block.vector
    def _(vector):
        for sm in c.chQ.sems:
            vector.wait_ge(sm.h, sm.n)
        n = 0
        for ch in range(nch):
            for cg in range(ncg):
                for dc in range(DC):
                    pb = c.PB[4 + (dc % 2)]
                    vector.wait_ge(c.sT.h, t0 + n + 1)
                    stg = (c.OH if (n % 2 == 0) else c.OL)[:, (n // 2 % 2) * 128:(n // 2 % 2) * 128 + 128]
                    if n >= 4:
                        vector.wait_ge(*c.chQ.plan(snapQ, n - 4))
                    vector.wait_ge(c.sV.h, c.sV.n)
                    c.sV.inc(vector.tensor_scalar(c.qtmp[:, 0:128], pb[:, 0:128],
                                                  c.ARF[:, tneg:tneg + 1], None, ALU.is_le))
                    vector.wait_ge(c.sV.h, c.sV.n)
                    c.sV.inc(vector.scalar_tensor_tensor(
                        stg, pb[:, 0:128], c.ARF[:, tpos:tpos + 1],
                        c.qtmp[:, 0:128], ALU.is_ge, ALU.subtract))
                    n += 1

    @block.scalar
    def _(scalar):
        n = 0
        for ch in range(nch):
            for cg in range(ncg):
                for dc in range(DC):
                    scalar.wait_ge(c.sV.h, v_thr + (n + 1) * 2)
                    stg = (c.OH if (n % 2 == 0) else c.OL)[:, (n // 2 % 2) * 128:(n // 2 % 2) * 128 + 128]
                    tkn = c.chQ.issue(scalar, dst_fn(ch, cg, dc), stg)
                    n += 1


def emit_ln_stats(c):
    block = c.block
    t0, a0, v0 = c.sT.n, c.sA.n, c.sV.n
    g0, s_0 = c.sG.n, c.sS.n

    @block.scalar
    def _(scalar):
        _wait_all(scalar, c)
        for dc in range(DC):
            h = (dc % 2) * T
            if dc >= 2:
                scalar.wait_ge(c.sT.h, t0 + (dc - 1) * 16)
            r = c.res[:, dc * T:(dc + 1) * T]
            c.sA.inc(scalar.activation(c.sqbuf[:, h:h + T], r, AF.Square, bias=0.0, scale=1.0))
            c.sA.inc(scalar.activation(c.sbh[:, h:h + T], r, AF.Copy, bias=0.0, scale=1.0))
            scalar.wait_ge(c.sA.h, c.sA.n - 1)
            c.sA.inc(scalar.activation(c.sqh[:, h:h + T], c.sqbuf[:, h:h + T],
                                       AF.Copy, bias=0.0, scale=1.0))

    @block.vector
    def _(vector):
        for dc in range(DC):
            h = (dc % 2) * T
            vector.wait_ge(c.sA.h, a0 + dc * 3 + 3)
            r = c.res[:, dc * T:(dc + 1) * T]
            c.sV.inc(vector.tensor_sub(c.sbl[:, h:h + T], r, c.sbh[:, h:h + T]))
            c.sV.inc(vector.tensor_sub(c.sql[:, h:h + T], c.sqbuf[:, h:h + T],
                                       c.sqh[:, h:h + T]))

    @block.tensor
    def _(tensor):
        tensor.wait_ge(c.sV.h, v0)
        tensor.wait_ge(c.sA.h, a0)
        tensor.wait_ge(c.sG.h, g0)
        for dc in range(DC):
            h = (dc % 2) * T
            tensor.wait_ge(c.sV.h, v0 + dc * 2 + 2)
            st = (dc == 0)
            sp = (dc == DC - 1)
            for q in range(4):
                sl = slice(h + q * 512, h + (q + 1) * 512)
                c.sT.inc(tensor.matmul(c.PB[q][0:64, :], c.ones_bf[:, 0:64],
                                       c.sbh[:, sl], start=st, stop=False))
                c.sT.inc(tensor.matmul(c.PB[q][0:64, :], c.ones_bf[:, 0:64],
                                       c.sbl[:, sl], start=False, stop=sp))
                c.sT.inc(tensor.matmul(c.PB[4 + q][0:64, :], c.ones_bf[:, 0:64],
                                       c.sqh[:, sl], start=st, stop=False))
                c.sT.inc(tensor.matmul(c.PB[4 + q][0:64, :], c.ones_bf[:, 0:64],
                                       c.sql[:, sl], start=False, stop=sp))

    t1 = c.sT.n

    @block.vector
    def _(vector):
        vector.wait_ge(c.sT.h, t1)
        for q in range(4):
            sl = slice(q * 512, (q + 1) * 512)
            c.sV.inc(vector.tensor_scalar_mul(c.rows[0:1, sl], c.PB[q][0:1, :], -1.0 / D))
            c.sV.inc(vector.tensor_scalar_mul(c.rows[32:33, sl], c.PB[4 + q][0:1, :], 1.0 / D))
            vector.wait_ge(c.sV.h, c.sV.n)
            c.sV.inc(vector.scalar_tensor_tensor(
                c.PB[4 + q][32:33, :], c.rows[0:1, sl], -1.0, c.rows[0:1, sl],
                ALU.mult, ALU.mult))
            vector.wait_ge(c.sV.h, c.sV.n)
            c.sV.inc(vector.tensor_add(c.rows[64:65, sl], c.rows[32:33, sl],
                                       c.PB[4 + q][32:33, :]))
        vector.wait_ge(c.sV.h, c.sV.n)
        c.sV.inc(vector.tensor_scalar_add(c.rows[64:65, 0:T], c.rows[64:65, 0:T], EPS_LN))

    a1 = c.sA.n
    @block.scalar
    def _(scalar):
        scalar.wait_ge(c.sV.h, c.sV.n)
        c.sA.inc(scalar.activation(c.rows[96:97, 0:T], c.rows[64:65, 0:T], AF.Sqrt,
                                   bias=0.0, scale=1.0))

    @block.vector
    def _(vector):
        vector.wait_ge(c.sA.h, a1 + 1)
        c.sV.inc(vector.reciprocal(c.rows[32:33, 0:T], c.rows[96:97, 0:T]))


def emit_ffn0_main(c):
    block = c.block
    X2H = c.ARB[:, 0:4096]
    X2L = c.ARB[:, 4096:8192]
    HH = c.ARB[:, 16384:32768]
    HL = c.ARB[:, 32768:49152]
    WQS = [c.ARB[:, 49152:51200], c.ARB[:, 51200:53248]]
    hfull = c.ARF[:, 2560:3072]

    @block.sync
    def _(sync):
        for sm in c.chQ.sems:
            sync.wait_ge(sm.h, sm.n)

    for tq in range(4):
        qsl = slice(tq * 512, (tq + 1) * 512)
        t0, a0, v0 = c.sT.n, c.sA.n, c.sV.n
        s_0 = c.sS.n
        wtok = {}

        @block.tensor
        def _(tensor):
            tensor.wait_ge(c.sV.h, v0)
            tensor.wait_ge(c.sA.h, a0)
            c.sT.inc(tensor.matmul(c.PB[0][:, :], c.ones_f32[0:1, :], c.rows[0:1, qsl],
                                   start=True, stop=True))
            c.sT.inc(tensor.matmul(c.PB[1][:, :], c.ones_f32[32:33, :], c.rows[32:33, qsl],
                                   start=True, stop=True))

        t_bc = c.sT.n

        @block.vector
        def _(vector):
            vector.wait_ge(c.sT.h, t_bc)
            for dc in range(DC):
                src = slice(dc * T + tq * 512, dc * T + (tq + 1) * 512)
                d2 = slice(dc * 512, (dc + 1) * 512)
                c.sV.inc(vector.tensor_add(c.tmp1[:, 0:512], c.res[:, src], c.PB[0][:, :]))
                vector.wait_ge(c.sV.h, c.sV.n)
                c.sV.inc(vector.tensor_mul(c.xfull[:, 0:512], c.tmp1[:, 0:512], c.PB[1][:, :]))
                vector.wait_ge(c.sV.h, c.sV.n)
                vector.wait_ge(c.sA.h, a0 + dc + 1)
                c.sV.inc(vector.tensor_sub(X2L[:, d2], c.xfull[:, 0:512], X2H[:, d2]))

        @block.scalar
        def _(scalar):
            for dc in range(DC):
                d2 = slice(dc * 512, (dc + 1) * 512)
                scalar.wait_ge(c.sV.h, v0 + dc * 3 + 2)
                c.sA.inc(scalar.activation(X2H[:, d2], c.xfull[:, 0:512], AF.Copy,
                                           bias=0.0, scale=1.0))

        v_norm = c.sV.n
        a_norm = c.sA.n

        @block.sync
        def _(sync):
            for fc in range(32):
                if fc >= 2:
                    sync.wait_ge(c.sT.h, t_bc + (fc - 1) * 16)
                if fc < 2:
                    sync.wait_ge(c.sT.h, t_bc)
                wtok[fc] = c.chW.issue(
                    sync, WQS[fc % 2][:, 0:1024].rearrange("p (dc f) -> p dc f", f=128),
                    c.wq1t_dram.ap()[fc].rearrange("(dc p) f -> p dc f", p=128))

        @block.tensor
        def _(tensor):
            tensor.wait_ge(c.sV.h, v_norm)
            for fc in range(32):
                tensor.wait_ge(*wtok[fc])
                pb = c.PB[6 + fc % 2]
                if fc >= 2:
                    tensor.wait_ge(c.sV.h, v_norm + (fc - 1))
                n = 0
                for dc in range(DC):
                    wt = WQS[fc % 2][:, dc * 128:(dc + 1) * 128]
                    for x in (X2H, X2L):
                        n += 1
                        c.sT.inc(tensor.matmul(pb[:, :], wt,
                                               x[:, dc * 512:(dc + 1) * 512],
                                               start=(n == 1), stop=(n == 16)))

        t_m1 = c.sT.n

        @block.scalar
        def _(scalar):
            for fc in range(32):
                pb = c.PB[6 + fc % 2]
                hsl = slice(fc * 512, (fc + 1) * 512)
                scalar.wait_ge(c.sT.h, t_bc + (fc + 1) * 16)
                if fc >= 1:
                    scalar.wait_ge(c.sV.h, v_norm + fc)
                bcol = c.bias_ffn[:, fc:fc + 1]
                c.sA.inc(scalar.activation(HH[:, hsl], pb[:, :], AF.Relu,
                                           bias=bcol, scale=1.0))
                c.sA.inc(scalar.activation(hfull, pb[:, :], AF.Relu,
                                           bias=bcol, scale=1.0))

        @block.vector
        def _(vector):
            for fc in range(32):
                hsl = slice(fc * 512, (fc + 1) * 512)
                vector.wait_ge(c.sA.h, a_norm + (fc + 1) * 2)
                c.sV.inc(vector.tensor_sub(HL[:, hsl], hfull, HH[:, hsl]))

        v_h = c.sV.n
        a_h = c.sA.n

        wtok2 = {}

        @block.sync
        def _(sync):
            for fc in range(32):
                sync.wait_ge(c.sT.h, t_m1 + (fc - 1) * 16 if fc >= 2 else t_m1)
                wtok2[fc] = c.chW.issue(sync, WQS[fc % 2][:, 0:1024],
                                        c.wq2t_dram.ap()[fc])

        @block.tensor
        def _(tensor):
            tensor.wait_ge(c.sV.h, v_h)
            tensor.wait_ge(c.sA.h, a_h)
            for fc in range(32):
                tensor.wait_ge(*wtok2[fc])
                hsl = slice(fc * 512, (fc + 1) * 512)
                n = 0
                for dco in range(DC):
                    wt = WQS[fc % 2][:, dco * 128:(dco + 1) * 128]
                    for hx in (HH, HL):
                        n += 1
                        c.sT.inc(tensor.matmul(c.PB[dco][:, :], wt, hx[:, hsl],
                                               start=(fc == 0 and n % 2 == 1),
                                               stop=(fc == 31 and n % 2 == 0)))

        t_m2 = c.sT.n

        @block.vector
        def _(vector):
            vector.wait_ge(c.sT.h, t_m2)
            for dco in range(DC):
                dst = c.res[:, dco * T + tq * 512: dco * T + (tq + 1) * 512]
                c.sV.inc(vector.scalar_tensor_tensor(
                    dst, c.PB[dco][:, :], c.bias_ffn[:, 32 + dco:32 + dco + 1], dst,
                    ALU.add, ALU.add))


def emit_ffn0(c):
    emit_ffn0_gamma(c, c.w1, FF0, 1024, 12)
    emit_ffn0_gamma(c, c.w2, D, 4096, 13)
    emit_ffn0_wprep(c, c.w1, FF0, 1024, 12, (5122, 5123),
                    lambda ch, cg, dc: c.wq1t_dram.ap()[ch, dc * 128:(dc + 1) * 128, :])
    emit_ffn0_wprep(c, c.w2, D, 4096, 13, (5124, 5125),
                    lambda ch, cg, dc: c.wq2t_dram.ap()[cg * 8 + dc, :, ch * 128:(ch + 1) * 128])
    emit_ln_stats(c)
    emit_ffn0_main(c)



def emit_ssm(c):
    block = c.block
    F = lambda a, b: c.ARB[:, 2 * a:2 * b].bitcast(F32)
    X3C = F(0, 4112)
    DTC = F(4112, 8208)
    UC = c.ARF[:, 0:4096]
    ALPHA2 = [F(8208, 12304), F(12304, 16400)]
    BETA2 = [F(16400, 20496), F(20496, 24592)]
    YR = F(24592, 25104)
    NMBc = F(25104, 25616)
    RSBc = F(25616, 26128)
    STATE = F(26128, 26256)
    SEEDT = F(26256, 26264)
    ACO = c.ropeA[:, 0:128]
    DCO = c.ropeA[:, 128:136]
    BDTP = c.ropeA[:, 136:144]
    CONVK = c.ropeA[:, 144:168]
    BDT = c.ropeA[:, 168:169]
    BBC = c.ropeA[:, 169:170]
    WBCDT = c.ropeA[:, 256:1024]
    WDTP = c.ropeB[:, 0:1024]
    DTL = c.ARF[0:64, 4096:4608]
    BCR = c.ARF[64:96, 4096:4608]

    @block.sync
    def _(sync):
        _wait_all(sync, c)
        c.tok_ssm = []
        ctx_nc = c.nc.allow_non_contiguous_dma(reason="tiny ssm weight vectors")
        ctx_nc.__enter__()
        c.tok_ssm.append(c.chW.issue(
            sync, c.ropeA[:, 0:128].rearrange("p (dc n) -> p dc n", n=16),
            c.a_log.rearrange("(dc p) n -> p dc n", p=128)))
        c.tok_ssm.append(c.chW.issue(
            sync, DCO, c.dcoeff.rearrange("(dc p) -> p dc", p=128)))
        c.tok_ssm.append(c.chW.issue(
            sync, BDTP, c.b_dtp.rearrange("(dc p) -> p dc", p=128)))
        c.tok_ssm.append(c.chW.issue(
            sync, CONVK.rearrange("p (k dc) -> p k dc", k=3),
            c.conv_k.rearrange("k o (dc p) -> p k (o dc)", p=128)))
        c.tok_ssm.append(c.chW.issue(sync, BDT[0:64, :], c.b_dt.rearrange("(a o) -> a o", o=1)))
        c.tok_ssm.append(c.chW.issue(sync, c.bbc_col,
                                     c.b_bc.rearrange("(a o) -> a o", o=1)))
        c.tok_ssm.append(c.chW.issue(
            sync, WBCDT.rearrange("p (dc f) -> p dc f", f=96)[:, :, 0:64],
            c.w_dt.rearrange("(dc p) f -> p dc f", p=128)))
        c.tok_ssm.append(c.chW.issue(
            sync, WBCDT.rearrange("p (dc f) -> p dc f", f=96)[:, :, 64:96],
            c.w_bc.rearrange("(dc p) f -> p dc f", p=128)))
        c.tok_ssm.append(c.chW.issue(sync, WDTP[0:64, :], c.w_dtp))
        ctx_nc.__exit__(None, None, None)

    @block.scalar
    def _(scalar):
        for t in c.tok_ssm:
            scalar.wait_ge(*t)
        c.sA.inc(scalar.activation(ACO, ACO, AF.Exp, bias=0.0, scale=1.0))

    @block.vector
    def _(vector):
        vector.wait_ge(c.sA.h, c.sA.n)
        c.sV.inc(vector.tensor_scalar_mul(ACO, ACO, -1.0))
        c.sV.inc(vector.memset(STATE, 0.0))

    emit_ln_stats(c)

    @block.sync
    def _(sync):
        sync.wait_ge(c.sV.h, c.sV.n)
        c.tok_lns = [c.chC.issue(sync, c.ln_dram.ap()[0:1, :], c.rows[0:1, 0:2048]),
                     c.chC.issue(sync, c.ln_dram.ap()[1:2, :], c.rows[32:33, 0:2048])]

    for tc in range(4):
        _emit_ssm_chunk(c, tc, X3C, DTC, UC, ALPHA2, BETA2, YR, NMBc, RSBc,
                        SEEDT, ACO, DCO, BDTP, CONVK, BDT, WBCDT, WDTP, DTL,
                        BCR, STATE)


def _emit_ssm_chunk(c, tc, X3C, DTC, UC, ALPHA2, BETA2, YR, NMBc, RSBc,
                    SEEDT, ACO, DCO, BDTP, CONVK, BDT, WBCDT, WDTP, DTL,
                    BCR, STATE):
    block = c.block
    t0, a0, v0 = c.sT.n, c.sA.n, c.sV.n

    @block.sync
    def _(sync):
        if tc == 0:
            for t in c.tok_lns:
                sync.wait_ge(*t)
        else:
            sync.wait_ge(c.sV.h, v0)
        c.tok_nm = c.chC.issue(sync, NMBc,
                               c.ln_dram.ap()[0:1, tc * 512:(tc + 1) * 512]
                               .partition_broadcast(128))
        c.tok_rs = c.chC.issue(sync, RSBc,
                               c.ln_dram.ap()[1:2, tc * 512:(tc + 1) * 512]
                               .partition_broadcast(128))

    @block.vector
    def _(vector):
        vector.wait_ge(*c.tok_nm)
        vector.wait_ge(*c.tok_rs)
        if tc > 0:
            vector.wait_ge(*c.tok_bcd)
        for dc in range(DC):
            base = dc * 514
            if tc > 0:
                c.sV.inc(vector.tensor_copy(X3C[:, base:base + 1],
                                            X3C[:, base + 512:base + 513]))
                vector.wait_ge(c.sV.h, c.sV.n)
            src = slice(dc * T + tc * 512, dc * T + (tc + 1) * 512)
            c.sV.inc(vector.tensor_add(c.tmp1[:, 0:512], c.res[:, src], NMBc))
            vector.wait_ge(c.sV.h, c.sV.n)
            c.sV.inc(vector.tensor_mul(X3C[:, base + 1:base + 513],
                                       c.tmp1[:, 0:512], RSBc))
            vector.wait_ge(c.sV.h, c.sV.n)

    @block.vector
    def _(vector):
        for dc in range(DC):
            b = dc * 514
            ub = dc * 512
            k0 = CONVK[:, dc:dc + 1]
            k1 = CONVK[:, 8 + dc:8 + dc + 1]
            k2 = CONVK[:, 16 + dc:16 + dc + 1]
            c.sV.inc(vector.tensor_scalar_mul(c.tmp1[:, 0:512], X3C[:, b:b + 512], k0))
            vector.wait_ge(c.sV.h, c.sV.n)
            c.sV.inc(vector.scalar_tensor_tensor(
                c.tmp1[:, 0:512], X3C[:, b + 1:b + 513], k1, c.tmp1[:, 0:512],
                ALU.mult, ALU.add))
            vector.wait_ge(c.sV.h, c.sV.n)
            c.sV.inc(vector.scalar_tensor_tensor(
                UC[:, ub:ub + 512], X3C[:, b + 2:b + 514], k2, c.tmp1[:, 0:512],
                ALU.mult, ALU.add))
            vector.wait_ge(c.sV.h, c.sV.n)
            if tc == 0:
                c.sV.inc(vector.tensor_scalar_mul(c.tmp1[:, 0:1], X3C[:, b + 1:b + 2], k1))
                vector.wait_ge(c.sV.h, c.sV.n)
                c.sV.inc(vector.scalar_tensor_tensor(
                    UC[:, ub:ub + 1], X3C[:, b + 2:b + 3], k2, c.tmp1[:, 0:1],
                    ALU.mult, ALU.add))
                vector.wait_ge(c.sV.h, c.sV.n)
            if tc == 3:
                c.sV.inc(vector.tensor_scalar_mul(c.tmp1[:, 0:1], X3C[:, b + 511:b + 512], k0))
                vector.wait_ge(c.sV.h, c.sV.n)
                c.sV.inc(vector.scalar_tensor_tensor(
                    UC[:, ub + 511:ub + 512], X3C[:, b + 512:b + 513], k1, c.tmp1[:, 0:1],
                    ALU.mult, ALU.add))
                vector.wait_ge(c.sV.h, c.sV.n)

    v_conv = c.sV.n

    @block.scalar
    def _(scalar):
        scalar.wait_ge(c.sV.h, v_conv)
        c.sA.inc(scalar.activation(UC[:, 0:4096], UC[:, 0:4096], AF.Silu,
                                   bias=0.0, scale=1.0))

    a_silu = c.sA.n

    @block.tensor
    def _(tensor):
        tensor.wait_ge(c.sA.h, a_silu)
        tensor.wait_ge(c.sV.h, v_conv)
        for dc in range(DC):
            c.sT.inc(tensor.matmul(c.PB[6][0:96, :],
                                   WBCDT[:, dc * 96:(dc + 1) * 96],
                                   UC[:, dc * 512:(dc + 1) * 512],
                                   start=(dc == 0), stop=(dc == DC - 1)))

    t_p1 = c.sT.n

    @block.scalar
    def _(scalar):
        scalar.wait_ge(c.sT.h, t_p1)
        c.sA.inc(scalar.activation(DTL, c.PB[6][0:64, :], AF.Identity,
                                   bias=BDT[0:64, :], scale=1.0))
        c.sA.inc(scalar.activation(BCR, c.PB[6][64:96, :], AF.Identity,
                                   bias=c.bbc_col, scale=1.0))

    a_p1 = c.sA.n

    @block.tensor
    def _(tensor):
        tensor.wait_ge(c.sA.h, a_p1)
        for dc in range(DC):
            pb = c.PB[7] if dc % 2 else c.PB[5]
            if dc >= 2:
                tensor.wait_ge(c.sA.h, a_p1 + (dc - 1) * 4)
            c.sT.inc(tensor.matmul(pb[:, :], WDTP[0:64, dc * 128:(dc + 1) * 128],
                                   DTL, start=True, stop=True))

    t_p2 = c.sT.n

    sp = c.xfull[:, 0:512]
    v_sp0 = c.sV.n

    @block.scalar
    def _(scalar):
        for dc in range(DC):
            pb = c.PB[7] if dc % 2 else c.PB[5]
            scalar.wait_ge(c.sT.h, t_p2 - (DC - 1 - dc))
            if dc >= 1:
                scalar.wait_ge(c.sV.h, v_sp0 + dc)
            bcol = BDTP[:, dc:dc + 1]
            c.sA.inc(scalar.activation(sp, pb[:, :], AF.Abs, bias=bcol, scale=1.0))
            scalar.wait_ge(c.sA.h, c.sA.n)
            c.sA.inc(scalar.activation(sp, sp, AF.Exp, bias=0.0, scale=-1.0))
            scalar.wait_ge(c.sA.h, c.sA.n)
            c.sA.inc(scalar.activation(sp, sp, AF.Ln, bias=1.0, scale=1.0))
            c.sA.inc(scalar.activation(DTC[:, dc * 512:(dc + 1) * 512], pb[:, :],
                                       AF.Relu, bias=bcol, scale=1.0))

    a_dt0 = c.sA.n

    @block.vector
    def _(vector):
        for dc in range(DC):
            vector.wait_ge(c.sA.h, a_dt0 - (DC - 1 - dc) * 4)
            dsl_ = slice(dc * 512, (dc + 1) * 512)
            c.sV.inc(vector.tensor_add(DTC[:, dsl_], DTC[:, dsl_], sp))

    a_dt = c.sA.n

    @block.vector
    def _(vector):
        vector.wait_ge(c.sA.h, a_dt)
        vector.wait_ge(c.sV.h, c.sV.n)
        for dc in range(DC):
            c.sV.inc(vector.tensor_mul(UC[:, dc * 512:(dc + 1) * 512],
                                       X3C[:, dc * 514 + 1:dc * 514 + 513],
                                       DTC[:, dc * 512:(dc + 1) * 512]))

    @block.sync
    def _(sync):
        sync.wait_ge(c.sA.h, a_p1)
        if tc > 0:
            sync.wait_ge(*c.ssm_last_restage)
        c.tok_bcd = c.chC.issue(sync, c.bc_dram.ap(), BCR)

    for nh in range(2):
        for pair in range(4):
            _emit_ssm_pair(c, tc, nh, pair, X3C, DTC, UC, ALPHA2, BETA2, YR,
                           SEEDT, ACO, DCO, STATE)


def _emit_ssm_pair(c, tc, nh, pair, X3C, DTC, UC, ALPHA2, BETA2, YR, SEEDT,
                   ACO, DCO, STATE):
    block = c.block
    dcs = (2 * pair, 2 * pair + 1)
    t0, a0, v0 = c.sT.n, c.sA.n, c.sV.n

    @block.sync
    def _(sync):
        sync.wait_ge(*c.tok_bcd)
        sync.wait_ge(c.sT.h, t0)
        tokB = [c.chC.issue(sync, c.rows[0:1, 0:2048],
                            c.bc_dram.ap()[nh * 8:nh * 8 + 4, :]),
                c.chC.issue(sync, c.rows[32:33, 0:2048],
                            c.bc_dram.ap()[nh * 8 + 4:nh * 8 + 8, :])]
        c.ssm_tokB = tokB

    @block.tensor
    def _(tensor):
        for t in c.ssm_tokB:
            tensor.wait_ge(*t)
        tensor.wait_ge(c.sV.h, v0)
        for j in range(4):
            c.sT.inc(tensor.matmul(c.PB[j][:, :], c.ones_f32[0:1, :],
                                   c.rows[0:1, j * 512:(j + 1) * 512],
                                   start=True, stop=True))
        for j in range(4):
            c.sT.inc(tensor.matmul(c.PB[4 + j][:, :], c.ones_f32[32:33, :],
                                   c.rows[32:33, j * 512:(j + 1) * 512],
                                   start=True, stop=True))

    t_bmm = c.sT.n

    @block.scalar
    def _(scalar):
        scalar.wait_ge(c.sV.h, v0)
        for i, dc in enumerate(dcs):
            for n8 in range(8):
                n = nh * 8 + n8
                c.sA.inc(scalar.activation(
                    ALPHA2[i][:, n8 * 512:(n8 + 1) * 512],
                    DTC[:, dc * 512:(dc + 1) * 512], AF.Exp,
                    scale=ACO[:, dc * 16 + n:dc * 16 + n + 1], bias=0.0))

    a_alpha = c.sA.n

    @block.vector
    def _(vector):
        vector.wait_ge(c.sT.h, t_bmm)
        vector.wait_ge(c.sA.h, a_alpha)
        for i, dc in enumerate(dcs):
            ssl = slice(dc * 16 + nh * 8, dc * 16 + nh * 8 + 8)
            for n8 in range(8):
                c.sV.inc(vector.tensor_tensor(
                    BETA2[i][:, n8 * 512:(n8 + 1) * 512],
                    UC[:, dc * 512:(dc + 1) * 512],
                    c.PB[n8][:, :], ALU.mult))
            vector.wait_ge(c.sV.h, c.sV.n)
            a_c0 = ALPHA2[i].rearrange("p (n t) -> p n t", t=512)[:, :, 0:1]
            b_c0 = BETA2[i].rearrange("p (n t) -> p n t", t=512)[:, :, 0:1]
            c.sV.inc(vector.tensor_tensor(SEEDT.unsqueeze(2), a_c0,
                                          STATE[:, ssl].unsqueeze(2), ALU.mult))
            vector.wait_ge(c.sV.h, c.sV.n)
            c.sV.inc(vector.tensor_tensor(b_c0, b_c0, SEEDT.unsqueeze(2),
                                          ALU.add))
            vector.wait_ge(c.sV.h, c.sV.n)
            c.sV.inc(vector.memset(a_c0, 0.0))
            vector.wait_ge(c.sV.h, c.sV.n)
            c.sV.inc(vector.tensor_tensor_scan(
                BETA2[i], ALPHA2[i], BETA2[i], 0.0, ALU.mult, ALU.add))
            vector.wait_ge(c.sV.h, c.sV.n)
            c.sV.inc(vector.tensor_copy(
                STATE[:, ssl],
                BETA2[i].rearrange("p (n t) -> p n t", t=512)[:, :, 511]))
            vector.wait_ge(c.sV.h, c.sV.n)

    v_scan = c.sV.n

    @block.sync
    def _(sync):
        sync.wait_ge(c.sT.h, t_bmm)
        tokC = [c.chC.issue(sync, c.rows[0:1, 0:2048],
                            c.bc_dram.ap()[16 + nh * 8:16 + nh * 8 + 4, :]),
                c.chC.issue(sync, c.rows[32:33, 0:2048],
                            c.bc_dram.ap()[16 + nh * 8 + 4:16 + nh * 8 + 8, :])]
        c.ssm_tokC = tokC
        c.ssm_last_restage = tokC[1]

    @block.tensor
    def _(tensor):
        for t in c.ssm_tokC:
            tensor.wait_ge(*t)
        tensor.wait_ge(c.sV.h, v_scan)
        for j in range(4):
            c.sT.inc(tensor.matmul(c.PB[j][:, :], c.ones_f32[0:1, :],
                                   c.rows[0:1, j * 512:(j + 1) * 512],
                                   start=True, stop=True))
        for j in range(4):
            c.sT.inc(tensor.matmul(c.PB[4 + j][:, :], c.ones_f32[32:33, :],
                                   c.rows[32:33, j * 512:(j + 1) * 512],
                                   start=True, stop=True))

    t_cmm = c.sT.n

    @block.vector
    def _(vector):
        vector.wait_ge(c.sT.h, t_cmm)
        for i, dc in enumerate(dcs):
            for n8 in range(8):
                c.sV.inc(vector.tensor_tensor(
                    BETA2[i][:, n8 * 512:(n8 + 1) * 512],
                    BETA2[i][:, n8 * 512:(n8 + 1) * 512],
                    c.PB[n8][:, :], ALU.mult))
            vector.wait_ge(c.sV.h, c.sV.n)
            c.sV.inc(vector.tensor_reduce(
                YR, BETA2[i].rearrange("p (n t) -> p t n", t=512), AX.X,
                ALU.add))
            vector.wait_ge(c.sV.h, c.sV.n)
            if nh == 0:
                xsl = X3C[:, dc * 514 + 1:dc * 514 + 513]
                c.sV.inc(vector.scalar_tensor_tensor(
                    YR, xsl, DCO[:, dc:dc + 1], YR, ALU.mult, ALU.add))
                vector.wait_ge(c.sV.h, c.sV.n)
            dst = c.res[:, dc * T + tc * 512: dc * T + (tc + 1) * 512]
            c.sV.inc(vector.tensor_add(dst, dst, YR))
            vector.wait_ge(c.sV.h, c.sV.n)



QMAXA = 32767.0


def _emit_ar(c, k, srcs, dsts, v_pre):
    block = c.block
    n = len(srcs)
    stg = c.ARF[0:1, 1432:1432 + n]

    @block.gpsimd
    def _(gpsimd):
        gpsimd.wait_ge(c.sV.h, v_pre)
        for i, s in enumerate(srcs):
            c.sG.inc(gpsimd.tensor_reduce(c.ARF[0:1, 1432 + i:1433 + i], s,
                                          AX.C, ALU.max))

    g_red = c.sG.n

    @block.sync
    def _(sync):
        sync.wait_ge(c.sG.h, g_red)
        c.tok_ar = c.chC.issue(sync, c.cc_in[k].ap()[:, 0:n], stg)

    @block.gpsimd
    def _(gpsimd):
        gpsimd.wait_ge(*c.tok_ar)
        c.sG.inc(gpsimd.collective_compute(
            "AllReduce", ALU.max, replica_groups=[c.ar_group],
            ins=[c.cc_in[k].ap()[:, 0:n]], outs=[c.cc_out[k].ap()[:, 0:n]]))

    g_cc = c.sG.n
    toks = []

    @block.sync
    def _(sync):
        sync.wait_ge(c.sG.h, g_cc)
        for i in range(n):
            raw = c.ARF[:, 1435 + i:1436 + i]
            toks.append(c.chC.issue(
                sync, raw, c.cc_out[k].ap()[:, i:i + 1].partition_broadcast(128)))

    @block.vector
    def _(vector):
        for t in toks:
            vector.wait_ge(*t)
        for i, (s_col, r_col, qden) in enumerate(dsts):
            raw = c.ARF[:, 1435 + i:1436 + i]
            c.sV.inc(vector.tensor_scalar(s_col, raw, 1.0 / qden, 1e-12,
                                          ALU.mult, ALU.max))
            vector.wait_ge(c.sV.h, c.sV.n)
            c.sV.inc(vector.reciprocal(r_col, s_col))
            vector.wait_ge(c.sV.h, c.sV.n)


def emit_l1ffn(c):
    block = c.block
    Bv = lambda a, b: c.ARB[:, a:b].bitcast(F32)
    X4F = Bv(0, 32768)
    X4I = c.ARB[:, 0:32768].bitcast(I32)
    NMB2 = Bv(36864, 40960)
    RSB2 = Bv(40960, 45056)
    S = lambda i: c.ARF[:, 1024 + i:1025 + i]
    mcol = S(0)
    sx, rx = S(1), S(2)
    sg, rg = S(3), S(4)
    su, ru = S(5), S(6)
    s3, r3 = S(7), S(8)
    s4, r4 = S(9), S(10)
    MXC = c.ARF[:, 1040:1048]
    MXWG = c.ARF[:, 2048:2144]
    MXWU = c.ARF[:, 2144:2240]
    MXW3 = c.ARF[:, 2240:2336]
    MXW4 = c.ARF[:, 2336:2432]
    mg, mu = c.ARF[:, 2432:2433], c.ARF[:, 2433:2434]

    emit_ln_stats(c)
    a_st, v_st = c.sA.n, c.sV.n

    @block.tensor
    def _(tensor):
        tensor.wait_ge(c.sA.h, a_st)
        tensor.wait_ge(c.sV.h, v_st)
        for q in range(4):
            sl = slice(q * 512, (q + 1) * 512)
            c.sT.inc(tensor.matmul(c.PB[q][:, :], c.ones_f32[0:1, :],
                                   c.rows[0:1, sl], start=True, stop=True))
            c.sT.inc(tensor.matmul(c.PB[4 + q][:, :], c.ones_f32[32:33, :],
                                   c.rows[32:33, sl], start=True, stop=True))

    t_stb = c.sT.n

    @block.scalar
    def _(scalar):
        scalar.wait_ge(c.sT.h, t_stb)
        for q in range(4):
            sl = slice(q * 512, (q + 1) * 512)
            c.sA.inc(scalar.activation(NMB2[:, sl], c.PB[q][:, :], AF.Copy,
                                       bias=0.0, scale=1.0))
            c.sA.inc(scalar.activation(RSB2[:, sl], c.PB[4 + q][:, :], AF.Copy,
                                       bias=0.0, scale=1.0))

    a_stb = c.sA.n
    tmpn = Bv(32768, 36864)

    @block.vector
    def _(vector):
        vector.wait_ge(c.sA.h, a_stb)
        for dc in range(DC):
            xsl = slice(dc * T, (dc + 1) * T)
            c.sV.inc(vector.tensor_add(tmpn, c.res[:, xsl], NMB2))
            vector.wait_ge(c.sV.h, c.sV.n)
            c.sV.inc(vector.tensor_mul(X4F[:, xsl], tmpn, RSB2))
            vector.wait_ge(c.sV.h, c.sV.n)
            c.sV.inc(vector.tensor_reduce(MXC[:, dc:dc + 1], X4F[:, xsl],
                                          AX.X, ALU.max, apply_absolute_value=True))
        vector.wait_ge(c.sV.h, c.sV.n)
        c.sV.inc(vector.tensor_reduce(mcol, MXC, AX.X, ALU.max))

    _emit_ar(c, 0, [mcol], [(sx, rx, QMAXA)], c.sV.n)
    v_ar1 = c.sV.n

    @block.scalar
    def _(scalar):
        scalar.wait_ge(c.sV.h, v_ar1)
        for dc in range(DC):
            xsl = slice(dc * T, (dc + 1) * T)
            c.sA.inc(scalar.activation(X4I[:, xsl], X4F[:, xsl], AF.Copy,
                                       bias=0.0, scale=rx))
            scalar.wait_ge(c.sA.h, c.sA.n)
            c.sA.inc(scalar.activation(X4F[:, xsl], X4I[:, xsl], AF.Copy,
                                       bias=0.0, scale=sx))
            scalar.wait_ge(c.sA.h, c.sA.n)

    c.l1_a_xq = c.sA.n

    _emit_l1_wmm(c, c.w_gate, c.gate_dram, MXWG)
    _emit_l1_wmm(c, c.w_up, c.up_dram, MXWU)

    @block.vector
    def _(vector):
        vector.wait_ge(c.sV.h, c.sV.n)
        c.sV.inc(vector.tensor_reduce(mg, MXWG, AX.X, ALU.max))
        c.sV.inc(vector.tensor_reduce(mu, MXWU, AX.X, ALU.max))

    _emit_ar(c, 1, [mg, mu], [(sg, rg, QMAXA), (su, ru, QMAXA)], c.sV.n)

    _emit_l1_downprep(c)
    _emit_l1_sig(c, MXW3)

    @block.vector
    def _(vector):
        vector.wait_ge(c.sV.h, c.sV.n)
        c.sV.inc(vector.tensor_reduce(mg, MXW3[:, 0:16], AX.X, ALU.max))

    _emit_ar(c, 2, [mg], [(s3, r3, QMAXA)], c.sV.n)

    _emit_l1_prod(c, MXW4)

    @block.vector
    def _(vector):
        vector.wait_ge(c.sV.h, c.sV.n)
        c.sV.inc(vector.tensor_reduce(mg, MXW4[:, 0:16], AX.X, ALU.max))

    _emit_ar(c, 3, [mg], [(s4, r4, QMAXA)], c.sV.n)

    _emit_l1_down(c)


def _emit_l1_wmm(c, wsrc, dst_dram, mxw):
    block = c.block
    Bv = lambda a, b: c.ARB[:, a:b].bitcast(F32)
    WST = [Bv(49152, 51200), Bv(51200, 53248)]
    WSC = Bv(53248, 55296)
    WSCI = c.ARB[:, 53248:55296].bitcast(I32)
    WQ2 = Bv(32768, 34816)
    STA = [Bv(45056, 47104), Bv(47104, 49152)]
    EVS = [Bv(36864 + 1024 * i, 37888 + 1024 * i) for i in range(4)]
    MMB = [c.PB[0], c.PB[1], c.PB[6], c.PB[7]]
    BMX = c.ARF[:, 1048:1080]
    SBL = c.ARF[:, 1080:1112]
    RBL = c.ARF[:, 1112:1144]

    v_entry, t_entry = c.sV.n, c.sT.n
    spt_carry = getattr(c, "l1_spt_carry", None)
    nmm_carry = getattr(c, "l1_nmm_carry", None)

    ntr = {}
    nst = {}
    nmm = {}
    nvq = {}
    nev = {}
    nmx = {}
    spt = {}
    ldt = {}

    for fc in range(24):
        @block.sync
        def _(sync, fc=fc):
            if fc >= 2:
                sync.wait_ge(c.sV.h, nvq[fc - 2])
            elif v_entry:
                sync.wait_ge(c.sV.h, v_entry)
            ldt[fc] = c.chW.issue(sync, WST[fc % 2],
                                  wsrc[fc * 128:(fc + 1) * 128, :])

        @block.vector
        def _(vector, fc=fc):
            vector.wait_ge(*ldt[fc])
            if fc >= 1:
                vector.wait_ge(c.sT.h, ntr[fc - 1])
            else:
                vector.wait_ge(c.sT.h, t_entry)
            c.sV.inc(vector.tensor_reduce(
                BMX, WST[fc % 2].rearrange("p (b k) -> p b k", k=32),
                AX.X, ALU.max, apply_absolute_value=True))
            vector.wait_ge(c.sV.h, c.sV.n)
            c.sV.inc(vector.tensor_scalar(SBL, BMX, 1.0 / 7.0, 1e-12,
                                          ALU.mult, ALU.max))
            vector.wait_ge(c.sV.h, c.sV.n)
            c.sV.inc(vector.reciprocal(RBL, SBL))
            vector.wait_ge(c.sV.h, c.sV.n)
            c.sV.inc(vector.tensor_tensor(
                WSC.rearrange("p (b k) -> p b k", k=32),
                WST[fc % 2].rearrange("p (b k) -> p b k", k=32),
                RBL.unsqueeze(2).broadcast_to([128, 32, 32]), ALU.mult))

        v_wsc = c.sV.n

        @block.scalar
        def _(scalar, fc=fc, v_wsc=v_wsc):
            scalar.wait_ge(c.sV.h, v_wsc)
            c.sA.inc(scalar.activation(WSCI, WSC, AF.Copy, bias=0.0, scale=1.0))
            scalar.wait_ge(c.sA.h, c.sA.n)
            c.sA.inc(scalar.activation(WSC, WSCI, AF.Copy, bias=0.0, scale=1.0))

        a_wq = c.sA.n

        @block.vector
        def _(vector, fc=fc, a_wq=a_wq):
            vector.wait_ge(c.sA.h, a_wq)
            c.sV.inc(vector.tensor_tensor(
                WQ2.rearrange("p (b k) -> p b k", k=32),
                WSC.rearrange("p (b k) -> p b k", k=32),
                SBL.unsqueeze(2).broadcast_to([128, 32, 32]), ALU.mult))

        nvq[fc] = c.sV.n
        a_tr_base = c.sA.n

        @block.tensor
        def _(tensor, fc=fc, a_tr_base=a_tr_base):
            tensor.wait_ge(c.sV.h, nvq[fc])
            for dc in range(DC):
                pb = c.PB[2 + dc % 4]
                if fc >= 1 and dc < 4:
                    tensor.wait_ge(c.sA.h, nst[fc - 1] - 8 + dc + 4 + 1)
                if dc >= 4:
                    tensor.wait_ge(c.sA.h, a_tr_base + (dc - 4) + 1)
                c.sT.inc(tensor.transpose(pb[:, 0:128],
                                          WQ2[:, dc * 128:(dc + 1) * 128],
                                          c.ident_t[:]))

        ntr[fc] = c.sT.n
        t_tr0 = ntr[fc] - DC

        @block.scalar
        def _(scalar, fc=fc, t_tr0=t_tr0):
            for dc in range(DC):
                pb = c.PB[2 + dc % 4]
                scalar.wait_ge(c.sT.h, t_tr0 + dc + 1)
                if dc == 0:
                    if fc >= 2:
                        scalar.wait_ge(c.sT.h, nmm[(fc - 2, 3)])
                    elif nmm_carry:
                        scalar.wait_ge(c.sT.h, nmm_carry[fc])
                c.sA.inc(scalar.activation(STA[fc % 2][:, dc * 128:(dc + 1) * 128],
                                           pb[0:128, 0:128], AF.Copy,
                                           bias=0.0, scale=1.0))

        nst[fc] = c.sA.n

        for tq in range(4):
            g = fc * 4 + tq
            bank = MMB[g % 4]

            @block.tensor
            def _(tensor, fc=fc, tq=tq, g=g, bank=bank):
                tensor.wait_ge(c.sA.h, nst[fc])
                if fc == 0 and tq == 0:
                    tensor.wait_ge(c.sA.h, c.l1_a_xq)
                if g >= 4:
                    tensor.wait_ge(c.sA.h, nev[g - 4])
                    tensor.wait_ge(c.sV.h, nmx[g - 4])
                for dc in range(DC):
                    c.sT.inc(tensor.matmul(
                        bank[:, :], STA[fc % 2][:, dc * 128:(dc + 1) * 128],
                        Bv(0, 32768)[:, dc * T + tq * 512: dc * T + (tq + 1) * 512],
                        start=(dc == 0), stop=(dc == DC - 1)))

            nmm[(fc, tq)] = c.sT.n

            @block.scalar
            def _(scalar, fc=fc, tq=tq, g=g, bank=bank):
                scalar.wait_ge(c.sT.h, nmm[(fc, tq)])
                if g >= 4:
                    scalar.wait_ge(*spt[g - 4])
                elif spt_carry:
                    scalar.wait_ge(*spt_carry[g])
                c.sA.inc(scalar.activation(EVS[g % 4], bank[:, :], AF.Copy,
                                           bias=0.0, scale=1.0))

            nev[g] = c.sA.n

            @block.vector
            def _(vector, fc=fc, tq=tq, g=g, bank=bank):
                vector.wait_ge(c.sT.h, nmm[(fc, tq)])
                c.sV.inc(vector.tensor_reduce(mxw[:, g:g + 1], bank[:, :],
                                              AX.X, ALU.max,
                                              apply_absolute_value=True))

            nmx[g] = c.sV.n

            @block.sync
            def _(sync, g=g):
                sync.wait_ge(c.sA.h, nev[g])
                spt[g] = c.chO.issue(sync, dst_dram.ap()[g], EVS[g % 4])

    c.l1_spt_carry = (spt[92], spt[93], spt[94], spt[95])
    c.l1_nmm_carry = (nmm[(22, 3)], nmm[(23, 3)])


def _emit_l1_downprep(c):
    block = c.block
    Bv = lambda a, b: c.ARB[:, a:b].bitcast(F32)
    WSTD = [Bv(0, 6144), Bv(6144, 12288)]
    WSCD = Bv(12288, 18432)
    WSCDI = c.ARB[:, 12288:18432].bitcast(I32)
    WQ2D = Bv(18432, 24576)
    DSTT = [Bv(24576 + 256 * i, 24832 + 256 * i) for i in range(4)]
    BMX9 = c.ARF[:, 1144:1240]
    SBL9 = c.ARF[:, 1240:1336]
    RBL9 = c.ARF[:, 1336:1432]

    ldt = {}
    nvq = {}
    ntr = {}
    nev = {}
    wbt = {}
    aev = {}
    t0 = c.sT.n

    for ch in range(8):
        @block.sync
        def _(sync, ch=ch):
            if ch == 0:
                sync.wait_ge(c.sT.h, t0)
            if ch >= 2:
                sync.wait_ge(c.sV.h, nvq[ch - 2])
            ldt[ch] = c.chW.issue(sync, WSTD[ch % 2],
                                  c.w_down[ch * 128:(ch + 1) * 128, :])

        @block.vector
        def _(vector, ch=ch):
            vector.wait_ge(*ldt[ch])
            if ch >= 1:
                vector.wait_ge(c.sT.h, ntr[ch - 1])
            c.sV.inc(vector.tensor_reduce(
                BMX9, WSTD[ch % 2].rearrange("p (b k) -> p b k", k=32),
                AX.X, ALU.max, apply_absolute_value=True))
            vector.wait_ge(c.sV.h, c.sV.n)
            c.sV.inc(vector.tensor_scalar(SBL9, BMX9, 1.0 / 7.0, 1e-12,
                                          ALU.mult, ALU.max))
            vector.wait_ge(c.sV.h, c.sV.n)
            c.sV.inc(vector.reciprocal(RBL9, SBL9))
            vector.wait_ge(c.sV.h, c.sV.n)
            if ch >= 1:
                vector.wait_ge(c.sA.h, nev[ch - 1])
            c.sV.inc(vector.tensor_tensor(
                WSCD.rearrange("p (b k) -> p b k", k=32),
                WSTD[ch % 2].rearrange("p (b k) -> p b k", k=32),
                RBL9.unsqueeze(2).broadcast_to([128, 96, 32]), ALU.mult))

        v_wsc = c.sV.n

        @block.scalar
        def _(scalar, ch=ch, v_wsc=v_wsc):
            scalar.wait_ge(c.sV.h, v_wsc)
            c.sA.inc(scalar.activation(WSCDI, WSCD, AF.Copy, bias=0.0, scale=1.0))
            scalar.wait_ge(c.sA.h, c.sA.n)
            c.sA.inc(scalar.activation(WSCD, WSCDI, AF.Copy, bias=0.0, scale=1.0))

        a_wq = c.sA.n

        @block.vector
        def _(vector, ch=ch, a_wq=a_wq):
            vector.wait_ge(c.sA.h, a_wq)
            c.sV.inc(vector.tensor_tensor(
                WQ2D.rearrange("p (b k) -> p b k", k=32),
                WSCD.rearrange("p (b k) -> p b k", k=32),
                SBL9.unsqueeze(2).broadcast_to([128, 96, 32]), ALU.mult))

        nvq[ch] = c.sV.n

        for fc in range(24):
            i = ch * 24 + fc

            @block.tensor
            def _(tensor, ch=ch, fc=fc, i=i):
                if fc == 0:
                    tensor.wait_ge(c.sV.h, nvq[ch])
                if i >= 4:
                    tensor.wait_ge(c.sA.h, aev[i - 4])
                c.sT.inc(tensor.transpose(c.PB[2 + i % 4][:, 0:128],
                                          WQ2D[:, fc * 128:(fc + 1) * 128],
                                          c.ident_t[:]))

            t_tr = c.sT.n

            @block.scalar
            def _(scalar, ch=ch, fc=fc, i=i, t_tr=t_tr):
                scalar.wait_ge(c.sT.h, t_tr)
                if i >= 4:
                    scalar.wait_ge(*wbt[i - 4])
                c.sA.inc(scalar.activation(DSTT[i % 4],
                                           c.PB[2 + i % 4][0:128, 0:128],
                                           AF.Copy, bias=0.0, scale=1.0))
                aev[i] = c.sA.n
                scalar.wait_ge(c.sA.h, aev[i])
                wbt[i] = c.chQ.issue(
                    scalar, c.wdt_dram.ap()[fc, :, ch * 128:(ch + 1) * 128],
                    DSTT[i % 4])

        ntr[ch] = c.sT.n
        nev[ch] = c.sA.n


def _emit_l1_sig(c, MXW3):
    block = c.block
    Bv = lambda a, b: c.ARB[:, a:b].bitcast(F32)
    W = 6
    NBATCH = 96 // W
    GIN = [Bv(25600, 31744), Bv(31744, 37888)]
    GINI = [c.ARB[:, 25600:31744].bitcast(I32),
            c.ARB[:, 31744:37888].bitcast(I32)]
    GQF = [Bv(37888, 44032), Bv(44032, 50176)]
    sg, rg = c.ARF[:, 1027:1028], c.ARF[:, 1028:1029]

    v_ar2 = c.sV.n
    ldt = {}
    na = {}
    nv = {}
    spt = {}

    @block.sync
    def _(sync):
        c.chO.drain(sync)

    def tiles(b):
        return c.gate_dram.ap()[b * W:(b + 1) * W].transpose([1, 0, 2])

    for b in range(NBATCH):
        @block.sync
        def _(sync, b=b):
            if b >= 2:
                sync.wait_ge(*spt[b - 2])
            ldt[b] = c.chW.issue(sync, GIN[b % 2], tiles(b))

        @block.scalar
        def _(scalar, b=b, a_prev=c.sA.n):
            scalar.wait_ge(*ldt[b])
            scalar.wait_ge(c.sA.h, a_prev)
            if b == 0:
                scalar.wait_ge(c.sV.h, v_ar2)
            if b >= 2:
                scalar.wait_ge(c.sV.h, nv[b - 2])
            a1 = c.sA.n + 1
            c.sA.inc(scalar.activation(GINI[b % 2], GIN[b % 2], AF.Copy,
                                       bias=0.0, scale=rg))
            scalar.wait_ge(c.sA.h, c.sA.n)
            c.sA.inc(scalar.activation(GQF[b % 2], GINI[b % 2], AF.Copy,
                                       bias=0.0, scale=sg))
            scalar.wait_ge(c.sA.h, c.sA.n)
            c.sA.inc(scalar.activation(GIN[b % 2], GQF[b % 2], AF.Sigmoid,
                                       bias=0.0, scale=1.0))
            scalar.wait_ge(c.sA.h, c.sA.n)
            c.sA.inc(scalar.activation(GINI[b % 2], GIN[b % 2], AF.Copy,
                                       bias=0.0, scale=65536.0))
            scalar.wait_ge(c.sA.h, c.sA.n)
            c.sA.inc(scalar.activation(GIN[b % 2], GINI[b % 2], AF.Copy,
                                       bias=0.0, scale=1.0))
            na[b] = (a1, c.sA.n)

        @block.vector
        def _(vector, b=b):
            vector.wait_ge(c.sA.h, na[b][1])
            c.sV.inc(vector.tensor_scalar(GIN[b % 2], GIN[b % 2], 65535.0,
                                          1.0 / 65536.0, ALU.min, ALU.mult))
            vector.wait_ge(c.sV.h, c.sV.n)
            c.sV.inc(vector.tensor_mul(GIN[b % 2], GQF[b % 2], GIN[b % 2]))
            vector.wait_ge(c.sV.h, c.sV.n)
            c.sV.inc(vector.tensor_reduce(MXW3[:, b:b + 1], GIN[b % 2],
                                          AX.X, ALU.max, apply_absolute_value=True))
            nv[b] = c.sV.n

        @block.sync
        def _(sync, b=b):
            sync.wait_ge(c.sV.h, nv[b])
            spt[b] = c.chO.issue(sync, tiles(b), GIN[b % 2])


def _emit_l1_prod(c, MXW4):
    block = c.block
    Bv = lambda a, b: c.ARB[:, a:b].bitcast(F32)
    W = 6
    NBATCH = 96 // W
    GSI = [Bv(25600, 31744), Bv(31744, 37888)]
    GSII = [c.ARB[:, 25600:31744].bitcast(I32),
            c.ARB[:, 31744:37888].bitcast(I32)]
    UIN = [Bv(37888, 44032), Bv(44032, 50176)]
    UINI = [c.ARB[:, 37888:44032].bitcast(I32),
            c.ARB[:, 44032:50176].bitcast(I32)]
    su, ru = c.ARF[:, 1029:1030], c.ARF[:, 1030:1031]
    s3, r3 = c.ARF[:, 1031:1032], c.ARF[:, 1032:1033]

    v_ar3 = c.sV.n
    ldt = {}
    na = {}
    nv = {}
    spt = {}

    @block.sync
    def _(sync):
        c.chO.drain(sync)

    def gtiles(b):
        return c.gate_dram.ap()[b * W:(b + 1) * W].transpose([1, 0, 2])

    def utiles(b):
        return c.up_dram.ap()[b * W:(b + 1) * W].transpose([1, 0, 2])

    for b in range(NBATCH):
        @block.sync
        def _(sync, b=b):
            if b >= 2:
                sync.wait_ge(*spt[b - 2])
                sync.wait_ge(c.sV.h, nv[b - 2])
            ldt[b] = (c.chW.issue(sync, GSI[b % 2], gtiles(b)),
                      c.chL.issue(sync, UIN[b % 2], utiles(b)))

        @block.scalar
        def _(scalar, b=b, a_prev=c.sA.n):
            scalar.wait_ge(*ldt[b][0])
            scalar.wait_ge(*ldt[b][1])
            scalar.wait_ge(c.sA.h, a_prev)
            if b == 0:
                scalar.wait_ge(c.sV.h, v_ar3)
            c.sA.inc(scalar.activation(GSII[b % 2], GSI[b % 2], AF.Copy,
                                       bias=0.0, scale=r3))
            scalar.wait_ge(c.sA.h, c.sA.n)
            c.sA.inc(scalar.activation(GSI[b % 2], GSII[b % 2], AF.Copy,
                                       bias=0.0, scale=s3))
            scalar.wait_ge(c.sA.h, c.sA.n)
            c.sA.inc(scalar.activation(UINI[b % 2], UIN[b % 2], AF.Copy,
                                       bias=0.0, scale=ru))
            scalar.wait_ge(c.sA.h, c.sA.n)
            c.sA.inc(scalar.activation(UIN[b % 2], UINI[b % 2], AF.Copy,
                                       bias=0.0, scale=su))
            na[b] = c.sA.n

        @block.vector
        def _(vector, b=b):
            vector.wait_ge(c.sA.h, na[b])
            c.sV.inc(vector.tensor_mul(GSI[b % 2], GSI[b % 2], UIN[b % 2]))
            vector.wait_ge(c.sV.h, c.sV.n)
            c.sV.inc(vector.tensor_reduce(MXW4[:, b:b + 1], GSI[b % 2],
                                          AX.X, ALU.max, apply_absolute_value=True))
            nv[b] = c.sV.n

        @block.sync
        def _(sync, b=b):
            sync.wait_ge(c.sV.h, nv[b])
            spt[b] = c.chO.issue(sync, utiles(b), GSI[b % 2])


def _emit_l1_down(c):
    block = c.block
    Bv = lambda a, b: c.ARB[:, a:b].bitcast(F32)
    OF = Bv(0, 24576)
    PIN = [Bv(24576 + 1024 * i, 25600 + 1024 * i) for i in range(4)]
    OI = c.ARB[:, 28672:29696].bitcast(I32)
    STD = [Bv(45056, 47104), Bv(47104, 49152)]
    s4, r4 = c.ARF[:, 1033:1034], c.ARF[:, 1034:1035]

    v_ar4 = c.sV.n

    @block.sync
    def _(sync):
        c.chO.drain(sync)
        for sm in c.chQ.sems:
            sync.wait_ge(sm.h, sm.n)

    t_dn_prev = c.sT.n
    a_prev_end = c.sA.n

    for tq in range(4):
        ldt = {}
        stt = {}
        na = {}
        nmm = {}
        v_res_prev = c.sV.n

        for fc in range(24):
            @block.sync
            def _(sync, fc=fc, tq=tq, t_dn_prev=t_dn_prev, a_prev_end=a_prev_end):
                if fc >= 4:
                    sync.wait_ge(c.sA.h, na[fc - 4])
                elif tq > 0:
                    sync.wait_ge(c.sA.h, a_prev_end)
                ldt[fc] = c.chW.issue(sync, PIN[fc % 4],
                                      c.up_dram.ap()[fc * 4 + tq])
                if fc >= 2:
                    sync.wait_ge(c.sT.h, nmm[fc - 2])
                elif tq > 0:
                    sync.wait_ge(c.sT.h, t_dn_prev)
                stt[fc] = c.chL.issue(sync, STD[fc % 2], c.wdt_dram.ap()[fc])

            @block.scalar
            def _(scalar, fc=fc, tq=tq, t_dn_prev=t_dn_prev, a_prev=c.sA.n):
                scalar.wait_ge(*ldt[fc])
                scalar.wait_ge(c.sA.h, a_prev)
                if fc == 0 and tq == 0:
                    scalar.wait_ge(c.sV.h, v_ar4)
                if fc == 0 and tq > 0:
                    scalar.wait_ge(c.sT.h, t_dn_prev)
                c.sA.inc(scalar.activation(OI, PIN[fc % 4], AF.Copy,
                                           bias=0.0, scale=r4))
                scalar.wait_ge(c.sA.h, c.sA.n)
                c.sA.inc(scalar.activation(OF[:, fc * 512:(fc + 1) * 512], OI,
                                           AF.Copy, bias=0.0, scale=s4))
                na[fc] = c.sA.n

            @block.tensor
            def _(tensor, fc=fc, tq=tq, v_res_prev=v_res_prev):
                tensor.wait_ge(*stt[fc])
                tensor.wait_ge(c.sA.h, na[fc])
                if fc == 0:
                    tensor.wait_ge(c.sV.h, v_res_prev)
                for dc in range(DC):
                    c.sT.inc(tensor.matmul(
                        c.PB[dc][:, :], STD[fc % 2][:, dc * 128:(dc + 1) * 128],
                        OF[:, fc * 512:(fc + 1) * 512],
                        start=(fc == 0), stop=(fc == 23)))
                nmm[fc] = c.sT.n

        t_dn = c.sT.n

        @block.vector
        def _(vector, tq=tq, t_dn=t_dn):
            vector.wait_ge(c.sT.h, t_dn)
            for dc in range(DC):
                dst = c.res[:, dc * T + tq * 512: dc * T + (tq + 1) * 512]
                c.sV.inc(vector.tensor_add(dst, dst, c.PB[dc][:, :]))

        t_dn_prev = t_dn
        a_prev_end = c.sA.n


def emit_ph6_store(c):
    block = c.block
    t0, a0 = c.sT.n, c.sA.n
    tok = {}

    @block.tensor
    def _(tensor):
        _wait_all(tensor, c)
        for tt in range(TT):
            for dc in range(DC):
                pb = c.PB[dc]
                if tt >= 1:
                    tensor.wait_ge(c.sA.h, a0 + (tt - 1) * DC + dc + 1)
                c.sT.inc(tensor.transpose(
                    pb[:, 0:128],
                    c.res[:, dc * T + tt * 128: dc * T + tt * 128 + 128],
                    c.ident_t[:]))

    @block.sync
    def _(sync):
        for tt in range(TT):
            buf = c.stage[:, (tt % 2) * D:(tt % 2) * D + D]
            sync.wait_ge(c.sA.h, a0 + (tt + 1) * DC)
            tok[tt] = c.chO.issue(sync, c.out[tt * 128:(tt + 1) * 128, :], buf)
        c.chO.drain(sync)

    @block.scalar
    def _(scalar):
        for tt in range(TT):
            buf = c.stage[:, (tt % 2) * D:(tt % 2) * D + D]
            if tt >= 2:
                scalar.wait_ge(*tok[tt - 2])
            for dc in range(DC):
                scalar.wait_ge(c.sT.h, t0 + tt * DC + dc + 1)
                c.sA.inc(scalar.activation(
                    buf[:, dc * 128:(dc + 1) * 128],
                    c.PB[dc][:, 0:128], AF.Copy, bias=0.0, scale=1.0))


def build(stop_after=None, ar_group=None):
    nc = bass.Bass("TRN2", target_bir_lowering=False, dynamic_dma_scratch_size=2048)
    c = Ctx()
    c.nc = nc

    c.x_in = nc.dram_tensor("x_in", [T, D], F32, kind="ExternalInput").ap()
    c.out = nc.dram_tensor("out", [T, D], F32, kind="ExternalOutput").ap()
    c.ident = nc.dram_tensor("ident", [128, 128], F32, kind="ExternalInput").ap()
    c.ones_in = nc.dram_tensor("ones_in", [128, 128], F32, kind="ExternalInput").ap()
    c.ropeA_in = nc.dram_tensor("ropeA_in", [128, T], F32, kind="ExternalInput").ap()
    c.ropeB_in = nc.dram_tensor("ropeB_in", [128, T], F32, kind="ExternalInput").ap()
    c.masks_in = nc.dram_tensor("masks_in", [128, 896], BF16, kind="ExternalInput").ap()
    c.bias_in = nc.dram_tensor("bias_in", [128, 32], F32, kind="ExternalInput").ap()
    c.w_q = nc.dram_tensor("w_q", [D, D], F32, kind="ExternalInput").ap()
    c.w_k = nc.dram_tensor("w_k", [D, D], F32, kind="ExternalInput").ap()
    c.w_v = nc.dram_tensor("w_v", [D, D], F32, kind="ExternalInput").ap()
    c.w_o = nc.dram_tensor("w_o", [D, D], F32, kind="ExternalInput").ap()
    c.w1 = nc.dram_tensor("w1", [FF0, D], F32, kind="ExternalInput").ap()
    c.w2 = nc.dram_tensor("w2", [D, FF0], F32, kind="ExternalInput").ap()
    c.bias_ffn_in = nc.dram_tensor("bias_ffn_in", [128, 40], F32, kind="ExternalInput").ap()
    c.scal_dram = nc.dram_tensor("scal_dram", [1, 8], F32)
    c.wq1t_dram = nc.dram_tensor("wq1t_dram", [32, 1024, 128], BF16)
    c.wq2t_dram = nc.dram_tensor("wq2t_dram", [32, 128, 1024], BF16)
    c.bc_dram = nc.dram_tensor("bc_dram", [32, 512], F32)
    c.ln_dram = nc.dram_tensor("ln_dram", [2, 2048], F32)
    c.conv_k = nc.dram_tensor("conv_k", [3, 1, D], F32, kind="ExternalInput").ap()
    c.a_log = nc.dram_tensor("a_log", [D, N_SSM], F32, kind="ExternalInput").ap()
    c.w_bc = nc.dram_tensor("w_bc", [D, 2 * N_SSM], F32, kind="ExternalInput").ap()
    c.b_bc = nc.dram_tensor("b_bc", [2 * N_SSM], F32, kind="ExternalInput").ap()
    c.dcoeff = nc.dram_tensor("dcoeff", [D], F32, kind="ExternalInput").ap()
    c.w_dt = nc.dram_tensor("w_dt", [D, DT_RANK], F32, kind="ExternalInput").ap()
    c.b_dt = nc.dram_tensor("b_dt", [DT_RANK], F32, kind="ExternalInput").ap()
    c.w_dtp = nc.dram_tensor("w_dtp", [DT_RANK, D], F32, kind="ExternalInput").ap()
    c.b_dtp = nc.dram_tensor("b_dtp", [D], F32, kind="ExternalInput").ap()
    c.w_gate = nc.dram_tensor("w_gate", [FF1, D], F32, kind="ExternalInput").ap()
    c.w_up = nc.dram_tensor("w_up", [FF1, D], F32, kind="ExternalInput").ap()
    c.w_down = nc.dram_tensor("w_down", [D, FF1], F32, kind="ExternalInput").ap()
    c.gate_dram = nc.dram_tensor("gate_dram", [96, 128, 512], F32)
    c.up_dram = nc.dram_tensor("up_dram", [96, 128, 512], F32)
    c.wdt_dram = nc.dram_tensor("wdt_dram", [24, 128, 1024], F32)
    c.cc_in = [nc.dram_tensor(f"cc_in{i}", [1, 2], F32) for i in range(4)]
    c.cc_out = [nc.dram_tensor(f"cc_out{i}", [1, 2], F32) for i in range(4)]
    c.ar_group = list(range(8)) if ar_group is None else ar_group

    ctx = ExitStack()
    with ctx:
        e = ctx.enter_context
        c.res = e(nc.sbuf_tensor("res", [128, DC * T], F32))
        c.rows = e(nc.sbuf_tensor("rows", [128, T], F32))
        c.ropeA = e(nc.sbuf_tensor("ropeA", [128, T], F32))
        c.ropeB = e(nc.sbuf_tensor("ropeB", [128, T], F32))
        c.ident_t = e(nc.sbuf_tensor("ident_t", [128, 128], F32))
        c.ones_f32 = e(nc.sbuf_tensor("ones_f32", [128, 128], F32))
        c.ones_bf = e(nc.sbuf_tensor("ones_bf", [128, 64], BF16))
        c.bias_qkv = e(nc.sbuf_tensor("bias_qkv", [128, 32], F32))
        c.bias_ffn = e(nc.sbuf_tensor("bias_ffn", [128, 40], F32))
        c.nrm = e(nc.sbuf_tensor("nrm", [1, 16], F32))
        c.ARF = e(nc.sbuf_tensor("ARF", [128, 5136], F32))
        c.ARB = e(nc.sbuf_tensor("ARB", [128, 57216], BF16))

        c.stage = c.ARF[:, 0:2048]
        c.sqbuf = c.ARF[:, 0:4096]
        c.tmp1 = c.ARF[:, 4096:4608]
        c.xfull = c.ARF[:, 4608:5120]
        c.AVS = c.ARF[:, 3072:5120]
        c.biasA = c.ARF[:, 5120:5121]
        c.biasB = c.ARF[:, 5121:5122]
        c.qtmp = c.ARF[:, 2048:2176]
        c.bbc_col = c.ropeA[64:96, 169:170]
        c.acol = c.ARF[:, 5126:5127]
        c.gcol = c.ARF[:, 5127:5128]

        def B(a, b):
            return c.ARB[:, a:b]
        c.XH = B(0, 16384)
        c.XL = B(16384, 32768)
        c.QH = B(32768, 34816)
        c.QL = B(34816, 36864)
        c.KH = B(36864, 38912)
        c.KL = B(38912, 40960)
        c.VSA = B(40960, 43008)
        c.VSB = B(43008, 45056)
        c.WTQH = B(45056, 46080)
        c.WTQL = B(46080, 47104)
        c.WTKH = B(47104, 48128)
        c.WTKL = B(48128, 49152)
        c.WTVH = B(49152, 50176)
        c.WTVL = B(50176, 51200)
        c.PHt = [B(51200, 52224), B(52224, 53248)]
        c.PLt = [B(53248, 54272), B(54272, 55296)]
        c.OH = B(55296, 55808)
        c.OL = B(55808, 56320)
        c.MASKS = B(56320, 57216)
        c.sbh = B(32768, 36864)
        c.sbl = B(36864, 40960)
        c.sqh = B(40960, 45056)
        c.sql = B(45056, 49152)

        c.PB = [e(nc.psum_tensor(f"pb{i}", [128, 512], F32)) for i in range(8)]

        c.sS = Sem(e(nc.semaphore("sS")))
        c.sG = Sem(e(nc.semaphore("sG")))
        c.sT = Sem(e(nc.semaphore("sT")))
        c.sV = Sem(e(nc.semaphore("sV")))
        c.sA = Sem(e(nc.semaphore("sA")))

        c.block = e(nc.Block())

        c.chL = DmaChan(nc, ctx, "chL", 4)
        c.chO = DmaChan(nc, ctx, "chO", 4)
        c.chW = DmaChan(nc, ctx, "chW", 4)
        c.chC = DmaChan(nc, ctx, "chC", 2)
        c.chQ = DmaChan(nc, ctx, "chQ", 4)
        c.wtok = {}

        @c.block.sync
        def _(sync):
            c.tok_ident = c.chL.issue(sync, c.ident_t[:], c.ident)
            c.tok_ones = c.chL.issue(sync, c.ones_f32[:], c.ones_in)
            c.tok_ropeA = c.chL.issue(sync, c.ropeA[:], c.ropeA_in)
            c.tok_ropeB = c.chL.issue(sync, c.ropeB[:], c.ropeB_in)
            c.tok_masks = c.chL.issue(sync, c.MASKS, c.masks_in)
            c.tok_bias_in = c.chL.issue(sync, c.bias_qkv[:], c.bias_in)
            c.tok_bias_ffn = c.chL.issue(sync, c.bias_ffn[:], c.bias_ffn_in)

        @c.block.vector
        def _(vector):
            vector.wait_ge(*c.tok_ones)
            c.sV.inc(vector.tensor_copy(c.ones_bf[:, 0:64], c.ones_f32[:, 0:64]))

        emit_ph0_load(c)

        @c.block.tensor
        def _(tensor):
            tensor.wait_ge(*c.tok_ones)
            tensor.wait_ge(*c.tok_ropeA)
            tensor.wait_ge(*c.tok_ropeB)
            tensor.wait_ge(*c.tok_masks)
            tensor.wait_ge(*c.tok_bias_in)
            tensor.wait_ge(c.sV.h, c.sV.n)

        if stop_after != "ph0":
            emit_ln(c, c.XH, c.XL)
            if stop_after != "ln1":
                emit_l0_attn(c)
                if stop_after != "attn":
                    emit_ffn0(c)
                    if stop_after != "ffn0":
                        emit_ssm(c)
                        if stop_after != "ssm":
                            emit_l1ffn(c)

        emit_ph6_store(c)

    return nc


def _host_consts():
    ident = np.eye(128, dtype=np.float32)
    ones = np.ones((128, 128), dtype=np.float32)
    half = DK // 2
    freq = (1.0 / 10000.0 ** (2.0 * np.arange(half, dtype=np.float64) / DK))
    ang = np.arange(T, dtype=np.float64)[:, None] * freq[None, :]
    cos = np.cos(ang).astype(np.float32)
    sin = np.sin(ang).astype(np.float32)
    ropeA = np.zeros((128, T), dtype=np.float32)
    ropeB = np.zeros((128, T), dtype=np.float32)
    for j in range(64):
        i = j // 2
        ropeA[j] = cos[:, i]
        ropeB[j] = (np.float32(-1.0) if j % 2 == 0 else np.float32(1.0)) * sin[:, i]
    ropeA[64:128] = ropeA[0:64]
    ropeB[64:128] = ropeB[0:64]
    import ml_dtypes
    k_rel = np.arange(128)[:, None]
    u = np.arange(896)[None, :]
    masks = np.where(k_rel > u - 384, -1e30, 0.0).astype(ml_dtypes.bfloat16)
    return ident, ones, ropeA, ropeB, masks


_PERM = None

def _perm_idx():
    global _PERM
    if _PERM is None:
        j = np.arange(128)
        h = j // 64
        jj = j % 64
        _PERM = h * 64 + (jj % 2) * 32 + (jj // 2)
    return _PERM


def _bias_tile(inputs):
    bias = np.zeros((128, 32), dtype=np.float32)
    perm = _perm_idx()
    bq = np.asarray(inputs["L0_attn_WQ_b"], np.float32)
    bk = np.asarray(inputs["L0_attn_WK_b"], np.float32)
    bv = np.asarray(inputs["L0_attn_WV_b"], np.float32)
    bo = np.asarray(inputs["L0_attn_Wo_b"], np.float32)
    for p in range(8):
        bias[:, p] = bq[p * 128 + perm]
        bias[:, 8 + p] = bk[p * 128 + perm]
        bias[:, 16 + p] = bv[p * 128:(p + 1) * 128]
        bias[:, 24 + p] = bo[p * 128:(p + 1) * 128]
    return bias


_NC_CACHE = None


def _get_nc():
    global _NC_CACHE
    if _NC_CACHE is None:
        _NC_CACHE = build()
    return _NC_CACHE


def kernel(**inputs):
    nc = _get_nc()
    x = np.ascontiguousarray(np.asarray(inputs["L0_attn_x"], dtype=np.float32))
    ident, ones, ropeA, ropeB, masks = _host_consts()
    bias = _bias_tile(inputs)
    wq = np.ascontiguousarray(np.asarray(inputs["L0_attn_WQ_w"], np.float32))
    wk = np.ascontiguousarray(np.asarray(inputs["L0_attn_WK_w"], np.float32))
    wv = np.ascontiguousarray(np.asarray(inputs["L0_attn_WV_w"], np.float32))
    wo = np.ascontiguousarray(np.asarray(inputs["L0_attn_Wo_w"], np.float32))
    w1 = np.ascontiguousarray(np.asarray(inputs["L0_ffn_W1"], np.float32))
    w2 = np.ascontiguousarray(np.asarray(inputs["L0_ffn_W2"], np.float32))
    bias_ffn = np.zeros((128, 40), dtype=np.float32)
    b1 = np.asarray(inputs["L0_ffn_b1"], np.float32)
    b2 = np.asarray(inputs["L0_ffn_b2"], np.float32)
    for fc in range(32):
        bias_ffn[:, fc] = b1[fc * 128:(fc + 1) * 128]
    for dco in range(8):
        bias_ffn[:, 32 + dco] = b2[dco * 128:(dco + 1) * 128]
    in_maps = []
    for b in range(8):
        m = {"x_in": np.ascontiguousarray(x[b]), "ident": ident, "ones_in": ones,
             "ropeA_in": ropeA, "ropeB_in": ropeB, "masks_in": masks,
             "bias_in": bias, "w_q": wq, "w_k": wk, "w_v": wv, "w_o": wo,
             "w1": w1, "w2": w2, "bias_ffn_in": bias_ffn,
             "conv_k": np.ascontiguousarray(np.asarray(inputs["L1_attn_conv_kernel"], np.float32)),
             "a_log": np.ascontiguousarray(np.asarray(inputs["L1_attn_A_log"], np.float32)),
             "w_bc": np.ascontiguousarray(np.asarray(inputs["L1_attn_W_bc"], np.float32)),
             "b_bc": np.ascontiguousarray(np.asarray(inputs["L1_attn_b_bc"], np.float32)),
             "dcoeff": np.ascontiguousarray(np.asarray(inputs["L1_attn_Dcoeff"], np.float32)),
             "w_dt": np.ascontiguousarray(np.asarray(inputs["L1_attn_W_dt"], np.float32)),
             "b_dt": np.ascontiguousarray(np.asarray(inputs["L1_attn_b_dt"], np.float32)),
             "w_dtp": np.ascontiguousarray(np.asarray(inputs["L1_attn_W_dtp"], np.float32)),
             "b_dtp": np.ascontiguousarray(np.asarray(inputs["L1_attn_b_dtp"], np.float32)),
             "w_gate": np.ascontiguousarray(np.asarray(inputs["L1_ffn_w_gate"], np.float32)),
             "w_up": np.ascontiguousarray(np.asarray(inputs["L1_ffn_w_up"], np.float32)),
             "w_down": np.ascontiguousarray(np.asarray(inputs["L1_ffn_w_down"], np.float32))}
        in_maps.append(m)
    res = run_bass_kernel_spmd(nc, in_maps, core_ids=list(range(8)))
    out = np.stack([np.asarray(res.results[b]["out"], dtype=np.float32) for b in range(8)], axis=0)
    return out


# version 06
# speedup vs baseline: 1.1218x; 1.0219x over previous
import numpy as np
from contextlib import ExitStack

import concourse.bass as bass
import concourse.mybir as mybir
from concourse.bass_utils import run_bass_kernel_spmd

F32 = mybir.dt.float32
BF16 = mybir.dt.bfloat16
I32 = mybir.dt.int32
AF = mybir.ActivationFunctionType
ALU = mybir.AluOpType
AX = mybir.AxisListType

D = 1024
T = 2048
NH = 16
DK = 64
DC = D // 128
TT = T // 128
FF0 = 4096
FF1 = 3072
N_SSM = 16
DT_RANK = 64
EPS_LN = 1e-5


class Sem:
    __slots__ = ("h", "n")
    def __init__(self, h):
        self.h = h
        self.n = 0
    def inc(self, inst, k=1):
        inst.then_inc(self.h, k)
        self.n += k
        return inst


class DmaChan:
    def __init__(self, nc, ctx, name, depth):
        self.sems = [Sem(ctx.enter_context(nc.semaphore(f"{name}{i}")))
                     for i in range(depth)]
        self.i = 0

    def issue(self, eng, out, in_):
        s = self.sems[self.i % len(self.sems)]
        self.i += 1
        eng.wait_ge(s.h, s.n)
        s.inc(eng.dma_start(out=out, in_=in_), 16)
        return (s.h, s.n)

    def drain(self, eng):
        for s in self.sems:
            eng.wait_ge(s.h, s.n)

    def snapshot(self):
        return (self.i, [s.n for s in self.sems])

    def plan(self, snap, j):
        i0, vals = snap
        k = len(self.sems)
        idx = (i0 + j) % k
        uses = j // k + 1
        return (self.sems[idx].h, vals[idx] + 16 * uses)


class Ctx:
    pass


def _wait_all(eng, c):
    for s in (c.sV, c.sA, c.sT, c.sG, c.sS):
        eng.wait_ge(s.h, s.n)


def emit_ph0_load(c):
    block = c.block
    t0, a0 = c.sT.n, c.sA.n
    tok = {}

    @block.sync
    def _(sync):
        for tt in range(TT):
            buf = c.stage[:, (tt % 2) * D:(tt % 2) * D + D]
            if tt >= 2:
                sync.wait_ge(c.sT.h, t0 + (tt - 1) * DC)
            tok[tt] = c.chL.issue(sync, buf, c.x_in[tt * 128:(tt + 1) * 128, :])

    @block.tensor
    def _(tensor):
        for tt in range(TT):
            buf = c.stage[:, (tt % 2) * D:(tt % 2) * D + D]
            tensor.wait_ge(*tok[tt])
            if tt == 0:
                tensor.wait_ge(*c.tok_ident)
            for dc in range(DC):
                pb = c.PB[dc]
                if tt >= 1:
                    tensor.wait_ge(c.sA.h, a0 + (tt - 1) * DC + dc + 1)
                c.sT.inc(tensor.transpose(pb[:, 0:128], buf[:, dc * 128:(dc + 1) * 128],
                                          c.ident_t[:]))

    @block.scalar
    def _(scalar):
        for tt in range(TT):
            for dc in range(DC):
                scalar.wait_ge(c.sT.h, t0 + tt * DC + dc + 1)
                c.sA.inc(scalar.activation(
                    c.res[:, dc * T + tt * 128: dc * T + tt * 128 + 128],
                    c.PB[dc][:, 0:128], AF.Copy, bias=0.0, scale=1.0))


def emit_ln(c, xh, xl, extra_full=None):
    block = c.block
    t0, a0, v0 = c.sT.n, c.sA.n, c.sV.n
    g0, s0 = c.sG.n, c.sS.n

    @block.scalar
    def _(scalar):
        _wait_all(scalar, c)
        for dc in range(DC):
            h = (dc % 2) * T
            if dc >= 2:
                scalar.wait_ge(c.sT.h, t0 + (dc - 1) * 16)
            r = c.res[:, dc * T:(dc + 1) * T]
            c.sA.inc(scalar.activation(c.sqbuf[:, h:h + T], r, AF.Square, bias=0.0, scale=1.0))
            c.sA.inc(scalar.activation(c.sbh[:, h:h + T], r, AF.Copy, bias=0.0, scale=1.0))
            scalar.wait_ge(c.sA.h, c.sA.n - 1)
            c.sA.inc(scalar.activation(c.sqh[:, h:h + T], c.sqbuf[:, h:h + T],
                                       AF.Copy, bias=0.0, scale=1.0))

    @block.vector
    def _(vector):
        for dc in range(DC):
            h = (dc % 2) * T
            vector.wait_ge(c.sA.h, a0 + dc * 3 + 3)
            r = c.res[:, dc * T:(dc + 1) * T]
            c.sV.inc(vector.tensor_sub(c.sbl[:, h:h + T], r, c.sbh[:, h:h + T]))
            c.sV.inc(vector.tensor_sub(c.sql[:, h:h + T], c.sqbuf[:, h:h + T],
                                       c.sqh[:, h:h + T]))

    @block.tensor
    def _(tensor):
        tensor.wait_ge(c.sV.h, v0)
        tensor.wait_ge(c.sA.h, a0)
        tensor.wait_ge(c.sG.h, g0)
        for dc in range(DC):
            h = (dc % 2) * T
            tensor.wait_ge(c.sV.h, v0 + dc * 2 + 2)
            st = (dc == 0)
            sp = (dc == DC - 1)
            for q in range(4):
                sl = slice(h + q * 512, h + (q + 1) * 512)
                c.sT.inc(tensor.matmul(c.PB[q][0:64, :], c.ones_bf[:, 0:64],
                                       c.sbh[:, sl], start=st, stop=False))
                c.sT.inc(tensor.matmul(c.PB[q][0:64, :], c.ones_bf[:, 0:64],
                                       c.sbl[:, sl], start=False, stop=sp))
                c.sT.inc(tensor.matmul(c.PB[4 + q][0:64, :], c.ones_bf[:, 0:64],
                                       c.sqh[:, sl], start=st, stop=False))
                c.sT.inc(tensor.matmul(c.PB[4 + q][0:64, :], c.ones_bf[:, 0:64],
                                       c.sql[:, sl], start=False, stop=sp))

    t1 = c.sT.n

    @block.vector
    def _(vector):
        vector.wait_ge(c.sT.h, t1)
        for q in range(4):
            sl = slice(q * 512, (q + 1) * 512)
            c.sV.inc(vector.tensor_scalar_mul(c.rows[0:1, sl], c.PB[q][0:1, :], -1.0 / D))
            c.sV.inc(vector.tensor_scalar_mul(c.rows[32:33, sl], c.PB[4 + q][0:1, :], 1.0 / D))
            vector.wait_ge(c.sV.h, c.sV.n)
            c.sV.inc(vector.scalar_tensor_tensor(
                c.PB[4 + q][32:33, :], c.rows[0:1, sl], -1.0, c.rows[0:1, sl],
                ALU.mult, ALU.mult))
            vector.wait_ge(c.sV.h, c.sV.n)
            c.sV.inc(vector.tensor_add(c.rows[64:65, sl], c.rows[32:33, sl],
                                       c.PB[4 + q][32:33, :]))
        vector.wait_ge(c.sV.h, c.sV.n)
        c.sV.inc(vector.tensor_scalar_add(c.rows[64:65, 0:T], c.rows[64:65, 0:T], EPS_LN))

    a1 = c.sA.n
    @block.scalar
    def _(scalar):
        scalar.wait_ge(c.sV.h, c.sV.n)
        c.sA.inc(scalar.activation(c.rows[96:97, 0:T], c.rows[64:65, 0:T], AF.Sqrt,
                                   bias=0.0, scale=1.0))

    v1 = c.sV.n
    @block.vector
    def _(vector):
        vector.wait_ge(c.sA.h, a1 + 1)
        c.sV.inc(vector.reciprocal(c.rows[32:33, 0:T], c.rows[96:97, 0:T]))

    t2, a2, v2 = c.sT.n, c.sA.n, c.sV.n

    @block.tensor
    def _(tensor):
        tensor.wait_ge(c.sV.h, v1 + 1)
        for q in range(4):
            sl = slice(q * 512, (q + 1) * 512)
            pm = c.PB[q % 2]
            pr = c.PB[2 + q % 2]
            if q >= 2:
                tensor.wait_ge(c.sV.h, v2 + (q - 1) * DC * 3)
            c.sT.inc(tensor.matmul(pm[:, :], c.ones_f32[0:1, :], c.rows[0:1, sl],
                                   start=True, stop=True))
            c.sT.inc(tensor.matmul(pr[:, :], c.ones_f32[32:33, :], c.rows[32:33, sl],
                                   start=True, stop=True))

    @block.vector
    def _(vector):
        for q in range(4):
            pm = c.PB[q % 2]
            pr = c.PB[2 + q % 2]
            vector.wait_ge(c.sT.h, t2 + 2 * (q + 1))
            for dc in range(DC):
                src = slice(dc * T + q * 512, dc * T + (q + 1) * 512)
                dst_full = c.xfull[:, 0:512] if extra_full is None else extra_full[:, src]
                c.sV.inc(vector.tensor_add(c.tmp1[:, 0:512], c.res[:, src], pm[:, :]))
                vector.wait_ge(c.sV.h, c.sV.n)
                c.sV.inc(vector.tensor_mul(dst_full, c.tmp1[:, 0:512], pr[:, :]))
                vector.wait_ge(c.sV.h, c.sV.n)
                vector.wait_ge(c.sA.h, a2 + q * DC + dc + 1)
                c.sV.inc(vector.tensor_sub(xl[:, src], dst_full, xh[:, src]))

    @block.scalar
    def _(scalar):
        for q in range(4):
            for dc in range(DC):
                src = slice(dc * T + q * 512, dc * T + (q + 1) * 512)
                dst_full = c.xfull[:, 0:512] if extra_full is None else extra_full[:, src]
                scalar.wait_ge(c.sV.h, v2 + (q * DC + dc) * 3 + 2)
                c.sA.inc(scalar.activation(xh[:, src], dst_full, AF.Copy,
                                           bias=0.0, scale=1.0))



def emit_l0_attn(c):
    for p in range(NH // 2):
        for wi, (wsrc, dsth, dstl, permuted) in enumerate([
                (c.w_q, c.WTQH, c.WTQL, True),
                (c.w_k, c.WTKH, c.WTKL, True),
                (c.w_v, c.WTVH, c.WTVL, False)]):
            _emit_wprep(c, p, wsrc, dsth, dstl, permuted)

        for which, (dsth, dstl, wth, wtl, bcol) in enumerate([
                (c.QH, c.QL, c.WTQH, c.WTQL, c.bias_qkv[:, 0 * 8 + p:0 * 8 + p + 1]),
                (c.KH, c.KL, c.WTKH, c.WTKL, c.bias_qkv[:, 1 * 8 + p:1 * 8 + p + 1])]):
            for tq in range(4):
                _emit_qk_proj_tile(c, p, which, tq, dsth, dstl, wth, wtl, bcol)

        _emit_bounds(c, p)

        for tq in range(4):
            _emit_v_proj_tile(c, p, tq)

        _emit_wprep(c, p, c.w_o, c.WTVH, c.WTVL, False, col_slice=True)

        c.attn_prev = {}
        for tq in range(4):
            _emit_attn_tile(c, p, tq)

        _emit_recip(c, p)

        for tq in range(4):
            _emit_attn_epilogue_tile(c, p, tq)


def _emit_wprep(c, p, wsrc, dsth, dstl, permuted, col_slice=False):
    block = c.block
    wraw = c.ARF[:, 0:1024]
    t0, a0, v0 = c.sT.n, c.sA.n, c.sV.n

    @block.sync
    def _(sync):
        sync.wait_ge(c.sT.h, t0)
        if col_slice:
            src = wsrc[:, p * 128:(p + 1) * 128].rearrange("(ec r) f -> ec r f", r=128)
            for ec in range(DC):
                c.wtok[ec] = c.chW.issue(sync, wraw[:, ec * 128:(ec + 1) * 128], src[ec])
        else:
            src = wsrc[p * 128:(p + 1) * 128, :]
            c.wtok[0] = c.chW.issue(sync, wraw, src)

    @block.tensor
    def _(tensor):
        tensor.wait_ge(c.sA.h, a0)
        tensor.wait_ge(c.sV.h, v0)
        for dc in range(DC):
            tensor.wait_ge(*c.wtok[dc if col_slice else 0])
            pb = c.PB[6 + dc % 2]
            if dc >= 2:
                tensor.wait_ge(c.sV.h, v0 + (dc - 1))
                tensor.wait_ge(c.sA.h, a0 + (dc - 1))
            c.sT.inc(tensor.transpose(pb[:, 0:128], wraw[:, dc * 128:(dc + 1) * 128],
                                      c.ident_t[:]))

    def psrc(pb):
        if not permuted:
            return pb[:, 0:128]
        return pb[:, 0:128].rearrange("p (h b a) -> p h b a", h=2, b=2, a=32).transpose([0, 1, 3, 2])

    def dvw(t, dc):
        sl = t[:, dc * 128:(dc + 1) * 128]
        if not permuted:
            return sl
        return sl.rearrange("p (h a b) -> p h a b", h=2, a=32, b=2)

    @block.scalar
    def _(scalar):
        for dc in range(DC):
            pb = c.PB[6 + dc % 2]
            scalar.wait_ge(c.sT.h, t0 + dc + 1)
            c.sA.inc(scalar.activation(dvw(dsth, dc), psrc(pb),
                                       AF.Copy, bias=0.0, scale=1.0))

    @block.vector
    def _(vector):
        for dc in range(DC):
            pb = c.PB[6 + dc % 2]
            vector.wait_ge(c.sA.h, a0 + dc + 1)
            c.sV.inc(vector.tensor_sub(dvw(dstl, dc), psrc(pb), dvw(dsth, dc)))


def _emit_qk_proj_tile(c, p, which, tq, dsth, dstl, wth, wtl, bcol):
    block = c.block
    sl = slice(tq * 512, (tq + 1) * 512)
    qb = c.ARF[:, 0:512]
    qs = c.ARF[:, 512:1024]
    m2 = c.ARF[:, 1024:1536]
    qf = c.ARF[:, 1536:2048]
    sqn = c.ARF[:, 2048:2560]
    t0, a0, v0 = c.sT.n, c.sA.n, c.sV.n

    @block.tensor
    def _(tensor):
        tensor.wait_ge(c.sA.h, a0)
        tensor.wait_ge(c.sV.h, v0)
        n = 0
        for wt, x in ((wth, XH_g(c)), (wth, XL_g(c)), (wtl, XH_g(c))):
            for dc in range(DC):
                n += 1
                c.sT.inc(tensor.matmul(c.PB[5][:, :],
                                       wt[:, dc * 128:(dc + 1) * 128],
                                       x[:, dc * T + tq * 512: dc * T + (tq + 1) * 512],
                                       start=(n == 1), stop=(n == 24)))

    t_mm = c.sT.n

    @block.scalar
    def _(scalar):
        scalar.wait_ge(c.sT.h, t_mm)
        c.sA.inc(scalar.activation(qb, c.PB[5][:, :], AF.Identity, bias=bcol, scale=1.0))

    @block.vector
    def _(vector):
        vector.wait_ge(c.sA.h, a0 + 1)
        c.sV.inc(vector.stream_shuffle(qs, qb, [(i ^ 1) for i in range(32)]))
        c.sV.inc(vector.tensor_mul(c.PB[6][:, :], qb, c.ropeA[:, sl]))
        vector.wait_ge(c.sV.h, c.sV.n)
        c.sV.inc(vector.tensor_mul(m2, qs, c.ropeB[:, sl]))
        vector.wait_ge(c.sV.h, c.sV.n)
        c.sV.inc(vector.tensor_add(qf, c.PB[6][:, :], m2))

    @block.scalar
    def _(scalar):
        scalar.wait_ge(c.sV.h, v0 + 4)
        c.sA.inc(scalar.activation(dsth[:, sl], qf, AF.Copy, bias=0.0, scale=1.0))
        c.sA.inc(scalar.activation(sqn, qf, AF.Square, bias=0.0, scale=1.0))

    @block.tensor
    def _(tensor):
        tensor.wait_ge(c.sA.h, a0 + 3)
        tensor.wait_ge(c.sV.h, v0)
        c.sT.inc(tensor.matmul(c.PB[6][0:64, :], c.ones_f32[0:64, 0:64], sqn[0:64, :],
                               start=True, stop=True))
        c.sT.inc(tensor.matmul(c.PB[7][0:64, :], c.ones_f32[64:128, 0:64], sqn[64:128, :],
                               start=True, stop=True))

    @block.vector
    def _(vector):
        vector.wait_ge(c.sA.h, a0 + 3)
        vector.wait_ge(c.sV.h, c.sV.n)
        c.sV.inc(vector.tensor_sub(dstl[:, sl], qf, dsth[:, sl]))
        vector.wait_ge(c.sT.h, t_mm + 2)
        cellA = c.nrm[0:1, which * 2:which * 2 + 1]
        cellB = c.nrm[0:1, which * 2 + 1:which * 2 + 2]
        scr = c.nrm[0:1, 8:9]
        c.sV.inc(vector.tensor_reduce(scr, c.PB[6][0:1, :], AX.X, ALU.max))
        vector.wait_ge(c.sV.h, c.sV.n)
        if tq == 0:
            c.sV.inc(vector.tensor_copy(cellA, scr))
        else:
            c.sV.inc(vector.tensor_max(cellA, cellA, scr))
        vector.wait_ge(c.sV.h, c.sV.n)
        c.sV.inc(vector.tensor_reduce(scr, c.PB[7][0:1, :], AX.X, ALU.max))
        vector.wait_ge(c.sV.h, c.sV.n)
        if tq == 0:
            c.sV.inc(vector.tensor_copy(cellB, scr))
        else:
            c.sV.inc(vector.tensor_max(cellB, cellB, scr))


def _emit_bounds(c, p):
    block = c.block
    v0, a0 = c.sV.n, c.sA.n

    @block.scalar
    def _(scalar):
        scalar.wait_ge(c.sV.h, v0)
        c.sA.inc(scalar.activation(c.nrm[0:1, 4:8], c.nrm[0:1, 0:4], AF.Sqrt,
                                   bias=0.0, scale=1.0))

    @block.vector
    def _(vector):
        vector.wait_ge(c.sA.h, a0 + 1)
        c.sV.inc(vector.tensor_mul(c.nrm[0:1, 9:10], c.nrm[0:1, 4:5], c.nrm[0:1, 6:7]))
        c.sV.inc(vector.tensor_mul(c.nrm[0:1, 10:11], c.nrm[0:1, 5:6], c.nrm[0:1, 7:8]))
        vector.wait_ge(c.sV.h, c.sV.n)
        c.sV.inc(vector.tensor_scalar_mul(c.nrm[0:1, 9:11], c.nrm[0:1, 9:11], -0.125))

    @block.sync
    def _(sync):
        sync.wait_ge(c.sV.h, c.sV.n)
        t1 = c.chC.issue(sync, c.scal_dram.ap()[0:1, 0:2], c.nrm[0:1, 9:11])
        sync.wait_ge(*t1)
        t2 = c.chC.issue(sync, c.biasA, c.scal_dram.ap()[0:1, 0:1].partition_broadcast(128))
        t3 = c.chC.issue(sync, c.biasB, c.scal_dram.ap()[0:1, 1:2].partition_broadcast(128))
        c.tok_bias = (t2, t3)


def _emit_v_proj_tile(c, p, tq):
    block = c.block
    qf = c.ARF[:, 1536:2048]
    bcol = c.bias_qkv[:, 2 * 8 + p:2 * 8 + p + 1]
    t0, a0, v0 = c.sT.n, c.sA.n, c.sV.n

    @block.tensor
    def _(tensor):
        tensor.wait_ge(c.sA.h, a0)
        n = 0
        for wt, x in ((c.WTVH, XH_g(c)), (c.WTVH, XL_g(c)), (c.WTVL, XH_g(c))):
            for dc in range(DC):
                n += 1
                c.sT.inc(tensor.matmul(c.PB[5][:, :],
                                       wt[:, dc * 128:(dc + 1) * 128],
                                       x[:, dc * T + tq * 512: dc * T + (tq + 1) * 512],
                                       start=(n == 1), stop=(n == 24)))

    t_mm = c.sT.n

    @block.scalar
    def _(scalar):
        scalar.wait_ge(c.sT.h, t_mm)
        c.sA.inc(scalar.activation(qf, c.PB[5][:, :], AF.Identity, bias=bcol, scale=1.0))

    @block.tensor
    def _(tensor):
        tensor.wait_ge(c.sA.h, a0 + 1)
        tensor.wait_ge(c.sV.h, v0)
        for sub in range(4):
            pb = c.PB[6 + sub % 2]
            if sub >= 2:
                tensor.wait_ge(c.sV.h, v0 + (sub - 1) * 2)
            c.sT.inc(tensor.transpose(pb[:, 0:128], qf[:, sub * 128:(sub + 1) * 128],
                                      c.ident_t[:]))

    @block.scalar
    def _(scalar):
        for sub in range(4):
            kc = tq * 4 + sub
            pb = c.PB[6 + sub % 2]
            scalar.wait_ge(c.sT.h, t_mm + sub + 1)
            c.sA.inc(scalar.activation(c.VSA[:, kc * 128: kc * 128 + 64], pb[:, 0:64],
                                       AF.Copy, bias=0.0, scale=1.0))
            c.sA.inc(scalar.activation(c.VSB[:, kc * 128: kc * 128 + 64], pb[:, 64:128],
                                       AF.Copy, bias=0.0, scale=1.0))

    @block.vector
    def _(vector):
        for sub in range(4):
            kc = tq * 4 + sub
            pb = c.PB[6 + sub % 2]
            vector.wait_ge(c.sA.h, a0 + 1 + 2 * (sub + 1))
            c.sV.inc(vector.tensor_sub(c.VSA[:, kc * 128 + 64: kc * 128 + 128],
                                       pb[:, 0:64], c.VSA[:, kc * 128: kc * 128 + 64]))
            c.sV.inc(vector.tensor_sub(c.VSB[:, kc * 128 + 64: kc * 128 + 128],
                                       pb[:, 64:128], c.VSB[:, kc * 128: kc * 128 + 64]))


def _emit_attn_tile(c, p, tq):
    block = c.block
    nk = 4 * (tq + 1)
    qsl = slice(tq * 512, (tq + 1) * 512)
    t0, a0, v0 = c.sT.n, c.sA.n, c.sV.n

    tS, tAV, aA, aB, vM, vPA, vPB = {}, {}, {}, {}, {}, {}, {}
    t = t0
    for kc in range(nk):
        t += 6
        tS[kc] = t
        if kc >= 1:
            t += 10
            tAV[kc - 1] = t
    t += 10
    tAV[nk - 1] = t
    a = a0
    for kc in range(nk):
        a += 2
        aA[kc] = a
        a += 2
        aB[kc] = a
    a_evac = a + 4
    v = v0
    for kc in range(nk):
        if 0 <= kc - 4 * tq < 4:
            v += 2
        vM[kc] = v
        v += 1
        vPA[kc] = v
        v += 1
        vPB[kc] = v

    prev_sub = c.attn_prev.get("sub", {})
    prev_av = c.attn_prev.get("av", {})
    prev_evac = c.attn_prev.get("evac", 0)

    @block.tensor
    def _(tensor):
        tensor.wait_ge(c.sA.h, prev_evac)
        if tq == 0:
            tensor.wait_ge(c.sV.h, v0)
        for kc in range(nk):
            psA = c.PB[kc % 2]
            psB = c.PB[2 + kc % 2]
            if kc >= 2:
                tensor.wait_ge(c.sV.h, vPB[kc - 2])
            elif (kc % 2) in prev_sub:
                tensor.wait_ge(c.sV.h, prev_sub[kc % 2])
            ksl = slice(kc * 128, (kc + 1) * 128)
            for ps, lo in ((psA, 0), (psB, 64)):
                h = slice(lo, lo + 64)
                c.sT.inc(tensor.matmul(ps[:, :], c.KH[h, ksl], c.QH[h, qsl],
                                       start=True, stop=False))
                c.sT.inc(tensor.matmul(ps[:, :], c.KH[h, ksl], c.QL[h, qsl],
                                       start=False, stop=False))
                c.sT.inc(tensor.matmul(ps[:, :], c.KL[h, ksl], c.QH[h, qsl],
                                       start=False, stop=True))
            if kc >= 1:
                _av_mms(c, tensor, tq, kc - 1, nk, vPB)
        _av_mms(c, tensor, tq, nk - 1, nk, vPB)

    @block.scalar
    def _(scalar):
        if tq == 0:
            scalar.wait_ge(*c.tok_bias[0])
            scalar.wait_ge(*c.tok_bias[1])
        for kc in range(nk):
            psA = c.PB[kc % 2]
            psB = c.PB[2 + kc % 2]
            ph = c.PHt[kc % 2]
            scalar.wait_ge(c.sT.h, tS[kc])
            scalar.wait_ge(c.sV.h, vM[kc])
            if kc >= 2:
                scalar.wait_ge(c.sT.h, tAV[kc - 2])
            elif (kc % 2) in prev_av:
                scalar.wait_ge(c.sT.h, prev_av[kc % 2])
            c.sA.inc(scalar.activation(ph[:, 0:512], psA[:, :], AF.Exp,
                                       bias=c.biasA[:, 0:1], scale=0.125))
            scalar.wait_ge(c.sA.h, c.sA.n)
            c.sA.inc(scalar.activation(psA[:, :], psA[:, :], AF.Exp,
                                       bias=c.biasA[:, 0:1], scale=0.125))
            c.sA.inc(scalar.activation(ph[:, 512:1024], psB[:, :], AF.Exp,
                                       bias=c.biasB[:, 0:1], scale=0.125))
            scalar.wait_ge(c.sA.h, c.sA.n)
            c.sA.inc(scalar.activation(psB[:, :], psB[:, :], AF.Exp,
                                       bias=c.biasB[:, 0:1], scale=0.125))
        scalar.wait_ge(c.sT.h, tAV[nk - 1])
        c.sA.inc(scalar.activation(c.AVS[0:64, tq * 512:(tq + 1) * 512],
                                   c.PB[4][0:64, :], AF.Copy, bias=0.0, scale=1.0))
        c.sA.inc(scalar.activation(c.AVS[64:128, tq * 512:(tq + 1) * 512],
                                   c.PB[5][64:128, :], AF.Copy, bias=0.0, scale=1.0))
        c.sA.inc(scalar.activation(c.rows[64:65, qsl], c.PB[6][0:1, :],
                                   AF.Copy, bias=0.0, scale=1.0))
        c.sA.inc(scalar.activation(c.rows[96:97, qsl], c.PB[7][0:1, :],
                                   AF.Copy, bias=0.0, scale=1.0))

    @block.vector
    def _(vector):
        for kc in range(nk):
            psA = c.PB[kc % 2]
            psB = c.PB[2 + kc % 2]
            ph = c.PHt[kc % 2]
            pl = c.PLt[kc % 2]
            mi = kc - 4 * tq
            vector.wait_ge(c.sT.h, tS[kc])
            if 0 <= mi < 4:
                msl = slice(384 - mi * 128, 384 - mi * 128 + 512)
                c.sV.inc(vector.tensor_add(psA[:, :], psA[:, :], c.MASKS[:, msl]))
                c.sV.inc(vector.tensor_add(psB[:, :], psB[:, :], c.MASKS[:, msl]))
            if kc >= 2:
                vector.wait_ge(c.sT.h, tAV[kc - 2])
            elif (kc % 2) in prev_av:
                vector.wait_ge(c.sT.h, prev_av[kc % 2])
            vector.wait_ge(c.sA.h, aA[kc])
            c.sV.inc(vector.tensor_sub(pl[:, 0:512], psA[:, :], ph[:, 0:512]))
            vector.wait_ge(c.sA.h, aB[kc])
            c.sV.inc(vector.tensor_sub(pl[:, 512:1024], psB[:, :], ph[:, 512:1024]))

    c.attn_prev = {
        "sub": {(nk - 1) % 2: vPB[nk - 1], (nk - 2) % 2: vPB[nk - 2]},
        "av": {(nk - 1) % 2: tAV[nk - 1], (nk - 2) % 2: tAV[nk - 2]},
        "evac": a_evac,
    }


def _av_mms(c, tensor, tq, kc, nk, vPB):
    ph = c.PHt[kc % 2]
    pl = c.PLt[kc % 2]
    tensor.wait_ge(c.sV.h, vPB[kc])
    first = (kc == 0)
    last = (kc == nk - 1)
    for lo, avb in ((0, c.PB[4]), (64, c.PB[5])):
        phs = ph[:, lo * 8: lo * 8 + 512]
        pls = pl[:, lo * 8: lo * 8 + 512]
        vh = c.VSA if lo == 0 else c.VSB
        hi_t = vh[:, kc * 128: kc * 128 + 64]
        lo_t = vh[:, kc * 128 + 64: kc * 128 + 128]
        n = 0
        for (st_, mv) in ((hi_t, phs), (lo_t, phs), (hi_t, pls)):
            n += 1
            c.sT.inc(tensor.matmul(avb[lo:lo + 64, :], st_, mv,
                                   start=(first and n == 1), stop=(last and n == 3),
                                   tile_position=(0, lo)))
        dn = c.PB[6] if lo == 0 else c.PB[7]
        c.sT.inc(tensor.matmul(dn[0:64, :], c.ones_bf[:, 0:64], phs,
                               start=first, stop=False))
        c.sT.inc(tensor.matmul(dn[0:64, :], c.ones_bf[:, 0:64], pls,
                               start=False, stop=last))


def _emit_recip(c, p):
    block = c.block
    a0 = c.sA.n
    recD = c.ARF[0:8, 2048:2560]

    @block.sync
    def _(sync):
        sync.wait_ge(c.sA.h, a0)
        t1 = c.chC.issue(sync, recD[0:4, :], c.rows[64:65, :])
        t2 = c.chC.issue(sync, recD[4:8, :], c.rows[96:97, :])
        c.tok_recd = (t1, t2)

    v0 = c.sV.n
    @block.vector
    def _(vector):
        vector.wait_ge(*c.tok_recd[0])
        vector.wait_ge(*c.tok_recd[1])
        c.sV.inc(vector.reciprocal(recD[0:8, :], recD[0:8, :]))

    @block.sync
    def _(sync):
        sync.wait_ge(c.sV.h, c.sV.n)
        t1 = c.chC.issue(sync, c.rows[0:1, :], recD[0:4, :])
        t2 = c.chC.issue(sync, c.rows[32:33, :], recD[4:8, :])
        c.tok_recr = (t1, t2)


def _emit_attn_epilogue_tile(c, p, tq):
    block = c.block
    qsl = slice(tq * 512, (tq + 1) * 512)
    osb = c.ARF[:, 2560:3072]
    t0, a0, v0 = c.sT.n, c.sA.n, c.sV.n

    @block.tensor
    def _(tensor):
        tensor.wait_ge(c.sV.h, v0)
        if tq == 0:
            tensor.wait_ge(*c.tok_recr[0])
            tensor.wait_ge(*c.tok_recr[1])
        c.sT.inc(tensor.matmul(c.PB[0][0:64, :], c.ones_f32[0:1, 0:64],
                               c.rows[0:1, qsl], start=True, stop=True,
                               tile_position=(0, 0)))
        c.sT.inc(tensor.matmul(c.PB[1][64:128, :], c.ones_f32[32:33, 0:64],
                               c.rows[32:33, qsl], start=True, stop=True,
                               tile_position=(32, 64)))

    t_bc = c.sT.n

    @block.vector
    def _(vector):
        vector.wait_ge(c.sT.h, t_bc)
        c.sV.inc(vector.tensor_mul(osb[0:64, :], c.AVS[0:64, qsl], c.PB[0][0:64, :]))
        c.sV.inc(vector.tensor_mul(osb[64:128, :], c.AVS[64:128, qsl], c.PB[1][64:128, :]))

    @block.scalar
    def _(scalar):
        scalar.wait_ge(c.sV.h, v0 + 2)
        c.sA.inc(scalar.activation(c.OH[:, 0:512], osb, AF.Copy, bias=0.0, scale=1.0))

    @block.vector
    def _(vector):
        vector.wait_ge(c.sA.h, a0 + 1)
        vector.wait_ge(c.sV.h, c.sV.n)
        c.sV.inc(vector.tensor_sub(c.OL[:, 0:512], osb, c.OH[:, 0:512]))

    v_spl = c.sV.n

    @block.tensor
    def _(tensor):
        tensor.wait_ge(c.sV.h, v_spl)
        for ec in range(DC):
            pb = c.PB[6 + ec % 2]
            if ec >= 2:
                tensor.wait_ge(c.sV.h, v_spl + (ec - 1))
            n = 0
            for wt, mv in ((c.WTVH, c.OH[:, 0:512]), (c.WTVH, c.OL[:, 0:512]),
                           (c.WTVL, c.OH[:, 0:512])):
                n += 1
                c.sT.inc(tensor.matmul(pb[:, :], wt[:, ec * 128:(ec + 1) * 128], mv,
                                       start=(n == 1), stop=(n == 3)))

    t_wo = c.sT.n

    @block.vector
    def _(vector):
        for ec in range(DC):
            pb = c.PB[6 + ec % 2]
            vector.wait_ge(c.sT.h, t_wo - 3 * (DC - 1 - ec))
            dst = c.res[:, ec * T + tq * 512: ec * T + (tq + 1) * 512]
            if p == 0:
                c.sV.inc(vector.scalar_tensor_tensor(
                    dst, pb[:, :], c.bias_qkv[:, 3 * 8 + ec:3 * 8 + ec + 1], dst,
                    ALU.add, ALU.add))
            else:
                c.sV.inc(vector.tensor_add(dst, dst, pb[:, :]))


def XH_g(c):
    return c.XH


def XL_g(c):
    return c.XL



def emit_ffn0_gamma(c, wsrc, nrows, ncols, cell):
    block = c.block
    wstage = c.ARF[:, 0:2048]
    nch = nrows // 128
    ncg = ncols // 1024
    a0, v0 = c.sA.n, c.sV.n
    tok = {}

    @block.sync
    def _(sync):
        _wait_all(sync, c)
        k = 0
        for ch in range(nch):
            for cg in range(ncg):
                if k >= 2:
                    sync.wait_ge(c.sA.h, a0 + (k - 1))
                tok[k] = c.chW.issue(sync, wstage[:, (k % 2) * 1024:(k % 2) * 1024 + 1024],
                                     wsrc[ch * 128:(ch + 1) * 128, cg * 1024:(cg + 1) * 1024])
                k += 1

    nk = nch * ncg

    @block.scalar
    def _(scalar):
        for k in range(nk):
            scalar.wait_ge(*tok[k])
            if k >= 1:
                scalar.wait_ge(c.sV.h, v0 + k)
            h = (k % 2) * 1024
            c.sA.inc(scalar.activation(wstage[:, h:h + 1024], wstage[:, h:h + 1024],
                                       AF.Abs, bias=0.0, scale=1.0,
                                       accum_out=c.acol[:, 0:1]))

    @block.vector
    def _(vector):
        for k in range(nk):
            vector.wait_ge(c.sA.h, a0 + k + 1)
            if k == 0:
                c.sV.inc(vector.tensor_copy(c.gcol[:, 0:1], c.acol[:, 0:1]))
            else:
                c.sV.inc(vector.tensor_add(c.gcol[:, 0:1], c.gcol[:, 0:1], c.acol[:, 0:1]))
            vector.wait_ge(c.sV.h, c.sV.n)

    @block.gpsimd
    def _(gpsimd):
        gpsimd.wait_ge(c.sV.h, c.sV.n)
        c.sG.inc(gpsimd.tensor_reduce(c.nrm[0:1, cell:cell + 1], c.gcol[:, 0:1],
                                      AX.C, ALU.add))

    @block.vector
    def _(vector):
        vector.wait_ge(c.sG.h, c.sG.n)
        c.sV.inc(vector.tensor_scalar(c.nrm[0:1, cell:cell + 1],
                                      c.nrm[0:1, cell:cell + 1],
                                      1.0 / (nrows * float(ncols)), None, ALU.mult))
        vector.wait_ge(c.sV.h, c.sV.n)
        c.sV.inc(vector.tensor_scalar(c.nrm[0:1, cell:cell + 1],
                                      c.nrm[0:1, cell:cell + 1],
                                      1e-05, 0.5, ALU.add, ALU.mult))


def emit_ffn0_wprep(c, wsrc, nrows, ncols, cell, thr_cells, dst_fn):
    block = c.block
    wstage = c.ARF[:, 0:2048]
    tpos, tneg = thr_cells
    nch = nrows // 128
    ncg = ncols // 1024
    t0, a0, v0 = c.sT.n, c.sA.n, c.sV.n
    tok = {}
    snapQ = c.chQ.snapshot()

    @block.sync
    def _(sync):
        _wait_all(sync, c)
        t1 = c.chC.issue(sync, c.scal_dram.ap()[0:1, 0:1], c.nrm[0:1, cell:cell + 1])
        sync.wait_ge(*t1)
        t2 = c.chC.issue(sync, c.ARF[:, tpos:tpos + 1],
                         c.scal_dram.ap()[0:1, 0:1].partition_broadcast(128))
        c.tok_thr = t2

    @block.vector
    def _(vector):
        vector.wait_ge(*c.tok_thr)
        c.sV.inc(vector.tensor_scalar_mul(c.ARF[:, tneg:tneg + 1],
                                          c.ARF[:, tpos:tpos + 1], -1.0))

    v_thr = c.sV.n

    @block.sync
    def _(sync):
        k = 0
        for ch in range(nch):
            for cg in range(ncg):
                if k >= 2:
                    sync.wait_ge(c.sT.h, t0 + (k - 1) * DC)
                tok[k] = c.chW.issue(sync, wstage[:, (k % 2) * 1024:(k % 2) * 1024 + 1024],
                                     wsrc[ch * 128:(ch + 1) * 128, cg * 1024:(cg + 1) * 1024])
                k += 1

    @block.tensor
    def _(tensor):
        tensor.wait_ge(c.sV.h, v_thr)
        k = 0
        for ch in range(nch):
            for cg in range(ncg):
                tensor.wait_ge(*tok[k])
                h = (k % 2) * 1024
                for dc in range(DC):
                    pb = c.PB[4 + (dc % 2)]
                    n = k * DC + dc
                    if n >= 2:
                        tensor.wait_ge(c.sV.h, v_thr + (n - 1) * 2)
                    c.sT.inc(tensor.transpose(pb[:, 0:128],
                                              wstage[:, h + dc * 128:h + (dc + 1) * 128],
                                              c.ident_t[:]))
                k += 1

    @block.vector
    def _(vector):
        for sm in c.chQ.sems:
            vector.wait_ge(sm.h, sm.n)
        n = 0
        for ch in range(nch):
            for cg in range(ncg):
                for dc in range(DC):
                    pb = c.PB[4 + (dc % 2)]
                    vector.wait_ge(c.sT.h, t0 + n + 1)
                    stg = (c.OH if (n % 2 == 0) else c.OL)[:, (n // 2 % 2) * 128:(n // 2 % 2) * 128 + 128]
                    if n >= 4:
                        vector.wait_ge(*c.chQ.plan(snapQ, n - 4))
                    vector.wait_ge(c.sV.h, c.sV.n)
                    c.sV.inc(vector.tensor_scalar(c.qtmp[:, 0:128], pb[:, 0:128],
                                                  c.ARF[:, tneg:tneg + 1], None, ALU.is_le))
                    vector.wait_ge(c.sV.h, c.sV.n)
                    c.sV.inc(vector.scalar_tensor_tensor(
                        stg, pb[:, 0:128], c.ARF[:, tpos:tpos + 1],
                        c.qtmp[:, 0:128], ALU.is_ge, ALU.subtract))
                    n += 1

    @block.scalar
    def _(scalar):
        n = 0
        for ch in range(nch):
            for cg in range(ncg):
                for dc in range(DC):
                    scalar.wait_ge(c.sV.h, v_thr + (n + 1) * 2)
                    stg = (c.OH if (n % 2 == 0) else c.OL)[:, (n // 2 % 2) * 128:(n // 2 % 2) * 128 + 128]
                    tkn = c.chQ.issue(scalar, dst_fn(ch, cg, dc), stg)
                    n += 1


def emit_ln_stats(c):
    block = c.block
    t0, a0, v0 = c.sT.n, c.sA.n, c.sV.n
    g0, s_0 = c.sG.n, c.sS.n

    @block.scalar
    def _(scalar):
        _wait_all(scalar, c)
        for dc in range(DC):
            h = (dc % 2) * T
            if dc >= 2:
                scalar.wait_ge(c.sT.h, t0 + (dc - 1) * 16)
            r = c.res[:, dc * T:(dc + 1) * T]
            c.sA.inc(scalar.activation(c.sqbuf[:, h:h + T], r, AF.Square, bias=0.0, scale=1.0))
            c.sA.inc(scalar.activation(c.sbh[:, h:h + T], r, AF.Copy, bias=0.0, scale=1.0))
            scalar.wait_ge(c.sA.h, c.sA.n - 1)
            c.sA.inc(scalar.activation(c.sqh[:, h:h + T], c.sqbuf[:, h:h + T],
                                       AF.Copy, bias=0.0, scale=1.0))

    @block.vector
    def _(vector):
        for dc in range(DC):
            h = (dc % 2) * T
            vector.wait_ge(c.sA.h, a0 + dc * 3 + 3)
            r = c.res[:, dc * T:(dc + 1) * T]
            c.sV.inc(vector.tensor_sub(c.sbl[:, h:h + T], r, c.sbh[:, h:h + T]))
            c.sV.inc(vector.tensor_sub(c.sql[:, h:h + T], c.sqbuf[:, h:h + T],
                                       c.sqh[:, h:h + T]))

    @block.tensor
    def _(tensor):
        tensor.wait_ge(c.sV.h, v0)
        tensor.wait_ge(c.sA.h, a0)
        tensor.wait_ge(c.sG.h, g0)
        for dc in range(DC):
            h = (dc % 2) * T
            tensor.wait_ge(c.sV.h, v0 + dc * 2 + 2)
            st = (dc == 0)
            sp = (dc == DC - 1)
            for q in range(4):
                sl = slice(h + q * 512, h + (q + 1) * 512)
                c.sT.inc(tensor.matmul(c.PB[q][0:64, :], c.ones_bf[:, 0:64],
                                       c.sbh[:, sl], start=st, stop=False))
                c.sT.inc(tensor.matmul(c.PB[q][0:64, :], c.ones_bf[:, 0:64],
                                       c.sbl[:, sl], start=False, stop=sp))
                c.sT.inc(tensor.matmul(c.PB[4 + q][0:64, :], c.ones_bf[:, 0:64],
                                       c.sqh[:, sl], start=st, stop=False))
                c.sT.inc(tensor.matmul(c.PB[4 + q][0:64, :], c.ones_bf[:, 0:64],
                                       c.sql[:, sl], start=False, stop=sp))

    t1 = c.sT.n

    @block.vector
    def _(vector):
        vector.wait_ge(c.sT.h, t1)
        for q in range(4):
            sl = slice(q * 512, (q + 1) * 512)
            c.sV.inc(vector.tensor_scalar_mul(c.rows[0:1, sl], c.PB[q][0:1, :], -1.0 / D))
            c.sV.inc(vector.tensor_scalar_mul(c.rows[32:33, sl], c.PB[4 + q][0:1, :], 1.0 / D))
            vector.wait_ge(c.sV.h, c.sV.n)
            c.sV.inc(vector.scalar_tensor_tensor(
                c.PB[4 + q][32:33, :], c.rows[0:1, sl], -1.0, c.rows[0:1, sl],
                ALU.mult, ALU.mult))
            vector.wait_ge(c.sV.h, c.sV.n)
            c.sV.inc(vector.tensor_add(c.rows[64:65, sl], c.rows[32:33, sl],
                                       c.PB[4 + q][32:33, :]))
        vector.wait_ge(c.sV.h, c.sV.n)
        c.sV.inc(vector.tensor_scalar_add(c.rows[64:65, 0:T], c.rows[64:65, 0:T], EPS_LN))

    a1 = c.sA.n
    @block.scalar
    def _(scalar):
        scalar.wait_ge(c.sV.h, c.sV.n)
        c.sA.inc(scalar.activation(c.rows[96:97, 0:T], c.rows[64:65, 0:T], AF.Sqrt,
                                   bias=0.0, scale=1.0))

    @block.vector
    def _(vector):
        vector.wait_ge(c.sA.h, a1 + 1)
        c.sV.inc(vector.reciprocal(c.rows[32:33, 0:T], c.rows[96:97, 0:T]))


def emit_ffn0_main(c):
    block = c.block
    X2H = c.ARB[:, 0:4096]
    X2L = c.ARB[:, 4096:8192]
    HH = c.ARB[:, 16384:32768]
    HL = c.ARB[:, 32768:49152]
    WQS = [c.ARB[:, 49152:51200], c.ARB[:, 51200:53248]]
    hfull = c.ARF[:, 2560:3072]

    @block.sync
    def _(sync):
        for sm in c.chQ.sems:
            sync.wait_ge(sm.h, sm.n)

    for tq in range(4):
        qsl = slice(tq * 512, (tq + 1) * 512)
        t0, a0, v0 = c.sT.n, c.sA.n, c.sV.n
        s_0 = c.sS.n
        wtok = {}

        @block.tensor
        def _(tensor):
            tensor.wait_ge(c.sV.h, v0)
            tensor.wait_ge(c.sA.h, a0)
            c.sT.inc(tensor.matmul(c.PB[0][:, :], c.ones_f32[0:1, :], c.rows[0:1, qsl],
                                   start=True, stop=True))
            c.sT.inc(tensor.matmul(c.PB[1][:, :], c.ones_f32[32:33, :], c.rows[32:33, qsl],
                                   start=True, stop=True))

        t_bc = c.sT.n

        @block.vector
        def _(vector):
            vector.wait_ge(c.sT.h, t_bc)
            for dc in range(DC):
                src = slice(dc * T + tq * 512, dc * T + (tq + 1) * 512)
                d2 = slice(dc * 512, (dc + 1) * 512)
                c.sV.inc(vector.tensor_add(c.tmp1[:, 0:512], c.res[:, src], c.PB[0][:, :]))
                vector.wait_ge(c.sV.h, c.sV.n)
                c.sV.inc(vector.tensor_mul(c.xfull[:, 0:512], c.tmp1[:, 0:512], c.PB[1][:, :]))
                vector.wait_ge(c.sV.h, c.sV.n)
                vector.wait_ge(c.sA.h, a0 + dc + 1)
                c.sV.inc(vector.tensor_sub(X2L[:, d2], c.xfull[:, 0:512], X2H[:, d2]))

        @block.scalar
        def _(scalar):
            for dc in range(DC):
                d2 = slice(dc * 512, (dc + 1) * 512)
                scalar.wait_ge(c.sV.h, v0 + dc * 3 + 2)
                c.sA.inc(scalar.activation(X2H[:, d2], c.xfull[:, 0:512], AF.Copy,
                                           bias=0.0, scale=1.0))

        v_norm = c.sV.n
        a_norm = c.sA.n

        @block.sync
        def _(sync):
            for fc in range(32):
                if fc >= 2:
                    sync.wait_ge(c.sT.h, t_bc + (fc - 1) * 16)
                if fc < 2:
                    sync.wait_ge(c.sT.h, t_bc)
                wtok[fc] = c.chW.issue(
                    sync, WQS[fc % 2][:, 0:1024].rearrange("p (dc f) -> p dc f", f=128),
                    c.wq1t_dram.ap()[fc].rearrange("(dc p) f -> p dc f", p=128))

        @block.tensor
        def _(tensor):
            tensor.wait_ge(c.sV.h, v_norm)
            for fc in range(32):
                tensor.wait_ge(*wtok[fc])
                pb = c.PB[6 + fc % 2]
                if fc >= 2:
                    tensor.wait_ge(c.sV.h, v_norm + (fc - 1))
                n = 0
                for dc in range(DC):
                    wt = WQS[fc % 2][:, dc * 128:(dc + 1) * 128]
                    for x in (X2H, X2L):
                        n += 1
                        c.sT.inc(tensor.matmul(pb[:, :], wt,
                                               x[:, dc * 512:(dc + 1) * 512],
                                               start=(n == 1), stop=(n == 16)))

        t_m1 = c.sT.n

        @block.scalar
        def _(scalar):
            for fc in range(32):
                pb = c.PB[6 + fc % 2]
                hsl = slice(fc * 512, (fc + 1) * 512)
                scalar.wait_ge(c.sT.h, t_bc + (fc + 1) * 16)
                if fc >= 1:
                    scalar.wait_ge(c.sV.h, v_norm + fc)
                bcol = c.bias_ffn[:, fc:fc + 1]
                c.sA.inc(scalar.activation(HH[:, hsl], pb[:, :], AF.Relu,
                                           bias=bcol, scale=1.0))
                c.sA.inc(scalar.activation(hfull, pb[:, :], AF.Relu,
                                           bias=bcol, scale=1.0))

        @block.vector
        def _(vector):
            for fc in range(32):
                hsl = slice(fc * 512, (fc + 1) * 512)
                vector.wait_ge(c.sA.h, a_norm + (fc + 1) * 2)
                c.sV.inc(vector.tensor_sub(HL[:, hsl], hfull, HH[:, hsl]))

        v_h = c.sV.n
        a_h = c.sA.n

        wtok2 = {}

        @block.sync
        def _(sync):
            for fc in range(32):
                sync.wait_ge(c.sT.h, t_m1 + (fc - 1) * 16 if fc >= 2 else t_m1)
                wtok2[fc] = c.chW.issue(sync, WQS[fc % 2][:, 0:1024],
                                        c.wq2t_dram.ap()[fc])

        @block.tensor
        def _(tensor):
            tensor.wait_ge(c.sV.h, v_h)
            tensor.wait_ge(c.sA.h, a_h)
            for fc in range(32):
                tensor.wait_ge(*wtok2[fc])
                hsl = slice(fc * 512, (fc + 1) * 512)
                n = 0
                for dco in range(DC):
                    wt = WQS[fc % 2][:, dco * 128:(dco + 1) * 128]
                    for hx in (HH, HL):
                        n += 1
                        c.sT.inc(tensor.matmul(c.PB[dco][:, :], wt, hx[:, hsl],
                                               start=(fc == 0 and n % 2 == 1),
                                               stop=(fc == 31 and n % 2 == 0)))

        t_m2 = c.sT.n

        @block.vector
        def _(vector):
            vector.wait_ge(c.sT.h, t_m2)
            for dco in range(DC):
                dst = c.res[:, dco * T + tq * 512: dco * T + (tq + 1) * 512]
                c.sV.inc(vector.scalar_tensor_tensor(
                    dst, c.PB[dco][:, :], c.bias_ffn[:, 32 + dco:32 + dco + 1], dst,
                    ALU.add, ALU.add))


def emit_ffn0(c):
    emit_ffn0_gamma(c, c.w1, FF0, 1024, 12)
    emit_ffn0_gamma(c, c.w2, D, 4096, 13)
    emit_ffn0_wprep(c, c.w1, FF0, 1024, 12, (5122, 5123),
                    lambda ch, cg, dc: c.wq1t_dram.ap()[ch, dc * 128:(dc + 1) * 128, :])
    emit_ffn0_wprep(c, c.w2, D, 4096, 13, (5124, 5125),
                    lambda ch, cg, dc: c.wq2t_dram.ap()[cg * 8 + dc, :, ch * 128:(ch + 1) * 128])
    emit_ln_stats(c)
    emit_ffn0_main(c)



def emit_ssm(c):
    block = c.block
    F = lambda a, b: c.ARB[:, 2 * a:2 * b].bitcast(F32)
    X3C = F(0, 4112)
    DTC = F(4112, 8208)
    UC = c.ARF[:, 0:4096]
    ALPHA2 = [F(8208, 12304), F(12304, 16400)]
    BETA2 = [F(16400, 20496), F(20496, 24592)]
    YR = F(24592, 25104)
    NMBc = F(25104, 25616)
    RSBc = F(25616, 26128)
    STATE = F(26128, 26256)
    SEEDT = F(26256, 26264)
    ACO = c.ropeA[:, 0:128]
    DCO = c.ropeA[:, 128:136]
    BDTP = c.ropeA[:, 136:144]
    CONVK = c.ropeA[:, 144:168]
    BDT = c.ropeA[:, 168:169]
    BBC = c.ropeA[:, 169:170]
    WBCDT = c.ropeA[:, 256:1024]
    WDTP = c.ropeB[:, 0:1024]
    DTL = c.ARF[0:64, 4096:4608]
    BCR = c.ARF[64:96, 4096:4608]

    @block.sync
    def _(sync):
        _wait_all(sync, c)
        c.tok_ssm = []
        ctx_nc = c.nc.allow_non_contiguous_dma(reason="tiny ssm weight vectors")
        ctx_nc.__enter__()
        c.tok_ssm.append(c.chW.issue(
            sync, c.ropeA[:, 0:128].rearrange("p (dc n) -> p dc n", n=16),
            c.a_log.rearrange("(dc p) n -> p dc n", p=128)))
        c.tok_ssm.append(c.chW.issue(
            sync, DCO, c.dcoeff.rearrange("(dc p) -> p dc", p=128)))
        c.tok_ssm.append(c.chW.issue(
            sync, BDTP, c.b_dtp.rearrange("(dc p) -> p dc", p=128)))
        c.tok_ssm.append(c.chW.issue(
            sync, CONVK.rearrange("p (k dc) -> p k dc", k=3),
            c.conv_k.rearrange("k o (dc p) -> p k (o dc)", p=128)))
        c.tok_ssm.append(c.chW.issue(sync, BDT[0:64, :], c.b_dt.rearrange("(a o) -> a o", o=1)))
        c.tok_ssm.append(c.chW.issue(sync, c.bbc_col,
                                     c.b_bc.rearrange("(a o) -> a o", o=1)))
        c.tok_ssm.append(c.chW.issue(
            sync, WBCDT.rearrange("p (dc f) -> p dc f", f=96)[:, :, 0:64],
            c.w_dt.rearrange("(dc p) f -> p dc f", p=128)))
        c.tok_ssm.append(c.chW.issue(
            sync, WBCDT.rearrange("p (dc f) -> p dc f", f=96)[:, :, 64:96],
            c.w_bc.rearrange("(dc p) f -> p dc f", p=128)))
        c.tok_ssm.append(c.chW.issue(sync, WDTP[0:64, :], c.w_dtp))
        ctx_nc.__exit__(None, None, None)

    @block.scalar
    def _(scalar):
        for t in c.tok_ssm:
            scalar.wait_ge(*t)
        c.sA.inc(scalar.activation(ACO, ACO, AF.Exp, bias=0.0, scale=1.0))

    @block.vector
    def _(vector):
        vector.wait_ge(c.sA.h, c.sA.n)
        c.sV.inc(vector.tensor_scalar_mul(ACO, ACO, -1.0))
        c.sV.inc(vector.memset(STATE, 0.0))

    emit_ln_stats(c)

    @block.sync
    def _(sync):
        sync.wait_ge(c.sV.h, c.sV.n)
        c.tok_lns = [c.chC.issue(sync, c.ln_dram.ap()[0:1, :], c.rows[0:1, 0:2048]),
                     c.chC.issue(sync, c.ln_dram.ap()[1:2, :], c.rows[32:33, 0:2048])]

    for tc in range(4):
        _emit_ssm_chunk(c, tc, X3C, DTC, UC, ALPHA2, BETA2, YR, NMBc, RSBc,
                        SEEDT, ACO, DCO, BDTP, CONVK, BDT, WBCDT, WDTP, DTL,
                        BCR, STATE)


def _emit_ssm_chunk(c, tc, X3C, DTC, UC, ALPHA2, BETA2, YR, NMBc, RSBc,
                    SEEDT, ACO, DCO, BDTP, CONVK, BDT, WBCDT, WDTP, DTL,
                    BCR, STATE):
    block = c.block
    t0, a0, v0 = c.sT.n, c.sA.n, c.sV.n

    @block.sync
    def _(sync):
        if tc == 0:
            for t in c.tok_lns:
                sync.wait_ge(*t)
        else:
            sync.wait_ge(c.sV.h, v0)
        c.tok_nm = c.chC.issue(sync, NMBc,
                               c.ln_dram.ap()[0:1, tc * 512:(tc + 1) * 512]
                               .partition_broadcast(128))
        c.tok_rs = c.chC.issue(sync, RSBc,
                               c.ln_dram.ap()[1:2, tc * 512:(tc + 1) * 512]
                               .partition_broadcast(128))

    @block.vector
    def _(vector):
        vector.wait_ge(*c.tok_nm)
        vector.wait_ge(*c.tok_rs)
        if tc > 0:
            vector.wait_ge(*c.tok_bcd)
        for dc in range(DC):
            base = dc * 514
            if tc > 0:
                c.sV.inc(vector.tensor_copy(X3C[:, base:base + 1],
                                            X3C[:, base + 512:base + 513]))
                vector.wait_ge(c.sV.h, c.sV.n)
            src = slice(dc * T + tc * 512, dc * T + (tc + 1) * 512)
            c.sV.inc(vector.tensor_add(c.tmp1[:, 0:512], c.res[:, src], NMBc))
            vector.wait_ge(c.sV.h, c.sV.n)
            c.sV.inc(vector.tensor_mul(X3C[:, base + 1:base + 513],
                                       c.tmp1[:, 0:512], RSBc))
            vector.wait_ge(c.sV.h, c.sV.n)

    @block.vector
    def _(vector):
        for dc in range(DC):
            b = dc * 514
            ub = dc * 512
            k0 = CONVK[:, dc:dc + 1]
            k1 = CONVK[:, 8 + dc:8 + dc + 1]
            k2 = CONVK[:, 16 + dc:16 + dc + 1]
            c.sV.inc(vector.tensor_scalar_mul(c.tmp1[:, 0:512], X3C[:, b:b + 512], k0))
            vector.wait_ge(c.sV.h, c.sV.n)
            c.sV.inc(vector.scalar_tensor_tensor(
                c.tmp1[:, 0:512], X3C[:, b + 1:b + 513], k1, c.tmp1[:, 0:512],
                ALU.mult, ALU.add))
            vector.wait_ge(c.sV.h, c.sV.n)
            c.sV.inc(vector.scalar_tensor_tensor(
                UC[:, ub:ub + 512], X3C[:, b + 2:b + 514], k2, c.tmp1[:, 0:512],
                ALU.mult, ALU.add))
            vector.wait_ge(c.sV.h, c.sV.n)
            if tc == 0:
                c.sV.inc(vector.tensor_scalar_mul(c.tmp1[:, 0:1], X3C[:, b + 1:b + 2], k1))
                vector.wait_ge(c.sV.h, c.sV.n)
                c.sV.inc(vector.scalar_tensor_tensor(
                    UC[:, ub:ub + 1], X3C[:, b + 2:b + 3], k2, c.tmp1[:, 0:1],
                    ALU.mult, ALU.add))
                vector.wait_ge(c.sV.h, c.sV.n)
            if tc == 3:
                c.sV.inc(vector.tensor_scalar_mul(c.tmp1[:, 0:1], X3C[:, b + 511:b + 512], k0))
                vector.wait_ge(c.sV.h, c.sV.n)
                c.sV.inc(vector.scalar_tensor_tensor(
                    UC[:, ub + 511:ub + 512], X3C[:, b + 512:b + 513], k1, c.tmp1[:, 0:1],
                    ALU.mult, ALU.add))
                vector.wait_ge(c.sV.h, c.sV.n)

    v_conv = c.sV.n

    @block.scalar
    def _(scalar):
        scalar.wait_ge(c.sV.h, v_conv)
        c.sA.inc(scalar.activation(UC[:, 0:4096], UC[:, 0:4096], AF.Silu,
                                   bias=0.0, scale=1.0))

    a_silu = c.sA.n

    @block.tensor
    def _(tensor):
        tensor.wait_ge(c.sA.h, a_silu)
        tensor.wait_ge(c.sV.h, v_conv)
        for dc in range(DC):
            c.sT.inc(tensor.matmul(c.PB[6][0:96, :],
                                   WBCDT[:, dc * 96:(dc + 1) * 96],
                                   UC[:, dc * 512:(dc + 1) * 512],
                                   start=(dc == 0), stop=(dc == DC - 1)))

    t_p1 = c.sT.n

    @block.scalar
    def _(scalar):
        scalar.wait_ge(c.sT.h, t_p1)
        c.sA.inc(scalar.activation(DTL, c.PB[6][0:64, :], AF.Identity,
                                   bias=BDT[0:64, :], scale=1.0))
        c.sA.inc(scalar.activation(BCR, c.PB[6][64:96, :], AF.Identity,
                                   bias=c.bbc_col, scale=1.0))

    a_p1 = c.sA.n

    @block.tensor
    def _(tensor):
        tensor.wait_ge(c.sA.h, a_p1)
        for dc in range(DC):
            pb = c.PB[7] if dc % 2 else c.PB[5]
            if dc >= 2:
                tensor.wait_ge(c.sA.h, a_p1 + (dc - 1) * 4)
            c.sT.inc(tensor.matmul(pb[:, :], WDTP[0:64, dc * 128:(dc + 1) * 128],
                                   DTL, start=True, stop=True))

    t_p2 = c.sT.n

    sp = c.xfull[:, 0:512]
    v_sp0 = c.sV.n

    @block.scalar
    def _(scalar):
        for dc in range(DC):
            pb = c.PB[7] if dc % 2 else c.PB[5]
            scalar.wait_ge(c.sT.h, t_p2 - (DC - 1 - dc))
            if dc >= 1:
                scalar.wait_ge(c.sV.h, v_sp0 + dc)
            bcol = BDTP[:, dc:dc + 1]
            c.sA.inc(scalar.activation(sp, pb[:, :], AF.Abs, bias=bcol, scale=1.0))
            scalar.wait_ge(c.sA.h, c.sA.n)
            c.sA.inc(scalar.activation(sp, sp, AF.Exp, bias=0.0, scale=-1.0))
            scalar.wait_ge(c.sA.h, c.sA.n)
            c.sA.inc(scalar.activation(sp, sp, AF.Ln, bias=1.0, scale=1.0))
            c.sA.inc(scalar.activation(DTC[:, dc * 512:(dc + 1) * 512], pb[:, :],
                                       AF.Relu, bias=bcol, scale=1.0))

    a_dt0 = c.sA.n

    @block.vector
    def _(vector):
        for dc in range(DC):
            vector.wait_ge(c.sA.h, a_dt0 - (DC - 1 - dc) * 4)
            dsl_ = slice(dc * 512, (dc + 1) * 512)
            c.sV.inc(vector.tensor_add(DTC[:, dsl_], DTC[:, dsl_], sp))

    a_dt = c.sA.n

    @block.vector
    def _(vector):
        vector.wait_ge(c.sA.h, a_dt)
        vector.wait_ge(c.sV.h, c.sV.n)
        for dc in range(DC):
            c.sV.inc(vector.tensor_mul(UC[:, dc * 512:(dc + 1) * 512],
                                       X3C[:, dc * 514 + 1:dc * 514 + 513],
                                       DTC[:, dc * 512:(dc + 1) * 512]))

    @block.sync
    def _(sync):
        sync.wait_ge(c.sA.h, a_p1)
        if tc > 0:
            sync.wait_ge(*c.ssm_last_restage)
        c.tok_bcd = c.chC.issue(sync, c.bc_dram.ap(), BCR)

    for nh in range(2):
        for pair in range(4):
            _emit_ssm_pair(c, tc, nh, pair, X3C, DTC, UC, ALPHA2, BETA2, YR,
                           SEEDT, ACO, DCO, STATE)


def _emit_ssm_pair(c, tc, nh, pair, X3C, DTC, UC, ALPHA2, BETA2, YR, SEEDT,
                   ACO, DCO, STATE):
    block = c.block
    dcs = (2 * pair, 2 * pair + 1)
    t0, a0, v0 = c.sT.n, c.sA.n, c.sV.n

    @block.sync
    def _(sync):
        sync.wait_ge(*c.tok_bcd)
        sync.wait_ge(c.sT.h, t0)
        c.ssm_tokB = [c.chC.issue(sync, c.rows[0:1, 0:2048],
                                  c.bc_dram.ap()[nh * 8:nh * 8 + 4, :]),
                      c.chC.issue(sync, c.rows[32:33, 0:2048],
                                  c.bc_dram.ap()[nh * 8 + 4:nh * 8 + 8, :])]

    @block.tensor
    def _(tensor):
        for t in c.ssm_tokB:
            tensor.wait_ge(*t)
        tensor.wait_ge(c.sV.h, v0)
        for j in range(4):
            c.sT.inc(tensor.matmul(c.PB[j][:, :], c.ones_f32[0:1, :],
                                   c.rows[0:1, j * 512:(j + 1) * 512],
                                   start=True, stop=True))
        for j in range(4):
            c.sT.inc(tensor.matmul(c.PB[4 + j][:, :], c.ones_f32[32:33, :],
                                   c.rows[32:33, j * 512:(j + 1) * 512],
                                   start=True, stop=True))

    t_bmm = c.sT.n

    @block.scalar
    def _(scalar):
        scalar.wait_ge(c.sV.h, v0)
        for i, dc in enumerate(dcs):
            for n8 in range(8):
                n = nh * 8 + n8
                c.sA.inc(scalar.activation(
                    ALPHA2[i][:, n8 * 512:(n8 + 1) * 512],
                    DTC[:, dc * 512:(dc + 1) * 512], AF.Exp,
                    scale=ACO[:, dc * 16 + n:dc * 16 + n + 1], bias=0.0))

    a_alpha = c.sA.n
    vq = {}

    @block.vector
    def _(vector):
        vector.wait_ge(c.sT.h, t_bmm)
        vector.wait_ge(c.sA.h, a_alpha)
        for q in range(2):
            for i, dc in enumerate(dcs):
                for j in range(4):
                    n8 = q * 4 + j
                    c.sV.inc(vector.tensor_tensor(
                        BETA2[i][:, n8 * 512:(n8 + 1) * 512],
                        UC[:, dc * 512:(dc + 1) * 512],
                        c.PB[n8][:, :], ALU.mult))
            vector.wait_ge(c.sV.h, c.sV.n)
            vq[q] = c.sV.n
        for i, dc in enumerate(dcs):
            ssl = slice(dc * 16 + nh * 8, dc * 16 + nh * 8 + 8)
            a_c0 = ALPHA2[i].rearrange("p (n t) -> p n t", t=512)[:, :, 0:1]
            b_c0 = BETA2[i].rearrange("p (n t) -> p n t", t=512)[:, :, 0:1]
            c.sV.inc(vector.tensor_tensor(SEEDT.unsqueeze(2), a_c0,
                                          STATE[:, ssl].unsqueeze(2), ALU.mult))
            vector.wait_ge(c.sV.h, c.sV.n)
            c.sV.inc(vector.tensor_tensor(b_c0, b_c0, SEEDT.unsqueeze(2),
                                          ALU.add))
            vector.wait_ge(c.sV.h, c.sV.n)
            c.sV.inc(vector.memset(a_c0, 0.0))
            vector.wait_ge(c.sV.h, c.sV.n)
            c.sV.inc(vector.tensor_tensor_scan(
                BETA2[i], ALPHA2[i], BETA2[i], 0.0, ALU.mult, ALU.add))
            vector.wait_ge(c.sV.h, c.sV.n)
            c.sV.inc(vector.tensor_copy(
                STATE[:, ssl],
                BETA2[i].rearrange("p (n t) -> p n t", t=512)[:, :, 511]))
            vector.wait_ge(c.sV.h, c.sV.n)

    @block.sync
    def _(sync):
        sync.wait_ge(c.sT.h, t_bmm)
        tokC = [c.chC.issue(sync, c.rows[0:1, 0:2048],
                            c.bc_dram.ap()[16 + nh * 8:16 + nh * 8 + 4, :]),
                c.chC.issue(sync, c.rows[32:33, 0:2048],
                            c.bc_dram.ap()[16 + nh * 8 + 4:16 + nh * 8 + 8, :])]
        c.ssm_tokC = tokC
        c.ssm_last_restage = tokC[1]

    @block.tensor
    def _(tensor):
        for q in range(2):
            tensor.wait_ge(*c.ssm_tokC[q])
            tensor.wait_ge(c.sV.h, vq[q])
            for j in range(4):
                n8 = q * 4 + j
                lhs = c.ones_f32[0:1, :] if q == 0 else c.ones_f32[32:33, :]
                row = c.rows[0:1] if q == 0 else c.rows[32:33]
                c.sT.inc(tensor.matmul(c.PB[n8][:, :], lhs,
                                       row[:, j * 512:(j + 1) * 512],
                                       start=True, stop=True))

    t_cmm = c.sT.n

    @block.vector
    def _(vector):
        vector.wait_ge(c.sT.h, t_cmm)
        for i, dc in enumerate(dcs):
            for n8 in range(8):
                c.sV.inc(vector.tensor_tensor(
                    BETA2[i][:, n8 * 512:(n8 + 1) * 512],
                    BETA2[i][:, n8 * 512:(n8 + 1) * 512],
                    c.PB[n8][:, :], ALU.mult))
            vector.wait_ge(c.sV.h, c.sV.n)
            c.sV.inc(vector.tensor_reduce(
                YR, BETA2[i].rearrange("p (n t) -> p t n", t=512), AX.X,
                ALU.add))
            vector.wait_ge(c.sV.h, c.sV.n)
            if nh == 0:
                xsl = X3C[:, dc * 514 + 1:dc * 514 + 513]
                c.sV.inc(vector.scalar_tensor_tensor(
                    YR, xsl, DCO[:, dc:dc + 1], YR, ALU.mult, ALU.add))
                vector.wait_ge(c.sV.h, c.sV.n)
            dst = c.res[:, dc * T + tc * 512: dc * T + (tc + 1) * 512]
            c.sV.inc(vector.tensor_add(dst, dst, YR))
            vector.wait_ge(c.sV.h, c.sV.n)



QMAXA = 32767.0


def _emit_ar(c, k, srcs, dsts, v_pre):
    block = c.block
    n = len(srcs)
    stg = c.ARF[0:1, 1432:1432 + n]

    @block.gpsimd
    def _(gpsimd):
        gpsimd.wait_ge(c.sV.h, v_pre)
        for i, s in enumerate(srcs):
            c.sG.inc(gpsimd.tensor_reduce(c.ARF[0:1, 1432 + i:1433 + i], s,
                                          AX.C, ALU.max))

    g_red = c.sG.n

    @block.sync
    def _(sync):
        sync.wait_ge(c.sG.h, g_red)
        c.tok_ar = c.chC.issue(sync, c.cc_in[k].ap()[:, 0:n], stg)

    @block.gpsimd
    def _(gpsimd):
        gpsimd.wait_ge(*c.tok_ar)
        c.sG.inc(gpsimd.collective_compute(
            "AllReduce", ALU.max, replica_groups=[c.ar_group],
            ins=[c.cc_in[k].ap()[:, 0:n]], outs=[c.cc_out[k].ap()[:, 0:n]]))

    g_cc = c.sG.n
    toks = []

    @block.sync
    def _(sync):
        sync.wait_ge(c.sG.h, g_cc)
        for i in range(n):
            raw = c.ARF[:, 1435 + i:1436 + i]
            toks.append(c.chC.issue(
                sync, raw, c.cc_out[k].ap()[:, i:i + 1].partition_broadcast(128)))

    @block.vector
    def _(vector):
        for t in toks:
            vector.wait_ge(*t)
        for i, (s_col, r_col, qden) in enumerate(dsts):
            raw = c.ARF[:, 1435 + i:1436 + i]
            c.sV.inc(vector.tensor_scalar(s_col, raw, 1.0 / qden, 1e-12,
                                          ALU.mult, ALU.max))
            vector.wait_ge(c.sV.h, c.sV.n)
            c.sV.inc(vector.reciprocal(r_col, s_col))
            vector.wait_ge(c.sV.h, c.sV.n)


def emit_l1ffn(c):
    block = c.block
    Bv = lambda a, b: c.ARB[:, a:b].bitcast(F32)
    X4F = Bv(0, 32768)
    X4I = c.ARB[:, 0:32768].bitcast(I32)
    NMB2 = Bv(36864, 40960)
    RSB2 = Bv(40960, 45056)
    S = lambda i: c.ARF[:, 1024 + i:1025 + i]
    mcol = S(0)
    sx, rx = S(1), S(2)
    sg, rg = S(3), S(4)
    su, ru = S(5), S(6)
    s3, r3 = S(7), S(8)
    s4, r4 = S(9), S(10)
    MXC = c.ARF[:, 1040:1048]
    MXWG = c.ARF[:, 2048:2144]
    MXWU = c.ARF[:, 2144:2240]
    MXW3 = c.ARF[:, 2240:2336]
    MXW4 = c.ARF[:, 2336:2432]
    mg, mu = c.ARF[:, 2432:2433], c.ARF[:, 2433:2434]

    emit_ln_stats(c)
    a_st, v_st = c.sA.n, c.sV.n

    @block.tensor
    def _(tensor):
        tensor.wait_ge(c.sA.h, a_st)
        tensor.wait_ge(c.sV.h, v_st)
        for q in range(4):
            sl = slice(q * 512, (q + 1) * 512)
            c.sT.inc(tensor.matmul(c.PB[q][:, :], c.ones_f32[0:1, :],
                                   c.rows[0:1, sl], start=True, stop=True))
            c.sT.inc(tensor.matmul(c.PB[4 + q][:, :], c.ones_f32[32:33, :],
                                   c.rows[32:33, sl], start=True, stop=True))

    t_stb = c.sT.n

    @block.scalar
    def _(scalar):
        scalar.wait_ge(c.sT.h, t_stb)
        for q in range(4):
            sl = slice(q * 512, (q + 1) * 512)
            c.sA.inc(scalar.activation(NMB2[:, sl], c.PB[q][:, :], AF.Copy,
                                       bias=0.0, scale=1.0))
            c.sA.inc(scalar.activation(RSB2[:, sl], c.PB[4 + q][:, :], AF.Copy,
                                       bias=0.0, scale=1.0))

    a_stb = c.sA.n
    tmpn = Bv(32768, 36864)

    @block.vector
    def _(vector):
        vector.wait_ge(c.sA.h, a_stb)
        for dc in range(DC):
            xsl = slice(dc * T, (dc + 1) * T)
            c.sV.inc(vector.tensor_add(tmpn, c.res[:, xsl], NMB2))
            vector.wait_ge(c.sV.h, c.sV.n)
            c.sV.inc(vector.tensor_mul(X4F[:, xsl], tmpn, RSB2))
            vector.wait_ge(c.sV.h, c.sV.n)
            c.sV.inc(vector.tensor_reduce(MXC[:, dc:dc + 1], X4F[:, xsl],
                                          AX.X, ALU.max, apply_absolute_value=True))
        vector.wait_ge(c.sV.h, c.sV.n)
        c.sV.inc(vector.tensor_reduce(mcol, MXC, AX.X, ALU.max))

    _emit_ar(c, 0, [mcol], [(sx, rx, QMAXA)], c.sV.n)
    v_ar1 = c.sV.n

    @block.scalar
    def _(scalar):
        scalar.wait_ge(c.sV.h, v_ar1)
        for dc in range(DC):
            xsl = slice(dc * T, (dc + 1) * T)
            c.sA.inc(scalar.activation(X4I[:, xsl], X4F[:, xsl], AF.Copy,
                                       bias=0.0, scale=rx))
            scalar.wait_ge(c.sA.h, c.sA.n)
            c.sA.inc(scalar.activation(X4F[:, xsl], X4I[:, xsl], AF.Copy,
                                       bias=0.0, scale=sx))
            scalar.wait_ge(c.sA.h, c.sA.n)

    c.l1_a_xq = c.sA.n

    _emit_l1_wmm(c, c.w_gate, c.gate_dram, MXWG)
    _emit_l1_wmm(c, c.w_up, c.up_dram, MXWU)

    @block.vector
    def _(vector):
        vector.wait_ge(c.sV.h, c.sV.n)
        c.sV.inc(vector.tensor_reduce(mg, MXWG, AX.X, ALU.max))
        c.sV.inc(vector.tensor_reduce(mu, MXWU, AX.X, ALU.max))

    _emit_ar(c, 1, [mg, mu], [(sg, rg, QMAXA), (su, ru, QMAXA)], c.sV.n)

    _emit_l1_downprep(c)
    _emit_l1_sig(c, MXW3)

    @block.vector
    def _(vector):
        vector.wait_ge(c.sV.h, c.sV.n)
        c.sV.inc(vector.tensor_reduce(mg, MXW3[:, 0:16], AX.X, ALU.max))

    _emit_ar(c, 2, [mg], [(s3, r3, QMAXA)], c.sV.n)

    _emit_l1_prod(c, MXW4)

    @block.vector
    def _(vector):
        vector.wait_ge(c.sV.h, c.sV.n)
        c.sV.inc(vector.tensor_reduce(mg, MXW4[:, 0:16], AX.X, ALU.max))

    _emit_ar(c, 3, [mg], [(s4, r4, QMAXA)], c.sV.n)

    _emit_l1_down(c)


def _emit_l1_wmm(c, wsrc, dst_dram, mxw):
    block = c.block
    Bv = lambda a, b: c.ARB[:, a:b].bitcast(F32)
    WST = [Bv(49152, 51200), Bv(51200, 53248)]
    WSC = Bv(53248, 55296)
    WSCI = c.ARB[:, 53248:55296].bitcast(I32)
    WQ2 = Bv(32768, 34816)
    STA = [Bv(45056, 47104), Bv(47104, 49152)]
    EVS = [Bv(36864 + 1024 * i, 37888 + 1024 * i) for i in range(4)]
    MMB = [c.PB[0], c.PB[1], c.PB[6], c.PB[7]]
    BMX = c.ARF[:, 1048:1080]
    SBL = c.ARF[:, 1080:1112]
    RBL = c.ARF[:, 1112:1144]

    v_entry, t_entry = c.sV.n, c.sT.n
    spt_carry = getattr(c, "l1_spt_carry", None)
    nmm_carry = getattr(c, "l1_nmm_carry", None)

    ntr = {}
    nst = {}
    nmm = {}
    nvq = {}
    nev = {}
    nmx = {}
    spt = {}
    ldt = {}

    for fc in range(24):
        @block.sync
        def _(sync, fc=fc):
            if fc >= 2:
                sync.wait_ge(c.sV.h, nvq[fc - 2])
            elif v_entry:
                sync.wait_ge(c.sV.h, v_entry)
            ldt[fc] = c.chW.issue(sync, WST[fc % 2],
                                  wsrc[fc * 128:(fc + 1) * 128, :])

        @block.vector
        def _(vector, fc=fc):
            vector.wait_ge(*ldt[fc])
            if fc >= 1:
                vector.wait_ge(c.sT.h, ntr[fc - 1])
            else:
                vector.wait_ge(c.sT.h, t_entry)
            c.sV.inc(vector.tensor_reduce(
                BMX, WST[fc % 2].rearrange("p (b k) -> p b k", k=32),
                AX.X, ALU.max, apply_absolute_value=True))
            vector.wait_ge(c.sV.h, c.sV.n)
            c.sV.inc(vector.tensor_scalar(SBL, BMX, 1.0 / 7.0, 1e-12,
                                          ALU.mult, ALU.max))
            vector.wait_ge(c.sV.h, c.sV.n)
            c.sV.inc(vector.reciprocal(RBL, SBL))
            vector.wait_ge(c.sV.h, c.sV.n)
            c.sV.inc(vector.tensor_tensor(
                WSC.rearrange("p (b k) -> p b k", k=32),
                WST[fc % 2].rearrange("p (b k) -> p b k", k=32),
                RBL.unsqueeze(2).broadcast_to([128, 32, 32]), ALU.mult))

        v_wsc = c.sV.n

        @block.scalar
        def _(scalar, fc=fc, v_wsc=v_wsc):
            scalar.wait_ge(c.sV.h, v_wsc)
            c.sA.inc(scalar.activation(WSCI, WSC, AF.Copy, bias=0.0, scale=1.0))
            scalar.wait_ge(c.sA.h, c.sA.n)
            c.sA.inc(scalar.activation(WSC, WSCI, AF.Copy, bias=0.0, scale=1.0))

        a_wq = c.sA.n

        @block.vector
        def _(vector, fc=fc, a_wq=a_wq):
            vector.wait_ge(c.sA.h, a_wq)
            c.sV.inc(vector.tensor_tensor(
                WQ2.rearrange("p (b k) -> p b k", k=32),
                WSC.rearrange("p (b k) -> p b k", k=32),
                SBL.unsqueeze(2).broadcast_to([128, 32, 32]), ALU.mult))

        nvq[fc] = c.sV.n
        a_tr_base = c.sA.n

        @block.tensor
        def _(tensor, fc=fc, a_tr_base=a_tr_base):
            tensor.wait_ge(c.sV.h, nvq[fc])
            for dc in range(DC):
                pb = c.PB[2 + dc % 4]
                if fc >= 1 and dc < 4:
                    tensor.wait_ge(c.sA.h, nst[fc - 1] - 8 + dc + 4 + 1)
                if dc >= 4:
                    tensor.wait_ge(c.sA.h, a_tr_base + (dc - 4) + 1)
                c.sT.inc(tensor.transpose(pb[:, 0:128],
                                          WQ2[:, dc * 128:(dc + 1) * 128],
                                          c.ident_t[:]))

        ntr[fc] = c.sT.n
        t_tr0 = ntr[fc] - DC

        @block.scalar
        def _(scalar, fc=fc, t_tr0=t_tr0):
            for dc in range(DC):
                pb = c.PB[2 + dc % 4]
                scalar.wait_ge(c.sT.h, t_tr0 + dc + 1)
                if dc == 0:
                    if fc >= 2:
                        scalar.wait_ge(c.sT.h, nmm[(fc - 2, 3)])
                    elif nmm_carry:
                        scalar.wait_ge(c.sT.h, nmm_carry[fc])
                c.sA.inc(scalar.activation(STA[fc % 2][:, dc * 128:(dc + 1) * 128],
                                           pb[0:128, 0:128], AF.Copy,
                                           bias=0.0, scale=1.0))

        nst[fc] = c.sA.n

        for tq in range(4):
            g = fc * 4 + tq
            bank = MMB[g % 4]

            @block.tensor
            def _(tensor, fc=fc, tq=tq, g=g, bank=bank):
                tensor.wait_ge(c.sA.h, nst[fc])
                if fc == 0 and tq == 0:
                    tensor.wait_ge(c.sA.h, c.l1_a_xq)
                if g >= 4:
                    tensor.wait_ge(c.sA.h, nev[g - 4])
                    tensor.wait_ge(c.sV.h, nmx[g - 4])
                for dc in range(DC):
                    c.sT.inc(tensor.matmul(
                        bank[:, :], STA[fc % 2][:, dc * 128:(dc + 1) * 128],
                        Bv(0, 32768)[:, dc * T + tq * 512: dc * T + (tq + 1) * 512],
                        start=(dc == 0), stop=(dc == DC - 1)))

            nmm[(fc, tq)] = c.sT.n

            @block.scalar
            def _(scalar, fc=fc, tq=tq, g=g, bank=bank):
                scalar.wait_ge(c.sT.h, nmm[(fc, tq)])
                if g >= 4:
                    scalar.wait_ge(*spt[g - 4])
                elif spt_carry:
                    scalar.wait_ge(*spt_carry[g])
                c.sA.inc(scalar.activation(EVS[g % 4], bank[:, :], AF.Copy,
                                           bias=0.0, scale=1.0))

            nev[g] = c.sA.n

            @block.vector
            def _(vector, fc=fc, tq=tq, g=g, bank=bank):
                vector.wait_ge(c.sT.h, nmm[(fc, tq)])
                c.sV.inc(vector.tensor_reduce(mxw[:, g:g + 1], bank[:, :],
                                              AX.X, ALU.max,
                                              apply_absolute_value=True))

            nmx[g] = c.sV.n

            @block.sync
            def _(sync, g=g):
                sync.wait_ge(c.sA.h, nev[g])
                spt[g] = c.chO.issue(sync, dst_dram.ap()[g], EVS[g % 4])

    c.l1_spt_carry = (spt[92], spt[93], spt[94], spt[95])
    c.l1_nmm_carry = (nmm[(22, 3)], nmm[(23, 3)])


def _emit_l1_downprep(c):
    block = c.block
    Bv = lambda a, b: c.ARB[:, a:b].bitcast(F32)
    WSTD = [Bv(0, 6144), Bv(6144, 12288)]
    WSCD = Bv(12288, 18432)
    WSCDI = c.ARB[:, 12288:18432].bitcast(I32)
    WQ2D = Bv(18432, 24576)
    DSTT = [Bv(24576 + 256 * i, 24832 + 256 * i) for i in range(4)]
    BMX9 = c.ARF[:, 1144:1240]
    SBL9 = c.ARF[:, 1240:1336]
    RBL9 = c.ARF[:, 1336:1432]

    ldt = {}
    nvq = {}
    ntr = {}
    nev = {}
    wbt = {}
    aev = {}
    t0 = c.sT.n

    for ch in range(8):
        @block.sync
        def _(sync, ch=ch):
            if ch == 0:
                sync.wait_ge(c.sT.h, t0)
            if ch >= 2:
                sync.wait_ge(c.sV.h, nvq[ch - 2])
            ldt[ch] = c.chW.issue(sync, WSTD[ch % 2],
                                  c.w_down[ch * 128:(ch + 1) * 128, :])

        @block.vector
        def _(vector, ch=ch):
            vector.wait_ge(*ldt[ch])
            if ch >= 1:
                vector.wait_ge(c.sT.h, ntr[ch - 1])
            c.sV.inc(vector.tensor_reduce(
                BMX9, WSTD[ch % 2].rearrange("p (b k) -> p b k", k=32),
                AX.X, ALU.max, apply_absolute_value=True))
            vector.wait_ge(c.sV.h, c.sV.n)
            c.sV.inc(vector.tensor_scalar(SBL9, BMX9, 1.0 / 7.0, 1e-12,
                                          ALU.mult, ALU.max))
            vector.wait_ge(c.sV.h, c.sV.n)
            c.sV.inc(vector.reciprocal(RBL9, SBL9))
            vector.wait_ge(c.sV.h, c.sV.n)
            if ch >= 1:
                vector.wait_ge(c.sA.h, nev[ch - 1])
            c.sV.inc(vector.tensor_tensor(
                WSCD.rearrange("p (b k) -> p b k", k=32),
                WSTD[ch % 2].rearrange("p (b k) -> p b k", k=32),
                RBL9.unsqueeze(2).broadcast_to([128, 96, 32]), ALU.mult))

        v_wsc = c.sV.n

        @block.scalar
        def _(scalar, ch=ch, v_wsc=v_wsc):
            scalar.wait_ge(c.sV.h, v_wsc)
            c.sA.inc(scalar.activation(WSCDI, WSCD, AF.Copy, bias=0.0, scale=1.0))
            scalar.wait_ge(c.sA.h, c.sA.n)
            c.sA.inc(scalar.activation(WSCD, WSCDI, AF.Copy, bias=0.0, scale=1.0))

        a_wq = c.sA.n

        @block.vector
        def _(vector, ch=ch, a_wq=a_wq):
            vector.wait_ge(c.sA.h, a_wq)
            c.sV.inc(vector.tensor_tensor(
                WQ2D.rearrange("p (b k) -> p b k", k=32),
                WSCD.rearrange("p (b k) -> p b k", k=32),
                SBL9.unsqueeze(2).broadcast_to([128, 96, 32]), ALU.mult))

        nvq[ch] = c.sV.n

        for fc in range(24):
            i = ch * 24 + fc

            @block.tensor
            def _(tensor, ch=ch, fc=fc, i=i):
                if fc == 0:
                    tensor.wait_ge(c.sV.h, nvq[ch])
                if i >= 4:
                    tensor.wait_ge(c.sA.h, aev[i - 4])
                c.sT.inc(tensor.transpose(c.PB[2 + i % 4][:, 0:128],
                                          WQ2D[:, fc * 128:(fc + 1) * 128],
                                          c.ident_t[:]))

            t_tr = c.sT.n

            @block.scalar
            def _(scalar, ch=ch, fc=fc, i=i, t_tr=t_tr):
                scalar.wait_ge(c.sT.h, t_tr)
                if i >= 4:
                    scalar.wait_ge(*wbt[i - 4])
                c.sA.inc(scalar.activation(DSTT[i % 4],
                                           c.PB[2 + i % 4][0:128, 0:128],
                                           AF.Copy, bias=0.0, scale=1.0))
                aev[i] = c.sA.n
                scalar.wait_ge(c.sA.h, aev[i])
                wbt[i] = c.chQ.issue(
                    scalar, c.wdt_dram.ap()[fc, :, ch * 128:(ch + 1) * 128],
                    DSTT[i % 4])

        ntr[ch] = c.sT.n
        nev[ch] = c.sA.n


def _emit_l1_sig(c, MXW3):
    block = c.block
    Bv = lambda a, b: c.ARB[:, a:b].bitcast(F32)
    W = 6
    NBATCH = 96 // W
    GIN = [Bv(25600, 31744), Bv(31744, 37888)]
    GINI = [c.ARB[:, 25600:31744].bitcast(I32),
            c.ARB[:, 31744:37888].bitcast(I32)]
    GQF = [Bv(37888, 44032), Bv(44032, 50176)]
    sg, rg = c.ARF[:, 1027:1028], c.ARF[:, 1028:1029]

    v_ar2 = c.sV.n
    ldt = {}
    na = {}
    nv = {}
    spt = {}

    @block.sync
    def _(sync):
        c.chO.drain(sync)

    def tiles(b):
        return c.gate_dram.ap()[b * W:(b + 1) * W].transpose([1, 0, 2])

    for b in range(NBATCH):
        @block.sync
        def _(sync, b=b):
            if b >= 2:
                sync.wait_ge(*spt[b - 2])
            ldt[b] = c.chW.issue(sync, GIN[b % 2], tiles(b))

        @block.scalar
        def _(scalar, b=b, a_prev=c.sA.n):
            scalar.wait_ge(*ldt[b])
            scalar.wait_ge(c.sA.h, a_prev)
            if b == 0:
                scalar.wait_ge(c.sV.h, v_ar2)
            if b >= 2:
                scalar.wait_ge(c.sV.h, nv[b - 2])
            a1 = c.sA.n + 1
            c.sA.inc(scalar.activation(GINI[b % 2], GIN[b % 2], AF.Copy,
                                       bias=0.0, scale=rg))
            scalar.wait_ge(c.sA.h, c.sA.n)
            c.sA.inc(scalar.activation(GQF[b % 2], GINI[b % 2], AF.Copy,
                                       bias=0.0, scale=sg))
            scalar.wait_ge(c.sA.h, c.sA.n)
            c.sA.inc(scalar.activation(GIN[b % 2], GQF[b % 2], AF.Sigmoid,
                                       bias=0.0, scale=1.0))
            scalar.wait_ge(c.sA.h, c.sA.n)
            c.sA.inc(scalar.activation(GINI[b % 2], GIN[b % 2], AF.Copy,
                                       bias=0.0, scale=65536.0))
            scalar.wait_ge(c.sA.h, c.sA.n)
            c.sA.inc(scalar.activation(GIN[b % 2], GINI[b % 2], AF.Copy,
                                       bias=0.0, scale=1.0))
            na[b] = (a1, c.sA.n)

        @block.vector
        def _(vector, b=b):
            vector.wait_ge(c.sA.h, na[b][1])
            c.sV.inc(vector.tensor_scalar(GIN[b % 2], GIN[b % 2], 65535.0,
                                          1.0 / 65536.0, ALU.min, ALU.mult))
            vector.wait_ge(c.sV.h, c.sV.n)
            c.sV.inc(vector.tensor_mul(GIN[b % 2], GQF[b % 2], GIN[b % 2]))
            vector.wait_ge(c.sV.h, c.sV.n)
            c.sV.inc(vector.tensor_reduce(MXW3[:, b:b + 1], GIN[b % 2],
                                          AX.X, ALU.max, apply_absolute_value=True))
            nv[b] = c.sV.n

        @block.sync
        def _(sync, b=b):
            sync.wait_ge(c.sV.h, nv[b])
            spt[b] = c.chO.issue(sync, tiles(b), GIN[b % 2])


def _emit_l1_prod(c, MXW4):
    block = c.block
    Bv = lambda a, b: c.ARB[:, a:b].bitcast(F32)
    W = 6
    NBATCH = 96 // W
    GSI = [Bv(25600, 31744), Bv(31744, 37888)]
    GSII = [c.ARB[:, 25600:31744].bitcast(I32),
            c.ARB[:, 31744:37888].bitcast(I32)]
    UIN = [Bv(37888, 44032), Bv(44032, 50176)]
    UINI = [c.ARB[:, 37888:44032].bitcast(I32),
            c.ARB[:, 44032:50176].bitcast(I32)]
    su, ru = c.ARF[:, 1029:1030], c.ARF[:, 1030:1031]
    s3, r3 = c.ARF[:, 1031:1032], c.ARF[:, 1032:1033]

    v_ar3 = c.sV.n
    ldt = {}
    na = {}
    nv = {}
    spt = {}

    @block.sync
    def _(sync):
        c.chO.drain(sync)

    def gtiles(b):
        return c.gate_dram.ap()[b * W:(b + 1) * W].transpose([1, 0, 2])

    def utiles(b):
        return c.up_dram.ap()[b * W:(b + 1) * W].transpose([1, 0, 2])

    for b in range(NBATCH):
        @block.sync
        def _(sync, b=b):
            if b >= 2:
                sync.wait_ge(*spt[b - 2])
                sync.wait_ge(c.sV.h, nv[b - 2])
            ldt[b] = (c.chW.issue(sync, GSI[b % 2], gtiles(b)),
                      c.chL.issue(sync, UIN[b % 2], utiles(b)))

        @block.scalar
        def _(scalar, b=b, a_prev=c.sA.n):
            scalar.wait_ge(*ldt[b][0])
            scalar.wait_ge(*ldt[b][1])
            scalar.wait_ge(c.sA.h, a_prev)
            if b == 0:
                scalar.wait_ge(c.sV.h, v_ar3)
            c.sA.inc(scalar.activation(GSII[b % 2], GSI[b % 2], AF.Copy,
                                       bias=0.0, scale=r3))
            scalar.wait_ge(c.sA.h, c.sA.n)
            c.sA.inc(scalar.activation(GSI[b % 2], GSII[b % 2], AF.Copy,
                                       bias=0.0, scale=s3))
            scalar.wait_ge(c.sA.h, c.sA.n)
            c.sA.inc(scalar.activation(UINI[b % 2], UIN[b % 2], AF.Copy,
                                       bias=0.0, scale=ru))
            scalar.wait_ge(c.sA.h, c.sA.n)
            c.sA.inc(scalar.activation(UIN[b % 2], UINI[b % 2], AF.Copy,
                                       bias=0.0, scale=su))
            na[b] = c.sA.n

        @block.vector
        def _(vector, b=b):
            vector.wait_ge(c.sA.h, na[b])
            c.sV.inc(vector.tensor_mul(GSI[b % 2], GSI[b % 2], UIN[b % 2]))
            vector.wait_ge(c.sV.h, c.sV.n)
            c.sV.inc(vector.tensor_reduce(MXW4[:, b:b + 1], GSI[b % 2],
                                          AX.X, ALU.max, apply_absolute_value=True))
            nv[b] = c.sV.n

        @block.sync
        def _(sync, b=b):
            sync.wait_ge(c.sV.h, nv[b])
            spt[b] = c.chO.issue(sync, utiles(b), GSI[b % 2])


def _emit_l1_down(c):
    block = c.block
    Bv = lambda a, b: c.ARB[:, a:b].bitcast(F32)
    OF = Bv(0, 24576)
    PIN = [Bv(24576 + 1024 * i, 25600 + 1024 * i) for i in range(4)]
    OI = c.ARB[:, 28672:29696].bitcast(I32)
    STD = [Bv(45056, 47104), Bv(47104, 49152)]
    s4, r4 = c.ARF[:, 1033:1034], c.ARF[:, 1034:1035]

    v_ar4 = c.sV.n

    @block.sync
    def _(sync):
        c.chO.drain(sync)
        for sm in c.chQ.sems:
            sync.wait_ge(sm.h, sm.n)

    t_dn_prev = c.sT.n
    a_prev_end = c.sA.n

    for tq in range(4):
        ldt = {}
        stt = {}
        na = {}
        nmm = {}
        v_res_prev = c.sV.n

        for fc in range(24):
            @block.sync
            def _(sync, fc=fc, tq=tq, t_dn_prev=t_dn_prev, a_prev_end=a_prev_end):
                if fc >= 4:
                    sync.wait_ge(c.sA.h, na[fc - 4])
                elif tq > 0:
                    sync.wait_ge(c.sA.h, a_prev_end)
                ldt[fc] = c.chW.issue(sync, PIN[fc % 4],
                                      c.up_dram.ap()[fc * 4 + tq])
                if fc >= 2:
                    sync.wait_ge(c.sT.h, nmm[fc - 2])
                elif tq > 0:
                    sync.wait_ge(c.sT.h, t_dn_prev)
                stt[fc] = c.chL.issue(sync, STD[fc % 2], c.wdt_dram.ap()[fc])

            @block.scalar
            def _(scalar, fc=fc, tq=tq, t_dn_prev=t_dn_prev, a_prev=c.sA.n):
                scalar.wait_ge(*ldt[fc])
                scalar.wait_ge(c.sA.h, a_prev)
                if fc == 0 and tq == 0:
                    scalar.wait_ge(c.sV.h, v_ar4)
                if fc == 0 and tq > 0:
                    scalar.wait_ge(c.sT.h, t_dn_prev)
                c.sA.inc(scalar.activation(OI, PIN[fc % 4], AF.Copy,
                                           bias=0.0, scale=r4))
                scalar.wait_ge(c.sA.h, c.sA.n)
                c.sA.inc(scalar.activation(OF[:, fc * 512:(fc + 1) * 512], OI,
                                           AF.Copy, bias=0.0, scale=s4))
                na[fc] = c.sA.n

            @block.tensor
            def _(tensor, fc=fc, tq=tq, v_res_prev=v_res_prev):
                tensor.wait_ge(*stt[fc])
                tensor.wait_ge(c.sA.h, na[fc])
                if fc == 0:
                    tensor.wait_ge(c.sV.h, v_res_prev)
                for dc in range(DC):
                    c.sT.inc(tensor.matmul(
                        c.PB[dc][:, :], STD[fc % 2][:, dc * 128:(dc + 1) * 128],
                        OF[:, fc * 512:(fc + 1) * 512],
                        start=(fc == 0), stop=(fc == 23)))
                nmm[fc] = c.sT.n

        t_dn = c.sT.n

        @block.vector
        def _(vector, tq=tq, t_dn=t_dn):
            vector.wait_ge(c.sT.h, t_dn)
            for dc in range(DC):
                dst = c.res[:, dc * T + tq * 512: dc * T + (tq + 1) * 512]
                c.sV.inc(vector.tensor_add(dst, dst, c.PB[dc][:, :]))

        t_dn_prev = t_dn
        a_prev_end = c.sA.n


def emit_ph6_store(c):
    block = c.block
    t0, a0 = c.sT.n, c.sA.n
    tok = {}

    @block.tensor
    def _(tensor):
        _wait_all(tensor, c)
        for tt in range(TT):
            for dc in range(DC):
                pb = c.PB[dc]
                if tt >= 1:
                    tensor.wait_ge(c.sA.h, a0 + (tt - 1) * DC + dc + 1)
                c.sT.inc(tensor.transpose(
                    pb[:, 0:128],
                    c.res[:, dc * T + tt * 128: dc * T + tt * 128 + 128],
                    c.ident_t[:]))

    @block.sync
    def _(sync):
        for tt in range(TT):
            buf = c.stage[:, (tt % 2) * D:(tt % 2) * D + D]
            sync.wait_ge(c.sA.h, a0 + (tt + 1) * DC)
            tok[tt] = c.chO.issue(sync, c.out[tt * 128:(tt + 1) * 128, :], buf)
        c.chO.drain(sync)

    @block.scalar
    def _(scalar):
        for tt in range(TT):
            buf = c.stage[:, (tt % 2) * D:(tt % 2) * D + D]
            if tt >= 2:
                scalar.wait_ge(*tok[tt - 2])
            for dc in range(DC):
                scalar.wait_ge(c.sT.h, t0 + tt * DC + dc + 1)
                c.sA.inc(scalar.activation(
                    buf[:, dc * 128:(dc + 1) * 128],
                    c.PB[dc][:, 0:128], AF.Copy, bias=0.0, scale=1.0))


def build(stop_after=None, ar_group=None):
    nc = bass.Bass("TRN2", target_bir_lowering=False, dynamic_dma_scratch_size=2048)
    c = Ctx()
    c.nc = nc

    c.x_in = nc.dram_tensor("x_in", [T, D], F32, kind="ExternalInput").ap()
    c.out = nc.dram_tensor("out", [T, D], F32, kind="ExternalOutput").ap()
    c.ident = nc.dram_tensor("ident", [128, 128], F32, kind="ExternalInput").ap()
    c.ones_in = nc.dram_tensor("ones_in", [128, 128], F32, kind="ExternalInput").ap()
    c.ropeA_in = nc.dram_tensor("ropeA_in", [128, T], F32, kind="ExternalInput").ap()
    c.ropeB_in = nc.dram_tensor("ropeB_in", [128, T], F32, kind="ExternalInput").ap()
    c.masks_in = nc.dram_tensor("masks_in", [128, 896], BF16, kind="ExternalInput").ap()
    c.bias_in = nc.dram_tensor("bias_in", [128, 32], F32, kind="ExternalInput").ap()
    c.w_q = nc.dram_tensor("w_q", [D, D], F32, kind="ExternalInput").ap()
    c.w_k = nc.dram_tensor("w_k", [D, D], F32, kind="ExternalInput").ap()
    c.w_v = nc.dram_tensor("w_v", [D, D], F32, kind="ExternalInput").ap()
    c.w_o = nc.dram_tensor("w_o", [D, D], F32, kind="ExternalInput").ap()
    c.w1 = nc.dram_tensor("w1", [FF0, D], F32, kind="ExternalInput").ap()
    c.w2 = nc.dram_tensor("w2", [D, FF0], F32, kind="ExternalInput").ap()
    c.bias_ffn_in = nc.dram_tensor("bias_ffn_in", [128, 40], F32, kind="ExternalInput").ap()
    c.scal_dram = nc.dram_tensor("scal_dram", [1, 8], F32)
    c.wq1t_dram = nc.dram_tensor("wq1t_dram", [32, 1024, 128], BF16)
    c.wq2t_dram = nc.dram_tensor("wq2t_dram", [32, 128, 1024], BF16)
    c.bc_dram = nc.dram_tensor("bc_dram", [32, 512], F32)
    c.ln_dram = nc.dram_tensor("ln_dram", [2, 2048], F32)
    c.conv_k = nc.dram_tensor("conv_k", [3, 1, D], F32, kind="ExternalInput").ap()
    c.a_log = nc.dram_tensor("a_log", [D, N_SSM], F32, kind="ExternalInput").ap()
    c.w_bc = nc.dram_tensor("w_bc", [D, 2 * N_SSM], F32, kind="ExternalInput").ap()
    c.b_bc = nc.dram_tensor("b_bc", [2 * N_SSM], F32, kind="ExternalInput").ap()
    c.dcoeff = nc.dram_tensor("dcoeff", [D], F32, kind="ExternalInput").ap()
    c.w_dt = nc.dram_tensor("w_dt", [D, DT_RANK], F32, kind="ExternalInput").ap()
    c.b_dt = nc.dram_tensor("b_dt", [DT_RANK], F32, kind="ExternalInput").ap()
    c.w_dtp = nc.dram_tensor("w_dtp", [DT_RANK, D], F32, kind="ExternalInput").ap()
    c.b_dtp = nc.dram_tensor("b_dtp", [D], F32, kind="ExternalInput").ap()
    c.w_gate = nc.dram_tensor("w_gate", [FF1, D], F32, kind="ExternalInput").ap()
    c.w_up = nc.dram_tensor("w_up", [FF1, D], F32, kind="ExternalInput").ap()
    c.w_down = nc.dram_tensor("w_down", [D, FF1], F32, kind="ExternalInput").ap()
    c.gate_dram = nc.dram_tensor("gate_dram", [96, 128, 512], F32)
    c.up_dram = nc.dram_tensor("up_dram", [96, 128, 512], F32)
    c.wdt_dram = nc.dram_tensor("wdt_dram", [24, 128, 1024], F32)
    c.cc_in = [nc.dram_tensor(f"cc_in{i}", [1, 2], F32) for i in range(4)]
    c.cc_out = [nc.dram_tensor(f"cc_out{i}", [1, 2], F32) for i in range(4)]
    c.ar_group = list(range(8)) if ar_group is None else ar_group

    ctx = ExitStack()
    with ctx:
        e = ctx.enter_context
        c.res = e(nc.sbuf_tensor("res", [128, DC * T], F32))
        c.rows = e(nc.sbuf_tensor("rows", [128, T], F32))
        c.ropeA = e(nc.sbuf_tensor("ropeA", [128, T], F32))
        c.ropeB = e(nc.sbuf_tensor("ropeB", [128, T], F32))
        c.ident_t = e(nc.sbuf_tensor("ident_t", [128, 128], F32))
        c.ones_f32 = e(nc.sbuf_tensor("ones_f32", [128, 128], F32))
        c.ones_bf = e(nc.sbuf_tensor("ones_bf", [128, 64], BF16))
        c.bias_qkv = e(nc.sbuf_tensor("bias_qkv", [128, 32], F32))
        c.bias_ffn = e(nc.sbuf_tensor("bias_ffn", [128, 40], F32))
        c.nrm = e(nc.sbuf_tensor("nrm", [1, 16], F32))
        c.ARF = e(nc.sbuf_tensor("ARF", [128, 5136], F32))
        c.ARB = e(nc.sbuf_tensor("ARB", [128, 57216], BF16))

        c.stage = c.ARF[:, 0:2048]
        c.sqbuf = c.ARF[:, 0:4096]
        c.tmp1 = c.ARF[:, 4096:4608]
        c.xfull = c.ARF[:, 4608:5120]
        c.AVS = c.ARF[:, 3072:5120]
        c.biasA = c.ARF[:, 5120:5121]
        c.biasB = c.ARF[:, 5121:5122]
        c.qtmp = c.ARF[:, 2048:2176]
        c.bbc_col = c.ropeA[64:96, 169:170]
        c.acol = c.ARF[:, 5126:5127]
        c.gcol = c.ARF[:, 5127:5128]

        def B(a, b):
            return c.ARB[:, a:b]
        c.XH = B(0, 16384)
        c.XL = B(16384, 32768)
        c.QH = B(32768, 34816)
        c.QL = B(34816, 36864)
        c.KH = B(36864, 38912)
        c.KL = B(38912, 40960)
        c.VSA = B(40960, 43008)
        c.VSB = B(43008, 45056)
        c.WTQH = B(45056, 46080)
        c.WTQL = B(46080, 47104)
        c.WTKH = B(47104, 48128)
        c.WTKL = B(48128, 49152)
        c.WTVH = B(49152, 50176)
        c.WTVL = B(50176, 51200)
        c.PHt = [B(51200, 52224), B(52224, 53248)]
        c.PLt = [B(53248, 54272), B(54272, 55296)]
        c.OH = B(55296, 55808)
        c.OL = B(55808, 56320)
        c.MASKS = B(56320, 57216)
        c.sbh = B(32768, 36864)
        c.sbl = B(36864, 40960)
        c.sqh = B(40960, 45056)
        c.sql = B(45056, 49152)

        c.PB = [e(nc.psum_tensor(f"pb{i}", [128, 512], F32)) for i in range(8)]

        c.sS = Sem(e(nc.semaphore("sS")))
        c.sG = Sem(e(nc.semaphore("sG")))
        c.sT = Sem(e(nc.semaphore("sT")))
        c.sV = Sem(e(nc.semaphore("sV")))
        c.sA = Sem(e(nc.semaphore("sA")))

        c.block = e(nc.Block())

        c.chL = DmaChan(nc, ctx, "chL", 4)
        c.chO = DmaChan(nc, ctx, "chO", 4)
        c.chW = DmaChan(nc, ctx, "chW", 4)
        c.chC = DmaChan(nc, ctx, "chC", 2)
        c.chQ = DmaChan(nc, ctx, "chQ", 4)
        c.wtok = {}

        @c.block.sync
        def _(sync):
            c.tok_ident = c.chL.issue(sync, c.ident_t[:], c.ident)
            c.tok_ones = c.chL.issue(sync, c.ones_f32[:], c.ones_in)
            c.tok_ropeA = c.chL.issue(sync, c.ropeA[:], c.ropeA_in)
            c.tok_ropeB = c.chL.issue(sync, c.ropeB[:], c.ropeB_in)
            c.tok_masks = c.chL.issue(sync, c.MASKS, c.masks_in)
            c.tok_bias_in = c.chL.issue(sync, c.bias_qkv[:], c.bias_in)
            c.tok_bias_ffn = c.chL.issue(sync, c.bias_ffn[:], c.bias_ffn_in)

        @c.block.vector
        def _(vector):
            vector.wait_ge(*c.tok_ones)
            c.sV.inc(vector.tensor_copy(c.ones_bf[:, 0:64], c.ones_f32[:, 0:64]))

        emit_ph0_load(c)

        @c.block.tensor
        def _(tensor):
            tensor.wait_ge(*c.tok_ones)
            tensor.wait_ge(*c.tok_ropeA)
            tensor.wait_ge(*c.tok_ropeB)
            tensor.wait_ge(*c.tok_masks)
            tensor.wait_ge(*c.tok_bias_in)
            tensor.wait_ge(c.sV.h, c.sV.n)

        if stop_after != "ph0":
            emit_ln(c, c.XH, c.XL)
            if stop_after != "ln1":
                emit_l0_attn(c)
                if stop_after != "attn":
                    emit_ffn0(c)
                    if stop_after != "ffn0":
                        emit_ssm(c)
                        if stop_after != "ssm":
                            emit_l1ffn(c)

        emit_ph6_store(c)

    return nc


def _host_consts():
    ident = np.eye(128, dtype=np.float32)
    ones = np.ones((128, 128), dtype=np.float32)
    half = DK // 2
    freq = (1.0 / 10000.0 ** (2.0 * np.arange(half, dtype=np.float64) / DK))
    ang = np.arange(T, dtype=np.float64)[:, None] * freq[None, :]
    cos = np.cos(ang).astype(np.float32)
    sin = np.sin(ang).astype(np.float32)
    ropeA = np.zeros((128, T), dtype=np.float32)
    ropeB = np.zeros((128, T), dtype=np.float32)
    for j in range(64):
        i = j // 2
        ropeA[j] = cos[:, i]
        ropeB[j] = (np.float32(-1.0) if j % 2 == 0 else np.float32(1.0)) * sin[:, i]
    ropeA[64:128] = ropeA[0:64]
    ropeB[64:128] = ropeB[0:64]
    import ml_dtypes
    k_rel = np.arange(128)[:, None]
    u = np.arange(896)[None, :]
    masks = np.where(k_rel > u - 384, -1e30, 0.0).astype(ml_dtypes.bfloat16)
    return ident, ones, ropeA, ropeB, masks


_PERM = None

def _perm_idx():
    global _PERM
    if _PERM is None:
        j = np.arange(128)
        h = j // 64
        jj = j % 64
        _PERM = h * 64 + (jj % 2) * 32 + (jj // 2)
    return _PERM


def _bias_tile(inputs):
    bias = np.zeros((128, 32), dtype=np.float32)
    perm = _perm_idx()
    bq = np.asarray(inputs["L0_attn_WQ_b"], np.float32)
    bk = np.asarray(inputs["L0_attn_WK_b"], np.float32)
    bv = np.asarray(inputs["L0_attn_WV_b"], np.float32)
    bo = np.asarray(inputs["L0_attn_Wo_b"], np.float32)
    for p in range(8):
        bias[:, p] = bq[p * 128 + perm]
        bias[:, 8 + p] = bk[p * 128 + perm]
        bias[:, 16 + p] = bv[p * 128:(p + 1) * 128]
        bias[:, 24 + p] = bo[p * 128:(p + 1) * 128]
    return bias


_NC_CACHE = None


def _get_nc():
    global _NC_CACHE
    if _NC_CACHE is None:
        _NC_CACHE = build()
    return _NC_CACHE


def kernel(**inputs):
    nc = _get_nc()
    x = np.ascontiguousarray(np.asarray(inputs["L0_attn_x"], dtype=np.float32))
    ident, ones, ropeA, ropeB, masks = _host_consts()
    bias = _bias_tile(inputs)
    wq = np.ascontiguousarray(np.asarray(inputs["L0_attn_WQ_w"], np.float32))
    wk = np.ascontiguousarray(np.asarray(inputs["L0_attn_WK_w"], np.float32))
    wv = np.ascontiguousarray(np.asarray(inputs["L0_attn_WV_w"], np.float32))
    wo = np.ascontiguousarray(np.asarray(inputs["L0_attn_Wo_w"], np.float32))
    w1 = np.ascontiguousarray(np.asarray(inputs["L0_ffn_W1"], np.float32))
    w2 = np.ascontiguousarray(np.asarray(inputs["L0_ffn_W2"], np.float32))
    bias_ffn = np.zeros((128, 40), dtype=np.float32)
    b1 = np.asarray(inputs["L0_ffn_b1"], np.float32)
    b2 = np.asarray(inputs["L0_ffn_b2"], np.float32)
    for fc in range(32):
        bias_ffn[:, fc] = b1[fc * 128:(fc + 1) * 128]
    for dco in range(8):
        bias_ffn[:, 32 + dco] = b2[dco * 128:(dco + 1) * 128]
    in_maps = []
    for b in range(8):
        m = {"x_in": np.ascontiguousarray(x[b]), "ident": ident, "ones_in": ones,
             "ropeA_in": ropeA, "ropeB_in": ropeB, "masks_in": masks,
             "bias_in": bias, "w_q": wq, "w_k": wk, "w_v": wv, "w_o": wo,
             "w1": w1, "w2": w2, "bias_ffn_in": bias_ffn,
             "conv_k": np.ascontiguousarray(np.asarray(inputs["L1_attn_conv_kernel"], np.float32)),
             "a_log": np.ascontiguousarray(np.asarray(inputs["L1_attn_A_log"], np.float32)),
             "w_bc": np.ascontiguousarray(np.asarray(inputs["L1_attn_W_bc"], np.float32)),
             "b_bc": np.ascontiguousarray(np.asarray(inputs["L1_attn_b_bc"], np.float32)),
             "dcoeff": np.ascontiguousarray(np.asarray(inputs["L1_attn_Dcoeff"], np.float32)),
             "w_dt": np.ascontiguousarray(np.asarray(inputs["L1_attn_W_dt"], np.float32)),
             "b_dt": np.ascontiguousarray(np.asarray(inputs["L1_attn_b_dt"], np.float32)),
             "w_dtp": np.ascontiguousarray(np.asarray(inputs["L1_attn_W_dtp"], np.float32)),
             "b_dtp": np.ascontiguousarray(np.asarray(inputs["L1_attn_b_dtp"], np.float32)),
             "w_gate": np.ascontiguousarray(np.asarray(inputs["L1_ffn_w_gate"], np.float32)),
             "w_up": np.ascontiguousarray(np.asarray(inputs["L1_ffn_w_up"], np.float32)),
             "w_down": np.ascontiguousarray(np.asarray(inputs["L1_ffn_w_down"], np.float32))}
        in_maps.append(m)
    res = run_bass_kernel_spmd(nc, in_maps, core_ids=list(range(8)))
    out = np.stack([np.asarray(res.results[b]["out"], dtype=np.float32) for b in range(8)], axis=0)
    return out
